# version 05
# baseline (speedup 1.0000x reference)
import os
import sys

for _p in ("/opt/trn_rl_repo",):
    if os.path.isdir(_p) and _p not in sys.path:
        sys.path.insert(0, _p)

import numpy as np
import ml_dtypes
from contextlib import ExitStack

import concourse.bass as bass
import concourse.mybir as mybir
from concourse.bass_utils import run_bass_kernel_spmd

F32 = mybir.dt.float32
BF16 = mybir.dt.bfloat16
AF = mybir.ActivationFunctionType
ALU = mybir.AluOpType
AX = mybir.AxisListType

NCORES = 8
T = 8192
D = 2048
NJ = D // 128
NCH = T // 128
TOWN = T // NCORES
NOC = TOWN // 128

AH = 2
ADK = 128
AM = 64
FQH = 4
FHD = 64
NE = 16
EH = 2048
NH = EH // 128

LN_EPS = 1e-5
CLAMP = 32.0
ATTN_ROPE_BASE = 10000.0
FFN_ROPE_BASE = 150000.0
ASCALE = float(ADK) ** -0.5
FSCALE = float(FHD) ** -0.5


class S:

    def __init__(self, ctx, nc, name):
        self.h = ctx.enter_context(nc.semaphore(name))
        self.n = 0

    def inc(self, instr, k=1):
        instr.then_inc(self.h, k)
        self.n += k
        return self.n


class Eng:

    def __init__(self, eng, sem):
        self.e = eng
        self.s = sem

    def do(self, instr):
        return self.s.inc(instr)

    def wait(self, sem, val):
        if val > 0:
            self.e.wait_ge(sem.h, val)


def _rope_tables_attn(tlen, dk, base):
    inv = (1.0 / base ** (np.arange(0, dk, 2, dtype=np.float32) / dk)).astype(np.float32)
    ang = (np.arange(tlen, dtype=np.float32)[:, None] * inv[None, :]).astype(np.float32)
    return np.cos(ang).astype(np.float32), np.sin(ang).astype(np.float32)


def _rope_tables_ffn(tlen, hd, base):
    half = hd // 2
    inv = (1.0 / base ** (np.arange(half, dtype=np.float32) / half)).astype(np.float32)
    ang = (np.arange(tlen, dtype=np.float32)[:, None] * inv[None, :]).astype(np.float32)
    return np.cos(ang).astype(np.float32), np.sin(ang).astype(np.float32)


def build_program(debug_taps=(), phases=("ln",), fake_cc=False):
    nc = bass.Bass("TRN2", target_bir_lowering=False)
    taps = set(debug_taps)
    ph = set(phases)
    ccu = 16 if fake_cc else 1

    x_full = nc.dram_tensor("x_full", [T, D], F32, kind="ExternalInput").ap()
    x_own = nc.dram_tensor("x_own", [TOWN, D], F32, kind="ExternalInput").ap()
    aWq = nc.dram_tensor("aWq", [D, AH * ADK], F32, kind="ExternalInput").ap()
    aWk = nc.dram_tensor("aWk", [D, AH * ADK], F32, kind="ExternalInput").ap()
    aWv = nc.dram_tensor("aWv", [D, AH * ADK], F32, kind="ExternalInput").ap()
    aWg = nc.dram_tensor("aWg", [D, AH * ADK], F32, kind="ExternalInput").ap()
    aWo = nc.dram_tensor("aWo", [AH * ADK, D], F32, kind="ExternalInput").ap()
    aWs1 = nc.dram_tensor("aWs1", [D, 16], F32, kind="ExternalInput").ap()
    aWs2 = nc.dram_tensor("aWs2", [17, AH * AM], F32, kind="ExternalInput").ap()
    gnw = nc.dram_tensor("gnw", [1, ADK], F32, kind="ExternalInput").ap()
    lnw = nc.dram_tensor("lnw", [1, D], F32, kind="ExternalInput").ap()
    flnw = nc.dram_tensor("flnw", [1, D], F32, kind="ExternalInput").ap()
    fWq = nc.dram_tensor("fWq", [D, FQH * FHD], F32, kind="ExternalInput").ap()
    fWk = nc.dram_tensor("fWk", [D, FHD], F32, kind="ExternalInput").ap()
    fWv = nc.dram_tensor("fWv", [D, FHD], F32, kind="ExternalInput").ap()
    fWo = nc.dram_tensor("fWo", [FQH * FHD, D], F32, kind="ExternalInput").ap()
    fWg = nc.dram_tensor("fWg", [D, NE], F32, kind="ExternalInput").ap()
    w1 = nc.dram_tensor("w1", [NE * NH, 128, NJ * 128], F32, kind="ExternalInput").ap()
    w3 = nc.dram_tensor("w3", [NE * NH, 128, NJ * 128], F32, kind="ExternalInput").ap()
    w2 = nc.dram_tensor("w2", [NE * NJ, 128, NH * 128], F32, kind="ExternalInput").ap()

    o_own = nc.dram_tensor("o_own", [TOWN, D], F32, kind="ExternalOutput").ap()

    ident_bf = nc.inline_tensor(np.eye(128, dtype=ml_dtypes.bfloat16), "ident_bf").ap()
    ident_f32 = nc.inline_tensor(np.eye(128, dtype=np.float32), "ident_f32").ap()
    utri_np = np.triu(np.ones((128, 128), dtype=np.float32)).astype(ml_dtypes.bfloat16)
    utri = nc.inline_tensor(utri_np, "utri").ap()
    ca, sa_ = _rope_tables_attn(T, ADK, ATTN_ROPE_BASE)
    cf, sf_ = _rope_tables_ffn(T, FHD, FFN_ROPE_BASE)
    ca4 = np.tile(ca, (1, 4))
    sa4 = np.tile(sa_, (1, 4))
    cf5 = np.tile(cf, (1, 5))
    sf5 = np.tile(sf_, (1, 5))
    cos_a4 = nc.inline_tensor(ca4.astype(ml_dtypes.bfloat16), "cos_a4").ap()
    sin_a4 = nc.inline_tensor(sa4.astype(ml_dtypes.bfloat16), "sin_a4").ap()
    cos_f5 = nc.inline_tensor(cf5.astype(ml_dtypes.bfloat16), "cos_f5").ap()
    sin_f5 = nc.inline_tensor(sf5.astype(ml_dtypes.bfloat16), "sin_f5").ap()
    dm = np.zeros((128, 4 * 512), dtype=np.float32)
    for di, delta in enumerate((0, 128, 256, 384)):
        dm[:, di * 512:(di + 1) * 512] = (np.arange(128)[:, None] + delta <= np.arange(512)[None, :])
    dmask = nc.inline_tensor(dm.astype(ml_dtypes.bfloat16), "dmask").ap()
    selnp = np.zeros((16, 16 * 128), dtype=np.float32)
    for e in range(16):
        selnp[e, e * 128:(e + 1) * 128] = 1.0
    sel = nc.inline_tensor(selnp, "sel").ap()
    ones_r128_np = np.ones((1, 128), dtype=np.float32)
    ones_r128 = nc.inline_tensor(ones_r128_np, "ones_r128").ap()
    ones_r64 = nc.inline_tensor(np.ones((1, 64), dtype=np.float32), "ones_r64").ap()
    ones_c128bf = nc.inline_tensor(np.ones((128, 1), dtype=ml_dtypes.bfloat16), "ones_c128bf").ap()

    d_xnfm = nc.dram_tensor("d_xnfm", [T // 512, 128, NJ * 512], BF16).ap()
    d_rr = nc.dram_tensor("d_rr", [1, T], F32).ap()
    d_xn32 = nc.dram_tensor("d_xn32", [T, D], F32).ap()
    d_att = nc.dram_tensor("d_att", [T, 1152], BF16).ap()
    cc_f_in = nc.dram_tensor("cc_f_in", [T, D], F32).ap()
    cc_f_out = nc.dram_tensor("cc_f_out", [TOWN, D], F32).ap()
    cc_a_in = nc.dram_tensor("cc_a_in", [T, D], F32).ap()
    cc_a_out = nc.dram_tensor("cc_a_out", [TOWN, D], F32).ap()

    dbg = {}

    def tap(name, shape, dtype=F32):
        if name in taps:
            dbg[name] = nc.dram_tensor("dbg_" + name, shape, dtype, kind="ExternalOutput").ap()
            return dbg[name]
        return None

    t_xn = tap("xn", [NJ, 128, T], BF16)
    t_rr = tap("rr", [1, T])
    t_ffnq = tap("ffnq", [4, 64, T], BF16)
    t_ffnk = tap("ffnk", [64, T], BF16)
    t_ffnv = tap("ffnv", [128, NCH * 65], BF16)
    t_mhat = tap("mhat", [1, 8])
    t_flash = tap("flash", [2, 128, T], BF16)
    t_ffnpart = tap("ffnpart", [T, D])
    t_att = tap("att", [T, 1152], BF16)
    t_og = tap("og", [2, 128, T], BF16)
    t_attpart = tap("attpart", [T, D])
    t_x2 = tap("x2", [128, NJ * TOWN])
    t_h2 = tap("h2", [128, NJ * TOWN], BF16)
    t_gate = tap("gate", [16, TOWN])
    t_moe = tap("moe", [128, NJ * TOWN])

    with ExitStack() as ctx:
        E = ctx.enter_context

        s_idbf = E(nc.sbuf_tensor("s_idbf", [128, 128], BF16))
        s_idf32 = E(nc.sbuf_tensor("s_idf32", [128, 128], F32))
        s_utri = E(nc.sbuf_tensor("s_utri", [128, 128], BF16))
        s_ones_r128 = E(nc.sbuf_tensor("s_ones_r128", [1, 128], F32))
        s_ones_r64 = E(nc.sbuf_tensor("s_ones_r64", [1, 64], F32))
        s_ones_c128bf = E(nc.sbuf_tensor("s_ones_c128bf", [128, 1], BF16))
        s_lnw = E(nc.sbuf_tensor("s_lnw", [128, NJ], F32))
        s_flnw = E(nc.sbuf_tensor("s_flnw", [128, NJ], F32))
        s_gnwb = E(nc.sbuf_tensor("s_gnwb", [128, 128], F32))
        s_eps = E(nc.sbuf_tensor("s_eps", [128, 1], F32))
        s_dmask = E(nc.sbuf_tensor("s_dmask", [128, 4 * 512], BF16))
        s_sel = E(nc.sbuf_tensor("s_sel", [16, 16 * 128], F32))

        sy = S(ctx, nc, "sy")
        st = S(ctx, nc, "st")
        gp = S(ctx, nc, "gp")
        gq = S(ctx, nc, "gq")
        mmS = S(ctx, nc, "mm")
        dvS = S(ctx, nc, "dv")
        scS = S(ctx, nc, "sc")
        plS = S(ctx, nc, "pl")
        ccS = S(ctx, nc, "cc")
        w13s = [S(ctx, nc, f"w13s{p}") for p in range(2)]
        w2s = [S(ctx, nc, f"w2s{p}") for p in range(2)]
        qs = [S(ctx, nc, f"qs{k}") for k in range(12)]

        te = Eng(nc.tensor, mmS)
        dv = Eng(nc.vector, dvS)
        sc = Eng(nc.scalar, scS)
        pl = Eng(nc.gpsimd, plS)

        def bar():
            nc.all_engine_barrier()

        dv.do(nc.vector.memset(s_eps[:], LN_EPS))
        sy.inc(nc.sync.dma_start(out=s_idbf[:], in_=ident_bf[:, :]), 16)
        sy.inc(nc.sync.dma_start(out=s_idf32[:], in_=ident_f32[:, :]), 16)
        sy.inc(nc.sync.dma_start(out=s_utri[:], in_=utri[:, :]), 16)
        sy.inc(nc.sync.dma_start(out=s_ones_r128[:], in_=ones_r128[:, :]), 16)
        sy.inc(nc.sync.dma_start(out=s_ones_r64[:], in_=ones_r64[:, :]), 16)
        sy.inc(nc.sync.dma_start(out=s_ones_c128bf[:], in_=ones_c128bf[:, :]), 16)
        sy.inc(nc.sync.dma_start(out=s_dmask[:], in_=dmask[:, :]), 16)
        sy.inc(nc.sync.dma_start(out=s_sel[:], in_=sel[:, :]), 16)
        lnw_src = bass.AP(lnw.tensor, 0, [[1, 128], [128, NJ]])
        flnw_src = bass.AP(flnw.tensor, 0, [[1, 128], [128, NJ]])
        sy.inc(nc.sync.dma_start(out=s_lnw[:], in_=lnw_src, allow_slow_non_contiguous=True), 16)
        sy.inc(nc.sync.dma_start(out=s_flnw[:], in_=flnw_src, allow_slow_non_contiguous=True), 16)
        gnw_row = E(nc.sbuf_tensor("gnw_row", [1, ADK], F32))
        sy.inc(nc.sync.dma_start(out=gnw_row[:], in_=gnw[:, :]), 16)
        te.wait(sy, sy.n)
        p_c = E(nc.psum_tensor("p_c", [128, 512], F32))
        vb = te.do(nc.tensor.matmul(p_c[:, 0:128], s_ones_r128[:, :], gnw_row[:, :], start=True, stop=True))
        dv.wait(mmS, vb)
        dv.do(nc.vector.tensor_copy(s_gnwb[:], p_c[:, 0:128]))
        bar()

        if "ln" in ph:
          with ExitStack() as lctx:
            LE = lctx.enter_context
            sx = [LE(nc.sbuf_tensor(f"sx{p}", [128, D], F32)) for p in range(2)]
            xn32 = [LE(nc.sbuf_tensor(f"xn32_{p}", [128, D], F32)) for p in range(2)]
            xnbf = [LE(nc.sbuf_tensor(f"xnbf_{p}", [128, D], BF16)) for p in range(2)]
            xnfm_st = [LE(nc.sbuf_tensor(f"xnfm_st{p}", [128, NJ * 512], BF16)) for p in range(2)]
            bnst = [LE(nc.sbuf_tensor(f"bnst{p}", [128, 24], F32)) for p in range(2)]
            bnag = [LE(nc.sbuf_tensor(f"bnag{p}", [128, 2], F32)) for p in range(2)]
            ln_sm = [LE(nc.sbuf_tensor(f"ln_sm{p}", [128, 4], F32)) for p in range(2)]
            p_trb = LE(nc.psum_tensor("p_trb", [128, 1024], BF16))
            p_trb2 = LE(nc.psum_tensor("p_trb2", [128, 1024], BF16))
            p_rr = LE(nc.psum_tensor("p_rr", [1, 128], F32))
            rr_sb = [LE(nc.sbuf_tensor(f"rr_sb{p}", [1, 128], F32)) for p in range(2)]

            store_done = [0, 0]
            rr_evac_prev = 0
            rr_store_prev = [0, 0]
            ldq = [qs[0], qs[1]]
            stq = [qs[2], qs[3]]
            xn32_done = [0, 0]
            for c in range(NCH):
                p = c % 2
                if xn32_done[p]:
                    nc.sync.wait_ge(qs[8 + p].h, xn32_done[p])
                vld = ldq[p].inc(nc.sync.dma_start(out=sx[p][:], in_=x_full[c * 128:(c + 1) * 128, :]), 16)
                dv.wait(ldq[p], vld)
                vbn = 0
                for k in range(4):
                    vbn = dv.do(nc.vector.bn_stats(bnst[p][:, k * 6:(k + 1) * 6], sx[p][:, k * 512:(k + 1) * 512]))
                dv.wait(dvS, vbn)
                vagg = dv.do(nc.vector.bn_aggr(bnag[p][:], bnst[p][:]))
                sc.wait(dvS, vagg)
                vs = sc.do(nc.scalar.activation(ln_sm[p][:, 1:2], bnag[p][:, 1:2], AF.Ln, bias=s_eps[:, 0:1]))
                sc.wait(scS, vs)
                vs_rstd = sc.do(nc.scalar.activation(ln_sm[p][:, 0:1], ln_sm[p][:, 1:2], AF.Exp, scale=-0.5))
                dv.wait(scS, vs_rstd)
                v1 = dv.do(nc.vector.tensor_tensor(ln_sm[p][:, 3:4], ln_sm[p][:, 0:1], ln_sm[p][:, 0:1], ALU.mult))
                dv.wait(dvS, v1)
                v2 = dv.do(nc.vector.tensor_tensor(ln_sm[p][:, 1:2], bnag[p][:, 1:2], ln_sm[p][:, 3:4], ALU.mult))
                sc.wait(dvS, v2)
                vs = sc.do(nc.scalar.activation(ln_sm[p][:, 2:3], ln_sm[p][:, 1:2], AF.Ln, bias=s_eps[:, 0:1]))
                sc.wait(scS, vs)
                vs_rr = sc.do(nc.scalar.activation(ln_sm[p][:, 2:3], ln_sm[p][:, 2:3], AF.Exp, scale=-0.5))
                dv.wait(scS, vs_rstd)
                vx = dv.do(nc.vector.tensor_scalar(xn32[p][:], sx[p][:], bnag[p][:, 0:1], ln_sm[p][:, 0:1], ALU.subtract, ALU.mult))
                dv.wait(dvS, vx)
                vxb = dv.do(nc.vector.tensor_copy(xnbf[p][:], xn32[p][:]))
                te.wait(scS, vs_rr)
                te.wait(dvS, rr_evac_prev)
                dv.wait(qs[7], qs[7].n)
                vtm = te.do(nc.tensor.transpose(p_rr[:, :], ln_sm[p][:, 2:3], s_idf32[:]))
                dv.wait(mmS, vtm)
                rr_evac_prev = dv.do(nc.vector.tensor_copy(rr_sb[p][:, :], p_rr[:, :]))
                nc.sync.wait_ge(dvS.h, rr_evac_prev)
                rr_store_prev[p] = qs[7].inc(nc.sync.dma_start(out=d_rr[:, c * 128:(c + 1) * 128], in_=rr_sb[p][:, :]), 16)
                te.wait(dvS, vxb)
                tp2 = (c // 4) % 2
                csub = c % 4
                vcp = 0
                for j in range(NJ):
                    pt = p_trb if j < 8 else p_trb2
                    tp = nc.tensor.transpose(pt[:, (j % 8) * 128:(j % 8 + 1) * 128], xnbf[p][:, j * 128:(j + 1) * 128], s_idbf[:])
                    if j % 8 == 7:
                        vt = te.do(tp)
                        dv.wait(mmS, vt)
                        if csub == 0 and j == 7:
                            nc.vector.wait_ge(stq[tp2].h, store_done[tp2])
                        for jj in range(j - 7, j + 1):
                            vcp = dv.do(nc.vector.tensor_copy(xnfm_st[tp2][:, jj * 512 + csub * 128:jj * 512 + (csub + 1) * 128], pt[:, (jj % 8) * 128:(jj % 8 + 1) * 128]))
                te.wait(dvS, vcp)
                nc.sync.wait_ge(dvS.h, vcp)
                if csub == 3:
                    stq[tp2].inc(nc.sync.dma_start(out=d_xnfm[c // 4, :, :], in_=xnfm_st[tp2][:]), 16)
                vstore = qs[8 + p].inc(nc.sync.dma_start(out=d_xn32[c * 128:(c + 1) * 128, :], in_=xn32[p][:]), 16)
                if t_xn is not None and csub == 3:
                    vstore = stq[tp2].inc(nc.sync.dma_start(out=bass.AP(t_xn.tensor, (c // 4) * 512, [[T, 128], [128 * T, NJ], [1, 512]]),
                                                            in_=xnfm_st[tp2][:].rearrange("q (a b) -> q a b", a=NJ)), 16)
                if csub == 3:
                    store_done[tp2] = stq[tp2].n
                xn32_done[p] = vstore

            nc.sync.wait_ge(sy.h, sy.n)
            nc.sync.wait_ge(stq[0].h, stq[0].n)
            nc.sync.wait_ge(stq[1].h, stq[1].n)
            nc.sync.wait_ge(qs[7].h, qs[7].n)
            nc.sync.wait_ge(qs[8].h, qs[8].n)
            nc.sync.wait_ge(qs[9].h, qs[9].n)
            bar()

        ffn_ctx = ExitStack()
        if "ffnproj" in ph or "flash" in ph:
            qres = ffn_ctx.enter_context(nc.sbuf_tensor("qres", [128, 4 * T], BF16))
            kres = ffn_ctx.enter_context(nc.sbuf_tensor("kres", [128, T], BF16))
            vres = ffn_ctx.enter_context(nc.sbuf_tensor("vres", [128, NCH * 65], BF16))
            mbias = ffn_ctx.enter_context(nc.sbuf_tensor("mbias", [128, 4], F32))
            ofm = ffn_ctx.enter_context(nc.sbuf_tensor("ofm", [128, 2 * T], BF16))
        if "ffnproj" in ph:
            with ExitStack() as fctx:
                FE = fctx.enter_context
                wqkv = FE(nc.sbuf_tensor("wqkv", [128, NJ * 384], BF16))
                tmpw = FE(nc.sbuf_tensor("tmpw", [128, 384], F32))
                xtl = [FE(nc.sbuf_tensor(f"xtl{p}", [128, NJ * 512], BF16)) for p in range(2)]
                stage = FE(nc.sbuf_tensor("stage", [128, 4 * 384], BF16))
                tmp1 = FE(nc.sbuf_tensor("tmp1", [128, 160], F32))
                tmp2 = FE(nc.sbuf_tensor("tmp2", [128, 160], F32))
                rrl = FE(nc.sbuf_tensor("rrl", [1, 512], F32))
                sqb = FE(nc.sbuf_tensor("sqb", [128, 320], F32))
                redb = FE(nc.sbuf_tensor("redb", [128, 5], F32))
                nmax = FE(nc.sbuf_tensor("nmax", [128, 5], F32))
                cosl = FE(nc.sbuf_tensor("cosl", [128, 4 * 160], BF16))
                sinl = FE(nc.sbuf_tensor("sinl", [128, 4 * 160], BF16))
                nrow = FE(nc.sbuf_tensor("nrow", [1, 5], F32))
                mrow = FE(nc.sbuf_tensor("mrow", [1, 4], F32))
                p_qkv = [FE(nc.psum_tensor(f"p_qkv{p}", [128, 384], F32)) for p in range(2)]
                p_rb = FE(nc.psum_tensor("p_rb", [128, 512], F32))
                p_tr = FE(nc.psum_tensor("p_tr", [128, 1024], BF16))

                v_ones = dv.do(nc.vector.memset(bass.AP(vres, 64, [[65 * NCH, 128], [65, NCH], [1, 1]]), 1.0))
                vnm = dv.do(nc.vector.memset(nmax[:], 0.0))
                dv.do(nc.vector.memset(qres[64:128, :], 0.0))
                dv.do(nc.vector.memset(kres[64:128, :], 0.0))

                vcast = 0
                for j in range(NJ):
                    if vcast:
                        nc.sync.wait_ge(dvS.h, vcast)
                    sy.inc(nc.sync.dma_start(out=tmpw[:, 0:256], in_=fWq[j * 128:(j + 1) * 128, :]), 16)
                    sy.inc(nc.sync.dma_start(out=tmpw[:, 256:320], in_=fWk[j * 128:(j + 1) * 128, :]), 16)
                    vwl = sy.inc(nc.sync.dma_start(out=tmpw[:, 320:384], in_=fWv[j * 128:(j + 1) * 128, :]), 16)
                    dv.wait(sy, vwl)
                    vq = dv.do(nc.vector.tensor_scalar(tmpw[:, 0:256], tmpw[:, 0:256], s_lnw[:, j:j + 1], float(FSCALE), ALU.mult, ALU.mult))
                    dv.wait(dvS, vq)
                    vkv = dv.do(nc.vector.tensor_scalar(tmpw[:, 256:384], tmpw[:, 256:384], s_lnw[:, j:j + 1], None, ALU.mult))
                    dv.wait(dvS, vkv)
                    vcast = dv.do(nc.vector.tensor_copy(wqkv[:, j * 384:(j + 1) * 384], tmpw[:]))
                    dv.wait(dvS, vcast)
                wq_ready = dvS.n

                ld_done = [0, 0]
                mm_qkv_done = [0, 0]
                rope_done = 0
                tr_evac = 0
                prev_tile_mm = 0
                prev_rb_mm = 0
                slot_mm = [0, 0]
                for tt in range(T // 512):
                    p2 = tt % 2
                    if slot_mm[p2]:
                        nc.sync.wait_ge(mmS.h, slot_mm[p2])
                    vld = qs[4 + p2].inc(nc.sync.dma_start(out=xtl[p2][:], in_=d_xnfm[tt, :, :]), 16)
                    nc.sync.wait_ge(dvS.h, rope_done)
                    nc.sync.wait_ge(mmS.h, prev_rb_mm)
                    qs[6].inc(nc.sync.dma_start(out=rrl[:, :], in_=d_rr[:, tt * 512:(tt + 1) * 512]), 16)
                    csrc = bass.AP(cos_f5.tensor, tt * 512 * 160, [[160, 128], [128 * 160, 4], [1, 160]])
                    ssrc = bass.AP(sin_f5.tensor, tt * 512 * 160, [[160, 128], [128 * 160, 4], [1, 160]])
                    qs[6].inc(nc.sync.dma_start(out=cosl[:].rearrange("q (a b) -> q a b", a=4), in_=csrc), 16)
                    vcs = qs[6].inc(nc.sync.dma_start(out=sinl[:].rearrange("q (a b) -> q a b", a=4), in_=ssrc), 16)
                    te.wait(dvS, rope_done)
                    te.wait(qs[6], qs[6].n)
                    vrb = te.do(nc.tensor.matmul(p_rb[:, 0:512], s_ones_r128[:, :], rrl[:, :], start=True, stop=True))
                    prev_rb_mm = vrb
                    dv.wait(qs[4 + p2], vld)
                    dv.wait(mmS, vrb)
                    dv.wait(dvS, wq_ready)
                    vxr = 0
                    for j in range(NJ):
                        vxr = dv.do(nc.vector.tensor_tensor(xtl[p2][:, j * 512:(j + 1) * 512], xtl[p2][:, j * 512:(j + 1) * 512], p_rb[:, 0:512], ALU.mult))
                    ld_done[p2] = vxr
                    slot_mm[p2] = None
                    rope_done_tile = 0
                    for sub in range(4):
                        pq = sub % 2
                        te.wait(dvS, vxr)
                        if mm_qkv_done[pq]:
                            te.wait(dvS, mm_qkv_done[pq])
                        vmm = 0
                        for j in range(NJ):
                            vmm = te.do(nc.tensor.matmul(p_qkv[pq][:, 0:384],
                                                         xtl[p2][:, j * 512 + sub * 128:j * 512 + (sub + 1) * 128],
                                                         wqkv[:, j * 384:(j + 1) * 384],
                                                         start=(j == 0), stop=(j == NJ - 1)))
                        dv.wait(mmS, vmm)
                        dv.wait(qs[6], vcs)
                        dv.wait(dvS, dvS.n)
                        h1 = bass.AP(p_qkv[pq], 0, [[384, 128], [64, 5], [1, 32]])
                        h2ap = bass.AP(p_qkv[pq], 32, [[384, 128], [64, 5], [1, 32]])
                        co = cosl[:, sub * 160:(sub + 1) * 160].rearrange("q (a b) -> q a b", a=5)
                        si = sinl[:, sub * 160:(sub + 1) * 160].rearrange("q (a b) -> q a b", a=5)
                        tm1 = tmp1[:].rearrange("q (a b) -> q a b", a=5)
                        tm2 = tmp2[:].rearrange("q (a b) -> q a b", a=5)
                        va = dv.do(nc.vector.tensor_tensor(tm1, h1, co, ALU.mult))
                        vb2 = dv.do(nc.vector.tensor_tensor(tm2, h2ap, si, ALU.mult))
                        dv.wait(dvS, vb2)
                        sg_h1 = bass.AP(stage, sub * 384, [[4 * 384, 128], [64, 5], [1, 32]])
                        sg_h2 = bass.AP(stage, sub * 384 + 32, [[4 * 384, 128], [64, 5], [1, 32]])
                        v_1 = dv.do(nc.vector.tensor_tensor(sg_h1, tm1, tm2, ALU.subtract))
                        dv.wait(dvS, v_1)
                        vc2 = dv.do(nc.vector.tensor_tensor(tm1, h2ap, co, ALU.mult))
                        vd2 = dv.do(nc.vector.tensor_tensor(tm2, h1, si, ALU.mult))
                        dv.wait(dvS, vd2)
                        v_2 = dv.do(nc.vector.tensor_tensor(sg_h2, tm1, tm2, ALU.add))
                        vv = dv.do(nc.vector.tensor_copy(vres[:, (tt * 4 + sub) * 65:(tt * 4 + sub) * 65 + 64], p_qkv[pq][:, 320:384]))
                        dv.wait(dvS, v_2)
                        vsq = dv.do(nc.vector.tensor_tensor(sqb[:], stage[:, sub * 384:sub * 384 + 320], stage[:, sub * 384:sub * 384 + 320], ALU.mult))
                        dv.wait(dvS, vsq)
                        vrd = dv.do(nc.vector.reduce_sum(redb[:], sqb[:].rearrange("q (a b) -> q a b", a=5), axis=AX.X))
                        dv.wait(dvS, vrd)
                        vmx = dv.do(nc.vector.tensor_max(nmax[:], nmax[:], redb[:]))
                        mm_qkv_done[pq] = vmx
                        rope_done_tile = max(rope_done_tile, vv, v_2)
                        te.wait(dvS, v_2)
                        te.wait(dvS, tr_evac)
                        t0_ = te.do(nc.tensor.transpose(p_tr[:, 0:128], stage[:, sub * 384:sub * 384 + 128], s_idbf[:]))
                        t1_ = te.do(nc.tensor.transpose(p_tr[:, 128:256], stage[:, sub * 384 + 128:sub * 384 + 256], s_idbf[:]))
                        t2_ = te.do(nc.tensor.transpose(p_tr[0:64, 256:384], stage[:, sub * 384 + 256:sub * 384 + 320], s_idbf[:]))
                        dv.wait(mmS, t2_)
                        tcol = tt * 512 + sub * 128
                        dv.do(nc.vector.tensor_copy(qres[0:64, 0 * T + tcol:0 * T + tcol + 128], p_tr[0:64, 0:128]))
                        dv.do(nc.vector.tensor_copy(qres[0:64, 1 * T + tcol:1 * T + tcol + 128], p_tr[64:128, 0:128]))
                        dv.do(nc.vector.tensor_copy(qres[0:64, 2 * T + tcol:2 * T + tcol + 128], p_tr[0:64, 128:256]))
                        dv.do(nc.vector.tensor_copy(qres[0:64, 3 * T + tcol:3 * T + tcol + 128], p_tr[64:128, 128:256]))
                        tr_evac = dv.do(nc.vector.tensor_copy(kres[0:64, tcol:tcol + 128], p_tr[0:64, 256:384]))
                    rope_done = max(rope_done_tile, tr_evac)
                    prev_tile_mm = vmm
                    slot_mm[p2] = vmm
                pl.wait(dvS, dvS.n)
                vgr = pl.do(nc.gpsimd.tensor_reduce(nrow[:], nmax[:], axis=AX.C, op=ALU.max))
                sc.wait(plS, vgr)
                vsr = sc.do(nc.scalar.activation(nrow[:], nrow[:], AF.Ln))
                sc.wait(scS, vsr)
                vsr = sc.do(nc.scalar.activation(nrow[:], nrow[:], AF.Exp, scale=0.5))
                dv.wait(scS, vsr)
                vmr = dv.do(nc.vector.tensor_scalar(mrow[:], nrow[:, 0:4], nrow[:, 4:5], -1.0, ALU.mult, ALU.mult))
                te.wait(dvS, vmr)
                for h in range(FQH):
                    vmb = te.do(nc.tensor.matmul(p_c[:, 0:1], s_ones_r128[:, :], mrow[:, h:h + 1], start=True, stop=True))
                    dv.wait(mmS, vmb)
                    vme = dv.do(nc.vector.tensor_copy(mbias[:, h:h + 1], p_c[:, 0:1]))
                    te.wait(dvS, vme)
                if t_ffnq is not None:
                    nc.sync.wait_ge(dvS.h, dvS.n)
                    for r in range(4):
                        sy.inc(nc.sync.dma_start(out=bass.AP(t_ffnq.tensor, r * 64 * T, [[T, 64], [1, T]]), in_=qres[0:64, r * T:(r + 1) * T]), 16)
                if t_ffnk is not None:
                    nc.sync.wait_ge(dvS.h, dvS.n)
                    sy.inc(nc.sync.dma_start(out=t_ffnk[:, :], in_=kres[0:64, :]), 16)
                if t_ffnv is not None:
                    nc.sync.wait_ge(dvS.h, dvS.n)
                    sy.inc(nc.sync.dma_start(out=t_ffnv[:, :], in_=vres[:, :]), 16)
                if t_mhat is not None:
                    nc.sync.wait_ge(dvS.h, dvS.n)
                    sy.inc(nc.sync.dma_start(out=t_mhat[:, 0:4], in_=mrow[:, :]), 16)
                    sy.inc(nc.sync.dma_start(out=t_mhat[:, 4:5], in_=nrow[:, 4:5]), 16)
                nc.sync.wait_ge(sy.h, sy.n)
            bar()

        if "flash" in ph:
            with ExitStack() as fctx:
                FE = fctx.enter_context
                psf = [FE(nc.psum_tensor(f"psf{p}", [128, 1024], F32)) for p in range(2)]
                p_o = [FE(nc.psum_tensor(f"p_o{p}", [65, 512], F32)) for p in range(2)]
                p_bc = FE(nc.psum_tensor("p_bc", [64, 512], F32))
                esb = FE(nc.sbuf_tensor("esb", [128, 2048], BF16))
                rl = [FE(nc.sbuf_tensor(f"rl{p}", [1, 512], F32)) for p in range(2)]
                rb = [FE(nc.sbuf_tensor(f"rb{p}", [64, 512], F32)) for p in range(2)]
                exp_done = [0, 0]
                av_done = [0, 0]
                fin_done = [0, 0]
                blocks = [(h, qb) for h in range(FQH) for qb in range(16)]

                def emit_stile(h, qb, b2):
                    par = b2 % 2
                    te.wait(scS, exp_done[par])
                    vmm = 0
                    for j2 in range(2):
                        kt = b2 * 2 + j2
                        vmm = te.do(nc.tensor.matmul(
                            psf[par][:, j2 * 512:(j2 + 1) * 512],
                            kres[:, kt * 128:(kt + 1) * 128],
                            qres[:, h * T + qb * 512:h * T + (qb + 1) * 512],
                            start=True, stop=True))
                    return vmm

                p_bc2 = [p_bc, None]
                obuf = [FE(nc.sbuf_tensor(f"obuf{p}", [64, 512], BF16)) for p in range(2)]
                pending_fin = [None]

                def emit_finalize():
                    if pending_fin[0] is None:
                        return
                    (fh, fqb, fopar, fav) = pending_fin[0]
                    pending_fin[0] = None
                    dv.wait(mmS, fav)
                    vrl = dv.do(nc.vector.reciprocal(rl[fopar][:, :], p_o[fopar][64:65, :]))
                    te.wait(dvS, vrl)
                    te.wait(dvS, fin_done[fopar])
                    bca = p_bc[:, :] if fopar == 0 else p_c[0:64, 0:512]
                    vbc = te.do(nc.tensor.matmul(bca, s_ones_r64[:, :], rl[fopar][:, :], start=True, stop=True))
                    dv.wait(mmS, vbc)
                    vrb = dv.do(nc.vector.tensor_copy(rb[fopar][:, :], bca))
                    dv.wait(dvS, vrb)
                    vtt = dv.do(nc.vector.tensor_tensor(obuf[fopar][:, :], p_o[fopar][0:64, :], rb[fopar][:, :], ALU.mult))
                    dv.wait(dvS, vtt)
                    fin_done[fopar] = dv.do(nc.vector.tensor_copy(
                        ofm[(fh % 2) * 64:(fh % 2) * 64 + 64, (fh // 2) * T + fqb * 512:(fh // 2) * T + (fqb + 1) * 512],
                        obuf[fopar][:, :]))

                for bi, (h, qb) in enumerate(blocks):
                    opar = bi % 2
                    nkt = 4 * (qb + 1)
                    nb = nkt // 2
                    st_mm = emit_stile(h, qb, 0)
                    stile_mm = {0: st_mm}
                    te.wait(dvS, fin_done[opar])
                    for b2 in range(nb):
                        par = b2 % 2
                        if b2 + 1 < nb:
                            stile_mm[b2 + 1] = emit_stile(h, qb, b2 + 1)
                        sc.wait(mmS, stile_mm[b2])
                        sc.wait(mmS, av_done[par])
                        vex = sc.do(nc.scalar.activation(esb[:, par * 1024:(par + 1) * 1024], psf[par][:, 0:1024],
                                                         AF.Exp, bias=mbias[:, h:h + 1]))
                        exp_done[par] = vex
                        vmask = 0
                        for j2 in range(2):
                            kt = b2 * 2 + j2
                            di = kt - 4 * qb
                            if di >= 0:
                                dv.wait(scS, vex)
                                eslice = esb[:, par * 1024 + j2 * 512:par * 1024 + (j2 + 1) * 512]
                                vmask = dv.do(nc.vector.tensor_tensor(eslice, eslice, s_dmask[:, di * 512:(di + 1) * 512], ALU.mult))
                        te.wait(scS, vex)
                        if vmask:
                            te.wait(dvS, vmask)
                        vav = 0
                        for j2 in range(2):
                            kt = b2 * 2 + j2
                            vav = te.do(nc.tensor.matmul(
                                p_o[opar][:, :],
                                vres[:, kt * 65:(kt + 1) * 65],
                                esb[:, par * 1024 + j2 * 512:par * 1024 + (j2 + 1) * 512],
                                start=(kt == 0), stop=(kt == nkt - 1)))
                        av_done[par] = vav
                        if b2 == 0:
                            emit_finalize()
                    pending_fin[0] = (h, qb, opar, av_done[par])
                emit_finalize()
                if t_flash is not None:
                    nc.sync.wait_ge(dvS.h, dvS.n)
                    for r in range(2):
                        sy.inc(nc.sync.dma_start(out=bass.AP(t_flash.tensor, r * 128 * T, [[T, 128], [1, T]]), in_=ofm[:, r * T:(r + 1) * T]), 16)
                    nc.sync.wait_ge(sy.h, sy.n)
            bar()

        if "ffnpart" in ph:
            with ExitStack() as fctx:
                FE = fctx.enter_context
                wo_sb = FE(nc.sbuf_tensor("wo_sb", [128, 2 * D], BF16))
                pstg = [FE(nc.sbuf_tensor(f"pstg{p}", [128, D], F32)) for p in range(2)]
                p_pp = [FE(nc.psum_tensor(f"p_pp{p}", [128, 512], F32)) for p in range(2)]
                for r in range(2):
                    gp.inc(nc.gpsimd.dma_start(out=wo_sb[:, r * D:(r + 1) * D], in_=fWo[r * 128:(r + 1) * 128, :]), 16)
                te.wait(gp, gp.n)
                part_st = [0, 0]
                evac_done = [0, 0]
                for tc in range(NCH):
                    pp = tc % 2
                    if part_st[pp]:
                        nc.vector.wait_ge(qs[pp].h, part_st[pp])
                    vev = 0
                    for dsl in range(4):
                        pb2 = dsl % 2
                        te.wait(dvS, evac_done[pb2])
                        vmm = 0
                        for r in range(2):
                            vmm = te.do(nc.tensor.matmul(
                                p_pp[pb2][:, :],
                                ofm[:, r * T + tc * 128:r * T + (tc + 1) * 128],
                                wo_sb[:, r * D + dsl * 512:r * D + (dsl + 1) * 512],
                                start=(r == 0), stop=(r == 1)))
                        dv.wait(mmS, vmm)
                        vev = dv.do(nc.vector.tensor_copy(pstg[pp][:, dsl * 512:(dsl + 1) * 512], p_pp[pb2][:, :]))
                        evac_done[pb2] = vev
                    nc.sync.wait_ge(dvS.h, vev)
                    part_st[pp] = qs[pp].inc(nc.sync.dma_start(out=cc_f_in[tc * 128:(tc + 1) * 128, :], in_=pstg[pp][:]), 16)
                if t_ffnpart is not None:
                    nc.sync.wait_ge(qs[0].h, qs[0].n)
                    nc.sync.wait_ge(qs[1].h, qs[1].n)
                    for tc in range(NCH):
                        sy.inc(nc.sync.dma_start(out=t_ffnpart[tc * 128:(tc + 1) * 128, :], in_=cc_f_in[tc * 128:(tc + 1) * 128, :]), 16)
                    nc.sync.wait_ge(sy.h, sy.n)
                nc.gpsimd.wait_ge(qs[0].h, qs[0].n)
                nc.gpsimd.wait_ge(qs[1].h, qs[1].n)
                if fake_cc:
                    ccS.inc(nc.gpsimd.dma_start(out=cc_f_out[:, :], in_=cc_f_in[0:TOWN, :]), 16)
                else:
                    ccS.inc(nc.gpsimd.collective_compute(
                        "ReduceScatter", ALU.add,
                        replica_groups=[list(range(NCORES))],
                        ins=[cc_f_in.tensor.ap().opt()],
                        outs=[cc_f_out.tensor.ap().opt()]), 1)
            bar()
        ffn_ctx.close()

        if "attproj" in ph:
            with ExitStack() as actx:
                AE = actx.enter_context
                awb = AE(nc.sbuf_tensor("awb", [128, NJ * 1024], BF16))
                ws1b = AE(nc.sbuf_tensor("ws1b", [128, NJ * 16], BF16))
                ws2b = AE(nc.sbuf_tensor("ws2b", [17, 128], BF16))
                tmpw2 = AE(nc.sbuf_tensor("tmpw2", [128, 1024], F32))
                xtl2 = [AE(nc.sbuf_tensor(f"xtl2_{p}", [128, NJ * 512], BF16)) for p in range(2)]
                astg = AE(nc.sbuf_tensor("astg", [128, 4 * 1152], BF16))
                s16a = AE(nc.sbuf_tensor("s16a", [17, 512], BF16))
                atm1 = AE(nc.sbuf_tensor("atm1", [128, 256], F32))
                atm2 = AE(nc.sbuf_tensor("atm2", [128, 256], F32))
                cosl2 = AE(nc.sbuf_tensor("cosl2", [128, 4 * 256], BF16))
                sinl2 = AE(nc.sbuf_tensor("sinl2", [128, 4 * 256], BF16))
                p_qg = [AE(nc.psum_tensor(f"p_qg{p}", [128, 512], F32)) for p in range(2)]
                p_vg = [AE(nc.psum_tensor(f"p_vg{p}", [128, 512], F32)) for p in range(2)]
                p_s16 = AE(nc.psum_tensor("p_s16", [16, 512], F32))
                p_s = AE(nc.psum_tensor("p_s", [128, 128], F32))

                vw_ones = dv.do(nc.vector.memset(s16a[:, :], 1.0))
                vcast = 0
                for j in range(NJ):
                    if vcast:
                        nc.sync.wait_ge(dvS.h, vcast)
                    sy.inc(nc.sync.dma_start(out=tmpw2[:, 0:256], in_=aWq[j * 128:(j + 1) * 128, :]), 16)
                    sy.inc(nc.sync.dma_start(out=tmpw2[:, 256:512], in_=aWk[j * 128:(j + 1) * 128, :]), 16)
                    sy.inc(nc.sync.dma_start(out=tmpw2[:, 512:768], in_=aWv[j * 128:(j + 1) * 128, :]), 16)
                    vwl = sy.inc(nc.sync.dma_start(out=tmpw2[:, 768:1024], in_=aWg[j * 128:(j + 1) * 128, :]), 16)
                    dv.wait(sy, vwl)
                    vq = dv.do(nc.vector.tensor_scalar(tmpw2[:, 0:256], tmpw2[:, 0:256], float(ASCALE), None, ALU.mult))
                    dv.wait(dvS, vq)
                    vcast = dv.do(nc.vector.tensor_copy(awb[:, j * 1024:(j + 1) * 1024], tmpw2[:]))
                vg1 = gp.inc(nc.gpsimd.dma_start(out=ws1b[:].rearrange("q (a b) -> q a b", a=NJ),
                                                 in_=bass.AP(aWs1.tensor, 0, [[16, 128], [128 * 16, NJ], [1, 16]])), 16)
                vg2 = gp.inc(nc.gpsimd.dma_start(out=ws2b[:], in_=aWs2[:, :]), 16)

                ld2_done = [0, 0]
                es_clip_done = 0
                vs16e = 0
                prev_mm2 = 0
                rope2_done = 0
                st2_done = [0, 0]
                es_mm_done = 0
                slot_mm2 = [0, 0]
                for tt in range(T // 512):
                    p2 = tt % 2
                    if slot_mm2[p2]:
                        nc.sync.wait_ge(mmS.h, slot_mm2[p2])
                    vld = qs[4 + p2].inc(nc.sync.dma_start(out=xtl2[p2][:], in_=d_xnfm[tt, :, :]), 16)
                    nc.sync.wait_ge(dvS.h, rope2_done)
                    csrc = bass.AP(cos_a4.tensor, tt * 512 * 256, [[256, 128], [128 * 256, 4], [1, 256]])
                    ssrc = bass.AP(sin_a4.tensor, tt * 512 * 256, [[256, 128], [128 * 256, 4], [1, 256]])
                    qs[6].inc(nc.sync.dma_start(out=cosl2[:].rearrange("q (a b) -> q a b", a=4), in_=csrc), 16)
                    vcs = qs[6].inc(nc.sync.dma_start(out=sinl2[:].rearrange("q (a b) -> q a b", a=4), in_=ssrc), 16)
                    te.wait(qs[4 + p2], vld)
                    te.wait(gp, gp.n)
                    te.wait(dvS, vcast)
                    te.wait(dvS, vw_ones)
                    te.wait(dvS, vs16e)
                    vs16 = 0
                    for j in range(NJ):
                        vs16 = te.do(nc.tensor.matmul(p_s16[:, :], ws1b[:, j * 16:(j + 1) * 16],
                                                      xtl2[p2][:, j * 512:(j + 1) * 512],
                                                      start=(j == 0), stop=(j == NJ - 1)))
                    dv.wait(mmS, vs16)
                    dv.wait(mmS, es_mm_done)
                    vs16e = dv.do(nc.vector.tensor_copy(s16a[0:16, :], p_s16[:, :]))
                    vmm = 0
                    rope2_tile = 0
                    for sub in range(4):
                        pq = sub % 2
                        te.wait(dvS, ld2_done[pq])
                        for j in range(NJ):
                            te.do(nc.tensor.matmul(p_qg[pq][:, :],
                                                   xtl2[p2][:, j * 512 + sub * 128:j * 512 + (sub + 1) * 128],
                                                   awb[:, j * 1024:j * 1024 + 512],
                                                   start=(j == 0), stop=(j == NJ - 1)))
                            vmm = te.do(nc.tensor.matmul(p_vg[pq][:, :],
                                                         xtl2[p2][:, j * 512 + sub * 128:j * 512 + (sub + 1) * 128],
                                                         awb[:, j * 1024 + 512:(j + 1) * 1024],
                                                         start=(j == 0), stop=(j == NJ - 1)))
                        dv.wait(mmS, vmm)
                        dv.wait(qs[6], vcs)
                        dv.wait(dvS, dvS.n)
                        h1 = bass.AP(p_qg[pq], 0, [[512, 128], [128, 4], [1, 64]])
                        h2ap = bass.AP(p_qg[pq], 64, [[512, 128], [128, 4], [1, 64]])
                        co = cosl2[:, sub * 256:(sub + 1) * 256].rearrange("q (a b) -> q a b", a=4)
                        si = sinl2[:, sub * 256:(sub + 1) * 256].rearrange("q (a b) -> q a b", a=4)
                        tm1 = atm1[:].rearrange("q (a b) -> q a b", a=4)
                        tm2 = atm2[:].rearrange("q (a b) -> q a b", a=4)
                        sg_h1 = bass.AP(astg, sub * 1152, [[4 * 1152, 128], [128, 4], [1, 64]])
                        sg_h2 = bass.AP(astg, sub * 1152 + 64, [[4 * 1152, 128], [128, 4], [1, 64]])
                        dv.do(nc.vector.tensor_tensor(tm1, h1, co, ALU.mult))
                        vb2 = dv.do(nc.vector.tensor_tensor(tm2, h2ap, si, ALU.mult))
                        dv.wait(dvS, vb2)
                        v_1 = dv.do(nc.vector.tensor_tensor(sg_h1, tm1, tm2, ALU.subtract))
                        dv.wait(dvS, v_1)
                        dv.do(nc.vector.tensor_tensor(tm1, h2ap, co, ALU.mult))
                        vd2 = dv.do(nc.vector.tensor_tensor(tm2, h1, si, ALU.mult))
                        dv.wait(dvS, vd2)
                        v_2 = dv.do(nc.vector.tensor_tensor(sg_h2, tm1, tm2, ALU.add))
                        vvg = dv.do(nc.vector.tensor_copy(astg[:, sub * 1152 + 512:sub * 1152 + 1024], p_vg[pq][:, :]))
                        ld2_done[pq] = vvg
                        rope2_tile = max(rope2_tile, v_2, vvg)
                        te.wait(dvS, vs16e)
                        te.wait(dvS, es_clip_done)
                        ves = te.do(nc.tensor.matmul(p_s[:, :], s16a[:, sub * 128:(sub + 1) * 128], ws2b[:, :], start=True, stop=True))
                        es_mm_done = ves
                        dv.wait(mmS, ves)
                        vclip = dv.do(nc.vector.tensor_scalar(astg[:, sub * 1152 + 1024:sub * 1152 + 1152], p_s[:, :], -CLAMP, CLAMP, ALU.max, ALU.min))
                        es_clip_done = vclip
                        sc.wait(dvS, vclip)
                        vexp = sc.do(nc.scalar.activation(astg[:, sub * 1152 + 1024:sub * 1152 + 1152],
                                                          astg[:, sub * 1152 + 1024:sub * 1152 + 1152], AF.Exp))
                        rope2_tile = max(rope2_tile, vclip)
                    rope2_done = rope2_tile
                    nc.sync.wait_ge(dvS.h, rope2_tile)
                    nc.sync.wait_ge(scS.h, scS.n)
                    adst = bass.AP(d_att.tensor, tt * 512 * 1152, [[1152, 128], [128 * 1152, 4], [1, 1152]])
                    vstt = qs[p2].inc(nc.sync.dma_start(out=adst, in_=astg[:].rearrange("q (a b) -> q a b", a=4)), 16)
                    st2_done[p2] = vstt
                    prev_mm2 = vmm
                    slot_mm2[p2] = vmm
                    if tt + 1 < T // 512:
                        nc.vector.wait_ge(qs[p2].h, vstt)
                if t_att is not None:
                    nc.sync.wait_ge(qs[0].h, qs[0].n)
                    nc.sync.wait_ge(qs[1].h, qs[1].n)
                    for tc in range(0, NCH, 8):
                        sy.inc(nc.sync.dma_start(out=t_att[tc * 128:(tc + 8) * 128, :], in_=d_att[tc * 128:(tc + 8) * 128, :]), 16)
                    nc.sync.wait_ge(sy.h, sy.n)
            nc.sync.wait_ge(qs[0].h, qs[0].n)
            nc.sync.wait_ge(qs[1].h, qs[1].n)
            bar()

        att_ctx = ExitStack()
        if "scan" in ph or "attpart" in ph:
            og_all = att_ctx.enter_context(nc.sbuf_tensor("og_all", [128, 2 * T], BF16))
        if "scan" in ph:
            with ExitStack() as sctx:
                SE = sctx.enter_context
                attc = [SE(nc.sbuf_tensor(f"attc{p}", [128, 1152], BF16)) for p in range(2)]
                S_sb = [SE(nc.sbuf_tensor(f"S_sb{p}", [128, 128], BF16)) for p in range(2)]
                Z_sb = [SE(nc.sbuf_tensor(f"Z_sb{p}", [64, 256], BF16)) for p in range(2)]
                R_sb = [SE(nc.sbuf_tensor(f"R_sb{p}", [1, 128], F32)) for p in range(2)]
                kfm_sb = [SE(nc.sbuf_tensor(f"kfm_sb{h}", [128, 128], BF16)) for h in range(2)]
                qfm_sb = [SE(nc.sbuf_tensor(f"qfm_sb{h}", [128, 128], BF16)) for h in range(2)]
                amask_sb = [SE(nc.sbuf_tensor(f"amask_sb{h}", [128, 128], BF16)) for h in range(2)]
                enz_sb = [SE(nc.sbuf_tensor(f"enz_sb{h}", [128, 64], F32)) for h in range(2)]
                ok_sb = [SE(nc.sbuf_tensor(f"ok_sb{h}", [128, 64], F32)) for h in range(2)]
                esm_sb = [SE(nc.sbuf_tensor(f"esm_sb{h}", [128, 64], F32)) for h in range(2)]
                u_sb = [SE(nc.sbuf_tensor(f"u_sb{h}", [128, 64], BF16)) for h in range(2)]
                ufm_sb = [SE(nc.sbuf_tensor(f"ufm_sb{h}", [64, 128], BF16)) for h in range(2)]
                esfm_sb = [SE(nc.sbuf_tensor(f"esfm_sb{h}", [64, 128], BF16)) for h in range(2)]
                w2m_sb = [SE(nc.sbuf_tensor(f"w2m_sb{h}", [128, 128], BF16)) for h in range(2)]
                o_sb = [SE(nc.sbuf_tensor(f"o_sb{h}", [128, 128], F32)) for h in range(2)]
                gg_sb = [SE(nc.sbuf_tensor(f"gg_sb{h}", [128, 128], F32)) for h in range(2)]
                eg_sb = [SE(nc.sbuf_tensor(f"eg_sb{h}", [128, 128], F32)) for h in range(2)]
                og_sb = [SE(nc.sbuf_tensor(f"og_sb{h}", [128, 128], BF16)) for h in range(2)]
                sm1 = [SE(nc.sbuf_tensor(f"sm1_{h}", [128, 8], F32)) for h in range(2)]
                bnst2 = [SE(nc.sbuf_tensor(f"bnst2_{h}", [128, 6], F32)) for h in range(2)]
                bnag2 = [SE(nc.sbuf_tensor(f"bnag2_{h}", [128, 2], F32)) for h in range(2)]
                pS = SE(nc.psum_tensor("pS", [128, 128], F32))
                pZ = SE(nc.psum_tensor("pZ", [64, 256], F32))
                pR = SE(nc.psum_tensor("pR", [1, 128], F32))
                p_co = SE(nc.psum_tensor("p_co", [128, 256], F32))
                p_wo = SE(nc.psum_tensor("p_wo", [128, 256], F32))
                p_oo = SE(nc.psum_tensor("p_oo", [128, 256], F32))
                p_trS = SE(nc.psum_tensor("p_trS", [128, 1024], BF16))
                p_a = p_c

                upd_done = 0
                snap_done = 0
                attc_mm = [0, 0]
                attc_dv = [0, 0]
                trS_free = 0
                trU_free = 0
                co_free = 0
                wo_free = 0
                oo_free = 0
                am_free = 0
                for c in range(NCH):
                    p = c % 2
                    if attc_mm[p]:
                        nc.sync.wait_ge(mmS.h, attc_mm[p])
                        nc.sync.wait_ge(dvS.h, attc_dv[p])
                    vld = qs[4 + p].inc(nc.sync.dma_start(out=attc[p][:], in_=d_att[c * 128:(c + 1) * 128, :]), 16)
                    ap = attc[p]
                    qsl = [ap[:, h * 128:(h + 1) * 128] for h in range(2)]
                    ksl = [ap[:, 256 + h * 128:256 + (h + 1) * 128] for h in range(2)]
                    vsl = [ap[:, 512 + h * 128:512 + (h + 1) * 128] for h in range(2)]
                    gsl = [ap[:, 768 + h * 128:768 + (h + 1) * 128] for h in range(2)]
                    essl = [ap[:, 1024 + h * 64:1024 + (h + 1) * 64] for h in range(2)]
                    if c > 0:
                        dv.wait(mmS, upd_done)
                        dv.do(nc.vector.tensor_copy(S_sb[p][:], pS[:, :]))
                        dv.do(nc.vector.tensor_copy(Z_sb[p][:], pZ[:, :]))
                        snap_done = dv.do(nc.vector.tensor_copy(R_sb[p][:], pR[:, :]))
                    te.wait(qs[4 + p], vld)
                    if c > 0:
                        te.wait(dvS, snap_done)
                    for h in range(2):
                        te.do(nc.tensor.matmul(pS[:, h * 64:(h + 1) * 64], ksl[h], essl[h],
                                               start=(c == 0 and h == 0), stop=(c == NCH - 1), skip_group_check=True))
                        te.do(nc.tensor.matmul(pZ[0:64, h * 128:(h + 1) * 128], essl[h], vsl[h],
                                               start=(c == 0 and h == 0), stop=(c == NCH - 1), skip_group_check=True))
                    upd_done = te.do(nc.tensor.matmul(pR[:, :], s_ones_c128bf[:, :], ap[:, 1024:1152],
                                                      start=(c == 0), stop=(c == NCH - 1), skip_group_check=True))
                    te.wait(dvS, trS_free)
                    tks = []
                    for h in range(2):
                        te.do(nc.tensor.transpose(p_trS[:, h * 256:h * 256 + 128], ksl[h], s_idbf[:]))
                        tks.append(te.do(nc.tensor.transpose(p_trS[:, h * 256 + 128:h * 256 + 256], qsl[h], s_idbf[:])))
                    vq2 = [0, 0]
                    for h in range(2):
                        dv.wait(mmS, tks[1])
                        dv.do(nc.vector.tensor_copy(kfm_sb[h][:], p_trS[:, h * 256:h * 256 + 128]))
                        vq2[h] = dv.do(nc.vector.tensor_copy(qfm_sb[h][:], p_trS[:, h * 256 + 128:h * 256 + 256]))
                    va2 = [0, 0]
                    te.wait(dvS, am_free)
                    for h in range(2):
                        te.wait(dvS, vq2[h])
                        va2[h] = te.do(nc.tensor.matmul(p_a[:, h * 128:(h + 1) * 128], kfm_sb[h][:], qfm_sb[h][:], start=True, stop=True))
                    vam = [0, 0]
                    for h in range(2):
                        dv.wait(mmS, va2[1])
                        vam[h] = dv.do(nc.vector.tensor_tensor(amask_sb[h][:], p_a[:, h * 128:(h + 1) * 128], s_utri[:], ALU.mult))
                    am_free = max(vam)
                    vok = [0, 0]
                    te.wait(dvS, co_free)
                    for h in range(2):
                        te.wait(dvS, vam[h])
                        te.do(nc.tensor.matmul(p_co[:, h * 128:h * 128 + 64], s_utri[:], essl[h], start=(h == 0), stop=False, skip_group_check=True))
                        if c > 0:
                            te.do(nc.tensor.matmul(p_co[:, h * 128:h * 128 + 64], s_ones_r128[:, :], R_sb[p][:, h * 64:(h + 1) * 64], start=False, stop=True, skip_group_check=True))
                        te.do(nc.tensor.matmul(p_co[:, h * 128 + 64:h * 128 + 128], amask_sb[h][:], essl[h], start=False, stop=(c == 0), skip_group_check=True))
                        if c > 0:
                            vok[h] = te.do(nc.tensor.matmul(p_co[:, h * 128 + 64:h * 128 + 128], qfm_sb[h][:], S_sb[p][:, h * 64:(h + 1) * 64], start=False, stop=True, skip_group_check=True))
                        else:
                            vok[h] = mmS.n
                    vu = [0, 0]
                    vnm2 = [0, 0]
                    for h in range(2):
                        dv.wait(mmS, max(vok))
                        vez = dv.do(nc.vector.reciprocal(enz_sb[h][:], p_co[:, h * 128:h * 128 + 64]))
                        dv.wait(dvS, vez)
                        vko = dv.do(nc.vector.tensor_tensor(ok_sb[h][:], p_co[:, h * 128 + 64:h * 128 + 128], enz_sb[h][:], ALU.mult))
                        co_free = vko
                        dv.wait(dvS, vko)
                        vnm2[h] = dv.do(nc.vector.reduce_max(sm1[h][:, 0:1], ok_sb[h][:], axis=AX.X, negate=True))
                    vex2 = [0, 0]
                    for h in range(2):
                        sc.wait(dvS, vnm2[h])
                        vex2[h] = sc.do(nc.scalar.activation(esm_sb[h][:], ok_sb[h][:], AF.Exp, bias=sm1[h][:, 0:1], accum_out=sm1[h][:, 1:2]))
                    for h in range(2):
                        dv.wait(scS, vex2[h])
                        vri = dv.do(nc.vector.reciprocal(sm1[h][:, 2:3], sm1[h][:, 1:2]))
                        dv.wait(dvS, vri)
                        vue = dv.do(nc.vector.tensor_tensor(esm_sb[h][:], esm_sb[h][:], enz_sb[h][:], ALU.mult))
                        dv.wait(dvS, vue)
                        vu[h] = dv.do(nc.vector.tensor_scalar(u_sb[h][:], esm_sb[h][:], sm1[h][:, 2:3], None, ALU.mult))
                    te.wait(dvS, trU_free)
                    tus = []
                    for h in range(2):
                        te.wait(dvS, vu[h])
                        te.do(nc.tensor.transpose(p_trS[0:64, 512 + h * 256:512 + h * 256 + 128], u_sb[h][:], s_idbf[:]))
                        tus.append(te.do(nc.tensor.transpose(p_trS[0:64, 512 + h * 256 + 128:512 + h * 256 + 256], essl[h], s_idbf[:])))
                    vesf = [0, 0]
                    for h in range(2):
                        dv.wait(mmS, tus[1])
                        dv.do(nc.vector.tensor_copy(ufm_sb[h][:], p_trS[0:64, 512 + h * 256:512 + h * 256 + 128]))
                        vesf[h] = dv.do(nc.vector.tensor_copy(esfm_sb[h][:], p_trS[0:64, 512 + h * 256 + 128:512 + h * 256 + 256]))
                    trU_free = max(vesf)
                    vw2 = [0, 0]
                    te.wait(dvS, wo_free)
                    for h in range(2):
                        te.wait(dvS, vesf[h])
                        vw2[h] = te.do(nc.tensor.matmul(p_wo[:, h * 128:(h + 1) * 128], esfm_sb[h][:], ufm_sb[h][:], start=True, stop=True))
                    vwm = [0, 0]
                    for h in range(2):
                        dv.wait(mmS, vw2[1])
                        vwm[h] = dv.do(nc.vector.tensor_tensor(w2m_sb[h][:], p_wo[:, h * 128:(h + 1) * 128], s_utri[:], ALU.mult))
                    wo_free = max(vwm)
                    vo2 = [0, 0]
                    te.wait(scS, oo_free)
                    for h in range(2):
                        te.wait(dvS, vwm[h])
                        te.do(nc.tensor.matmul(p_oo[:, h * 128:(h + 1) * 128], w2m_sb[h][:], vsl[h], start=True, stop=(c == 0), skip_group_check=True))
                        if c > 0:
                            vo2[h] = te.do(nc.tensor.matmul(p_oo[:, h * 128:(h + 1) * 128], ufm_sb[h][:], Z_sb[p][0:64, h * 128:(h + 1) * 128], start=False, stop=True, skip_group_check=True))
                        else:
                            vo2[h] = mmS.n
                    vosb = [0, 0]
                    for h in range(2):
                        sc.wait(mmS, max(vo2))
                        vosb[h] = sc.do(nc.scalar.copy(o_sb[h][:], p_oo[:, h * 128:(h + 1) * 128]))
                    oo_free = max(vosb)
                    veg = [0, 0]
                    for h in range(2):
                        veg[h] = sc.do(nc.scalar.activation(eg_sb[h][:], gsl[h], AF.Exp, scale=-1.0))
                    vrstd2 = [0, 0]
                    for h in range(2):
                        dv.wait(scS, vosb[h])
                        vbn2 = dv.do(nc.vector.bn_stats(bnst2[h][:], o_sb[h][:]))
                        dv.wait(dvS, vbn2)
                        vag2 = dv.do(nc.vector.bn_aggr(bnag2[h][:], bnst2[h][:]))
                        dv.wait(dvS, vag2)
                        vq3 = dv.do(nc.vector.tensor_tensor(sm1[h][:, 3:4], bnag2[h][:, 0:1], bnag2[h][:, 0:1], ALU.mult))
                        dv.wait(dvS, vq3)
                        vm2 = dv.do(nc.vector.tensor_add(sm1[h][:, 4:5], bnag2[h][:, 1:2], sm1[h][:, 3:4]))
                        sc.wait(dvS, vm2)
                        vl2 = sc.do(nc.scalar.activation(sm1[h][:, 5:6], sm1[h][:, 4:5], AF.Ln, bias=s_eps[:, 0:1]))
                        sc.wait(scS, vl2)
                        vrstd2[h] = sc.do(nc.scalar.activation(sm1[h][:, 5:6], sm1[h][:, 5:6], AF.Exp, scale=-0.5))
                    vog = [0, 0]
                    for h in range(2):
                        dv.wait(scS, veg[h])
                        vd1 = dv.do(nc.vector.tensor_scalar(eg_sb[h][:], eg_sb[h][:], 1.0, None, ALU.add))
                        dv.wait(dvS, vd1)
                        vr1 = dv.do(nc.vector.reciprocal(eg_sb[h][:], eg_sb[h][:]))
                        dv.wait(dvS, vr1)
                        vgg = dv.do(nc.vector.tensor_tensor(gg_sb[h][:], gsl[h], eg_sb[h][:], ALU.mult))
                        dv.wait(scS, vrstd2[h])
                        von = dv.do(nc.vector.tensor_scalar(o_sb[h][:], o_sb[h][:], sm1[h][:, 5:6], None, ALU.mult))
                        dv.wait(dvS, max(von, vgg))
                        von2 = dv.do(nc.vector.tensor_tensor(o_sb[h][:], o_sb[h][:], s_gnwb[:], ALU.mult))
                        dv.wait(dvS, von2)
                        vog[h] = dv.do(nc.vector.tensor_tensor(og_sb[h][:], o_sb[h][:], gg_sb[h][:], ALU.mult))
                    te.wait(dvS, max(vq2))
                    tgs = []
                    for h in range(2):
                        te.wait(dvS, vog[h])
                        tgs.append(te.do(nc.tensor.transpose(p_trS[:, h * 256:h * 256 + 128], og_sb[h][:], s_idbf[:])))
                    vga = [0, 0]
                    for h in range(2):
                        dv.wait(mmS, tgs[1])
                        vga[h] = dv.do(nc.vector.tensor_copy(og_all[:, h * T + c * 128:h * T + (c + 1) * 128], p_trS[:, h * 256:h * 256 + 128]))
                    trS_free = max(vga)
                    attc_mm[p] = mmS.n
                    attc_dv[p] = dvS.n
                if t_og is not None:
                    nc.sync.wait_ge(dvS.h, dvS.n)
                    for r in range(2):
                        sy.inc(nc.sync.dma_start(out=bass.AP(t_og.tensor, r * 128 * T, [[T, 128], [1, T]]), in_=og_all[:, r * T:(r + 1) * T]), 16)
                    nc.sync.wait_ge(sy.h, sy.n)
            bar()

        if "attpart" in ph:
            with ExitStack() as fctx:
                FE = fctx.enter_context
                awo_sb = FE(nc.sbuf_tensor("awo_sb", [128, 2 * D], BF16))
                pstg2 = [FE(nc.sbuf_tensor(f"pstg2_{p}", [128, D], F32)) for p in range(2)]
                p_pp2 = [FE(nc.psum_tensor(f"p_pp2_{p}", [128, 512], F32)) for p in range(2)]
                for r in range(2):
                    gp.inc(nc.gpsimd.dma_start(out=awo_sb[:, r * D:(r + 1) * D], in_=aWo[r * 128:(r + 1) * 128, :]), 16)
                te.wait(gp, gp.n)
                part_st = [0, 0]
                evac_done = [0, 0]
                for tc in range(NCH):
                    pp = tc % 2
                    if part_st[pp]:
                        nc.vector.wait_ge(qs[pp].h, part_st[pp])
                    vev = 0
                    for dsl in range(4):
                        pb2 = dsl % 2
                        te.wait(dvS, evac_done[pb2])
                        vmm = 0
                        for r in range(2):
                            vmm = te.do(nc.tensor.matmul(
                                p_pp2[pb2][:, :],
                                og_all[:, r * T + tc * 128:r * T + (tc + 1) * 128],
                                awo_sb[:, r * D + dsl * 512:r * D + (dsl + 1) * 512],
                                start=(r == 0), stop=(r == 1)))
                        dv.wait(mmS, vmm)
                        vev = dv.do(nc.vector.tensor_copy(pstg2[pp][:, dsl * 512:(dsl + 1) * 512], p_pp2[pb2][:, :]))
                        evac_done[pb2] = vev
                    nc.sync.wait_ge(dvS.h, vev)
                    part_st[pp] = qs[pp].inc(nc.sync.dma_start(out=cc_a_in[tc * 128:(tc + 1) * 128, :], in_=pstg2[pp][:]), 16)
                if t_attpart is not None:
                    nc.sync.wait_ge(qs[0].h, qs[0].n)
                    nc.sync.wait_ge(qs[1].h, qs[1].n)
                    for tc in range(NCH):
                        sy.inc(nc.sync.dma_start(out=t_attpart[tc * 128:(tc + 1) * 128, :], in_=cc_a_in[tc * 128:(tc + 1) * 128, :]), 16)
                    nc.sync.wait_ge(sy.h, sy.n)
                nc.gpsimd.wait_ge(qs[0].h, qs[0].n)
                nc.gpsimd.wait_ge(qs[1].h, qs[1].n)
                if "ffnpart" in ph:
                    nc.gpsimd.wait_ge(ccS.h, ccu)
                if fake_cc:
                    ccS.inc(nc.gpsimd.dma_start(out=cc_a_out[:, :], in_=cc_a_in[0:TOWN, :]), 16)
                else:
                    ccS.inc(nc.gpsimd.collective_compute(
                        "ReduceScatter", ALU.add,
                        replica_groups=[list(range(NCORES))],
                        ins=[cc_a_in.tensor.ap().opt()],
                        outs=[cc_a_out.tensor.ap().opt()]), 1)
            bar()
        att_ctx.close()

        if "x2h2" in ph:
            x2a = E(nc.sbuf_tensor("x2a", [128, NJ * TOWN], F32))
            h2sb = E(nc.sbuf_tensor("h2sb", [128, NJ * TOWN], BF16))
            wfm = E(nc.sbuf_tensor("wfm", [16, TOWN], F32))
            with ExitStack() as xctx:
                XE = xctx.enter_context
                rsf = [XE(nc.sbuf_tensor(f"rsf{p}", [128, D], F32)) for p in range(2)]
                xno = [XE(nc.sbuf_tensor(f"xno{p}", [128, D], F32)) for p in range(2)]
                x2tm = XE(nc.sbuf_tensor("x2tm", [128, D], F32))
                bnst3 = XE(nc.sbuf_tensor("bnst3", [128, 24], F32))
                bnag3 = XE(nc.sbuf_tensor("bnag3", [128, 2], F32))
                sm3 = XE(nc.sbuf_tensor("sm3", [128, 4], F32))
                rr2row = XE(nc.sbuf_tensor("rr2row", [1, TOWN], F32))
                rr2sb = XE(nc.sbuf_tensor("rr2sb", [1, 128], F32))
                p_tx = XE(nc.psum_tensor("p_tx", [128, 512], F32))
                p_tx2 = XE(nc.psum_tensor("p_tx2", [128, 512], F32))
                p_rr2 = XE(nc.psum_tensor("p_rr2", [1, 128], F32))
                p_rb2 = XE(nc.psum_tensor("p_rb2", [128, 512], F32))

                if "ffnpart" in ph:
                    nc.sync.wait_ge(ccS.h, ccu)
                cid = nc.gpsimd.partition_id()
                x2_done = 0
                rr2_ev = 0
                xno_done = [0, 0]
                for oc in range(NOC):
                    p = oc % 2
                    if xno_done[p]:
                        nc.sync.wait_ge(dvS.h, xno_done[p])
                        nc.gpsimd.wait_ge(dvS.h, xno_done[p])
                    vld1 = qs[p].inc(nc.sync.dma_start(out=rsf[p][:], in_=cc_f_out[oc * 128:(oc + 1) * 128, :]), 16)
                    xsrc = bass.AP(d_xn32.tensor, (cid * TOWN + oc * 128) * D, [[D, 128], [1, D]])
                    gqp = gp if p == 0 else gq
                    vld2 = gqp.inc(nc.gpsimd.dma_start(out=xno[p][:], in_=xsrc), 16)
                    dv.wait(qs[p], vld1)
                    dv.wait(gqp, vld2)
                    dv.wait(mmS, x2_done)
                    vx2 = dv.do(nc.vector.tensor_add(x2tm[:], xno[p][:], rsf[p][:]))
                    xno_done[p] = vx2
                    dv.wait(dvS, vx2)
                    for k in range(4):
                        vbn = dv.do(nc.vector.bn_stats(bnst3[:, k * 6:(k + 1) * 6], x2tm[:, k * 512:(k + 1) * 512]))
                    dv.wait(dvS, vbn)
                    vag = dv.do(nc.vector.bn_aggr(bnag3[:], bnst3[:]))
                    dv.wait(dvS, vag)
                    vq4 = dv.do(nc.vector.tensor_tensor(sm3[:, 0:1], bnag3[:, 0:1], bnag3[:, 0:1], ALU.mult))
                    dv.wait(dvS, vq4)
                    vms = dv.do(nc.vector.tensor_add(sm3[:, 1:2], bnag3[:, 1:2], sm3[:, 0:1]))
                    sc.wait(dvS, vms)
                    vl3 = sc.do(nc.scalar.activation(sm3[:, 2:3], sm3[:, 1:2], AF.Ln, bias=s_eps[:, 0:1]))
                    sc.wait(scS, vl3)
                    vrr2 = sc.do(nc.scalar.activation(sm3[:, 2:3], sm3[:, 2:3], AF.Exp, scale=-0.5))
                    te.wait(scS, vrr2)
                    te.wait(dvS, rr2_ev)
                    vtr2 = te.do(nc.tensor.transpose(p_rr2[:, :], sm3[:, 2:3], s_idf32[:]))
                    dv.wait(mmS, vtr2)
                    rr2_ev = dv.do(nc.vector.tensor_copy(rr2row[:, oc * 128:(oc + 1) * 128], p_rr2[:, :]))
                    vtx = 0
                    for j in range(NJ):
                        ptx = p_tx if (j // 4) % 2 == 0 else p_tx2
                        tp = nc.tensor.transpose(ptx[:, (j % 4) * 128:(j % 4 + 1) * 128], x2tm[:, j * 128:(j + 1) * 128], s_idf32[:])
                        if j % 4 == 3:
                            vt = te.do(tp)
                            dv.wait(mmS, vt)
                            for jj in range(j - 3, j + 1):
                                vtx = dv.do(nc.vector.tensor_copy(x2a[:, jj * TOWN + oc * 128:jj * TOWN + (oc + 1) * 128], ptx[:, (jj % 4) * 128:(jj % 4 + 1) * 128]))
                            te.wait(dvS, vtx)
                    x2_done = mmS.n
                vh2 = 0
                for tt2 in range(TOWN // 512):
                    te.wait(dvS, vh2)
                    vrb2 = te.do(nc.tensor.matmul(p_rb2[:, 0:512], s_ones_r128[:, :], rr2row[:, tt2 * 512:(tt2 + 1) * 512], start=True, stop=True))
                    dv.wait(mmS, vrb2)
                    for j in range(NJ):
                        vt1 = dv.do(nc.vector.tensor_tensor(h2sb[:, j * TOWN + tt2 * 512:j * TOWN + tt2 * 512 + 512],
                                                            x2a[:, j * TOWN + tt2 * 512:j * TOWN + tt2 * 512 + 512], p_rb2[:, 0:512], ALU.mult))
                        dv.wait(dvS, vt1)
                        vh2 = dv.do(nc.vector.tensor_scalar(h2sb[:, j * TOWN + tt2 * 512:j * TOWN + tt2 * 512 + 512],
                                                            h2sb[:, j * TOWN + tt2 * 512:j * TOWN + tt2 * 512 + 512], s_flnw[:, j:j + 1], None, ALU.mult))
                if t_x2 is not None:
                    nc.sync.wait_ge(dvS.h, dvS.n)
                    sy.inc(nc.sync.dma_start(out=t_x2[:, :], in_=x2a[:, :]), 16)
                if t_h2 is not None:
                    nc.sync.wait_ge(dvS.h, dvS.n)
                    sy.inc(nc.sync.dma_start(out=t_h2[:, :], in_=h2sb[:, :]), 16)
                nc.sync.wait_ge(sy.h, sy.n)
            bar()

        if "gate" in ph:
            with ExitStack() as gctx:
                GE = gctx.enter_context
                wg_sb = GE(nc.sbuf_tensor("wg_sb", [128, NJ * 16], BF16))
                lg = GE(nc.sbuf_tensor("lg", [128, 16], F32))
                lgt = GE(nc.sbuf_tensor("lgt", [128, 16], F32))
                msk = GE(nc.sbuf_tensor("msk", [128, 16], F32))
                gsm = GE(nc.sbuf_tensor("gsm", [128, 8], F32))
                p_lg = GE(nc.psum_tensor("p_lg", [128, 16], F32))
                p_wt = GE(nc.psum_tensor("p_wt", [16, 128], F32))
                vgw = gp.inc(nc.gpsimd.dma_start(out=wg_sb[:].rearrange("q (a b) -> q a b", a=NJ),
                                                 in_=bass.AP(fWg.tensor, 0, [[16, 128], [128 * 16, NJ], [1, 16]])), 16)
                te.wait(gp, gp.n)
                prev_lg = 0
                prev_wt_ev = 0
                for oc in range(NOC):
                    te.wait(dvS, prev_lg)
                    vml = 0
                    for j in range(NJ):
                        vml = te.do(nc.tensor.matmul(p_lg[:, :], h2sb[:, j * TOWN + oc * 128:j * TOWN + (oc + 1) * 128],
                                                     wg_sb[:, j * 16:(j + 1) * 16],
                                                     start=(j == 0), stop=(j == NJ - 1)))
                    dv.wait(mmS, vml)
                    prev_lg = dv.do(nc.vector.tensor_copy(lg[:], p_lg[:, :]))
                    dv.wait(dvS, prev_lg)
                    vcur = dv.do(nc.vector.tensor_copy(lgt[:], lg[:]))
                    for it in range(4):
                        dv.wait(dvS, vcur)
                        vmx = dv.do(nc.vector.reduce_max(gsm[:, it:it + 1], lgt[:], axis=AX.X))
                        if it < 3:
                            dv.wait(dvS, vmx)
                            vmk = dv.do(nc.vector.tensor_scalar(msk[:], lgt[:], gsm[:, it:it + 1], None, ALU.is_ge))
                            dv.wait(dvS, vmk)
                            vcur = dv.do(nc.vector.scalar_tensor_tensor(lgt[:], msk[:], -1e30, lgt[:], ALU.mult, ALU.add))
                    dv.wait(dvS, dvS.n)
                    vkp = dv.do(nc.vector.tensor_scalar(msk[:], lg[:], gsm[:, 3:4], None, ALU.is_ge))
                    vng = dv.do(nc.vector.tensor_scalar(gsm[:, 4:5], gsm[:, 0:1], -1.0, None, ALU.mult))
                    sc.wait(dvS, vng)
                    vem = sc.do(nc.scalar.activation(lgt[:], lg[:], AF.Exp, bias=gsm[:, 4:5]))
                    dv.wait(scS, vem)
                    vwu = dv.do(nc.vector.tensor_tensor(lgt[:], lgt[:], msk[:], ALU.mult))
                    dv.wait(dvS, vwu)
                    vde = dv.do(nc.vector.reduce_sum(gsm[:, 5:6], lgt[:], axis=AX.X))
                    dv.wait(dvS, vde)
                    vrd = dv.do(nc.vector.reciprocal(gsm[:, 6:7], gsm[:, 5:6]))
                    dv.wait(dvS, vrd)
                    vwt = dv.do(nc.vector.tensor_scalar(lgt[:], lgt[:], gsm[:, 6:7], None, ALU.mult))
                    te.wait(dvS, vwt)
                    te.wait(dvS, prev_wt_ev)
                    vtw = te.do(nc.tensor.transpose(p_wt[:, :], lgt[:], s_idf32[:]))
                    dv.wait(mmS, vtw)
                    prev_wt_ev = dv.do(nc.vector.tensor_copy(wfm[:, oc * 128:(oc + 1) * 128], p_wt[:, :]))
                if t_gate is not None:
                    nc.sync.wait_ge(dvS.h, dvS.n)
                    sy.inc(nc.sync.dma_start(out=t_gate[:, :], in_=wfm[:, :]), 16)
                    nc.sync.wait_ge(sy.h, sy.n)
            bar()

        if "moe" in ph:
            with ExitStack() as mctx:
                ME = mctx.enter_context
                w13 = [ME(nc.sbuf_tensor(f"w13_{p}", [128, 2 * EH], BF16)) for p in range(2)]
                w2b = [ME(nc.sbuf_tensor(f"w2b_{p}", [128, D], BF16)) for p in range(2)]
                a_t = ME(nc.sbuf_tensor("a_t", [128, NH * TOWN], BF16))
                silb = ME(nc.sbuf_tensor("silb", [128, TOWN], F32))
                wbt = ME(nc.sbuf_tensor("wbt", [128, TOWN], F32))
                p1m = [ME(nc.psum_tensor(f"p1m{p}", [128, 512], F32)) for p in range(2)]
                p3m = [ME(nc.psum_tensor(f"p3m{p}", [128, 512], F32)) for p in range(2)]
                pym = [ME(nc.psum_tensor(f"pym{p}", [128, 512], F32)) for p in range(2)]
                pbm = ME(nc.psum_tensor("pbm", [128, 512], F32))

                NT2 = TOWN // 512
                w13_done = [0, 0]
                w2_done = [0, 0]
                wb_mm_done = 0
                a_evac = [(0, 0), (0, 0)]
                y_evac = [0, 0]
                w13_ready = {}
                w2_ready = {}

                def issue_w13(e, hc):
                    p = hc % 2
                    if w13_done[p]:
                        nc.gpsimd.wait_ge(mmS.h, w13_done[p])
                    w13s[p].inc(nc.gpsimd.dma_start(out=w13[p][:, 0:EH], in_=w1[e * NH + hc, :, :]), 16)
                    w13_ready[hc] = w13s[p].inc(nc.gpsimd.dma_start(out=w13[p][:, EH:2 * EH], in_=w3[e * NH + hc, :, :]), 16)

                def issue_w2(e, dc):
                    p = dc % 2
                    if w2_done[p]:
                        nc.gpsimd.wait_ge(mmS.h, w2_done[p])
                    w2_ready[dc] = w2s[p].inc(nc.gpsimd.dma_start(out=w2b[p][:, :], in_=w2[e * NJ + dc, :, :]), 16)

                for e in range(NE):
                    for half in range(NT2):
                        te.wait(dvS, dvS.n)
                        vbm = te.do(nc.tensor.matmul(pbm[:, :], s_sel[:, e * 128:(e + 1) * 128], wfm[:, half * 512:(half + 1) * 512], start=True, stop=True))
                        wb_mm_done = vbm
                        dv.wait(mmS, vbm)
                        dv.do(nc.vector.tensor_copy(wbt[:, half * 512:(half + 1) * 512], pbm[:, :]))
                    wb_ready = dvS.n

                    issue_w13(e, 0)
                    issue_w13(e, 1)
                    for hc in range(NH):
                        p = hc % 2
                        te.wait(w13s[p], w13_ready[hc])
                        for th in range(NT2):
                            gp2 = (hc * NT2 + th) % 2
                            dvn, scn = a_evac[gp2]
                            if dvn:
                                te.wait(dvS, dvn)
                                te.wait(scS, scn)
                            vmm = 0
                            for dc in range(NJ):
                                te.do(nc.tensor.matmul(p1m[gp2][:, :], w13[p][:, dc * 128:(dc + 1) * 128],
                                                       h2sb[:, dc * TOWN + th * 512:dc * TOWN + th * 512 + 512],
                                                       start=(dc == 0), stop=(dc == NJ - 1)))
                                vmm = te.do(nc.tensor.matmul(p3m[gp2][:, :], w13[p][:, EH + dc * 128:EH + (dc + 1) * 128],
                                                             h2sb[:, dc * TOWN + th * 512:dc * TOWN + th * 512 + 512],
                                                             start=(dc == 0), stop=(dc == NJ - 1)))
                            if th == NT2 - 1:
                                w13_done[p] = vmm
                            sc.wait(mmS, vmm)
                            vsa = sc.do(nc.scalar.activation(silb[:, th * 512:(th + 1) * 512], p1m[gp2][:, :], AF.Silu))
                            dv.wait(scS, vsa)
                            dv.wait(dvS, wb_ready)
                            vt1 = dv.do(nc.vector.tensor_tensor(silb[:, th * 512:(th + 1) * 512], silb[:, th * 512:(th + 1) * 512], p3m[gp2][:, :], ALU.mult))
                            dv.wait(dvS, vt1)
                            vt2 = dv.do(nc.vector.tensor_tensor(a_t[:, hc * TOWN + th * 512:hc * TOWN + th * 512 + 512],
                                                                silb[:, th * 512:(th + 1) * 512], wbt[:, th * 512:(th + 1) * 512], ALU.mult))
                            a_evac[gp2] = (vt2, vsa)
                        if hc + 2 < NH:
                            issue_w13(e, hc + 2)
                        elif hc == NH - 2:
                            issue_w2(e, 0)
                        elif hc == NH - 1:
                            issue_w2(e, 1)
                    a_t_ready = dvS.n

                    te.wait(dvS, a_t_ready)
                    for dc in range(NJ):
                        p = dc % 2
                        te.wait(w2s[p], w2_ready[dc])
                        for th in range(NT2):
                            gp2 = (dc * NT2 + th) % 2
                            if y_evac[gp2]:
                                te.wait(dvS, y_evac[gp2])
                            vmm = 0
                            for hc in range(NH):
                                vmm = te.do(nc.tensor.matmul(pym[gp2][:, :], w2b[p][:, hc * 128:(hc + 1) * 128],
                                                             a_t[:, hc * TOWN + th * 512:hc * TOWN + th * 512 + 512],
                                                             start=(hc == 0), stop=(hc == NH - 1)))
                            if th == NT2 - 1:
                                w2_done[p] = vmm
                            dv.wait(mmS, vmm)
                            ad = dv.do(nc.vector.tensor_add(x2a[:, dc * TOWN + th * 512:dc * TOWN + th * 512 + 512],
                                                            x2a[:, dc * TOWN + th * 512:dc * TOWN + th * 512 + 512], pym[gp2][:, :]))
                            y_evac[gp2] = ad
                        if dc + 2 < NJ:
                            issue_w2(e, dc + 2)
                if t_moe is not None:
                    nc.sync.wait_ge(dvS.h, dvS.n)
                    sy.inc(nc.sync.dma_start(out=t_moe[:, :], in_=x2a[:, :]), 16)
                    nc.sync.wait_ge(sy.h, sy.n)
            bar()

        if "final" in ph:
            with ExitStack() as fctx:
                FE = fctx.enter_context
                sso = [FE(nc.sbuf_tensor(f"sso{p}", [128, D], F32)) for p in range(2)]
                xoo = [FE(nc.sbuf_tensor(f"xoo{p}", [128, D], F32)) for p in range(2)]
                ftm = FE(nc.sbuf_tensor("ftm", [128, D], F32))
                otm = [FE(nc.sbuf_tensor(f"otm{p}", [128, D], F32)) for p in range(2)]
                p_tf = FE(nc.psum_tensor("p_tf", [128, 512], F32))
                p_tf2 = FE(nc.psum_tensor("p_tf2", [128, 512], F32))
                if "attpart" in ph:
                    nc.sync.wait_ge(ccS.h, ccS.n)
                ftm_done = 0
                out_st = [0, 0]
                for oc in range(NOC):
                    p = oc % 2
                    vld1 = qs[p].inc(nc.sync.dma_start(out=sso[p][:], in_=cc_a_out[oc * 128:(oc + 1) * 128, :]), 16)
                    vld2 = qs[2 + p].inc(nc.sync.dma_start(out=xoo[p][:], in_=x_own[oc * 128:(oc + 1) * 128, :]), 16)
                    vtx = 0
                    for j in range(NJ):
                        ptx = p_tf if (j // 4) % 2 == 0 else p_tf2
                        tp = nc.tensor.transpose(ptx[:, (j % 4) * 128:(j % 4 + 1) * 128], x2a[:, j * TOWN + oc * 128:j * TOWN + (oc + 1) * 128], s_idf32[:])
                        if j % 4 == 3:
                            vt = te.do(tp)
                            dv.wait(mmS, vt)
                            dv.wait(scS, ftm_done)
                            for jj in range(j - 3, j + 1):
                                vtx = dv.do(nc.vector.tensor_copy(ftm[:, jj * 128:(jj + 1) * 128], ptx[:, (jj % 4) * 128:(jj % 4 + 1) * 128]))
                            te.wait(dvS, vtx)
                    sc.wait(dvS, vtx)
                    vsil = sc.do(nc.scalar.activation(ftm[:], ftm[:], AF.Silu))
                    ftm_done = vsil
                    dv.wait(scS, vsil)
                    dv.wait(qs[p], vld1)
                    dv.wait(qs[2 + p], vld2)
                    if out_st[p]:
                        dv.wait(st, st.n)
                    vp1 = dv.do(nc.vector.tensor_tensor(otm[p][:], ftm[:], sso[p][:], ALU.mult))
                    dv.wait(dvS, vp1)
                    vout = dv.do(nc.vector.tensor_add(otm[p][:], otm[p][:], xoo[p][:]))
                    nc.sync.wait_ge(dvS.h, vout)
                    out_st[p] = st.inc(nc.sync.dma_start(out=o_own[oc * 128:(oc + 1) * 128, :], in_=otm[p][:]), 16)
                nc.sync.wait_ge(st.h, st.n)
        else:
            zz = E(nc.sbuf_tensor("zz", [128, D], F32))
            vz = dv.do(nc.vector.memset(zz[:], 0.0))
            nc.sync.wait_ge(dvS.h, vz)
            for oc in range(NOC):
                st.inc(nc.sync.dma_start(out=o_own[oc * 128:(oc + 1) * 128, :], in_=zz[:]), 16)
            nc.sync.wait_ge(st.h, st.n)

    return nc, dbg



def _prep_inputs(inputs, ci):
    g = lambda k: np.asarray(inputs[k])
    x = np.ascontiguousarray(g("L0_attn_hidden_states").reshape(T, D))
    hs = slice(ci * AH * ADK, (ci + 1) * AH * ADK)
    fqs = slice(ci * FQH * FHD, (ci + 1) * FQH * FHD)
    fks = slice(ci * FHD, (ci + 1) * FHD)
    ams = slice(ci * AH * AM, (ci + 1) * AH * AM)
    aWs2T = np.ascontiguousarray(g("L0_attn_Ws2").T)
    ws2aug = np.concatenate([aWs2T[:, ams], g("L0_attn_bs2")[None, ams]], axis=0)
    return {
        "x_full": x,
        "x_own": np.ascontiguousarray(x[ci * TOWN:(ci + 1) * TOWN]),
        "aWq": np.ascontiguousarray(g("L0_attn_Wq").T[:, hs]),
        "aWk": np.ascontiguousarray(g("L0_attn_Wk").T[:, hs]),
        "aWv": np.ascontiguousarray(g("L0_attn_Wv").T[:, hs]),
        "aWg": np.ascontiguousarray(g("L0_attn_Wg").T[:, hs]),
        "aWo": np.ascontiguousarray(g("L0_attn_Wo").T[hs, :]),
        "aWs1": np.ascontiguousarray(g("L0_attn_Ws1").T),
        "aWs2": np.ascontiguousarray(ws2aug),
        "gnw": np.ascontiguousarray(g("L0_attn_g_norm_weight")[None, :]),
        "lnw": np.ascontiguousarray(g("L0_ffn_ln_w")[None, :]),
        "flnw": np.ascontiguousarray(g("L0_ffn_ff_ln_w")[None, :]),
        "fWq": np.ascontiguousarray(g("L0_ffn_wq")[:, fqs]),
        "fWk": np.ascontiguousarray(g("L0_ffn_wk")[:, fks]),
        "fWv": np.ascontiguousarray(g("L0_ffn_wv")[:, fks]),
        "fWo": np.ascontiguousarray(g("L0_ffn_wo")[fqs, :]),
        "fWg": np.ascontiguousarray(g("L0_ffn_wg")),
        "w1": np.ascontiguousarray(g("L0_ffn_w1").reshape(NE, NJ, 128, NH, 128).transpose(0, 3, 2, 1, 4).reshape(NE * NH, 128, NJ * 128)),
        "w3": np.ascontiguousarray(g("L0_ffn_w3").reshape(NE, NJ, 128, NH, 128).transpose(0, 3, 2, 1, 4).reshape(NE * NH, 128, NJ * 128)),
        "w2": np.ascontiguousarray(g("L0_ffn_w2").reshape(NE, NH, 128, NJ, 128).transpose(0, 3, 2, 1, 4).reshape(NE * NJ, 128, NH * 128)),
    }


ALL_PHASES = ("ln", "ffnproj", "flash", "ffnpart", "attproj", "scan", "attpart",
              "x2h2", "gate", "moe", "final")


def kernel(**inputs):
    nc, _ = build_program(phases=ALL_PHASES)
    in_maps = [_prep_inputs(inputs, ci) for ci in range(NCORES)]
    res = run_bass_kernel_spmd(nc, in_maps, core_ids=list(range(NCORES)))
    out = np.concatenate([np.asarray(res.results[ci]["o_own"]) for ci in range(NCORES)], axis=0)
    return out.reshape(1, T, D).astype(np.float32)


# version 07
# speedup vs baseline: 1.1563x; 1.1563x over previous
import os
import sys

for _p in ("/opt/trn_rl_repo",):
    if os.path.isdir(_p) and _p not in sys.path:
        sys.path.insert(0, _p)

import numpy as np
import ml_dtypes
from contextlib import ExitStack

import concourse.bass as bass
import concourse.mybir as mybir
from concourse.bass_utils import run_bass_kernel_spmd

F32 = mybir.dt.float32
BF16 = mybir.dt.bfloat16
AF = mybir.ActivationFunctionType
ALU = mybir.AluOpType
AX = mybir.AxisListType

NCORES = 8
T = 8192
D = 2048
NJ = D // 128
NCH = T // 128
TOWN = T // NCORES
NOC = TOWN // 128

AH = 2
ADK = 128
AM = 64
FQH = 4
FHD = 64
NE = 16
EH = 2048
NH = EH // 128

CAP = 384
NSC = CAP // 128
LN_EPS = 1e-5
CLAMP = 32.0
ATTN_ROPE_BASE = 10000.0
FFN_ROPE_BASE = 150000.0
ASCALE = float(ADK) ** -0.5
FSCALE = float(FHD) ** -0.5


class S:

    def __init__(self, ctx, nc, name):
        self.h = ctx.enter_context(nc.semaphore(name))
        self.n = 0

    def inc(self, instr, k=1):
        instr.then_inc(self.h, k)
        self.n += k
        return self.n


class Eng:

    def __init__(self, eng, sem):
        self.e = eng
        self.s = sem

    def do(self, instr):
        return self.s.inc(instr)

    def wait(self, sem, val):
        if val > 0:
            self.e.wait_ge(sem.h, val)


def _rope_tables_attn(tlen, dk, base):
    inv = (1.0 / base ** (np.arange(0, dk, 2, dtype=np.float32) / dk)).astype(np.float32)
    ang = (np.arange(tlen, dtype=np.float32)[:, None] * inv[None, :]).astype(np.float32)
    return np.cos(ang).astype(np.float32), np.sin(ang).astype(np.float32)


def _rope_tables_ffn(tlen, hd, base):
    half = hd // 2
    inv = (1.0 / base ** (np.arange(half, dtype=np.float32) / half)).astype(np.float32)
    ang = (np.arange(tlen, dtype=np.float32)[:, None] * inv[None, :]).astype(np.float32)
    return np.cos(ang).astype(np.float32), np.sin(ang).astype(np.float32)


def build_program(debug_taps=(), phases=("ln",), fake_cc=False):
    nc = bass.Bass("TRN2", target_bir_lowering=False)
    taps = set(debug_taps)
    ph = set(phases)
    ccu = 16 if fake_cc else 1

    x_full = nc.dram_tensor("x_full", [T, D], F32, kind="ExternalInput").ap()
    x_own = nc.dram_tensor("x_own", [TOWN, D], F32, kind="ExternalInput").ap()
    aWq = nc.dram_tensor("aWq", [D, AH * ADK], F32, kind="ExternalInput").ap()
    aWk = nc.dram_tensor("aWk", [D, AH * ADK], F32, kind="ExternalInput").ap()
    aWv = nc.dram_tensor("aWv", [D, AH * ADK], F32, kind="ExternalInput").ap()
    aWg = nc.dram_tensor("aWg", [D, AH * ADK], F32, kind="ExternalInput").ap()
    aWo = nc.dram_tensor("aWo", [AH * ADK, D], F32, kind="ExternalInput").ap()
    aWs1 = nc.dram_tensor("aWs1", [D, 16], F32, kind="ExternalInput").ap()
    aWs2 = nc.dram_tensor("aWs2", [17, AH * AM], F32, kind="ExternalInput").ap()
    gnw = nc.dram_tensor("gnw", [1, ADK], F32, kind="ExternalInput").ap()
    lnw = nc.dram_tensor("lnw", [1, D], F32, kind="ExternalInput").ap()
    flnw = nc.dram_tensor("flnw", [1, D], F32, kind="ExternalInput").ap()
    fWq = nc.dram_tensor("fWq", [D, FQH * FHD], F32, kind="ExternalInput").ap()
    fWk = nc.dram_tensor("fWk", [D, FHD], F32, kind="ExternalInput").ap()
    fWv = nc.dram_tensor("fWv", [D, FHD], F32, kind="ExternalInput").ap()
    fWo = nc.dram_tensor("fWo", [FQH * FHD, D], F32, kind="ExternalInput").ap()
    fWg = nc.dram_tensor("fWg", [D, NE], F32, kind="ExternalInput").ap()
    w1 = nc.dram_tensor("w1", [NE * NH, 128, NJ * 128], F32, kind="ExternalInput").ap()
    w3 = nc.dram_tensor("w3", [NE * NH, 128, NJ * 128], F32, kind="ExternalInput").ap()
    w2 = nc.dram_tensor("w2", [NE * NJ, 128, NH * 128], F32, kind="ExternalInput").ap()

    o_own = nc.dram_tensor("o_own", [TOWN, D], F32, kind="ExternalOutput").ap()

    ident_bf = nc.inline_tensor(np.eye(128, dtype=ml_dtypes.bfloat16), "ident_bf").ap()
    ident_f32 = nc.inline_tensor(np.eye(128, dtype=np.float32), "ident_f32").ap()
    utri_np = np.triu(np.ones((128, 128), dtype=np.float32)).astype(ml_dtypes.bfloat16)
    utri = nc.inline_tensor(utri_np, "utri").ap()
    ca, sa_ = _rope_tables_attn(T, ADK, ATTN_ROPE_BASE)
    cf, sf_ = _rope_tables_ffn(T, FHD, FFN_ROPE_BASE)
    ca4 = np.tile(ca, (1, 4))
    sa4 = np.tile(sa_, (1, 4))
    cf5 = np.tile(cf, (1, 5))
    sf5 = np.tile(sf_, (1, 5))
    cos_a4 = nc.inline_tensor(ca4.astype(ml_dtypes.bfloat16), "cos_a4").ap()
    sin_a4 = nc.inline_tensor(sa4.astype(ml_dtypes.bfloat16), "sin_a4").ap()
    cos_f5 = nc.inline_tensor(cf5.astype(ml_dtypes.bfloat16), "cos_f5").ap()
    sin_f5 = nc.inline_tensor(sf5.astype(ml_dtypes.bfloat16), "sin_f5").ap()
    dm = np.zeros((128, 4 * 512), dtype=np.float32)
    for di, delta in enumerate((0, 128, 256, 384)):
        dm[:, di * 512:(di + 1) * 512] = (np.arange(128)[:, None] + delta <= np.arange(512)[None, :])
    dmask = nc.inline_tensor(dm.astype(ml_dtypes.bfloat16), "dmask").ap()
    selnp = np.zeros((16, 16 * 128), dtype=np.float32)
    for e in range(16):
        selnp[e, e * 128:(e + 1) * 128] = 1.0
    sel = nc.inline_tensor(selnp, "sel").ap()
    ones_r128_np = np.ones((1, 128), dtype=np.float32)
    ones_r128 = nc.inline_tensor(ones_r128_np, "ones_r128").ap()
    iota_cap = nc.inline_tensor(np.tile(np.arange(CAP, dtype=np.float32)[None, :], (128, 1)), "iota_cap").ap()
    ones_sq = nc.inline_tensor(np.ones((128, 128), dtype=ml_dtypes.bfloat16), "ones_sq").ap()
    ones_r64 = nc.inline_tensor(np.ones((1, 64), dtype=np.float32), "ones_r64").ap()
    ones_c128bf = nc.inline_tensor(np.ones((128, 1), dtype=ml_dtypes.bfloat16), "ones_c128bf").ap()

    d_xnfm = nc.dram_tensor("d_xnfm", [T // 512, 128, NJ * 512], BF16).ap()
    d_rr = nc.dram_tensor("d_rr", [1, T], F32).ap()
    d_xn32 = nc.dram_tensor("d_xn32", [T, D], F32).ap()
    d_att = nc.dram_tensor("d_att", [T, 1152], BF16).ap()
    cc_f_in = nc.dram_tensor("cc_f_in", [T, D], F32).ap()
    cc_f_out = nc.dram_tensor("cc_f_out", [TOWN, D], F32).ap()
    cc_a_in = nc.dram_tensor("cc_a_in", [T, D], F32).ap()
    cc_a_out = nc.dram_tensor("cc_a_out", [TOWN, D], F32).ap()

    dbg = {}

    def tap(name, shape, dtype=F32):
        if name in taps:
            dbg[name] = nc.dram_tensor("dbg_" + name, shape, dtype, kind="ExternalOutput").ap()
            return dbg[name]
        return None

    t_xn = tap("xn", [NJ, 128, T], BF16)
    t_rr = tap("rr", [1, T])
    t_ffnq = tap("ffnq", [4, 64, T], BF16)
    t_ffnk = tap("ffnk", [64, T], BF16)
    t_ffnv = tap("ffnv", [128, NCH * 65], BF16)
    t_mhat = tap("mhat", [1, 8])
    t_flash = tap("flash", [2, 128, T], BF16)
    t_ffnpart = tap("ffnpart", [T, D])
    t_att = tap("att", [T, 1152], BF16)
    t_og = tap("og", [2, 128, T], BF16)
    t_attpart = tap("attpart", [T, D])
    t_x2 = tap("x2", [128, NJ * TOWN])
    t_h2 = tap("h2", [128, NJ * TOWN], BF16)
    t_gate = tap("gate", [16, TOWN])
    t_moe = tap("moe", [128, NJ * TOWN])

    with ExitStack() as ctx:
        E = ctx.enter_context

        s_idbf = E(nc.sbuf_tensor("s_idbf", [128, 128], BF16))
        s_idf32 = E(nc.sbuf_tensor("s_idf32", [128, 128], F32))
        s_utri = E(nc.sbuf_tensor("s_utri", [128, 128], BF16))
        s_ones_r128 = E(nc.sbuf_tensor("s_ones_r128", [1, 128], F32))
        s_ones_r64 = E(nc.sbuf_tensor("s_ones_r64", [1, 64], F32))
        s_ones_c128bf = E(nc.sbuf_tensor("s_ones_c128bf", [128, 1], BF16))
        s_lnw = E(nc.sbuf_tensor("s_lnw", [128, NJ], F32))
        s_flnw = E(nc.sbuf_tensor("s_flnw", [128, NJ], F32))
        s_gnwb = E(nc.sbuf_tensor("s_gnwb", [128, 128], F32))
        s_eps = E(nc.sbuf_tensor("s_eps", [128, 1], F32))
        s_dmask = E(nc.sbuf_tensor("s_dmask", [128, 4 * 512], BF16))
        s_sel = E(nc.sbuf_tensor("s_sel", [16, 16 * 128], F32))

        sy = S(ctx, nc, "sy")
        st = S(ctx, nc, "st")
        gp = S(ctx, nc, "gp")
        gq = S(ctx, nc, "gq")
        mmS = S(ctx, nc, "mm")
        dvS = S(ctx, nc, "dv")
        scS = S(ctx, nc, "sc")
        plS = S(ctx, nc, "pl")
        ccS = S(ctx, nc, "cc")
        w13s = [S(ctx, nc, f"w13s{p}") for p in range(2)]
        w2s = [S(ctx, nc, f"w2s{p}") for p in range(2)]
        qs = [S(ctx, nc, f"qs{k}") for k in range(12)]

        te = Eng(nc.tensor, mmS)
        dv = Eng(nc.vector, dvS)
        sc = Eng(nc.scalar, scS)
        pl = Eng(nc.gpsimd, plS)

        def bar():
            nc.all_engine_barrier()

        dv.do(nc.vector.memset(s_eps[:], LN_EPS))
        sy.inc(nc.sync.dma_start(out=s_idbf[:], in_=ident_bf[:, :]), 16)
        sy.inc(nc.sync.dma_start(out=s_idf32[:], in_=ident_f32[:, :]), 16)
        sy.inc(nc.sync.dma_start(out=s_utri[:], in_=utri[:, :]), 16)
        sy.inc(nc.sync.dma_start(out=s_ones_r128[:], in_=ones_r128[:, :]), 16)
        sy.inc(nc.sync.dma_start(out=s_ones_r64[:], in_=ones_r64[:, :]), 16)
        sy.inc(nc.sync.dma_start(out=s_ones_c128bf[:], in_=ones_c128bf[:, :]), 16)
        sy.inc(nc.sync.dma_start(out=s_dmask[:], in_=dmask[:, :]), 16)
        sy.inc(nc.sync.dma_start(out=s_sel[:], in_=sel[:, :]), 16)
        lnw_src = bass.AP(lnw.tensor, 0, [[1, 128], [128, NJ]])
        flnw_src = bass.AP(flnw.tensor, 0, [[1, 128], [128, NJ]])
        sy.inc(nc.sync.dma_start(out=s_lnw[:], in_=lnw_src, allow_slow_non_contiguous=True), 16)
        sy.inc(nc.sync.dma_start(out=s_flnw[:], in_=flnw_src, allow_slow_non_contiguous=True), 16)
        gnw_row = E(nc.sbuf_tensor("gnw_row", [1, ADK], F32))
        sy.inc(nc.sync.dma_start(out=gnw_row[:], in_=gnw[:, :]), 16)
        gate_ctx = ExitStack()
        te.wait(sy, sy.n)
        p_c = E(nc.psum_tensor("p_c", [128, 512], F32))
        vb = te.do(nc.tensor.matmul(p_c[:, 0:128], s_ones_r128[:, :], gnw_row[:, :], start=True, stop=True))
        dv.wait(mmS, vb)
        dv.do(nc.vector.tensor_copy(s_gnwb[:], p_c[:, 0:128]))
        bar()

        if "ln" in ph:
          with ExitStack() as lctx:
            LE = lctx.enter_context
            sx = [LE(nc.sbuf_tensor(f"sx{p}", [128, D], F32)) for p in range(2)]
            xn32 = [LE(nc.sbuf_tensor(f"xn32_{p}", [128, D], F32)) for p in range(2)]
            xnbf = [LE(nc.sbuf_tensor(f"xnbf_{p}", [128, D], BF16)) for p in range(2)]
            xnfm_st = [LE(nc.sbuf_tensor(f"xnfm_st{p}", [128, NJ * 512], BF16)) for p in range(2)]
            bnst = [LE(nc.sbuf_tensor(f"bnst{p}", [128, 24], F32)) for p in range(2)]
            bnag = [LE(nc.sbuf_tensor(f"bnag{p}", [128, 2], F32)) for p in range(2)]
            ln_sm = [LE(nc.sbuf_tensor(f"ln_sm{p}", [128, 4], F32)) for p in range(2)]
            p_trb = LE(nc.psum_tensor("p_trb", [128, 1024], BF16))
            p_trb2 = LE(nc.psum_tensor("p_trb2", [128, 1024], BF16))
            p_rr = LE(nc.psum_tensor("p_rr", [1, 128], F32))
            rr_sb = [LE(nc.sbuf_tensor(f"rr_sb{p}", [1, 128], F32)) for p in range(2)]

            store_done = [0, 0]
            rr_evac_prev = 0
            rr_store_prev = [0, 0]
            ldq = [qs[0], qs[1]]
            stq = [qs[2], qs[3]]
            xn32_done = [0, 0]
            for c in range(NCH):
                p = c % 2
                if xn32_done[p]:
                    nc.sync.wait_ge(qs[8 + p].h, xn32_done[p])
                vld = ldq[p].inc(nc.sync.dma_start(out=sx[p][:], in_=x_full[c * 128:(c + 1) * 128, :]), 16)
                dv.wait(ldq[p], vld)
                vbn = 0
                for k in range(4):
                    vbn = dv.do(nc.vector.bn_stats(bnst[p][:, k * 6:(k + 1) * 6], sx[p][:, k * 512:(k + 1) * 512]))
                dv.wait(dvS, vbn)
                vagg = dv.do(nc.vector.bn_aggr(bnag[p][:], bnst[p][:]))
                sc.wait(dvS, vagg)
                vs = sc.do(nc.scalar.activation(ln_sm[p][:, 1:2], bnag[p][:, 1:2], AF.Ln, bias=s_eps[:, 0:1]))
                sc.wait(scS, vs)
                vs_rstd = sc.do(nc.scalar.activation(ln_sm[p][:, 0:1], ln_sm[p][:, 1:2], AF.Exp, scale=-0.5))
                dv.wait(scS, vs_rstd)
                v1 = dv.do(nc.vector.tensor_tensor(ln_sm[p][:, 3:4], ln_sm[p][:, 0:1], ln_sm[p][:, 0:1], ALU.mult))
                dv.wait(dvS, v1)
                v2 = dv.do(nc.vector.tensor_tensor(ln_sm[p][:, 1:2], bnag[p][:, 1:2], ln_sm[p][:, 3:4], ALU.mult))
                sc.wait(dvS, v2)
                vs = sc.do(nc.scalar.activation(ln_sm[p][:, 2:3], ln_sm[p][:, 1:2], AF.Ln, bias=s_eps[:, 0:1]))
                sc.wait(scS, vs)
                vs_rr = sc.do(nc.scalar.activation(ln_sm[p][:, 2:3], ln_sm[p][:, 2:3], AF.Exp, scale=-0.5))
                dv.wait(scS, vs_rstd)
                vx = dv.do(nc.vector.tensor_scalar(xn32[p][:], sx[p][:], bnag[p][:, 0:1], ln_sm[p][:, 0:1], ALU.subtract, ALU.mult))
                dv.wait(dvS, vx)
                vxb = dv.do(nc.vector.tensor_copy(xnbf[p][:], xn32[p][:]))
                te.wait(scS, vs_rr)
                te.wait(dvS, rr_evac_prev)
                dv.wait(qs[7], qs[7].n)
                vtm = te.do(nc.tensor.transpose(p_rr[:, :], ln_sm[p][:, 2:3], s_idf32[:]))
                dv.wait(mmS, vtm)
                rr_evac_prev = dv.do(nc.vector.tensor_copy(rr_sb[p][:, :], p_rr[:, :]))
                nc.sync.wait_ge(dvS.h, rr_evac_prev)
                rr_store_prev[p] = qs[7].inc(nc.sync.dma_start(out=d_rr[:, c * 128:(c + 1) * 128], in_=rr_sb[p][:, :]), 16)
                te.wait(dvS, vxb)
                tp2 = (c // 4) % 2
                csub = c % 4
                vcp = 0
                for j in range(NJ):
                    pt = p_trb if j < 8 else p_trb2
                    tp = nc.tensor.transpose(pt[:, (j % 8) * 128:(j % 8 + 1) * 128], xnbf[p][:, j * 128:(j + 1) * 128], s_idbf[:])
                    if j % 8 == 7:
                        vt = te.do(tp)
                        dv.wait(mmS, vt)
                        if csub == 0 and j == 7:
                            nc.vector.wait_ge(stq[tp2].h, store_done[tp2])
                        for jj in range(j - 7, j + 1):
                            vcp = dv.do(nc.vector.tensor_copy(xnfm_st[tp2][:, jj * 512 + csub * 128:jj * 512 + (csub + 1) * 128], pt[:, (jj % 8) * 128:(jj % 8 + 1) * 128]))
                te.wait(dvS, vcp)
                nc.sync.wait_ge(dvS.h, vcp)
                if csub == 3:
                    stq[tp2].inc(nc.sync.dma_start(out=d_xnfm[c // 4, :, :], in_=xnfm_st[tp2][:]), 16)
                vstore = qs[8 + p].inc(nc.sync.dma_start(out=d_xn32[c * 128:(c + 1) * 128, :], in_=xn32[p][:]), 16)
                if t_xn is not None and csub == 3:
                    vstore = stq[tp2].inc(nc.sync.dma_start(out=bass.AP(t_xn.tensor, (c // 4) * 512, [[T, 128], [128 * T, NJ], [1, 512]]),
                                                            in_=xnfm_st[tp2][:].rearrange("q (a b) -> q a b", a=NJ)), 16)
                if csub == 3:
                    store_done[tp2] = stq[tp2].n
                xn32_done[p] = vstore

            nc.sync.wait_ge(sy.h, sy.n)
            nc.sync.wait_ge(stq[0].h, stq[0].n)
            nc.sync.wait_ge(stq[1].h, stq[1].n)
            nc.sync.wait_ge(qs[7].h, qs[7].n)
            nc.sync.wait_ge(qs[8].h, qs[8].n)
            nc.sync.wait_ge(qs[9].h, qs[9].n)
            bar()

        ffn_ctx = ExitStack()
        if "ffnproj" in ph or "flash" in ph:
            qres = ffn_ctx.enter_context(nc.sbuf_tensor("qres", [128, 4 * T], BF16))
            kres = ffn_ctx.enter_context(nc.sbuf_tensor("kres", [128, T], BF16))
            vres = ffn_ctx.enter_context(nc.sbuf_tensor("vres", [128, NCH * 65], BF16))
            mbias = ffn_ctx.enter_context(nc.sbuf_tensor("mbias", [128, 4], F32))
            ofm = ffn_ctx.enter_context(nc.sbuf_tensor("ofm", [128, 2 * T], BF16))
        if "ffnproj" in ph:
            with ExitStack() as fctx:
                FE = fctx.enter_context
                wqkv = FE(nc.sbuf_tensor("wqkv", [128, NJ * 384], BF16))
                tmpw = FE(nc.sbuf_tensor("tmpw", [128, 384], F32))
                xtl = [FE(nc.sbuf_tensor(f"xtl{p}", [128, NJ * 512], BF16)) for p in range(2)]
                stage = FE(nc.sbuf_tensor("stage", [128, 4 * 384], BF16))
                tmp1 = FE(nc.sbuf_tensor("tmp1", [128, 160], F32))
                tmp2 = FE(nc.sbuf_tensor("tmp2", [128, 160], F32))
                rrl = FE(nc.sbuf_tensor("rrl", [1, 512], F32))
                sqb = FE(nc.sbuf_tensor("sqb", [128, 320], F32))
                redb = FE(nc.sbuf_tensor("redb", [128, 5], F32))
                nmax = FE(nc.sbuf_tensor("nmax", [128, 5], F32))
                cosl = FE(nc.sbuf_tensor("cosl", [128, 4 * 160], BF16))
                sinl = FE(nc.sbuf_tensor("sinl", [128, 4 * 160], BF16))
                nrow = FE(nc.sbuf_tensor("nrow", [1, 5], F32))
                mrow = FE(nc.sbuf_tensor("mrow", [1, 4], F32))
                p_qkv = [FE(nc.psum_tensor(f"p_qkv{p}", [128, 384], F32)) for p in range(2)]
                p_rb = FE(nc.psum_tensor("p_rb", [128, 512], F32))
                p_tr = FE(nc.psum_tensor("p_tr", [128, 1024], BF16))

                v_ones = dv.do(nc.vector.memset(bass.AP(vres, 64, [[65 * NCH, 128], [65, NCH], [1, 1]]), 1.0))
                vnm = dv.do(nc.vector.memset(nmax[:], 0.0))
                dv.do(nc.vector.memset(qres[64:128, :], 0.0))
                dv.do(nc.vector.memset(kres[64:128, :], 0.0))

                vcast = 0
                for j in range(NJ):
                    if vcast:
                        nc.sync.wait_ge(dvS.h, vcast)
                    sy.inc(nc.sync.dma_start(out=tmpw[:, 0:256], in_=fWq[j * 128:(j + 1) * 128, :]), 16)
                    sy.inc(nc.sync.dma_start(out=tmpw[:, 256:320], in_=fWk[j * 128:(j + 1) * 128, :]), 16)
                    vwl = sy.inc(nc.sync.dma_start(out=tmpw[:, 320:384], in_=fWv[j * 128:(j + 1) * 128, :]), 16)
                    dv.wait(sy, vwl)
                    vq = dv.do(nc.vector.tensor_scalar(tmpw[:, 0:256], tmpw[:, 0:256], s_lnw[:, j:j + 1], float(FSCALE), ALU.mult, ALU.mult))
                    dv.wait(dvS, vq)
                    vkv = dv.do(nc.vector.tensor_scalar(tmpw[:, 256:384], tmpw[:, 256:384], s_lnw[:, j:j + 1], None, ALU.mult))
                    dv.wait(dvS, vkv)
                    vcast = dv.do(nc.vector.tensor_copy(wqkv[:, j * 384:(j + 1) * 384], tmpw[:]))
                    dv.wait(dvS, vcast)
                wq_ready = dvS.n

                ld_done = [0, 0]
                mm_qkv_done = [0, 0]
                rope_done = 0
                tr_evac = 0
                prev_tile_mm = 0
                prev_rb_mm = 0
                slot_mm = [0, 0]
                for tt in range(T // 512):
                    p2 = tt % 2
                    if slot_mm[p2]:
                        nc.sync.wait_ge(mmS.h, slot_mm[p2])
                    vld = qs[4 + p2].inc(nc.sync.dma_start(out=xtl[p2][:], in_=d_xnfm[tt, :, :]), 16)
                    nc.sync.wait_ge(dvS.h, rope_done)
                    nc.sync.wait_ge(mmS.h, prev_rb_mm)
                    qs[6].inc(nc.sync.dma_start(out=rrl[:, :], in_=d_rr[:, tt * 512:(tt + 1) * 512]), 16)
                    csrc = bass.AP(cos_f5.tensor, tt * 512 * 160, [[160, 128], [128 * 160, 4], [1, 160]])
                    ssrc = bass.AP(sin_f5.tensor, tt * 512 * 160, [[160, 128], [128 * 160, 4], [1, 160]])
                    qs[6].inc(nc.sync.dma_start(out=cosl[:].rearrange("q (a b) -> q a b", a=4), in_=csrc), 16)
                    vcs = qs[6].inc(nc.sync.dma_start(out=sinl[:].rearrange("q (a b) -> q a b", a=4), in_=ssrc), 16)
                    te.wait(dvS, rope_done)
                    te.wait(qs[6], qs[6].n)
                    vrb = te.do(nc.tensor.matmul(p_rb[:, 0:512], s_ones_r128[:, :], rrl[:, :], start=True, stop=True))
                    prev_rb_mm = vrb
                    dv.wait(qs[4 + p2], vld)
                    dv.wait(mmS, vrb)
                    dv.wait(dvS, wq_ready)
                    vxr = 0
                    for j in range(NJ):
                        vxr = dv.do(nc.vector.tensor_tensor(xtl[p2][:, j * 512:(j + 1) * 512], xtl[p2][:, j * 512:(j + 1) * 512], p_rb[:, 0:512], ALU.mult))
                    ld_done[p2] = vxr
                    slot_mm[p2] = None
                    rope_done_tile = 0
                    for sub in range(4):
                        pq = sub % 2
                        te.wait(dvS, vxr)
                        if mm_qkv_done[pq]:
                            te.wait(dvS, mm_qkv_done[pq])
                        vmm = 0
                        for j in range(NJ):
                            vmm = te.do(nc.tensor.matmul(p_qkv[pq][:, 0:384],
                                                         xtl[p2][:, j * 512 + sub * 128:j * 512 + (sub + 1) * 128],
                                                         wqkv[:, j * 384:(j + 1) * 384],
                                                         start=(j == 0), stop=(j == NJ - 1)))
                        dv.wait(mmS, vmm)
                        dv.wait(qs[6], vcs)
                        dv.wait(dvS, dvS.n)
                        h1 = bass.AP(p_qkv[pq], 0, [[384, 128], [64, 5], [1, 32]])
                        h2ap = bass.AP(p_qkv[pq], 32, [[384, 128], [64, 5], [1, 32]])
                        co = cosl[:, sub * 160:(sub + 1) * 160].rearrange("q (a b) -> q a b", a=5)
                        si = sinl[:, sub * 160:(sub + 1) * 160].rearrange("q (a b) -> q a b", a=5)
                        tm1 = tmp1[:].rearrange("q (a b) -> q a b", a=5)
                        tm2 = tmp2[:].rearrange("q (a b) -> q a b", a=5)
                        va = dv.do(nc.vector.tensor_tensor(tm1, h1, co, ALU.mult))
                        vb2 = dv.do(nc.vector.tensor_tensor(tm2, h2ap, si, ALU.mult))
                        dv.wait(dvS, vb2)
                        sg_h1 = bass.AP(stage, sub * 384, [[4 * 384, 128], [64, 5], [1, 32]])
                        sg_h2 = bass.AP(stage, sub * 384 + 32, [[4 * 384, 128], [64, 5], [1, 32]])
                        v_1 = dv.do(nc.vector.tensor_tensor(sg_h1, tm1, tm2, ALU.subtract))
                        dv.wait(dvS, v_1)
                        vc2 = dv.do(nc.vector.tensor_tensor(tm1, h2ap, co, ALU.mult))
                        vd2 = dv.do(nc.vector.tensor_tensor(tm2, h1, si, ALU.mult))
                        dv.wait(dvS, vd2)
                        v_2 = dv.do(nc.vector.tensor_tensor(sg_h2, tm1, tm2, ALU.add))
                        vv = dv.do(nc.vector.tensor_copy(vres[:, (tt * 4 + sub) * 65:(tt * 4 + sub) * 65 + 64], p_qkv[pq][:, 320:384]))
                        dv.wait(dvS, v_2)
                        vsq = dv.do(nc.vector.tensor_tensor(sqb[:], stage[:, sub * 384:sub * 384 + 320], stage[:, sub * 384:sub * 384 + 320], ALU.mult))
                        dv.wait(dvS, vsq)
                        vrd = dv.do(nc.vector.reduce_sum(redb[:], sqb[:].rearrange("q (a b) -> q a b", a=5), axis=AX.X))
                        dv.wait(dvS, vrd)
                        vmx = dv.do(nc.vector.tensor_max(nmax[:], nmax[:], redb[:]))
                        mm_qkv_done[pq] = vmx
                        rope_done_tile = max(rope_done_tile, vv, v_2)
                        te.wait(dvS, v_2)
                        te.wait(dvS, tr_evac)
                        t0_ = te.do(nc.tensor.transpose(p_tr[:, 0:128], stage[:, sub * 384:sub * 384 + 128], s_idbf[:]))
                        t1_ = te.do(nc.tensor.transpose(p_tr[:, 128:256], stage[:, sub * 384 + 128:sub * 384 + 256], s_idbf[:]))
                        t2_ = te.do(nc.tensor.transpose(p_tr[0:64, 256:384], stage[:, sub * 384 + 256:sub * 384 + 320], s_idbf[:]))
                        dv.wait(mmS, t2_)
                        tcol = tt * 512 + sub * 128
                        dv.do(nc.vector.tensor_copy(qres[0:64, 0 * T + tcol:0 * T + tcol + 128], p_tr[0:64, 0:128]))
                        dv.do(nc.vector.tensor_copy(qres[0:64, 1 * T + tcol:1 * T + tcol + 128], p_tr[64:128, 0:128]))
                        dv.do(nc.vector.tensor_copy(qres[0:64, 2 * T + tcol:2 * T + tcol + 128], p_tr[0:64, 128:256]))
                        dv.do(nc.vector.tensor_copy(qres[0:64, 3 * T + tcol:3 * T + tcol + 128], p_tr[64:128, 128:256]))
                        tr_evac = dv.do(nc.vector.tensor_copy(kres[0:64, tcol:tcol + 128], p_tr[0:64, 256:384]))
                    rope_done = max(rope_done_tile, tr_evac)
                    prev_tile_mm = vmm
                    slot_mm[p2] = vmm
                pl.wait(dvS, dvS.n)
                vgr = pl.do(nc.gpsimd.tensor_reduce(nrow[:], nmax[:], axis=AX.C, op=ALU.max))
                sc.wait(plS, vgr)
                vsr = sc.do(nc.scalar.activation(nrow[:], nrow[:], AF.Ln))
                sc.wait(scS, vsr)
                vsr = sc.do(nc.scalar.activation(nrow[:], nrow[:], AF.Exp, scale=0.5))
                dv.wait(scS, vsr)
                vmr = dv.do(nc.vector.tensor_scalar(mrow[:], nrow[:, 0:4], nrow[:, 4:5], -1.0, ALU.mult, ALU.mult))
                te.wait(dvS, vmr)
                for h in range(FQH):
                    vmb = te.do(nc.tensor.matmul(p_c[:, 0:1], s_ones_r128[:, :], mrow[:, h:h + 1], start=True, stop=True))
                    dv.wait(mmS, vmb)
                    vme = dv.do(nc.vector.tensor_copy(mbias[:, h:h + 1], p_c[:, 0:1]))
                    te.wait(dvS, vme)
                if t_ffnq is not None:
                    nc.sync.wait_ge(dvS.h, dvS.n)
                    for r in range(4):
                        sy.inc(nc.sync.dma_start(out=bass.AP(t_ffnq.tensor, r * 64 * T, [[T, 64], [1, T]]), in_=qres[0:64, r * T:(r + 1) * T]), 16)
                if t_ffnk is not None:
                    nc.sync.wait_ge(dvS.h, dvS.n)
                    sy.inc(nc.sync.dma_start(out=t_ffnk[:, :], in_=kres[0:64, :]), 16)
                if t_ffnv is not None:
                    nc.sync.wait_ge(dvS.h, dvS.n)
                    sy.inc(nc.sync.dma_start(out=t_ffnv[:, :], in_=vres[:, :]), 16)
                if t_mhat is not None:
                    nc.sync.wait_ge(dvS.h, dvS.n)
                    sy.inc(nc.sync.dma_start(out=t_mhat[:, 0:4], in_=mrow[:, :]), 16)
                    sy.inc(nc.sync.dma_start(out=t_mhat[:, 4:5], in_=nrow[:, 4:5]), 16)
                nc.sync.wait_ge(sy.h, sy.n)
            bar()

        if "flash" in ph:
            with ExitStack() as fctx:
                FE = fctx.enter_context
                psf = [FE(nc.psum_tensor(f"psf{p}", [128, 1024], F32)) for p in range(2)]
                p_o = [FE(nc.psum_tensor(f"p_o{p}", [65, 512], F32)) for p in range(2)]
                p_bc = FE(nc.psum_tensor("p_bc", [64, 512], F32))
                esb = FE(nc.sbuf_tensor("esb", [128, 2048], BF16))
                rl = [FE(nc.sbuf_tensor(f"rl{p}", [1, 512], F32)) for p in range(2)]
                rb = [FE(nc.sbuf_tensor(f"rb{p}", [64, 512], F32)) for p in range(2)]
                exp_done = [0, 0]
                av_done = [0, 0]
                fin_done = [0, 0]
                blocks = [(h, qb) for h in range(FQH) for qb in range(16)]

                def emit_stile(h, qb, b2):
                    par = b2 % 2
                    te.wait(scS, exp_done[par])
                    vmm = 0
                    for j2 in range(2):
                        kt = b2 * 2 + j2
                        vmm = te.do(nc.tensor.matmul(
                            psf[par][:, j2 * 512:(j2 + 1) * 512],
                            kres[:, kt * 128:(kt + 1) * 128],
                            qres[:, h * T + qb * 512:h * T + (qb + 1) * 512],
                            start=True, stop=True))
                    return vmm

                p_bc2 = [p_bc, None]
                obuf = [FE(nc.sbuf_tensor(f"obuf{p}", [64, 512], BF16)) for p in range(2)]
                pending_fin = [None]

                def emit_finalize():
                    if pending_fin[0] is None:
                        return
                    (fh, fqb, fopar, fav) = pending_fin[0]
                    pending_fin[0] = None
                    dv.wait(mmS, fav)
                    vrl = dv.do(nc.vector.reciprocal(rl[fopar][:, :], p_o[fopar][64:65, :]))
                    te.wait(dvS, vrl)
                    te.wait(dvS, fin_done[fopar])
                    bca = p_bc[:, :] if fopar == 0 else p_c[0:64, 0:512]
                    vbc = te.do(nc.tensor.matmul(bca, s_ones_r64[:, :], rl[fopar][:, :], start=True, stop=True))
                    dv.wait(mmS, vbc)
                    vrb = dv.do(nc.vector.tensor_copy(rb[fopar][:, :], bca))
                    dv.wait(dvS, vrb)
                    vtt = dv.do(nc.vector.tensor_tensor(obuf[fopar][:, :], p_o[fopar][0:64, :], rb[fopar][:, :], ALU.mult))
                    dv.wait(dvS, vtt)
                    fin_done[fopar] = dv.do(nc.vector.tensor_copy(
                        ofm[(fh % 2) * 64:(fh % 2) * 64 + 64, (fh // 2) * T + fqb * 512:(fh // 2) * T + (fqb + 1) * 512],
                        obuf[fopar][:, :]))

                for bi, (h, qb) in enumerate(blocks):
                    opar = bi % 2
                    nkt = 4 * (qb + 1)
                    nb = nkt // 2
                    st_mm = emit_stile(h, qb, 0)
                    stile_mm = {0: st_mm}
                    te.wait(dvS, fin_done[opar])
                    for b2 in range(nb):
                        par = b2 % 2
                        if b2 + 1 < nb:
                            stile_mm[b2 + 1] = emit_stile(h, qb, b2 + 1)
                        sc.wait(mmS, stile_mm[b2])
                        sc.wait(mmS, av_done[par])
                        vex = sc.do(nc.scalar.activation(esb[:, par * 1024:(par + 1) * 1024], psf[par][:, 0:1024],
                                                         AF.Exp, bias=mbias[:, h:h + 1]))
                        exp_done[par] = vex
                        vmask = 0
                        for j2 in range(2):
                            kt = b2 * 2 + j2
                            di = kt - 4 * qb
                            if di >= 0:
                                dv.wait(scS, vex)
                                eslice = esb[:, par * 1024 + j2 * 512:par * 1024 + (j2 + 1) * 512]
                                vmask = dv.do(nc.vector.tensor_tensor(eslice, eslice, s_dmask[:, di * 512:(di + 1) * 512], ALU.mult))
                        te.wait(scS, vex)
                        if vmask:
                            te.wait(dvS, vmask)
                        vav = 0
                        for j2 in range(2):
                            kt = b2 * 2 + j2
                            vav = te.do(nc.tensor.matmul(
                                p_o[opar][:, :],
                                vres[:, kt * 65:(kt + 1) * 65],
                                esb[:, par * 1024 + j2 * 512:par * 1024 + (j2 + 1) * 512],
                                start=(kt == 0), stop=(kt == nkt - 1)))
                        av_done[par] = vav
                        if b2 == 0:
                            emit_finalize()
                    pending_fin[0] = (h, qb, opar, av_done[par])
                emit_finalize()
                if t_flash is not None:
                    nc.sync.wait_ge(dvS.h, dvS.n)
                    for r in range(2):
                        sy.inc(nc.sync.dma_start(out=bass.AP(t_flash.tensor, r * 128 * T, [[T, 128], [1, T]]), in_=ofm[:, r * T:(r + 1) * T]), 16)
                    nc.sync.wait_ge(sy.h, sy.n)
            bar()

        if "ffnpart" in ph:
            with ExitStack() as fctx:
                FE = fctx.enter_context
                wo_sb = FE(nc.sbuf_tensor("wo_sb", [128, 2 * D], BF16))
                pstg = [FE(nc.sbuf_tensor(f"pstg{p}", [128, D], F32)) for p in range(2)]
                p_pp = [FE(nc.psum_tensor(f"p_pp{p}", [128, 512], F32)) for p in range(2)]
                for r in range(2):
                    gp.inc(nc.gpsimd.dma_start(out=wo_sb[:, r * D:(r + 1) * D], in_=fWo[r * 128:(r + 1) * 128, :]), 16)
                te.wait(gp, gp.n)
                part_st = [0, 0]
                evac_done = [0, 0]
                for tc in range(NCH):
                    pp = tc % 2
                    if part_st[pp]:
                        nc.vector.wait_ge(qs[pp].h, part_st[pp])
                    vev = 0
                    for dsl in range(4):
                        pb2 = dsl % 2
                        te.wait(dvS, evac_done[pb2])
                        vmm = 0
                        for r in range(2):
                            vmm = te.do(nc.tensor.matmul(
                                p_pp[pb2][:, :],
                                ofm[:, r * T + tc * 128:r * T + (tc + 1) * 128],
                                wo_sb[:, r * D + dsl * 512:r * D + (dsl + 1) * 512],
                                start=(r == 0), stop=(r == 1)))
                        dv.wait(mmS, vmm)
                        vev = dv.do(nc.vector.tensor_copy(pstg[pp][:, dsl * 512:(dsl + 1) * 512], p_pp[pb2][:, :]))
                        evac_done[pb2] = vev
                    nc.sync.wait_ge(dvS.h, vev)
                    part_st[pp] = qs[pp].inc(nc.sync.dma_start(out=cc_f_in[tc * 128:(tc + 1) * 128, :], in_=pstg[pp][:]), 16)
                if t_ffnpart is not None:
                    nc.sync.wait_ge(qs[0].h, qs[0].n)
                    nc.sync.wait_ge(qs[1].h, qs[1].n)
                    for tc in range(NCH):
                        sy.inc(nc.sync.dma_start(out=t_ffnpart[tc * 128:(tc + 1) * 128, :], in_=cc_f_in[tc * 128:(tc + 1) * 128, :]), 16)
                    nc.sync.wait_ge(sy.h, sy.n)
                nc.gpsimd.wait_ge(qs[0].h, qs[0].n)
                nc.gpsimd.wait_ge(qs[1].h, qs[1].n)
                if fake_cc:
                    ccS.inc(nc.gpsimd.dma_start(out=cc_f_out[:, :], in_=cc_f_in[0:TOWN, :]), 16)
                else:
                    ccS.inc(nc.gpsimd.collective_compute(
                        "ReduceScatter", ALU.add,
                        replica_groups=[list(range(NCORES))],
                        ins=[cc_f_in.tensor.ap().opt()],
                        outs=[cc_f_out.tensor.ap().opt()]), 1)
            bar()
        ffn_ctx.close()

        if "attproj" in ph:
            with ExitStack() as actx:
                AE = actx.enter_context
                awb = AE(nc.sbuf_tensor("awb", [128, NJ * 1024], BF16))
                ws1b = AE(nc.sbuf_tensor("ws1b", [128, NJ * 16], BF16))
                ws2b = AE(nc.sbuf_tensor("ws2b", [17, 128], BF16))
                tmpw2 = AE(nc.sbuf_tensor("tmpw2", [128, 1024], F32))
                xtl2 = [AE(nc.sbuf_tensor(f"xtl2_{p}", [128, NJ * 512], BF16)) for p in range(2)]
                astg = AE(nc.sbuf_tensor("astg", [128, 4 * 1152], BF16))
                s16a = AE(nc.sbuf_tensor("s16a", [17, 512], BF16))
                atm1 = AE(nc.sbuf_tensor("atm1", [128, 256], F32))
                atm2 = AE(nc.sbuf_tensor("atm2", [128, 256], F32))
                cosl2 = AE(nc.sbuf_tensor("cosl2", [128, 4 * 256], BF16))
                sinl2 = AE(nc.sbuf_tensor("sinl2", [128, 4 * 256], BF16))
                p_qg = [AE(nc.psum_tensor(f"p_qg{p}", [128, 512], F32)) for p in range(2)]
                p_vg = [AE(nc.psum_tensor(f"p_vg{p}", [128, 512], F32)) for p in range(2)]
                p_s16 = AE(nc.psum_tensor("p_s16", [16, 512], F32))
                p_s = AE(nc.psum_tensor("p_s", [128, 128], F32))

                vw_ones = dv.do(nc.vector.memset(s16a[:, :], 1.0))
                vcast = 0
                for j in range(NJ):
                    if vcast:
                        nc.sync.wait_ge(dvS.h, vcast)
                    sy.inc(nc.sync.dma_start(out=tmpw2[:, 0:256], in_=aWq[j * 128:(j + 1) * 128, :]), 16)
                    sy.inc(nc.sync.dma_start(out=tmpw2[:, 256:512], in_=aWk[j * 128:(j + 1) * 128, :]), 16)
                    sy.inc(nc.sync.dma_start(out=tmpw2[:, 512:768], in_=aWv[j * 128:(j + 1) * 128, :]), 16)
                    vwl = sy.inc(nc.sync.dma_start(out=tmpw2[:, 768:1024], in_=aWg[j * 128:(j + 1) * 128, :]), 16)
                    dv.wait(sy, vwl)
                    vq = dv.do(nc.vector.tensor_scalar(tmpw2[:, 0:256], tmpw2[:, 0:256], float(ASCALE), None, ALU.mult))
                    dv.wait(dvS, vq)
                    vcast = dv.do(nc.vector.tensor_copy(awb[:, j * 1024:(j + 1) * 1024], tmpw2[:]))
                vg1 = gp.inc(nc.gpsimd.dma_start(out=ws1b[:].rearrange("q (a b) -> q a b", a=NJ),
                                                 in_=bass.AP(aWs1.tensor, 0, [[16, 128], [128 * 16, NJ], [1, 16]])), 16)
                vg2 = gp.inc(nc.gpsimd.dma_start(out=ws2b[:], in_=aWs2[:, :]), 16)

                ld2_done = [0, 0]
                es_clip_done = 0
                vs16e = 0
                prev_mm2 = 0
                rope2_done = 0
                st2_done = [0, 0]
                es_mm_done = 0
                slot_mm2 = [0, 0]
                for tt in range(T // 512):
                    p2 = tt % 2
                    if slot_mm2[p2]:
                        nc.sync.wait_ge(mmS.h, slot_mm2[p2])
                    vld = qs[4 + p2].inc(nc.sync.dma_start(out=xtl2[p2][:], in_=d_xnfm[tt, :, :]), 16)
                    nc.sync.wait_ge(dvS.h, rope2_done)
                    csrc = bass.AP(cos_a4.tensor, tt * 512 * 256, [[256, 128], [128 * 256, 4], [1, 256]])
                    ssrc = bass.AP(sin_a4.tensor, tt * 512 * 256, [[256, 128], [128 * 256, 4], [1, 256]])
                    qs[6].inc(nc.sync.dma_start(out=cosl2[:].rearrange("q (a b) -> q a b", a=4), in_=csrc), 16)
                    vcs = qs[6].inc(nc.sync.dma_start(out=sinl2[:].rearrange("q (a b) -> q a b", a=4), in_=ssrc), 16)
                    te.wait(qs[4 + p2], vld)
                    te.wait(gp, gp.n)
                    te.wait(dvS, vcast)
                    te.wait(dvS, vw_ones)
                    te.wait(dvS, vs16e)
                    vs16 = 0
                    for j in range(NJ):
                        vs16 = te.do(nc.tensor.matmul(p_s16[:, :], ws1b[:, j * 16:(j + 1) * 16],
                                                      xtl2[p2][:, j * 512:(j + 1) * 512],
                                                      start=(j == 0), stop=(j == NJ - 1)))
                    dv.wait(mmS, vs16)
                    dv.wait(mmS, es_mm_done)
                    vs16e = dv.do(nc.vector.tensor_copy(s16a[0:16, :], p_s16[:, :]))
                    vmm = 0
                    rope2_tile = 0
                    for sub in range(4):
                        pq = sub % 2
                        te.wait(dvS, ld2_done[pq])
                        for j in range(NJ):
                            te.do(nc.tensor.matmul(p_qg[pq][:, :],
                                                   xtl2[p2][:, j * 512 + sub * 128:j * 512 + (sub + 1) * 128],
                                                   awb[:, j * 1024:j * 1024 + 512],
                                                   start=(j == 0), stop=(j == NJ - 1)))
                            vmm = te.do(nc.tensor.matmul(p_vg[pq][:, :],
                                                         xtl2[p2][:, j * 512 + sub * 128:j * 512 + (sub + 1) * 128],
                                                         awb[:, j * 1024 + 512:(j + 1) * 1024],
                                                         start=(j == 0), stop=(j == NJ - 1)))
                        dv.wait(mmS, vmm)
                        dv.wait(qs[6], vcs)
                        dv.wait(dvS, dvS.n)
                        h1 = bass.AP(p_qg[pq], 0, [[512, 128], [128, 4], [1, 64]])
                        h2ap = bass.AP(p_qg[pq], 64, [[512, 128], [128, 4], [1, 64]])
                        co = cosl2[:, sub * 256:(sub + 1) * 256].rearrange("q (a b) -> q a b", a=4)
                        si = sinl2[:, sub * 256:(sub + 1) * 256].rearrange("q (a b) -> q a b", a=4)
                        tm1 = atm1[:].rearrange("q (a b) -> q a b", a=4)
                        tm2 = atm2[:].rearrange("q (a b) -> q a b", a=4)
                        sg_h1 = bass.AP(astg, sub * 1152, [[4 * 1152, 128], [128, 4], [1, 64]])
                        sg_h2 = bass.AP(astg, sub * 1152 + 64, [[4 * 1152, 128], [128, 4], [1, 64]])
                        dv.do(nc.vector.tensor_tensor(tm1, h1, co, ALU.mult))
                        vb2 = dv.do(nc.vector.tensor_tensor(tm2, h2ap, si, ALU.mult))
                        dv.wait(dvS, vb2)
                        v_1 = dv.do(nc.vector.tensor_tensor(sg_h1, tm1, tm2, ALU.subtract))
                        dv.wait(dvS, v_1)
                        dv.do(nc.vector.tensor_tensor(tm1, h2ap, co, ALU.mult))
                        vd2 = dv.do(nc.vector.tensor_tensor(tm2, h1, si, ALU.mult))
                        dv.wait(dvS, vd2)
                        v_2 = dv.do(nc.vector.tensor_tensor(sg_h2, tm1, tm2, ALU.add))
                        vvg = dv.do(nc.vector.tensor_copy(astg[:, sub * 1152 + 512:sub * 1152 + 1024], p_vg[pq][:, :]))
                        ld2_done[pq] = vvg
                        rope2_tile = max(rope2_tile, v_2, vvg)
                        te.wait(dvS, vs16e)
                        te.wait(dvS, es_clip_done)
                        ves = te.do(nc.tensor.matmul(p_s[:, :], s16a[:, sub * 128:(sub + 1) * 128], ws2b[:, :], start=True, stop=True))
                        es_mm_done = ves
                        dv.wait(mmS, ves)
                        vclip = dv.do(nc.vector.tensor_scalar(astg[:, sub * 1152 + 1024:sub * 1152 + 1152], p_s[:, :], -CLAMP, CLAMP, ALU.max, ALU.min))
                        es_clip_done = vclip
                        sc.wait(dvS, vclip)
                        vexp = sc.do(nc.scalar.activation(astg[:, sub * 1152 + 1024:sub * 1152 + 1152],
                                                          astg[:, sub * 1152 + 1024:sub * 1152 + 1152], AF.Exp))
                        rope2_tile = max(rope2_tile, vclip)
                    rope2_done = rope2_tile
                    nc.sync.wait_ge(dvS.h, rope2_tile)
                    nc.sync.wait_ge(scS.h, scS.n)
                    adst = bass.AP(d_att.tensor, tt * 512 * 1152, [[1152, 128], [128 * 1152, 4], [1, 1152]])
                    vstt = qs[p2].inc(nc.sync.dma_start(out=adst, in_=astg[:].rearrange("q (a b) -> q a b", a=4)), 16)
                    st2_done[p2] = vstt
                    prev_mm2 = vmm
                    slot_mm2[p2] = vmm
                    if tt + 1 < T // 512:
                        nc.vector.wait_ge(qs[p2].h, vstt)
                if t_att is not None:
                    nc.sync.wait_ge(qs[0].h, qs[0].n)
                    nc.sync.wait_ge(qs[1].h, qs[1].n)
                    for tc in range(0, NCH, 8):
                        sy.inc(nc.sync.dma_start(out=t_att[tc * 128:(tc + 8) * 128, :], in_=d_att[tc * 128:(tc + 8) * 128, :]), 16)
                    nc.sync.wait_ge(sy.h, sy.n)
            nc.sync.wait_ge(qs[0].h, qs[0].n)
            nc.sync.wait_ge(qs[1].h, qs[1].n)
            bar()

        att_ctx = ExitStack()
        if "scan" in ph or "attpart" in ph:
            og_all = att_ctx.enter_context(nc.sbuf_tensor("og_all", [128, 2 * T], BF16))
        if "scan" in ph:
            with ExitStack() as sctx:
                SE = sctx.enter_context
                attc = [SE(nc.sbuf_tensor(f"attc{p}", [128, 1152], BF16)) for p in range(2)]
                S_sb = [SE(nc.sbuf_tensor(f"S_sb{p}", [128, 128], BF16)) for p in range(2)]
                Z_sb = [SE(nc.sbuf_tensor(f"Z_sb{p}", [64, 256], BF16)) for p in range(2)]
                R_sb = [SE(nc.sbuf_tensor(f"R_sb{p}", [1, 128], F32)) for p in range(2)]
                kfm_sb = [SE(nc.sbuf_tensor(f"kfm_sb{h}", [128, 128], BF16)) for h in range(2)]
                qfm_sb = [SE(nc.sbuf_tensor(f"qfm_sb{h}", [128, 128], BF16)) for h in range(2)]
                amask_sb = [SE(nc.sbuf_tensor(f"amask_sb{h}", [128, 128], BF16)) for h in range(2)]
                enz_sb = [SE(nc.sbuf_tensor(f"enz_sb{h}", [128, 64], F32)) for h in range(2)]
                ok_sb = [SE(nc.sbuf_tensor(f"ok_sb{h}", [128, 64], F32)) for h in range(2)]
                esm_sb = [SE(nc.sbuf_tensor(f"esm_sb{h}", [128, 64], F32)) for h in range(2)]
                u_sb = [SE(nc.sbuf_tensor(f"u_sb{h}", [128, 64], BF16)) for h in range(2)]
                ufm_sb = [SE(nc.sbuf_tensor(f"ufm_sb{h}", [64, 128], BF16)) for h in range(2)]
                esfm_sb = [SE(nc.sbuf_tensor(f"esfm_sb{h}", [64, 128], BF16)) for h in range(2)]
                w2m_sb = [SE(nc.sbuf_tensor(f"w2m_sb{h}", [128, 128], BF16)) for h in range(2)]
                o_sb = [SE(nc.sbuf_tensor(f"o_sb{h}", [128, 128], F32)) for h in range(2)]
                gg_sb = [SE(nc.sbuf_tensor(f"gg_sb{h}", [128, 128], F32)) for h in range(2)]
                eg_sb = [SE(nc.sbuf_tensor(f"eg_sb{h}", [128, 128], F32)) for h in range(2)]
                og_sb = [SE(nc.sbuf_tensor(f"og_sb{h}", [128, 128], BF16)) for h in range(2)]
                sm1 = [SE(nc.sbuf_tensor(f"sm1_{h}", [128, 8], F32)) for h in range(2)]
                bnst2 = [SE(nc.sbuf_tensor(f"bnst2_{h}", [128, 6], F32)) for h in range(2)]
                bnag2 = [SE(nc.sbuf_tensor(f"bnag2_{h}", [128, 2], F32)) for h in range(2)]
                pS = SE(nc.psum_tensor("pS", [128, 128], F32))
                pZ = SE(nc.psum_tensor("pZ", [64, 256], F32))
                pR = SE(nc.psum_tensor("pR", [1, 128], F32))
                p_co = SE(nc.psum_tensor("p_co", [128, 256], F32))
                p_wo = SE(nc.psum_tensor("p_wo", [128, 256], F32))
                p_oo = SE(nc.psum_tensor("p_oo", [128, 256], F32))
                p_trS = SE(nc.psum_tensor("p_trS", [128, 1024], BF16))
                p_a = p_c

                upd_done = 0
                snap_done = 0
                attc_mm = [0, 0]
                attc_dv = [0, 0]
                trS_free = 0
                trU_free = 0
                co_free = 0
                wo_free = 0
                oo_free = 0
                am_free = 0
                for c in range(NCH):
                    p = c % 2
                    if attc_mm[p]:
                        nc.sync.wait_ge(mmS.h, attc_mm[p])
                        nc.sync.wait_ge(dvS.h, attc_dv[p])
                    vld = qs[4 + p].inc(nc.sync.dma_start(out=attc[p][:], in_=d_att[c * 128:(c + 1) * 128, :]), 16)
                    ap = attc[p]
                    qsl = [ap[:, h * 128:(h + 1) * 128] for h in range(2)]
                    ksl = [ap[:, 256 + h * 128:256 + (h + 1) * 128] for h in range(2)]
                    vsl = [ap[:, 512 + h * 128:512 + (h + 1) * 128] for h in range(2)]
                    gsl = [ap[:, 768 + h * 128:768 + (h + 1) * 128] for h in range(2)]
                    essl = [ap[:, 1024 + h * 64:1024 + (h + 1) * 64] for h in range(2)]
                    if c > 0:
                        dv.wait(mmS, upd_done)
                        dv.do(nc.vector.tensor_copy(S_sb[p][:], pS[:, :]))
                        dv.do(nc.vector.tensor_copy(Z_sb[p][:], pZ[:, :]))
                        snap_done = dv.do(nc.vector.tensor_copy(R_sb[p][:], pR[:, :]))
                    te.wait(qs[4 + p], vld)
                    if c > 0:
                        te.wait(dvS, snap_done)
                    for h in range(2):
                        te.do(nc.tensor.matmul(pS[:, h * 64:(h + 1) * 64], ksl[h], essl[h],
                                               start=(c == 0 and h == 0), stop=(c == NCH - 1), skip_group_check=True))
                        te.do(nc.tensor.matmul(pZ[0:64, h * 128:(h + 1) * 128], essl[h], vsl[h],
                                               start=(c == 0 and h == 0), stop=(c == NCH - 1), skip_group_check=True))
                    upd_done = te.do(nc.tensor.matmul(pR[:, :], s_ones_c128bf[:, :], ap[:, 1024:1152],
                                                      start=(c == 0), stop=(c == NCH - 1), skip_group_check=True))
                    te.wait(dvS, trS_free)
                    tks = []
                    for h in range(2):
                        te.do(nc.tensor.transpose(p_trS[:, h * 256:h * 256 + 128], ksl[h], s_idbf[:]))
                        tks.append(te.do(nc.tensor.transpose(p_trS[:, h * 256 + 128:h * 256 + 256], qsl[h], s_idbf[:])))
                    vq2 = [0, 0]
                    for h in range(2):
                        dv.wait(mmS, tks[1])
                        dv.do(nc.vector.tensor_copy(kfm_sb[h][:], p_trS[:, h * 256:h * 256 + 128]))
                        vq2[h] = dv.do(nc.vector.tensor_copy(qfm_sb[h][:], p_trS[:, h * 256 + 128:h * 256 + 256]))
                    va2 = [0, 0]
                    te.wait(dvS, am_free)
                    for h in range(2):
                        te.wait(dvS, vq2[h])
                        va2[h] = te.do(nc.tensor.matmul(p_a[:, h * 128:(h + 1) * 128], kfm_sb[h][:], qfm_sb[h][:], start=True, stop=True))
                    vam = [0, 0]
                    for h in range(2):
                        dv.wait(mmS, va2[1])
                        vam[h] = dv.do(nc.vector.tensor_tensor(amask_sb[h][:], p_a[:, h * 128:(h + 1) * 128], s_utri[:], ALU.mult))
                    am_free = max(vam)
                    vok = [0, 0]
                    te.wait(dvS, co_free)
                    for h in range(2):
                        te.wait(dvS, vam[h])
                        te.do(nc.tensor.matmul(p_co[:, h * 128:h * 128 + 64], s_utri[:], essl[h], start=(h == 0), stop=False, skip_group_check=True))
                        if c > 0:
                            te.do(nc.tensor.matmul(p_co[:, h * 128:h * 128 + 64], s_ones_r128[:, :], R_sb[p][:, h * 64:(h + 1) * 64], start=False, stop=True, skip_group_check=True))
                        te.do(nc.tensor.matmul(p_co[:, h * 128 + 64:h * 128 + 128], amask_sb[h][:], essl[h], start=False, stop=(c == 0), skip_group_check=True))
                        if c > 0:
                            vok[h] = te.do(nc.tensor.matmul(p_co[:, h * 128 + 64:h * 128 + 128], qfm_sb[h][:], S_sb[p][:, h * 64:(h + 1) * 64], start=False, stop=True, skip_group_check=True))
                        else:
                            vok[h] = mmS.n
                    vu = [0, 0]
                    vnm2 = [0, 0]
                    for h in range(2):
                        dv.wait(mmS, max(vok))
                        vez = dv.do(nc.vector.reciprocal(enz_sb[h][:], p_co[:, h * 128:h * 128 + 64]))
                        dv.wait(dvS, vez)
                        vko = dv.do(nc.vector.tensor_tensor(ok_sb[h][:], p_co[:, h * 128 + 64:h * 128 + 128], enz_sb[h][:], ALU.mult))
                        co_free = vko
                        dv.wait(dvS, vko)
                        vnm2[h] = dv.do(nc.vector.reduce_max(sm1[h][:, 0:1], ok_sb[h][:], axis=AX.X, negate=True))
                    vex2 = [0, 0]
                    for h in range(2):
                        sc.wait(dvS, vnm2[h])
                        vex2[h] = sc.do(nc.scalar.activation(esm_sb[h][:], ok_sb[h][:], AF.Exp, bias=sm1[h][:, 0:1], accum_out=sm1[h][:, 1:2]))
                    for h in range(2):
                        dv.wait(scS, vex2[h])
                        vri = dv.do(nc.vector.reciprocal(sm1[h][:, 2:3], sm1[h][:, 1:2]))
                        dv.wait(dvS, vri)
                        vue = dv.do(nc.vector.tensor_tensor(esm_sb[h][:], esm_sb[h][:], enz_sb[h][:], ALU.mult))
                        dv.wait(dvS, vue)
                        vu[h] = dv.do(nc.vector.tensor_scalar(u_sb[h][:], esm_sb[h][:], sm1[h][:, 2:3], None, ALU.mult))
                    te.wait(dvS, trU_free)
                    tus = []
                    for h in range(2):
                        te.wait(dvS, vu[h])
                        te.do(nc.tensor.transpose(p_trS[0:64, 512 + h * 256:512 + h * 256 + 128], u_sb[h][:], s_idbf[:]))
                        tus.append(te.do(nc.tensor.transpose(p_trS[0:64, 512 + h * 256 + 128:512 + h * 256 + 256], essl[h], s_idbf[:])))
                    vesf = [0, 0]
                    for h in range(2):
                        dv.wait(mmS, tus[1])
                        dv.do(nc.vector.tensor_copy(ufm_sb[h][:], p_trS[0:64, 512 + h * 256:512 + h * 256 + 128]))
                        vesf[h] = dv.do(nc.vector.tensor_copy(esfm_sb[h][:], p_trS[0:64, 512 + h * 256 + 128:512 + h * 256 + 256]))
                    trU_free = max(vesf)
                    vw2 = [0, 0]
                    te.wait(dvS, wo_free)
                    for h in range(2):
                        te.wait(dvS, vesf[h])
                        vw2[h] = te.do(nc.tensor.matmul(p_wo[:, h * 128:(h + 1) * 128], esfm_sb[h][:], ufm_sb[h][:], start=True, stop=True))
                    vwm = [0, 0]
                    for h in range(2):
                        dv.wait(mmS, vw2[1])
                        vwm[h] = dv.do(nc.vector.tensor_tensor(w2m_sb[h][:], p_wo[:, h * 128:(h + 1) * 128], s_utri[:], ALU.mult))
                    wo_free = max(vwm)
                    vo2 = [0, 0]
                    te.wait(scS, oo_free)
                    for h in range(2):
                        te.wait(dvS, vwm[h])
                        te.do(nc.tensor.matmul(p_oo[:, h * 128:(h + 1) * 128], w2m_sb[h][:], vsl[h], start=True, stop=(c == 0), skip_group_check=True))
                        if c > 0:
                            vo2[h] = te.do(nc.tensor.matmul(p_oo[:, h * 128:(h + 1) * 128], ufm_sb[h][:], Z_sb[p][0:64, h * 128:(h + 1) * 128], start=False, stop=True, skip_group_check=True))
                        else:
                            vo2[h] = mmS.n
                    vosb = [0, 0]
                    for h in range(2):
                        sc.wait(mmS, max(vo2))
                        vosb[h] = sc.do(nc.scalar.copy(o_sb[h][:], p_oo[:, h * 128:(h + 1) * 128]))
                    oo_free = max(vosb)
                    veg = [0, 0]
                    for h in range(2):
                        veg[h] = sc.do(nc.scalar.activation(eg_sb[h][:], gsl[h], AF.Exp, scale=-1.0))
                    vrstd2 = [0, 0]
                    for h in range(2):
                        dv.wait(scS, vosb[h])
                        vbn2 = dv.do(nc.vector.bn_stats(bnst2[h][:], o_sb[h][:]))
                        dv.wait(dvS, vbn2)
                        vag2 = dv.do(nc.vector.bn_aggr(bnag2[h][:], bnst2[h][:]))
                        dv.wait(dvS, vag2)
                        vq3 = dv.do(nc.vector.tensor_tensor(sm1[h][:, 3:4], bnag2[h][:, 0:1], bnag2[h][:, 0:1], ALU.mult))
                        dv.wait(dvS, vq3)
                        vm2 = dv.do(nc.vector.tensor_add(sm1[h][:, 4:5], bnag2[h][:, 1:2], sm1[h][:, 3:4]))
                        sc.wait(dvS, vm2)
                        vl2 = sc.do(nc.scalar.activation(sm1[h][:, 5:6], sm1[h][:, 4:5], AF.Ln, bias=s_eps[:, 0:1]))
                        sc.wait(scS, vl2)
                        vrstd2[h] = sc.do(nc.scalar.activation(sm1[h][:, 5:6], sm1[h][:, 5:6], AF.Exp, scale=-0.5))
                    vog = [0, 0]
                    for h in range(2):
                        dv.wait(scS, veg[h])
                        vd1 = dv.do(nc.vector.tensor_scalar(eg_sb[h][:], eg_sb[h][:], 1.0, None, ALU.add))
                        dv.wait(dvS, vd1)
                        vr1 = dv.do(nc.vector.reciprocal(eg_sb[h][:], eg_sb[h][:]))
                        dv.wait(dvS, vr1)
                        vgg = dv.do(nc.vector.tensor_tensor(gg_sb[h][:], gsl[h], eg_sb[h][:], ALU.mult))
                        dv.wait(scS, vrstd2[h])
                        von = dv.do(nc.vector.tensor_scalar(o_sb[h][:], o_sb[h][:], sm1[h][:, 5:6], None, ALU.mult))
                        dv.wait(dvS, max(von, vgg))
                        von2 = dv.do(nc.vector.tensor_tensor(o_sb[h][:], o_sb[h][:], s_gnwb[:], ALU.mult))
                        dv.wait(dvS, von2)
                        vog[h] = dv.do(nc.vector.tensor_tensor(og_sb[h][:], o_sb[h][:], gg_sb[h][:], ALU.mult))
                    te.wait(dvS, max(vq2))
                    tgs = []
                    for h in range(2):
                        te.wait(dvS, vog[h])
                        tgs.append(te.do(nc.tensor.transpose(p_trS[:, h * 256:h * 256 + 128], og_sb[h][:], s_idbf[:])))
                    vga = [0, 0]
                    for h in range(2):
                        dv.wait(mmS, tgs[1])
                        vga[h] = dv.do(nc.vector.tensor_copy(og_all[:, h * T + c * 128:h * T + (c + 1) * 128], p_trS[:, h * 256:h * 256 + 128]))
                    trS_free = max(vga)
                    attc_mm[p] = mmS.n
                    attc_dv[p] = dvS.n
                if t_og is not None:
                    nc.sync.wait_ge(dvS.h, dvS.n)
                    for r in range(2):
                        sy.inc(nc.sync.dma_start(out=bass.AP(t_og.tensor, r * 128 * T, [[T, 128], [1, T]]), in_=og_all[:, r * T:(r + 1) * T]), 16)
                    nc.sync.wait_ge(sy.h, sy.n)
            bar()

        if "attpart" in ph:
            with ExitStack() as fctx:
                FE = fctx.enter_context
                awo_sb = FE(nc.sbuf_tensor("awo_sb", [128, 2 * D], BF16))
                pstg2 = [FE(nc.sbuf_tensor(f"pstg2_{p}", [128, D], F32)) for p in range(2)]
                p_pp2 = [FE(nc.psum_tensor(f"p_pp2_{p}", [128, 512], F32)) for p in range(2)]
                for r in range(2):
                    gp.inc(nc.gpsimd.dma_start(out=awo_sb[:, r * D:(r + 1) * D], in_=aWo[r * 128:(r + 1) * 128, :]), 16)
                te.wait(gp, gp.n)
                part_st = [0, 0]
                evac_done = [0, 0]
                for tc in range(NCH):
                    pp = tc % 2
                    if part_st[pp]:
                        nc.vector.wait_ge(qs[pp].h, part_st[pp])
                    vev = 0
                    for dsl in range(4):
                        pb2 = dsl % 2
                        te.wait(dvS, evac_done[pb2])
                        vmm = 0
                        for r in range(2):
                            vmm = te.do(nc.tensor.matmul(
                                p_pp2[pb2][:, :],
                                og_all[:, r * T + tc * 128:r * T + (tc + 1) * 128],
                                awo_sb[:, r * D + dsl * 512:r * D + (dsl + 1) * 512],
                                start=(r == 0), stop=(r == 1)))
                        dv.wait(mmS, vmm)
                        vev = dv.do(nc.vector.tensor_copy(pstg2[pp][:, dsl * 512:(dsl + 1) * 512], p_pp2[pb2][:, :]))
                        evac_done[pb2] = vev
                    nc.sync.wait_ge(dvS.h, vev)
                    part_st[pp] = qs[pp].inc(nc.sync.dma_start(out=cc_a_in[tc * 128:(tc + 1) * 128, :], in_=pstg2[pp][:]), 16)
                if t_attpart is not None:
                    nc.sync.wait_ge(qs[0].h, qs[0].n)
                    nc.sync.wait_ge(qs[1].h, qs[1].n)
                    for tc in range(NCH):
                        sy.inc(nc.sync.dma_start(out=t_attpart[tc * 128:(tc + 1) * 128, :], in_=cc_a_in[tc * 128:(tc + 1) * 128, :]), 16)
                    nc.sync.wait_ge(sy.h, sy.n)
                nc.gpsimd.wait_ge(qs[0].h, qs[0].n)
                nc.gpsimd.wait_ge(qs[1].h, qs[1].n)
                if "ffnpart" in ph:
                    nc.gpsimd.wait_ge(ccS.h, ccu)
                if fake_cc:
                    ccS.inc(nc.gpsimd.dma_start(out=cc_a_out[:, :], in_=cc_a_in[0:TOWN, :]), 16)
                else:
                    ccS.inc(nc.gpsimd.collective_compute(
                        "ReduceScatter", ALU.add,
                        replica_groups=[list(range(NCORES))],
                        ins=[cc_a_in.tensor.ap().opt()],
                        outs=[cc_a_out.tensor.ap().opt()]), 1)
            bar()
        att_ctx.close()

        if "x2h2" in ph:
            x2a = E(nc.sbuf_tensor("x2a", [128, NJ * TOWN], F32))
            h2tm = E(nc.sbuf_tensor("h2tm", [128, NOC * D], BF16))
            wfm = E(nc.sbuf_tensor("wfm", [16, TOWN], F32))
            wtm = E(nc.sbuf_tensor("wtm", [128, NOC * 16], F32))
            slottm = E(nc.sbuf_tensor("slottm", [128, NOC * 16], F32))
            s_flnwb = gate_ctx.enter_context(nc.sbuf_tensor("s_flnwb", [128, D], F32))
            row16 = gate_ctx.enter_context(nc.sbuf_tensor("row16", [16, 128], F32))
            h2sb = gate_ctx.enter_context(nc.sbuf_tensor("h2sb", [128, NJ * TOWN], BF16))
            vtr = te.do(nc.tensor.transpose(p_c[0:16, 0:128], s_flnw[:], s_idf32[:]))
            dv.wait(mmS, vtr)
            vr16 = dv.do(nc.vector.tensor_copy(row16[:], p_c[0:16, 0:128]))
            for j in range(NJ):
                te.wait(dvS, dvS.n)
                vb = te.do(nc.tensor.matmul(p_c[:, 128:256], s_sel[:, j * 128:(j + 1) * 128], row16[:, :], start=True, stop=True))
                dv.wait(mmS, vb)
                dv.do(nc.vector.tensor_copy(s_flnwb[:, j * 128:(j + 1) * 128], p_c[:, 128:256]))
            with ExitStack() as xctx:
                XE = xctx.enter_context
                rsf = [XE(nc.sbuf_tensor(f"rsf{p}", [128, D], F32)) for p in range(2)]
                xno = [XE(nc.sbuf_tensor(f"xno{p}", [128, D], F32)) for p in range(2)]
                x2tm = XE(nc.sbuf_tensor("x2tm", [128, D], F32))
                bnst3 = XE(nc.sbuf_tensor("bnst3", [128, 24], F32))
                bnag3 = XE(nc.sbuf_tensor("bnag3", [128, 2], F32))
                sm3 = XE(nc.sbuf_tensor("sm3", [128, 4], F32))
                rr2row = XE(nc.sbuf_tensor("rr2row", [1, TOWN], F32))
                rr2sb = XE(nc.sbuf_tensor("rr2sb", [1, 128], F32))
                p_tx = XE(nc.psum_tensor("p_tx", [128, 512], F32))
                p_tx2 = XE(nc.psum_tensor("p_tx2", [128, 512], F32))
                p_rr2 = XE(nc.psum_tensor("p_rr2", [1, 128], F32))
                p_rb2 = XE(nc.psum_tensor("p_rb2", [128, 512], F32))

                if "ffnpart" in ph:
                    nc.sync.wait_ge(ccS.h, ccu)
                cid = nc.gpsimd.partition_id()
                x2_done = 0
                rr2_ev = 0
                xno_done = [0, 0]
                for oc in range(NOC):
                    p = oc % 2
                    if xno_done[p]:
                        nc.sync.wait_ge(dvS.h, xno_done[p])
                        nc.gpsimd.wait_ge(dvS.h, xno_done[p])
                    vld1 = qs[p].inc(nc.sync.dma_start(out=rsf[p][:], in_=cc_f_out[oc * 128:(oc + 1) * 128, :]), 16)
                    xsrc = bass.AP(d_xn32.tensor, (cid * TOWN + oc * 128) * D, [[D, 128], [1, D]])
                    gqp = gp if p == 0 else gq
                    vld2 = gqp.inc(nc.gpsimd.dma_start(out=xno[p][:], in_=xsrc), 16)
                    dv.wait(qs[p], vld1)
                    dv.wait(gqp, vld2)
                    dv.wait(mmS, x2_done)
                    vx2 = dv.do(nc.vector.tensor_add(x2tm[:], xno[p][:], rsf[p][:]))
                    xno_done[p] = vx2
                    dv.wait(dvS, vx2)
                    for k in range(4):
                        vbn = dv.do(nc.vector.bn_stats(bnst3[:, k * 6:(k + 1) * 6], x2tm[:, k * 512:(k + 1) * 512]))
                    dv.wait(dvS, vbn)
                    vag = dv.do(nc.vector.bn_aggr(bnag3[:], bnst3[:]))
                    dv.wait(dvS, vag)
                    vq4 = dv.do(nc.vector.tensor_tensor(sm3[:, 0:1], bnag3[:, 0:1], bnag3[:, 0:1], ALU.mult))
                    dv.wait(dvS, vq4)
                    vms = dv.do(nc.vector.tensor_add(sm3[:, 1:2], bnag3[:, 1:2], sm3[:, 0:1]))
                    sc.wait(dvS, vms)
                    vl3 = sc.do(nc.scalar.activation(sm3[:, 2:3], sm3[:, 1:2], AF.Ln, bias=s_eps[:, 0:1]))
                    sc.wait(scS, vl3)
                    vrr2 = sc.do(nc.scalar.activation(sm3[:, 2:3], sm3[:, 2:3], AF.Exp, scale=-0.5))
                    dv.wait(scS, vrr2)
                    vhtm = dv.do(nc.vector.tensor_scalar(h2tm[:, oc * D:(oc + 1) * D], x2tm[:], sm3[:, 2:3], None, ALU.mult))
                    dv.wait(dvS, vhtm)
                    dv.do(nc.vector.tensor_tensor(h2tm[:, oc * D:(oc + 1) * D], h2tm[:, oc * D:(oc + 1) * D], s_flnwb[:], ALU.mult))
                    te.wait(scS, vrr2)
                    te.wait(dvS, rr2_ev)
                    vtr2 = te.do(nc.tensor.transpose(p_rr2[:, :], sm3[:, 2:3], s_idf32[:]))
                    dv.wait(mmS, vtr2)
                    rr2_ev = dv.do(nc.vector.tensor_copy(rr2row[:, oc * 128:(oc + 1) * 128], p_rr2[:, :]))
                    vtx = 0
                    for j in range(NJ):
                        ptx = p_tx if (j // 4) % 2 == 0 else p_tx2
                        tp = nc.tensor.transpose(ptx[:, (j % 4) * 128:(j % 4 + 1) * 128], x2tm[:, j * 128:(j + 1) * 128], s_idf32[:])
                        if j % 4 == 3:
                            vt = te.do(tp)
                            dv.wait(mmS, vt)
                            for jj in range(j - 3, j + 1):
                                vtx = dv.do(nc.vector.tensor_copy(x2a[:, jj * TOWN + oc * 128:jj * TOWN + (oc + 1) * 128], ptx[:, (jj % 4) * 128:(jj % 4 + 1) * 128]))
                            te.wait(dvS, vtx)
                    x2_done = mmS.n
                vh2 = 0
                for tt2 in range(TOWN // 512):
                    te.wait(dvS, vh2)
                    vrb2 = te.do(nc.tensor.matmul(p_rb2[:, 0:512], s_ones_r128[:, :], rr2row[:, tt2 * 512:(tt2 + 1) * 512], start=True, stop=True))
                    dv.wait(mmS, vrb2)
                    for j in range(NJ):
                        vt1 = dv.do(nc.vector.tensor_tensor(h2sb[:, j * TOWN + tt2 * 512:j * TOWN + tt2 * 512 + 512],
                                                            x2a[:, j * TOWN + tt2 * 512:j * TOWN + tt2 * 512 + 512], p_rb2[:, 0:512], ALU.mult))
                        dv.wait(dvS, vt1)
                        vh2 = dv.do(nc.vector.tensor_scalar(h2sb[:, j * TOWN + tt2 * 512:j * TOWN + tt2 * 512 + 512],
                                                            h2sb[:, j * TOWN + tt2 * 512:j * TOWN + tt2 * 512 + 512], s_flnw[:, j:j + 1], None, ALU.mult))
                if t_x2 is not None:
                    nc.sync.wait_ge(dvS.h, dvS.n)
                    sy.inc(nc.sync.dma_start(out=t_x2[:, :], in_=x2a[:, :]), 16)
                if t_h2 is not None:
                    nc.sync.wait_ge(dvS.h, dvS.n)
                    sy.inc(nc.sync.dma_start(out=t_h2[:, :], in_=h2sb[:, :]), 16)
                nc.sync.wait_ge(sy.h, sy.n)
            bar()

        if "gate" in ph:
            with ExitStack() as gctx:
                GE = gctx.enter_context
                wg_sb = GE(nc.sbuf_tensor("wg_sb", [128, NJ * 16], BF16))
                lg = GE(nc.sbuf_tensor("lg", [128, 16], F32))
                lgt = GE(nc.sbuf_tensor("lgt", [128, 16], F32))
                msk = GE(nc.sbuf_tensor("msk", [128, 16], F32))
                gsm = GE(nc.sbuf_tensor("gsm", [128, 8], F32))
                p_lg = GE(nc.psum_tensor("p_lg", [128, 16], F32))
                p_wt = GE(nc.psum_tensor("p_wt", [16, 128], F32))
                vgw = gp.inc(nc.gpsimd.dma_start(out=wg_sb[:].rearrange("q (a b) -> q a b", a=NJ),
                                                 in_=bass.AP(fWg.tensor, 0, [[16, 128], [128 * 16, NJ], [1, 16]])), 16)
                te.wait(gp, gp.n)
                prev_lg = 0
                prev_wt_ev = 0
                for oc in range(NOC):
                    te.wait(dvS, prev_lg)
                    vml = 0
                    for j in range(NJ):
                        vml = te.do(nc.tensor.matmul(p_lg[:, :], h2sb[:, j * TOWN + oc * 128:j * TOWN + (oc + 1) * 128],
                                                     wg_sb[:, j * 16:(j + 1) * 16],
                                                     start=(j == 0), stop=(j == NJ - 1)))
                    dv.wait(mmS, vml)
                    prev_lg = dv.do(nc.vector.tensor_copy(lg[:], p_lg[:, :]))
                    dv.wait(dvS, prev_lg)
                    vcur = dv.do(nc.vector.tensor_copy(lgt[:], lg[:]))
                    for it in range(4):
                        dv.wait(dvS, vcur)
                        vmx = dv.do(nc.vector.reduce_max(gsm[:, it:it + 1], lgt[:], axis=AX.X))
                        if it < 3:
                            dv.wait(dvS, vmx)
                            vmk = dv.do(nc.vector.tensor_scalar(msk[:], lgt[:], gsm[:, it:it + 1], None, ALU.is_ge))
                            dv.wait(dvS, vmk)
                            vcur = dv.do(nc.vector.scalar_tensor_tensor(lgt[:], msk[:], -1e30, lgt[:], ALU.mult, ALU.add))
                    dv.wait(dvS, dvS.n)
                    vkp = dv.do(nc.vector.tensor_scalar(msk[:], lg[:], gsm[:, 3:4], None, ALU.is_ge))
                    vng = dv.do(nc.vector.tensor_scalar(gsm[:, 4:5], gsm[:, 0:1], -1.0, None, ALU.mult))
                    sc.wait(dvS, vng)
                    vem = sc.do(nc.scalar.activation(lgt[:], lg[:], AF.Exp, bias=gsm[:, 4:5]))
                    dv.wait(scS, vem)
                    vwu = dv.do(nc.vector.tensor_tensor(lgt[:], lgt[:], msk[:], ALU.mult))
                    dv.wait(dvS, vwu)
                    vde = dv.do(nc.vector.reduce_sum(gsm[:, 5:6], lgt[:], axis=AX.X))
                    dv.wait(dvS, vde)
                    vrd = dv.do(nc.vector.reciprocal(gsm[:, 6:7], gsm[:, 5:6]))
                    dv.wait(dvS, vrd)
                    vwt = dv.do(nc.vector.tensor_scalar(lgt[:], lgt[:], gsm[:, 6:7], None, ALU.mult))
                    dv.wait(dvS, vwt)
                    vwtm = dv.do(nc.vector.tensor_copy(wtm[:, oc * 16:(oc + 1) * 16], lgt[:]))
                    te.wait(dvS, vwt)
                    te.wait(dvS, prev_wt_ev)
                    vtw = te.do(nc.tensor.transpose(p_wt[:, :], lgt[:], s_idf32[:]))
                    dv.wait(mmS, vtw)
                    prev_wt_ev = dv.do(nc.vector.tensor_copy(wfm[:, oc * 128:(oc + 1) * 128], p_wt[:, :]))
                if t_gate is not None:
                    nc.sync.wait_ge(dvS.h, dvS.n)
                    sy.inc(nc.sync.dma_start(out=t_gate[:, :], in_=wfm[:, :]), 16)
                    nc.sync.wait_ge(sy.h, sy.n)
            bar()
        gate_ctx.close()

        if "moe" in ph:
            with ExitStack() as mctx:
                ME = mctx.enter_context
                w13 = [ME(nc.sbuf_tensor(f"w13_{p}", [128, 2 * EH], BF16)) for p in range(2)]
                w2b = [ME(nc.sbuf_tensor(f"w2b_{p}", [128, D], BF16)) for p in range(2)]
                h2g = ME(nc.sbuf_tensor("h2g", [128, NJ * CAP], BF16))
                a_t = ME(nc.sbuf_tensor("a_t", [128, NH * CAP], BF16))
                gbf = ME(nc.sbuf_tensor("gbf", [128, NOC * CAP], BF16))
                gpp = ME(nc.sbuf_tensor("gpp", [128, NSC * TOWN], BF16))
                yts = ME(nc.sbuf_tensor("yts", [128, NSC * D], BF16))
                ysb = [ME(nc.sbuf_tensor(f"ysb{p}", [128, CAP], BF16)) for p in range(2)]
                g32 = [ME(nc.sbuf_tensor(f"g32_{p}", [128, CAP], F32)) for p in range(2)]
                silb = ME(nc.sbuf_tensor("silb", [128, 2 * CAP], F32))
                wgb = ME(nc.sbuf_tensor("wgb", [128, CAP], F32))
                wrow = ME(nc.sbuf_tensor("wrow", [1, CAP], F32))
                mskf = ME(nc.sbuf_tensor("mskf", [128, NOC * 16], F32))
                mskb = ME(nc.sbuf_tensor("mskb", [128, NOC * 16], BF16))
                s_onesq = ME(nc.sbuf_tensor("s_onesq", [128, 128], BF16))
                s_iotacap = ME(nc.sbuf_tensor("s_iotacap", [128, CAP], F32))
                ps = [ME(nc.psum_tensor(f"psm{i}", [128, 512], F32)) for i in range(6)]
                ptb = ME(nc.psum_tensor("ptb", [128, 1024], BF16))

                sy.inc(nc.sync.dma_start(out=s_onesq[:], in_=ones_sq[:, :]), 16)
                sy.inc(nc.sync.dma_start(out=s_iotacap[:], in_=iota_cap[:, :]), 16)

                w13_done = [0, 0]
                w2_done = [0, 0]
                w13_ready = {}
                w2_ready = {}

                def issue_w13(e, hc):
                    p = hc % 2
                    if w13_done[p]:
                        nc.gpsimd.wait_ge(mmS.h, w13_done[p])
                    w13s[p].inc(nc.gpsimd.dma_start(out=w13[p][:, 0:EH], in_=w1[e * NH + hc, :, :]), 16)
                    w13_ready[hc] = w13s[p].inc(nc.gpsimd.dma_start(out=w13[p][:, EH:2 * EH], in_=w3[e * NH + hc, :, :]), 16)

                def issue_w2(e, dc):
                    p = dc % 2
                    if w2_done[p]:
                        nc.gpsimd.wait_ge(mmS.h, w2_done[p])
                    w2_ready[dc] = w2s[p].inc(nc.gpsimd.dma_start(out=w2b[p][:, :], in_=w2[e * NJ + dc, :, :]), 16)

                issue_w13(0, 0)
                issue_w13(0, 1)

                vmskl = 0
                for tc in range(NOC):
                    vm = dv.do(nc.vector.tensor_scalar(mskf[:, tc * 16:(tc + 1) * 16], wtm[:, tc * 16:(tc + 1) * 16], 0.0, None, ALU.is_gt))
                    dv.wait(dvS, vm)
                    vmskl = dv.do(nc.vector.tensor_copy(mskb[:, tc * 16:(tc + 1) * 16], mskf[:, tc * 16:(tc + 1) * 16]))
                te.wait(sy, sy.n)
                te.wait(dvS, vmskl)
                vsl = 0
                for tc in range(NOC):
                    for tp in range(tc + 1):
                        vsl = te.do(nc.tensor.matmul(p_c[:, tc * 16:(tc + 1) * 16],
                                                     s_utri[:] if tp == tc else s_onesq[:],
                                                     mskb[:, tp * 16:(tp + 1) * 16],
                                                     start=(tp == 0), stop=(tp == tc)))
                dv.wait(mmS, vsl)
                vst = dv.do(nc.vector.tensor_tensor(slottm[:], p_c[:, 0:128], mskf[:], ALU.mult))
                dv.wait(dvS, vst)
                vst = dv.do(nc.vector.tensor_scalar(slottm[:], slottm[:], 1.0, None, ALU.subtract))
                slottm_ready = vst
                pc_free = vst

                a_evac = [(0, 0), (0, 0)]
                y_evac = [0, 0]
                hg_evac = [0, 0]
                sc_evac = [0, 0]
                gather_last = 0
                ptb_done = 0
                ysb_rd = [0, 0]
                AS = {"vts": {}, "vwg": {}, "vtbl": 0, "pc_free": vst, "gb_free": 0}

                def a_ts(e, tc):
                    if tc == 0:
                        dv.wait(dvS, slottm_ready)
                        if AS["gb_free"]:
                            dv.wait(mmS, AS["gb_free"])
                        if gather_last:
                            dv.wait(mmS, gather_last)
                    if tc >= 2 and AS["vwg"].get(tc - 2):
                        dv.wait(mmS, AS["vwg"][tc - 2])
                    col = slottm[:, tc * 16 + e:tc * 16 + e + 1]
                    AS["vts"][tc] = dv.do(nc.vector.tensor_scalar(g32[tc % 2][:], s_iotacap[:], col, None, ALU.is_equal))
                    AS["vtbl"] = dv.do(nc.vector.tensor_scalar(gbf[:, tc * CAP:(tc + 1) * CAP], s_iotacap[:], col, None, ALU.is_equal))

                def a_wg(e, tc):
                    if tc == 0:
                        te.wait(dvS, AS["pc_free"])
                    te.wait(dvS, AS["vts"][tc])
                    AS["vwg"][tc] = te.do(nc.tensor.matmul(p_c[0:1, 0:CAP], wtm[:, tc * 16 + e:tc * 16 + e + 1], g32[tc % 2][:],
                                                           start=(tc == 0), stop=(tc == NOC - 1)))

                def a_fin(e):
                    dv.wait(mmS, AS["vwg"][NOC - 1])
                    vwr = dv.do(nc.vector.tensor_copy(wrow[:], p_c[0:1, 0:CAP]))
                    te.wait(dvS, vwr)
                    vbc = te.do(nc.tensor.matmul(p_c[:, 0:CAP], s_ones_r128[:, :], wrow[:, :], start=True, stop=True))
                    dv.wait(mmS, vbc)
                    vwgb = dv.do(nc.vector.tensor_copy(wgb[:], p_c[:, 0:CAP]))
                    AS["pc_free"] = vwgb
                    AS["vwg"] = {}
                    return vwgb

                for tc in range(NOC):
                    a_ts(0, tc)
                    if tc >= 1:
                        a_wg(0, tc - 1)
                a_wg(0, NOC - 1)
                vwgb_cur = a_fin(0)

                for e in range(NE):
                    vtbl_cur = AS["vtbl"]

                    te.wait(dvS, vtbl_cur)
                    for dc in range(NJ):
                        g = dc % 2
                        if sc_evac[g]:
                            te.wait(dvS, sc_evac[g])
                        if hg_evac[g]:
                            te.wait(dvS, hg_evac[g])
                        vmm = 0
                        for tc in range(NOC):
                            vmm = te.do(nc.tensor.matmul(ps[4 + g][:, 0:CAP], h2tm[:, tc * D + dc * 128:tc * D + (dc + 1) * 128],
                                                         gbf[:, tc * CAP:(tc + 1) * CAP], start=(tc == 0), stop=(tc == NOC - 1)))
                        dv.wait(mmS, vmm)
                        hg_evac[g] = dv.do(nc.vector.tensor_copy(h2g[:, dc * CAP:(dc + 1) * CAP], ps[4 + g][:, 0:CAP]))
                    gather_last = vmm

                    te.wait(dvS, hg_evac[0])
                    te.wait(dvS, hg_evac[1])
                    dv.wait(dvS, vwgb_cur)
                    gptr = 0
                    for hc in range(NH):
                        p = hc % 2
                        te.wait(w13s[p], w13_ready[hc])
                        dvn, scn = a_evac[p]
                        if dvn:
                            te.wait(dvS, dvn)
                            te.wait(scS, scn)
                        vmm = 0
                        for dc in range(NJ):
                            te.do(nc.tensor.matmul(ps[0 + p][:, 0:CAP], w13[p][:, dc * 128:(dc + 1) * 128],
                                                   h2g[:, dc * CAP:(dc + 1) * CAP], start=(dc == 0), stop=(dc == NJ - 1)))
                            vmm = te.do(nc.tensor.matmul(ps[2 + p][:, 0:CAP], w13[p][:, EH + dc * 128:EH + (dc + 1) * 128],
                                                         h2g[:, dc * CAP:(dc + 1) * CAP], start=(dc == 0), stop=(dc == NJ - 1)))
                        w13_done[p] = vmm
                        sc.wait(mmS, vmm)
                        vsa = sc.do(nc.scalar.activation(silb[:, p * CAP:(p + 1) * CAP], ps[0 + p][:, 0:CAP], AF.Silu))
                        dv.wait(scS, vsa)
                        vt1 = dv.do(nc.vector.tensor_tensor(silb[:, p * CAP:(p + 1) * CAP], silb[:, p * CAP:(p + 1) * CAP], ps[2 + p][:, 0:CAP], ALU.mult))
                        dv.wait(dvS, vt1)
                        vt2 = dv.do(nc.vector.tensor_tensor(a_t[:, hc * CAP:(hc + 1) * CAP], silb[:, p * CAP:(p + 1) * CAP], wgb[:], ALU.mult))
                        a_evac[p] = (vt2, vsa)
                        if hc + 2 < NH:
                            issue_w13(e, hc + 2)
                        elif hc == NH - 2:
                            issue_w2(e, 0)
                        elif hc == NH - 1:
                            issue_w2(e, 1)
                        if hc in (1, 2, 3):
                            te.wait(dvS, ptb_done)
                            vtt = 0
                            for k in range(8):
                                i = gptr * 8 + k
                                tci, sc3 = i // NSC, i % NSC
                                vtt = te.do(nc.tensor.transpose(ptb[:, k * 128:(k + 1) * 128],
                                                                gbf[:, tci * CAP + sc3 * 128:tci * CAP + sc3 * 128 + 128], s_idbf[:]))
                            dv.wait(mmS, vtt)
                            vpe = 0
                            for k in range(8):
                                i = gptr * 8 + k
                                tci, sc3 = i // NSC, i % NSC
                                vpe = dv.do(nc.vector.tensor_copy(gpp[:, sc3 * TOWN + tci * 128:sc3 * TOWN + (tci + 1) * 128], ptb[:, k * 128:(k + 1) * 128]))
                            ptb_done = vpe
                            gptr += 1
                            if gptr == 3:
                                AS["gb_free"] = vtt

                    te.wait(dvS, a_evac[0][0])
                    te.wait(dvS, a_evac[1][0])

                    def emit_ytr(pdc, pg, pye):
                        nonlocal ptb_done
                        te.wait(dvS, pye)
                        te.wait(dvS, ptb_done)
                        vtt = 0
                        for sc3 in range(NSC):
                            vtt = te.do(nc.tensor.transpose(ptb[:, sc3 * 128:(sc3 + 1) * 128], ysb[pg][:, sc3 * 128:(sc3 + 1) * 128], s_idbf[:]))
                        ysb_rd[pg] = vtt
                        dv.wait(mmS, vtt)
                        vpe = 0
                        for sc3 in range(NSC):
                            vpe = dv.do(nc.vector.tensor_copy(yts[:, sc3 * D + pdc * 128:sc3 * D + (pdc + 1) * 128], ptb[:, sc3 * 128:(sc3 + 1) * 128]))
                        ptb_done = vpe

                    pend = None
                    nxt = e + 1 < NE
                    for dc in range(NJ):
                        p = dc % 2
                        g = dc % 2
                        te.wait(w2s[p], w2_ready[dc])
                        if y_evac[g]:
                            te.wait(dvS, y_evac[g])
                        vmm = 0
                        for hc in range(NH):
                            vmm = te.do(nc.tensor.matmul(ps[4 + g][:, 0:CAP], w2b[p][:, hc * 128:(hc + 1) * 128],
                                                         a_t[:, hc * CAP:(hc + 1) * CAP], start=(hc == 0), stop=(hc == NH - 1)))
                        w2_done[p] = vmm
                        dv.wait(mmS, max(vmm, ysb_rd[g]))
                        vye = dv.do(nc.vector.tensor_copy(ysb[g][:], ps[4 + g][:, 0:CAP]))
                        y_evac[g] = vye
                        if dc + 2 < NJ:
                            issue_w2(e, dc + 2)
                        if nxt:
                            if dc == 0:
                                issue_w13(e + 1, 0)
                                issue_w13(e + 1, 1)
                            if 3 <= dc <= 10:
                                a_wg(e + 1, dc - 3)
                            if 1 <= dc <= 8:
                                a_ts(e + 1, dc - 1)
                            if dc == 11:
                                vwgb_nxt = a_fin(e + 1)
                        if pend is not None:
                            emit_ytr(*pend)
                        pend = (dc, g, vye)
                    emit_ytr(*pend)
                    if nxt:
                        vwgb_cur = vwgb_nxt

                    te.wait(dvS, ptb_done)
                    for dc in range(NJ):
                        for tt in range(2):
                            g = (dc * 2 + tt) % 2
                            if sc_evac[g]:
                                te.wait(dvS, sc_evac[g])
                            vmm = 0
                            for sc3 in range(NSC):
                                vmm = te.do(nc.tensor.matmul(ps[4 + g][:, 0:512], yts[:, sc3 * D + dc * 128:sc3 * D + (dc + 1) * 128],
                                                             gpp[:, sc3 * TOWN + tt * 512:sc3 * TOWN + (tt + 1) * 512], start=(sc3 == 0), stop=(sc3 == NSC - 1)))
                            dv.wait(mmS, vmm)
                            sc_evac[g] = dv.do(nc.vector.tensor_add(x2a[:, dc * TOWN + tt * 512:dc * TOWN + tt * 512 + 512],
                                                                    x2a[:, dc * TOWN + tt * 512:dc * TOWN + tt * 512 + 512], ps[4 + g][:, 0:512]))
                if t_moe is not None:
                    nc.sync.wait_ge(dvS.h, dvS.n)
                    sy.inc(nc.sync.dma_start(out=t_moe[:, :], in_=x2a[:, :]), 16)
                    nc.sync.wait_ge(sy.h, sy.n)
            bar()

        if "final" in ph:
            with ExitStack() as fctx:
                FE = fctx.enter_context
                sso = [FE(nc.sbuf_tensor(f"sso{p}", [128, D], F32)) for p in range(2)]
                xoo = [FE(nc.sbuf_tensor(f"xoo{p}", [128, D], F32)) for p in range(2)]
                ftm = FE(nc.sbuf_tensor("ftm", [128, D], F32))
                otm = [FE(nc.sbuf_tensor(f"otm{p}", [128, D], F32)) for p in range(2)]
                p_tf = FE(nc.psum_tensor("p_tf", [128, 512], F32))
                p_tf2 = FE(nc.psum_tensor("p_tf2", [128, 512], F32))
                if "attpart" in ph:
                    nc.sync.wait_ge(ccS.h, ccS.n)
                ftm_done = 0
                out_st = [0, 0]
                for oc in range(NOC):
                    p = oc % 2
                    vld1 = qs[p].inc(nc.sync.dma_start(out=sso[p][:], in_=cc_a_out[oc * 128:(oc + 1) * 128, :]), 16)
                    vld2 = qs[2 + p].inc(nc.sync.dma_start(out=xoo[p][:], in_=x_own[oc * 128:(oc + 1) * 128, :]), 16)
                    vtx = 0
                    for j in range(NJ):
                        ptx = p_tf if (j // 4) % 2 == 0 else p_tf2
                        tp = nc.tensor.transpose(ptx[:, (j % 4) * 128:(j % 4 + 1) * 128], x2a[:, j * TOWN + oc * 128:j * TOWN + (oc + 1) * 128], s_idf32[:])
                        if j % 4 == 3:
                            vt = te.do(tp)
                            dv.wait(mmS, vt)
                            dv.wait(scS, ftm_done)
                            for jj in range(j - 3, j + 1):
                                vtx = dv.do(nc.vector.tensor_copy(ftm[:, jj * 128:(jj + 1) * 128], ptx[:, (jj % 4) * 128:(jj % 4 + 1) * 128]))
                            te.wait(dvS, vtx)
                    sc.wait(dvS, vtx)
                    vsil = sc.do(nc.scalar.activation(ftm[:], ftm[:], AF.Silu))
                    ftm_done = vsil
                    dv.wait(scS, vsil)
                    dv.wait(qs[p], vld1)
                    dv.wait(qs[2 + p], vld2)
                    if out_st[p]:
                        dv.wait(st, st.n)
                    vp1 = dv.do(nc.vector.tensor_tensor(otm[p][:], ftm[:], sso[p][:], ALU.mult))
                    dv.wait(dvS, vp1)
                    vout = dv.do(nc.vector.tensor_add(otm[p][:], otm[p][:], xoo[p][:]))
                    nc.sync.wait_ge(dvS.h, vout)
                    out_st[p] = st.inc(nc.sync.dma_start(out=o_own[oc * 128:(oc + 1) * 128, :], in_=otm[p][:]), 16)
                nc.sync.wait_ge(st.h, st.n)
        else:
            zz = E(nc.sbuf_tensor("zz", [128, D], F32))
            vz = dv.do(nc.vector.memset(zz[:], 0.0))
            nc.sync.wait_ge(dvS.h, vz)
            for oc in range(NOC):
                st.inc(nc.sync.dma_start(out=o_own[oc * 128:(oc + 1) * 128, :], in_=zz[:]), 16)
            nc.sync.wait_ge(st.h, st.n)

    return nc, dbg



def _prep_inputs(inputs, ci):
    g = lambda k: np.asarray(inputs[k])
    x = np.ascontiguousarray(g("L0_attn_hidden_states").reshape(T, D))
    hs = slice(ci * AH * ADK, (ci + 1) * AH * ADK)
    fqs = slice(ci * FQH * FHD, (ci + 1) * FQH * FHD)
    fks = slice(ci * FHD, (ci + 1) * FHD)
    ams = slice(ci * AH * AM, (ci + 1) * AH * AM)
    aWs2T = np.ascontiguousarray(g("L0_attn_Ws2").T)
    ws2aug = np.concatenate([aWs2T[:, ams], g("L0_attn_bs2")[None, ams]], axis=0)
    return {
        "x_full": x,
        "x_own": np.ascontiguousarray(x[ci * TOWN:(ci + 1) * TOWN]),
        "aWq": np.ascontiguousarray(g("L0_attn_Wq").T[:, hs]),
        "aWk": np.ascontiguousarray(g("L0_attn_Wk").T[:, hs]),
        "aWv": np.ascontiguousarray(g("L0_attn_Wv").T[:, hs]),
        "aWg": np.ascontiguousarray(g("L0_attn_Wg").T[:, hs]),
        "aWo": np.ascontiguousarray(g("L0_attn_Wo").T[hs, :]),
        "aWs1": np.ascontiguousarray(g("L0_attn_Ws1").T),
        "aWs2": np.ascontiguousarray(ws2aug),
        "gnw": np.ascontiguousarray(g("L0_attn_g_norm_weight")[None, :]),
        "lnw": np.ascontiguousarray(g("L0_ffn_ln_w")[None, :]),
        "flnw": np.ascontiguousarray(g("L0_ffn_ff_ln_w")[None, :]),
        "fWq": np.ascontiguousarray(g("L0_ffn_wq")[:, fqs]),
        "fWk": np.ascontiguousarray(g("L0_ffn_wk")[:, fks]),
        "fWv": np.ascontiguousarray(g("L0_ffn_wv")[:, fks]),
        "fWo": np.ascontiguousarray(g("L0_ffn_wo")[fqs, :]),
        "fWg": np.ascontiguousarray(g("L0_ffn_wg")),
        "w1": np.ascontiguousarray(g("L0_ffn_w1").reshape(NE, NJ, 128, NH, 128).transpose(0, 3, 2, 1, 4).reshape(NE * NH, 128, NJ * 128)),
        "w3": np.ascontiguousarray(g("L0_ffn_w3").reshape(NE, NJ, 128, NH, 128).transpose(0, 3, 2, 1, 4).reshape(NE * NH, 128, NJ * 128)),
        "w2": np.ascontiguousarray(g("L0_ffn_w2").reshape(NE, NH, 128, NJ, 128).transpose(0, 3, 2, 1, 4).reshape(NE * NJ, 128, NH * 128)),
    }


ALL_PHASES = ("ln", "ffnproj", "flash", "ffnpart", "attproj", "scan", "attpart",
              "x2h2", "gate", "moe", "final")


def kernel(**inputs):
    nc, _ = build_program(phases=ALL_PHASES)
    in_maps = [_prep_inputs(inputs, ci) for ci in range(NCORES)]
    res = run_bass_kernel_spmd(nc, in_maps, core_ids=list(range(NCORES)))
    out = np.concatenate([np.asarray(res.results[ci]["o_own"]) for ci in range(NCORES)], axis=0)
    return out.reshape(1, T, D).astype(np.float32)


# version 08
# speedup vs baseline: 1.1652x; 1.0077x over previous
import os
import sys

for _p in ("/opt/trn_rl_repo",):
    if os.path.isdir(_p) and _p not in sys.path:
        sys.path.insert(0, _p)

import numpy as np
import ml_dtypes
from contextlib import ExitStack

import concourse.bass as bass
import concourse.mybir as mybir
from concourse.bass_utils import run_bass_kernel_spmd

F32 = mybir.dt.float32
BF16 = mybir.dt.bfloat16
AF = mybir.ActivationFunctionType
ALU = mybir.AluOpType
AX = mybir.AxisListType

NCORES = 8
T = 8192
D = 2048
NJ = D // 128
NCH = T // 128
TOWN = T // NCORES
NOC = TOWN // 128

AH = 2
ADK = 128
AM = 64
FQH = 4
FHD = 64
NE = 16
EH = 2048
NH = EH // 128

CAP = 384
NSC = CAP // 128
LN_EPS = 1e-5
CLAMP = 32.0
ATTN_ROPE_BASE = 10000.0
FFN_ROPE_BASE = 150000.0
ASCALE = float(ADK) ** -0.5
FSCALE = float(FHD) ** -0.5


class S:

    def __init__(self, ctx, nc, name):
        self.h = ctx.enter_context(nc.semaphore(name))
        self.n = 0

    def inc(self, instr, k=1):
        instr.then_inc(self.h, k)
        self.n += k
        return self.n


class Eng:

    def __init__(self, eng, sem):
        self.e = eng
        self.s = sem

    def do(self, instr):
        return self.s.inc(instr)

    def wait(self, sem, val):
        if val > 0:
            self.e.wait_ge(sem.h, val)


def _rope_tables_attn(tlen, dk, base):
    inv = (1.0 / base ** (np.arange(0, dk, 2, dtype=np.float32) / dk)).astype(np.float32)
    ang = (np.arange(tlen, dtype=np.float32)[:, None] * inv[None, :]).astype(np.float32)
    return np.cos(ang).astype(np.float32), np.sin(ang).astype(np.float32)


def _rope_tables_ffn(tlen, hd, base):
    half = hd // 2
    inv = (1.0 / base ** (np.arange(half, dtype=np.float32) / half)).astype(np.float32)
    ang = (np.arange(tlen, dtype=np.float32)[:, None] * inv[None, :]).astype(np.float32)
    return np.cos(ang).astype(np.float32), np.sin(ang).astype(np.float32)


def build_program(debug_taps=(), phases=("ln",), fake_cc=False):
    nc = bass.Bass("TRN2", target_bir_lowering=False)
    taps = set(debug_taps)
    ph = set(phases)
    ccu = 16 if fake_cc else 1

    x_full = nc.dram_tensor("x_full", [T, D], F32, kind="ExternalInput").ap()
    x_own = nc.dram_tensor("x_own", [TOWN, D], F32, kind="ExternalInput").ap()
    aWq = nc.dram_tensor("aWq", [D, AH * ADK], F32, kind="ExternalInput").ap()
    aWk = nc.dram_tensor("aWk", [D, AH * ADK], F32, kind="ExternalInput").ap()
    aWv = nc.dram_tensor("aWv", [D, AH * ADK], F32, kind="ExternalInput").ap()
    aWg = nc.dram_tensor("aWg", [D, AH * ADK], F32, kind="ExternalInput").ap()
    aWo = nc.dram_tensor("aWo", [AH * ADK, D], F32, kind="ExternalInput").ap()
    aWs1 = nc.dram_tensor("aWs1", [D, 16], F32, kind="ExternalInput").ap()
    aWs2 = nc.dram_tensor("aWs2", [17, AH * AM], F32, kind="ExternalInput").ap()
    gnw = nc.dram_tensor("gnw", [1, ADK], F32, kind="ExternalInput").ap()
    lnw = nc.dram_tensor("lnw", [1, D], F32, kind="ExternalInput").ap()
    flnw = nc.dram_tensor("flnw", [1, D], F32, kind="ExternalInput").ap()
    fWq = nc.dram_tensor("fWq", [D, FQH * FHD], F32, kind="ExternalInput").ap()
    fWk = nc.dram_tensor("fWk", [D, FHD], F32, kind="ExternalInput").ap()
    fWv = nc.dram_tensor("fWv", [D, FHD], F32, kind="ExternalInput").ap()
    fWo = nc.dram_tensor("fWo", [FQH * FHD, D], F32, kind="ExternalInput").ap()
    fWg = nc.dram_tensor("fWg", [D, NE], F32, kind="ExternalInput").ap()
    w1 = nc.dram_tensor("w1", [NE * NH, 128, NJ * 128], F32, kind="ExternalInput").ap()
    w3 = nc.dram_tensor("w3", [NE * NH, 128, NJ * 128], F32, kind="ExternalInput").ap()
    w2 = nc.dram_tensor("w2", [NE * NJ, 128, NH * 128], F32, kind="ExternalInput").ap()

    o_own = nc.dram_tensor("o_own", [TOWN, D], F32, kind="ExternalOutput").ap()

    ident_bf = nc.inline_tensor(np.eye(128, dtype=ml_dtypes.bfloat16), "ident_bf").ap()
    ident_f32 = nc.inline_tensor(np.eye(128, dtype=np.float32), "ident_f32").ap()
    utri_np = np.triu(np.ones((128, 128), dtype=np.float32)).astype(ml_dtypes.bfloat16)
    utri = nc.inline_tensor(utri_np, "utri").ap()
    ca, sa_ = _rope_tables_attn(T, ADK, ATTN_ROPE_BASE)
    cf, sf_ = _rope_tables_ffn(T, FHD, FFN_ROPE_BASE)
    ca4 = np.tile(ca, (1, 4))
    sa4 = np.tile(sa_, (1, 4))
    cf5 = np.tile(cf, (1, 5))
    sf5 = np.tile(sf_, (1, 5))
    cos_a4 = nc.inline_tensor(ca4.astype(ml_dtypes.bfloat16), "cos_a4").ap()
    sin_a4 = nc.inline_tensor(sa4.astype(ml_dtypes.bfloat16), "sin_a4").ap()
    cos_f5 = nc.inline_tensor(cf5.astype(ml_dtypes.bfloat16), "cos_f5").ap()
    sin_f5 = nc.inline_tensor(sf5.astype(ml_dtypes.bfloat16), "sin_f5").ap()
    dm = np.zeros((128, 4 * 512), dtype=np.float32)
    for di, delta in enumerate((0, 128, 256, 384)):
        dm[:, di * 512:(di + 1) * 512] = (np.arange(128)[:, None] + delta <= np.arange(512)[None, :])
    dmask = nc.inline_tensor(dm.astype(ml_dtypes.bfloat16), "dmask").ap()
    selnp = np.zeros((16, 16 * 128), dtype=np.float32)
    for e in range(16):
        selnp[e, e * 128:(e + 1) * 128] = 1.0
    sel = nc.inline_tensor(selnp, "sel").ap()
    ones_r128_np = np.ones((1, 128), dtype=np.float32)
    ones_r128 = nc.inline_tensor(ones_r128_np, "ones_r128").ap()
    iota_cap = nc.inline_tensor(np.tile(np.arange(CAP, dtype=np.float32)[None, :], (128, 1)), "iota_cap").ap()
    ones_sq = nc.inline_tensor(np.ones((128, 128), dtype=ml_dtypes.bfloat16), "ones_sq").ap()
    ones_r64 = nc.inline_tensor(np.ones((1, 64), dtype=np.float32), "ones_r64").ap()
    ones_c128bf = nc.inline_tensor(np.ones((128, 1), dtype=ml_dtypes.bfloat16), "ones_c128bf").ap()

    d_xnfm = nc.dram_tensor("d_xnfm", [T // 512, 128, NJ * 512], BF16).ap()
    d_rr = nc.dram_tensor("d_rr", [1, T], F32).ap()
    d_xn32 = nc.dram_tensor("d_xn32", [T, D], F32).ap()
    d_att = nc.dram_tensor("d_att", [T, 1152], BF16).ap()
    cc_f_in = nc.dram_tensor("cc_f_in", [T, D], F32).ap()
    cc_f_out = nc.dram_tensor("cc_f_out", [TOWN, D], F32).ap()
    cc_a_in = nc.dram_tensor("cc_a_in", [T, D], F32).ap()
    cc_a_out = nc.dram_tensor("cc_a_out", [TOWN, D], F32).ap()

    dbg = {}

    def tap(name, shape, dtype=F32):
        if name in taps:
            dbg[name] = nc.dram_tensor("dbg_" + name, shape, dtype, kind="ExternalOutput").ap()
            return dbg[name]
        return None

    t_xn = tap("xn", [NJ, 128, T], BF16)
    t_rr = tap("rr", [1, T])
    t_ffnq = tap("ffnq", [4, 64, T], BF16)
    t_ffnk = tap("ffnk", [64, T], BF16)
    t_ffnv = tap("ffnv", [128, NCH * 65], BF16)
    t_mhat = tap("mhat", [1, 8])
    t_flash = tap("flash", [2, 128, T], BF16)
    t_ffnpart = tap("ffnpart", [T, D])
    t_att = tap("att", [T, 1152], BF16)
    t_og = tap("og", [2, 128, T], BF16)
    t_attpart = tap("attpart", [T, D])
    t_x2 = tap("x2", [128, NJ * TOWN])
    t_h2 = tap("h2", [128, NJ * TOWN], BF16)
    t_gate = tap("gate", [16, TOWN])
    t_moe = tap("moe", [128, NJ * TOWN])

    with ExitStack() as ctx:
        E = ctx.enter_context

        s_idbf = E(nc.sbuf_tensor("s_idbf", [128, 128], BF16))
        s_idf32 = E(nc.sbuf_tensor("s_idf32", [128, 128], F32))
        s_utri = E(nc.sbuf_tensor("s_utri", [128, 128], BF16))
        s_ones_r128 = E(nc.sbuf_tensor("s_ones_r128", [1, 128], F32))
        s_ones_r64 = E(nc.sbuf_tensor("s_ones_r64", [1, 64], F32))
        s_ones_c128bf = E(nc.sbuf_tensor("s_ones_c128bf", [128, 1], BF16))
        s_lnw = E(nc.sbuf_tensor("s_lnw", [128, NJ], F32))
        s_flnw = E(nc.sbuf_tensor("s_flnw", [128, NJ], F32))
        s_gnwb = E(nc.sbuf_tensor("s_gnwb", [128, 128], F32))
        s_eps = E(nc.sbuf_tensor("s_eps", [128, 1], F32))
        s_dmask = E(nc.sbuf_tensor("s_dmask", [128, 4 * 512], BF16))
        s_sel = E(nc.sbuf_tensor("s_sel", [16, 16 * 128], F32))

        sy = S(ctx, nc, "sy")
        st = S(ctx, nc, "st")
        gp = S(ctx, nc, "gp")
        gq = S(ctx, nc, "gq")
        mmS = S(ctx, nc, "mm")
        dvS = S(ctx, nc, "dv")
        scS = S(ctx, nc, "sc")
        plS = S(ctx, nc, "pl")
        ccS = S(ctx, nc, "cc")
        w13s = [S(ctx, nc, f"w13s{p}") for p in range(2)]
        w2s = [S(ctx, nc, f"w2s{p}") for p in range(2)]
        qs = [S(ctx, nc, f"qs{k}") for k in range(12)]

        te = Eng(nc.tensor, mmS)
        dv = Eng(nc.vector, dvS)
        sc = Eng(nc.scalar, scS)
        pl = Eng(nc.gpsimd, plS)

        def bar():
            nc.all_engine_barrier()

        dv.do(nc.vector.memset(s_eps[:], LN_EPS))
        sy.inc(nc.sync.dma_start(out=s_idbf[:], in_=ident_bf[:, :]), 16)
        sy.inc(nc.sync.dma_start(out=s_idf32[:], in_=ident_f32[:, :]), 16)
        sy.inc(nc.sync.dma_start(out=s_utri[:], in_=utri[:, :]), 16)
        sy.inc(nc.sync.dma_start(out=s_ones_r128[:], in_=ones_r128[:, :]), 16)
        sy.inc(nc.sync.dma_start(out=s_ones_r64[:], in_=ones_r64[:, :]), 16)
        sy.inc(nc.sync.dma_start(out=s_ones_c128bf[:], in_=ones_c128bf[:, :]), 16)
        sy.inc(nc.sync.dma_start(out=s_dmask[:], in_=dmask[:, :]), 16)
        sy.inc(nc.sync.dma_start(out=s_sel[:], in_=sel[:, :]), 16)
        lnw_src = bass.AP(lnw.tensor, 0, [[1, 128], [128, NJ]])
        flnw_src = bass.AP(flnw.tensor, 0, [[1, 128], [128, NJ]])
        sy.inc(nc.sync.dma_start(out=s_lnw[:], in_=lnw_src, allow_slow_non_contiguous=True), 16)
        sy.inc(nc.sync.dma_start(out=s_flnw[:], in_=flnw_src, allow_slow_non_contiguous=True), 16)
        gnw_row = E(nc.sbuf_tensor("gnw_row", [1, ADK], F32))
        sy.inc(nc.sync.dma_start(out=gnw_row[:], in_=gnw[:, :]), 16)
        gate_ctx = ExitStack()
        te.wait(sy, sy.n)
        p_c = E(nc.psum_tensor("p_c", [128, 512], F32))
        vb = te.do(nc.tensor.matmul(p_c[:, 0:128], s_ones_r128[:, :], gnw_row[:, :], start=True, stop=True))
        dv.wait(mmS, vb)
        dv.do(nc.vector.tensor_copy(s_gnwb[:], p_c[:, 0:128]))
        bar()

        if "ln" in ph:
          with ExitStack() as lctx:
            LE = lctx.enter_context
            sx = [LE(nc.sbuf_tensor(f"sx{p}", [128, D], F32)) for p in range(2)]
            xn32 = [LE(nc.sbuf_tensor(f"xn32_{p}", [128, D], F32)) for p in range(2)]
            xnbf = [LE(nc.sbuf_tensor(f"xnbf_{p}", [128, D], BF16)) for p in range(2)]
            xnfm_st = [LE(nc.sbuf_tensor(f"xnfm_st{p}", [128, NJ * 512], BF16)) for p in range(2)]
            bnst = [LE(nc.sbuf_tensor(f"bnst{p}", [128, 24], F32)) for p in range(2)]
            bnag = [LE(nc.sbuf_tensor(f"bnag{p}", [128, 2], F32)) for p in range(2)]
            ln_sm = [LE(nc.sbuf_tensor(f"ln_sm{p}", [128, 4], F32)) for p in range(2)]
            p_trb = LE(nc.psum_tensor("p_trb", [128, 1024], BF16))
            p_trb2 = LE(nc.psum_tensor("p_trb2", [128, 1024], BF16))
            p_rr = LE(nc.psum_tensor("p_rr", [1, 128], F32))
            rr_sb = [LE(nc.sbuf_tensor(f"rr_sb{p}", [1, 128], F32)) for p in range(2)]

            store_done = [0, 0]
            rr_evac_prev = 0
            rr_store_prev = [0, 0]
            ldq = [qs[0], qs[1]]
            stq = [qs[2], qs[3]]
            xn32_done = [0, 0]
            vs_rstd_c = [0, 0]
            vx_done = [0, 0]

            def ln_A(c):
                p = c % 2
                if vx_done[p]:
                    nc.sync.wait_ge(dvS.h, vx_done[p])
                vld = ldq[p].inc(nc.sync.dma_start(out=sx[p][:], in_=x_full[c * 128:(c + 1) * 128, :]), 16)
                dv.wait(ldq[p], vld)
                vbn = 0
                for k in range(4):
                    vbn = dv.do(nc.vector.bn_stats(bnst[p][:, k * 6:(k + 1) * 6], sx[p][:, k * 512:(k + 1) * 512]))
                dv.wait(dvS, vbn)
                vagg = dv.do(nc.vector.bn_aggr(bnag[p][:], bnst[p][:]))
                sc.wait(dvS, vagg)
                vs = sc.do(nc.scalar.activation(ln_sm[p][:, 1:2], bnag[p][:, 1:2], AF.Ln, bias=s_eps[:, 0:1]))
                sc.wait(scS, vs)
                vs_rstd_c[p] = sc.do(nc.scalar.activation(ln_sm[p][:, 0:1], ln_sm[p][:, 1:2], AF.Exp, scale=-0.5))

            def ln_B(c):
                nonlocal rr_evac_prev
                p = c % 2
                if xn32_done[p]:
                    dv.wait(qs[8 + p], xn32_done[p])
                dv.wait(scS, vs_rstd_c[p])
                vx = dv.do(nc.vector.tensor_scalar(xn32[p][:], sx[p][:], bnag[p][:, 0:1], ln_sm[p][:, 0:1], ALU.subtract, ALU.mult))
                vx_done[p] = vx
                dv.wait(dvS, vx)
                vxb = dv.do(nc.vector.tensor_copy(xnbf[p][:], xn32[p][:]))
                v1 = dv.do(nc.vector.tensor_tensor(ln_sm[p][:, 3:4], ln_sm[p][:, 0:1], ln_sm[p][:, 0:1], ALU.mult))
                dv.wait(dvS, v1)
                v2 = dv.do(nc.vector.tensor_tensor(ln_sm[p][:, 1:2], bnag[p][:, 1:2], ln_sm[p][:, 3:4], ALU.mult))
                sc.wait(dvS, v2)
                vs = sc.do(nc.scalar.activation(ln_sm[p][:, 2:3], ln_sm[p][:, 1:2], AF.Ln, bias=s_eps[:, 0:1]))
                sc.wait(scS, vs)
                vs_rr = sc.do(nc.scalar.activation(ln_sm[p][:, 2:3], ln_sm[p][:, 2:3], AF.Exp, scale=-0.5))
                te.wait(scS, vs_rr)
                te.wait(dvS, rr_evac_prev)
                dv.wait(qs[7], qs[7].n)
                vtm = te.do(nc.tensor.transpose(p_rr[:, :], ln_sm[p][:, 2:3], s_idf32[:]))
                dv.wait(mmS, vtm)
                rr_evac_prev = dv.do(nc.vector.tensor_copy(rr_sb[p][:, :], p_rr[:, :]))
                nc.sync.wait_ge(dvS.h, rr_evac_prev)
                rr_store_prev[p] = qs[7].inc(nc.sync.dma_start(out=d_rr[:, c * 128:(c + 1) * 128], in_=rr_sb[p][:, :]), 16)
                te.wait(dvS, vxb)
                tp2 = (c // 4) % 2
                csub = c % 4
                vcp = 0
                for j in range(NJ):
                    pt = p_trb if j < 8 else p_trb2
                    tp = nc.tensor.transpose(pt[:, (j % 8) * 128:(j % 8 + 1) * 128], xnbf[p][:, j * 128:(j + 1) * 128], s_idbf[:])
                    if j % 8 == 7:
                        vt = te.do(tp)
                        dv.wait(mmS, vt)
                        if csub == 0 and j == 7:
                            nc.vector.wait_ge(stq[tp2].h, store_done[tp2])
                        for jj in range(j - 7, j + 1):
                            vcp = dv.do(nc.vector.tensor_copy(xnfm_st[tp2][:, jj * 512 + csub * 128:jj * 512 + (csub + 1) * 128], pt[:, (jj % 8) * 128:(jj % 8 + 1) * 128]))
                te.wait(dvS, vcp)
                nc.sync.wait_ge(dvS.h, vcp)
                if csub == 3:
                    stq[tp2].inc(nc.sync.dma_start(out=d_xnfm[c // 4, :, :], in_=xnfm_st[tp2][:]), 16)
                vstore = qs[8 + p].inc(nc.sync.dma_start(out=d_xn32[c * 128:(c + 1) * 128, :], in_=xn32[p][:]), 16)
                if t_xn is not None and csub == 3:
                    stq[tp2].inc(nc.sync.dma_start(out=bass.AP(t_xn.tensor, (c // 4) * 512, [[T, 128], [128 * T, NJ], [1, 512]]),
                                                   in_=xnfm_st[tp2][:].rearrange("q (a b) -> q a b", a=NJ)), 16)
                if csub == 3:
                    store_done[tp2] = stq[tp2].n
                xn32_done[p] = vstore

            for c in range(NCH):
                ln_A(c)
                if c >= 1:
                    ln_B(c - 1)
            ln_B(NCH - 1)

            nc.sync.wait_ge(sy.h, sy.n)
            nc.sync.wait_ge(stq[0].h, stq[0].n)
            nc.sync.wait_ge(stq[1].h, stq[1].n)
            nc.sync.wait_ge(qs[7].h, qs[7].n)
            nc.sync.wait_ge(qs[8].h, qs[8].n)
            nc.sync.wait_ge(qs[9].h, qs[9].n)
            bar()

        ffn_ctx = ExitStack()
        if "ffnproj" in ph or "flash" in ph:
            qres = ffn_ctx.enter_context(nc.sbuf_tensor("qres", [128, 4 * T], BF16))
            kres = ffn_ctx.enter_context(nc.sbuf_tensor("kres", [128, T], BF16))
            vres = ffn_ctx.enter_context(nc.sbuf_tensor("vres", [128, NCH * 65], BF16))
            mbias = ffn_ctx.enter_context(nc.sbuf_tensor("mbias", [128, 4], F32))
            ofm = ffn_ctx.enter_context(nc.sbuf_tensor("ofm", [128, 2 * T], BF16))
        if "ffnproj" in ph:
            with ExitStack() as fctx:
                FE = fctx.enter_context
                wqkv = FE(nc.sbuf_tensor("wqkv", [128, NJ * 384], BF16))
                tmpw = FE(nc.sbuf_tensor("tmpw", [128, 384], F32))
                xtl = [FE(nc.sbuf_tensor(f"xtl{p}", [128, NJ * 512], BF16)) for p in range(2)]
                stage = FE(nc.sbuf_tensor("stage", [128, 4 * 384], BF16))
                tmp1 = FE(nc.sbuf_tensor("tmp1", [128, 160], F32))
                tmp2 = FE(nc.sbuf_tensor("tmp2", [128, 160], F32))
                rrl = FE(nc.sbuf_tensor("rrl", [1, 512], F32))
                sqb = FE(nc.sbuf_tensor("sqb", [128, 320], F32))
                redb = FE(nc.sbuf_tensor("redb", [128, 5], F32))
                nmax = FE(nc.sbuf_tensor("nmax", [128, 5], F32))
                cosl = FE(nc.sbuf_tensor("cosl", [128, 4 * 160], BF16))
                sinl = FE(nc.sbuf_tensor("sinl", [128, 4 * 160], BF16))
                nrow = FE(nc.sbuf_tensor("nrow", [1, 5], F32))
                mrow = FE(nc.sbuf_tensor("mrow", [1, 4], F32))
                p_qkv = [FE(nc.psum_tensor(f"p_qkv{p}", [128, 384], F32)) for p in range(2)]
                p_rb = FE(nc.psum_tensor("p_rb", [128, 512], F32))
                p_tr = FE(nc.psum_tensor("p_tr", [128, 1024], BF16))

                v_ones = dv.do(nc.vector.memset(bass.AP(vres, 64, [[65 * NCH, 128], [65, NCH], [1, 1]]), 1.0))
                vnm = dv.do(nc.vector.memset(nmax[:], 0.0))
                dv.do(nc.vector.memset(qres[64:128, :], 0.0))
                dv.do(nc.vector.memset(kres[64:128, :], 0.0))

                vcast = 0
                for j in range(NJ):
                    if vcast:
                        nc.sync.wait_ge(dvS.h, vcast)
                    sy.inc(nc.sync.dma_start(out=tmpw[:, 0:256], in_=fWq[j * 128:(j + 1) * 128, :]), 16)
                    sy.inc(nc.sync.dma_start(out=tmpw[:, 256:320], in_=fWk[j * 128:(j + 1) * 128, :]), 16)
                    vwl = sy.inc(nc.sync.dma_start(out=tmpw[:, 320:384], in_=fWv[j * 128:(j + 1) * 128, :]), 16)
                    dv.wait(sy, vwl)
                    vq = dv.do(nc.vector.tensor_scalar(tmpw[:, 0:256], tmpw[:, 0:256], s_lnw[:, j:j + 1], float(FSCALE), ALU.mult, ALU.mult))
                    dv.wait(dvS, vq)
                    vkv = dv.do(nc.vector.tensor_scalar(tmpw[:, 256:384], tmpw[:, 256:384], s_lnw[:, j:j + 1], None, ALU.mult))
                    dv.wait(dvS, vkv)
                    vcast = dv.do(nc.vector.tensor_copy(wqkv[:, j * 384:(j + 1) * 384], tmpw[:]))
                    dv.wait(dvS, vcast)
                wq_ready = dvS.n

                ld_done = [0, 0]
                mm_qkv_done = [0, 0]
                rope_done = 0
                tr_evac = 0
                prev_tile_mm = 0
                prev_rb_mm = 0
                slot_mm = [0, 0]
                for tt in range(T // 512):
                    p2 = tt % 2
                    if slot_mm[p2]:
                        nc.sync.wait_ge(mmS.h, slot_mm[p2])
                    vld = qs[4 + p2].inc(nc.sync.dma_start(out=xtl[p2][:], in_=d_xnfm[tt, :, :]), 16)
                    nc.sync.wait_ge(dvS.h, rope_done)
                    nc.sync.wait_ge(mmS.h, prev_rb_mm)
                    qs[6].inc(nc.sync.dma_start(out=rrl[:, :], in_=d_rr[:, tt * 512:(tt + 1) * 512]), 16)
                    csrc = bass.AP(cos_f5.tensor, tt * 512 * 160, [[160, 128], [128 * 160, 4], [1, 160]])
                    ssrc = bass.AP(sin_f5.tensor, tt * 512 * 160, [[160, 128], [128 * 160, 4], [1, 160]])
                    qs[6].inc(nc.sync.dma_start(out=cosl[:].rearrange("q (a b) -> q a b", a=4), in_=csrc), 16)
                    vcs = qs[6].inc(nc.sync.dma_start(out=sinl[:].rearrange("q (a b) -> q a b", a=4), in_=ssrc), 16)
                    te.wait(dvS, rope_done)
                    te.wait(qs[6], qs[6].n)
                    vrb = te.do(nc.tensor.matmul(p_rb[:, 0:512], s_ones_r128[:, :], rrl[:, :], start=True, stop=True))
                    prev_rb_mm = vrb
                    dv.wait(qs[4 + p2], vld)
                    dv.wait(mmS, vrb)
                    dv.wait(dvS, wq_ready)
                    vxr = 0
                    for j in range(NJ):
                        vxr = dv.do(nc.vector.tensor_tensor(xtl[p2][:, j * 512:(j + 1) * 512], xtl[p2][:, j * 512:(j + 1) * 512], p_rb[:, 0:512], ALU.mult))
                    ld_done[p2] = vxr
                    slot_mm[p2] = None
                    rope_done_tile = 0
                    for sub in range(4):
                        pq = sub % 2
                        te.wait(dvS, vxr)
                        if mm_qkv_done[pq]:
                            te.wait(dvS, mm_qkv_done[pq])
                        vmm = 0
                        for j in range(NJ):
                            vmm = te.do(nc.tensor.matmul(p_qkv[pq][:, 0:384],
                                                         xtl[p2][:, j * 512 + sub * 128:j * 512 + (sub + 1) * 128],
                                                         wqkv[:, j * 384:(j + 1) * 384],
                                                         start=(j == 0), stop=(j == NJ - 1)))
                        dv.wait(mmS, vmm)
                        dv.wait(qs[6], vcs)
                        dv.wait(dvS, dvS.n)
                        h1 = bass.AP(p_qkv[pq], 0, [[384, 128], [64, 5], [1, 32]])
                        h2ap = bass.AP(p_qkv[pq], 32, [[384, 128], [64, 5], [1, 32]])
                        co = cosl[:, sub * 160:(sub + 1) * 160].rearrange("q (a b) -> q a b", a=5)
                        si = sinl[:, sub * 160:(sub + 1) * 160].rearrange("q (a b) -> q a b", a=5)
                        tm1 = tmp1[:].rearrange("q (a b) -> q a b", a=5)
                        tm2 = tmp2[:].rearrange("q (a b) -> q a b", a=5)
                        va = dv.do(nc.vector.tensor_tensor(tm1, h1, co, ALU.mult))
                        vb2 = dv.do(nc.vector.tensor_tensor(tm2, h2ap, si, ALU.mult))
                        dv.wait(dvS, vb2)
                        sg_h1 = bass.AP(stage, sub * 384, [[4 * 384, 128], [64, 5], [1, 32]])
                        sg_h2 = bass.AP(stage, sub * 384 + 32, [[4 * 384, 128], [64, 5], [1, 32]])
                        v_1 = dv.do(nc.vector.tensor_tensor(sg_h1, tm1, tm2, ALU.subtract))
                        dv.wait(dvS, v_1)
                        vc2 = dv.do(nc.vector.tensor_tensor(tm1, h2ap, co, ALU.mult))
                        vd2 = dv.do(nc.vector.tensor_tensor(tm2, h1, si, ALU.mult))
                        dv.wait(dvS, vd2)
                        v_2 = dv.do(nc.vector.tensor_tensor(sg_h2, tm1, tm2, ALU.add))
                        vv = dv.do(nc.vector.tensor_copy(vres[:, (tt * 4 + sub) * 65:(tt * 4 + sub) * 65 + 64], p_qkv[pq][:, 320:384]))
                        dv.wait(dvS, v_2)
                        vsq = dv.do(nc.vector.tensor_tensor(sqb[:], stage[:, sub * 384:sub * 384 + 320], stage[:, sub * 384:sub * 384 + 320], ALU.mult))
                        dv.wait(dvS, vsq)
                        vrd = dv.do(nc.vector.reduce_sum(redb[:], sqb[:].rearrange("q (a b) -> q a b", a=5), axis=AX.X))
                        dv.wait(dvS, vrd)
                        vmx = dv.do(nc.vector.tensor_max(nmax[:], nmax[:], redb[:]))
                        mm_qkv_done[pq] = vmx
                        rope_done_tile = max(rope_done_tile, vv, v_2)
                        te.wait(dvS, v_2)
                        te.wait(dvS, tr_evac)
                        t0_ = te.do(nc.tensor.transpose(p_tr[:, 0:128], stage[:, sub * 384:sub * 384 + 128], s_idbf[:]))
                        t1_ = te.do(nc.tensor.transpose(p_tr[:, 128:256], stage[:, sub * 384 + 128:sub * 384 + 256], s_idbf[:]))
                        t2_ = te.do(nc.tensor.transpose(p_tr[0:64, 256:384], stage[:, sub * 384 + 256:sub * 384 + 320], s_idbf[:]))
                        dv.wait(mmS, t2_)
                        tcol = tt * 512 + sub * 128
                        dv.do(nc.vector.tensor_copy(qres[0:64, 0 * T + tcol:0 * T + tcol + 128], p_tr[0:64, 0:128]))
                        dv.do(nc.vector.tensor_copy(qres[0:64, 1 * T + tcol:1 * T + tcol + 128], p_tr[64:128, 0:128]))
                        dv.do(nc.vector.tensor_copy(qres[0:64, 2 * T + tcol:2 * T + tcol + 128], p_tr[0:64, 128:256]))
                        dv.do(nc.vector.tensor_copy(qres[0:64, 3 * T + tcol:3 * T + tcol + 128], p_tr[64:128, 128:256]))
                        tr_evac = dv.do(nc.vector.tensor_copy(kres[0:64, tcol:tcol + 128], p_tr[0:64, 256:384]))
                    rope_done = max(rope_done_tile, tr_evac)
                    prev_tile_mm = vmm
                    slot_mm[p2] = vmm
                pl.wait(dvS, dvS.n)
                vgr = pl.do(nc.gpsimd.tensor_reduce(nrow[:], nmax[:], axis=AX.C, op=ALU.max))
                sc.wait(plS, vgr)
                vsr = sc.do(nc.scalar.activation(nrow[:], nrow[:], AF.Ln))
                sc.wait(scS, vsr)
                vsr = sc.do(nc.scalar.activation(nrow[:], nrow[:], AF.Exp, scale=0.5))
                dv.wait(scS, vsr)
                vmr = dv.do(nc.vector.tensor_scalar(mrow[:], nrow[:, 0:4], nrow[:, 4:5], -1.0, ALU.mult, ALU.mult))
                te.wait(dvS, vmr)
                for h in range(FQH):
                    vmb = te.do(nc.tensor.matmul(p_c[:, 0:1], s_ones_r128[:, :], mrow[:, h:h + 1], start=True, stop=True))
                    dv.wait(mmS, vmb)
                    vme = dv.do(nc.vector.tensor_copy(mbias[:, h:h + 1], p_c[:, 0:1]))
                    te.wait(dvS, vme)
                if t_ffnq is not None:
                    nc.sync.wait_ge(dvS.h, dvS.n)
                    for r in range(4):
                        sy.inc(nc.sync.dma_start(out=bass.AP(t_ffnq.tensor, r * 64 * T, [[T, 64], [1, T]]), in_=qres[0:64, r * T:(r + 1) * T]), 16)
                if t_ffnk is not None:
                    nc.sync.wait_ge(dvS.h, dvS.n)
                    sy.inc(nc.sync.dma_start(out=t_ffnk[:, :], in_=kres[0:64, :]), 16)
                if t_ffnv is not None:
                    nc.sync.wait_ge(dvS.h, dvS.n)
                    sy.inc(nc.sync.dma_start(out=t_ffnv[:, :], in_=vres[:, :]), 16)
                if t_mhat is not None:
                    nc.sync.wait_ge(dvS.h, dvS.n)
                    sy.inc(nc.sync.dma_start(out=t_mhat[:, 0:4], in_=mrow[:, :]), 16)
                    sy.inc(nc.sync.dma_start(out=t_mhat[:, 4:5], in_=nrow[:, 4:5]), 16)
                nc.sync.wait_ge(sy.h, sy.n)
            bar()

        if "flash" in ph:
            with ExitStack() as fctx:
                FE = fctx.enter_context
                psf = [FE(nc.psum_tensor(f"psf{p}", [128, 1024], F32)) for p in range(2)]
                p_o = [FE(nc.psum_tensor(f"p_o{p}", [65, 512], F32)) for p in range(2)]
                p_bc = FE(nc.psum_tensor("p_bc", [64, 512], F32))
                esb = FE(nc.sbuf_tensor("esb", [128, 2048], BF16))
                rl = [FE(nc.sbuf_tensor(f"rl{p}", [1, 512], F32)) for p in range(2)]
                rb = [FE(nc.sbuf_tensor(f"rb{p}", [64, 512], F32)) for p in range(2)]
                exp_done = [0, 0]
                av_done = [0, 0]
                fin_done = [0, 0]
                blocks = [(h, qb) for h in range(FQH) for qb in range(16)]

                def emit_stile(h, qb, b2):
                    par = b2 % 2
                    te.wait(scS, exp_done[par])
                    vmm = 0
                    for j2 in range(2):
                        kt = b2 * 2 + j2
                        vmm = te.do(nc.tensor.matmul(
                            psf[par][:, j2 * 512:(j2 + 1) * 512],
                            kres[:, kt * 128:(kt + 1) * 128],
                            qres[:, h * T + qb * 512:h * T + (qb + 1) * 512],
                            start=True, stop=True))
                    return vmm

                p_bc2 = [p_bc, None]
                obuf = [FE(nc.sbuf_tensor(f"obuf{p}", [64, 512], BF16)) for p in range(2)]
                pending_fin = [None]

                def emit_finalize():
                    if pending_fin[0] is None:
                        return
                    (fh, fqb, fopar, fav) = pending_fin[0]
                    pending_fin[0] = None
                    dv.wait(mmS, fav)
                    vrl = dv.do(nc.vector.reciprocal(rl[fopar][:, :], p_o[fopar][64:65, :]))
                    te.wait(dvS, vrl)
                    te.wait(dvS, fin_done[fopar])
                    bca = p_bc[:, :] if fopar == 0 else p_c[0:64, 0:512]
                    vbc = te.do(nc.tensor.matmul(bca, s_ones_r64[:, :], rl[fopar][:, :], start=True, stop=True))
                    dv.wait(mmS, vbc)
                    vrb = dv.do(nc.vector.tensor_copy(rb[fopar][:, :], bca))
                    dv.wait(dvS, vrb)
                    vtt = dv.do(nc.vector.tensor_tensor(obuf[fopar][:, :], p_o[fopar][0:64, :], rb[fopar][:, :], ALU.mult))
                    dv.wait(dvS, vtt)
                    fin_done[fopar] = dv.do(nc.vector.tensor_copy(
                        ofm[(fh % 2) * 64:(fh % 2) * 64 + 64, (fh // 2) * T + fqb * 512:(fh // 2) * T + (fqb + 1) * 512],
                        obuf[fopar][:, :]))

                for bi, (h, qb) in enumerate(blocks):
                    opar = bi % 2
                    nkt = 4 * (qb + 1)
                    nb = nkt // 2
                    st_mm = emit_stile(h, qb, 0)
                    stile_mm = {0: st_mm}
                    te.wait(dvS, fin_done[opar])
                    for b2 in range(nb):
                        par = b2 % 2
                        if b2 + 1 < nb:
                            stile_mm[b2 + 1] = emit_stile(h, qb, b2 + 1)
                        sc.wait(mmS, stile_mm[b2])
                        sc.wait(mmS, av_done[par])
                        vex = sc.do(nc.scalar.activation(esb[:, par * 1024:(par + 1) * 1024], psf[par][:, 0:1024],
                                                         AF.Exp, bias=mbias[:, h:h + 1]))
                        exp_done[par] = vex
                        vmask = 0
                        for j2 in range(2):
                            kt = b2 * 2 + j2
                            di = kt - 4 * qb
                            if di >= 0:
                                dv.wait(scS, vex)
                                eslice = esb[:, par * 1024 + j2 * 512:par * 1024 + (j2 + 1) * 512]
                                vmask = dv.do(nc.vector.tensor_tensor(eslice, eslice, s_dmask[:, di * 512:(di + 1) * 512], ALU.mult))
                        te.wait(scS, vex)
                        if vmask:
                            te.wait(dvS, vmask)
                        vav = 0
                        for j2 in range(2):
                            kt = b2 * 2 + j2
                            vav = te.do(nc.tensor.matmul(
                                p_o[opar][:, :],
                                vres[:, kt * 65:(kt + 1) * 65],
                                esb[:, par * 1024 + j2 * 512:par * 1024 + (j2 + 1) * 512],
                                start=(kt == 0), stop=(kt == nkt - 1)))
                        av_done[par] = vav
                        if b2 == 0:
                            emit_finalize()
                    pending_fin[0] = (h, qb, opar, av_done[par])
                emit_finalize()
                if t_flash is not None:
                    nc.sync.wait_ge(dvS.h, dvS.n)
                    for r in range(2):
                        sy.inc(nc.sync.dma_start(out=bass.AP(t_flash.tensor, r * 128 * T, [[T, 128], [1, T]]), in_=ofm[:, r * T:(r + 1) * T]), 16)
                    nc.sync.wait_ge(sy.h, sy.n)
            bar()

        if "ffnpart" in ph:
            with ExitStack() as fctx:
                FE = fctx.enter_context
                wo_sb = FE(nc.sbuf_tensor("wo_sb", [128, 2 * D], BF16))
                pstg = [FE(nc.sbuf_tensor(f"pstg{p}", [128, D], F32)) for p in range(2)]
                p_pp = [FE(nc.psum_tensor(f"p_pp{p}", [128, 512], F32)) for p in range(2)]
                for r in range(2):
                    gp.inc(nc.gpsimd.dma_start(out=wo_sb[:, r * D:(r + 1) * D], in_=fWo[r * 128:(r + 1) * 128, :]), 16)
                te.wait(gp, gp.n)
                part_st = [0, 0]
                evac_done = [0, 0]
                for tc in range(NCH):
                    pp = tc % 2
                    if part_st[pp]:
                        nc.vector.wait_ge(qs[pp].h, part_st[pp])
                    vev = 0
                    for dsl in range(4):
                        pb2 = dsl % 2
                        te.wait(dvS, evac_done[pb2])
                        vmm = 0
                        for r in range(2):
                            vmm = te.do(nc.tensor.matmul(
                                p_pp[pb2][:, :],
                                ofm[:, r * T + tc * 128:r * T + (tc + 1) * 128],
                                wo_sb[:, r * D + dsl * 512:r * D + (dsl + 1) * 512],
                                start=(r == 0), stop=(r == 1)))
                        dv.wait(mmS, vmm)
                        vev = dv.do(nc.vector.tensor_copy(pstg[pp][:, dsl * 512:(dsl + 1) * 512], p_pp[pb2][:, :]))
                        evac_done[pb2] = vev
                    nc.sync.wait_ge(dvS.h, vev)
                    part_st[pp] = qs[pp].inc(nc.sync.dma_start(out=cc_f_in[tc * 128:(tc + 1) * 128, :], in_=pstg[pp][:]), 16)
                if t_ffnpart is not None:
                    nc.sync.wait_ge(qs[0].h, qs[0].n)
                    nc.sync.wait_ge(qs[1].h, qs[1].n)
                    for tc in range(NCH):
                        sy.inc(nc.sync.dma_start(out=t_ffnpart[tc * 128:(tc + 1) * 128, :], in_=cc_f_in[tc * 128:(tc + 1) * 128, :]), 16)
                    nc.sync.wait_ge(sy.h, sy.n)
                nc.gpsimd.wait_ge(qs[0].h, qs[0].n)
                nc.gpsimd.wait_ge(qs[1].h, qs[1].n)
                if fake_cc:
                    ccS.inc(nc.gpsimd.dma_start(out=cc_f_out[:, :], in_=cc_f_in[0:TOWN, :]), 16)
                else:
                    ccS.inc(nc.gpsimd.collective_compute(
                        "ReduceScatter", ALU.add,
                        replica_groups=[list(range(NCORES))],
                        ins=[cc_f_in.tensor.ap().opt()],
                        outs=[cc_f_out.tensor.ap().opt()]), 1)
            bar()
        ffn_ctx.close()

        if "attproj" in ph:
            with ExitStack() as actx:
                AE = actx.enter_context
                awb = AE(nc.sbuf_tensor("awb", [128, NJ * 1024], BF16))
                ws1b = AE(nc.sbuf_tensor("ws1b", [128, NJ * 16], BF16))
                ws2b = AE(nc.sbuf_tensor("ws2b", [17, 128], BF16))
                tmpw2 = AE(nc.sbuf_tensor("tmpw2", [128, 1024], F32))
                xtl2 = [AE(nc.sbuf_tensor(f"xtl2_{p}", [128, NJ * 512], BF16)) for p in range(2)]
                astg = [AE(nc.sbuf_tensor(f"astg{p}", [128, 4 * 1152], BF16)) for p in range(2)]
                s16a = AE(nc.sbuf_tensor("s16a", [17, 512], BF16))
                atm1 = AE(nc.sbuf_tensor("atm1", [128, 256], F32))
                atm2 = AE(nc.sbuf_tensor("atm2", [128, 256], F32))
                cosl2 = AE(nc.sbuf_tensor("cosl2", [128, 4 * 256], BF16))
                sinl2 = AE(nc.sbuf_tensor("sinl2", [128, 4 * 256], BF16))
                p_qg = [AE(nc.psum_tensor(f"p_qg{p}", [128, 512], F32)) for p in range(2)]
                p_vg = [AE(nc.psum_tensor(f"p_vg{p}", [128, 512], F32)) for p in range(2)]
                p_s16 = AE(nc.psum_tensor("p_s16", [16, 512], F32))
                p_s = AE(nc.psum_tensor("p_s", [128, 128], F32))

                vw_ones = dv.do(nc.vector.memset(s16a[:, :], 1.0))
                vcast = 0
                for j in range(NJ):
                    if vcast:
                        nc.sync.wait_ge(dvS.h, vcast)
                    sy.inc(nc.sync.dma_start(out=tmpw2[:, 0:256], in_=aWq[j * 128:(j + 1) * 128, :]), 16)
                    sy.inc(nc.sync.dma_start(out=tmpw2[:, 256:512], in_=aWk[j * 128:(j + 1) * 128, :]), 16)
                    sy.inc(nc.sync.dma_start(out=tmpw2[:, 512:768], in_=aWv[j * 128:(j + 1) * 128, :]), 16)
                    vwl = sy.inc(nc.sync.dma_start(out=tmpw2[:, 768:1024], in_=aWg[j * 128:(j + 1) * 128, :]), 16)
                    dv.wait(sy, vwl)
                    vq = dv.do(nc.vector.tensor_scalar(tmpw2[:, 0:256], tmpw2[:, 0:256], float(ASCALE), None, ALU.mult))
                    dv.wait(dvS, vq)
                    vcast = dv.do(nc.vector.tensor_copy(awb[:, j * 1024:(j + 1) * 1024], tmpw2[:]))
                vg1 = gp.inc(nc.gpsimd.dma_start(out=ws1b[:].rearrange("q (a b) -> q a b", a=NJ),
                                                 in_=bass.AP(aWs1.tensor, 0, [[16, 128], [128 * 16, NJ], [1, 16]])), 16)
                vg2 = gp.inc(nc.gpsimd.dma_start(out=ws2b[:], in_=aWs2[:, :]), 16)

                ld2_done = [0, 0]
                es_clip_done = 0
                vs16e = 0
                prev_mm2 = 0
                rope2_done = 0
                st2_done = [0, 0]
                es_mm_done = 0
                slot_mm2 = [0, 0]
                for tt in range(T // 512):
                    p2 = tt % 2
                    if slot_mm2[p2]:
                        nc.sync.wait_ge(mmS.h, slot_mm2[p2])
                    vld = qs[4 + p2].inc(nc.sync.dma_start(out=xtl2[p2][:], in_=d_xnfm[tt, :, :]), 16)
                    if st2_done[p2]:
                        dv.wait(qs[p2], st2_done[p2])
                        sc.wait(qs[p2], st2_done[p2])
                    nc.sync.wait_ge(dvS.h, rope2_done)
                    csrc = bass.AP(cos_a4.tensor, tt * 512 * 256, [[256, 128], [128 * 256, 4], [1, 256]])
                    ssrc = bass.AP(sin_a4.tensor, tt * 512 * 256, [[256, 128], [128 * 256, 4], [1, 256]])
                    qs[6].inc(nc.sync.dma_start(out=cosl2[:].rearrange("q (a b) -> q a b", a=4), in_=csrc), 16)
                    vcs = qs[6].inc(nc.sync.dma_start(out=sinl2[:].rearrange("q (a b) -> q a b", a=4), in_=ssrc), 16)
                    te.wait(qs[4 + p2], vld)
                    te.wait(gp, gp.n)
                    te.wait(dvS, vcast)
                    te.wait(dvS, vw_ones)
                    te.wait(dvS, vs16e)
                    vs16 = 0
                    for j in range(NJ):
                        vs16 = te.do(nc.tensor.matmul(p_s16[:, :], ws1b[:, j * 16:(j + 1) * 16],
                                                      xtl2[p2][:, j * 512:(j + 1) * 512],
                                                      start=(j == 0), stop=(j == NJ - 1)))
                    dv.wait(mmS, vs16)
                    dv.wait(mmS, es_mm_done)
                    vs16e = dv.do(nc.vector.tensor_copy(s16a[0:16, :], p_s16[:, :]))
                    vmm = 0
                    rope2_tile = 0
                    for sub in range(4):
                        pq = sub % 2
                        te.wait(dvS, ld2_done[pq])
                        for j in range(NJ):
                            te.do(nc.tensor.matmul(p_qg[pq][:, :],
                                                   xtl2[p2][:, j * 512 + sub * 128:j * 512 + (sub + 1) * 128],
                                                   awb[:, j * 1024:j * 1024 + 512],
                                                   start=(j == 0), stop=(j == NJ - 1)))
                            vmm = te.do(nc.tensor.matmul(p_vg[pq][:, :],
                                                         xtl2[p2][:, j * 512 + sub * 128:j * 512 + (sub + 1) * 128],
                                                         awb[:, j * 1024 + 512:(j + 1) * 1024],
                                                         start=(j == 0), stop=(j == NJ - 1)))
                        dv.wait(mmS, vmm)
                        dv.wait(qs[6], vcs)
                        dv.wait(dvS, dvS.n)
                        h1 = bass.AP(p_qg[pq], 0, [[512, 128], [128, 4], [1, 64]])
                        h2ap = bass.AP(p_qg[pq], 64, [[512, 128], [128, 4], [1, 64]])
                        co = cosl2[:, sub * 256:(sub + 1) * 256].rearrange("q (a b) -> q a b", a=4)
                        si = sinl2[:, sub * 256:(sub + 1) * 256].rearrange("q (a b) -> q a b", a=4)
                        tm1 = atm1[:].rearrange("q (a b) -> q a b", a=4)
                        tm2 = atm2[:].rearrange("q (a b) -> q a b", a=4)
                        sg_h1 = bass.AP(astg[p2], sub * 1152, [[4 * 1152, 128], [128, 4], [1, 64]])
                        sg_h2 = bass.AP(astg[p2], sub * 1152 + 64, [[4 * 1152, 128], [128, 4], [1, 64]])
                        dv.do(nc.vector.tensor_tensor(tm1, h1, co, ALU.mult))
                        vb2 = dv.do(nc.vector.tensor_tensor(tm2, h2ap, si, ALU.mult))
                        dv.wait(dvS, vb2)
                        v_1 = dv.do(nc.vector.tensor_tensor(sg_h1, tm1, tm2, ALU.subtract))
                        dv.wait(dvS, v_1)
                        dv.do(nc.vector.tensor_tensor(tm1, h2ap, co, ALU.mult))
                        vd2 = dv.do(nc.vector.tensor_tensor(tm2, h1, si, ALU.mult))
                        dv.wait(dvS, vd2)
                        v_2 = dv.do(nc.vector.tensor_tensor(sg_h2, tm1, tm2, ALU.add))
                        vvg = dv.do(nc.vector.tensor_copy(astg[p2][:, sub * 1152 + 512:sub * 1152 + 1024], p_vg[pq][:, :]))
                        ld2_done[pq] = vvg
                        rope2_tile = max(rope2_tile, v_2, vvg)
                        te.wait(dvS, vs16e)
                        te.wait(dvS, es_clip_done)
                        ves = te.do(nc.tensor.matmul(p_s[:, :], s16a[:, sub * 128:(sub + 1) * 128], ws2b[:, :], start=True, stop=True))
                        es_mm_done = ves
                        dv.wait(mmS, ves)
                        vclip = dv.do(nc.vector.tensor_scalar(astg[p2][:, sub * 1152 + 1024:sub * 1152 + 1152], p_s[:, :], -CLAMP, CLAMP, ALU.max, ALU.min))
                        es_clip_done = vclip
                        sc.wait(dvS, vclip)
                        vexp = sc.do(nc.scalar.activation(astg[p2][:, sub * 1152 + 1024:sub * 1152 + 1152],
                                                          astg[p2][:, sub * 1152 + 1024:sub * 1152 + 1152], AF.Exp))
                        rope2_tile = max(rope2_tile, vclip)
                    rope2_done = rope2_tile
                    nc.sync.wait_ge(dvS.h, rope2_tile)
                    nc.sync.wait_ge(scS.h, scS.n)
                    adst = bass.AP(d_att.tensor, tt * 512 * 1152, [[1152, 128], [128 * 1152, 4], [1, 1152]])
                    vstt = qs[p2].inc(nc.sync.dma_start(out=adst, in_=astg[p2][:].rearrange("q (a b) -> q a b", a=4)), 16)
                    st2_done[p2] = vstt
                    prev_mm2 = vmm
                    slot_mm2[p2] = vmm
                if t_att is not None:
                    nc.sync.wait_ge(qs[0].h, qs[0].n)
                    nc.sync.wait_ge(qs[1].h, qs[1].n)
                    for tc in range(0, NCH, 8):
                        sy.inc(nc.sync.dma_start(out=t_att[tc * 128:(tc + 8) * 128, :], in_=d_att[tc * 128:(tc + 8) * 128, :]), 16)
                    nc.sync.wait_ge(sy.h, sy.n)
            nc.sync.wait_ge(qs[0].h, qs[0].n)
            nc.sync.wait_ge(qs[1].h, qs[1].n)
            bar()

        att_ctx = ExitStack()
        if "scan" in ph or "attpart" in ph:
            og_all = att_ctx.enter_context(nc.sbuf_tensor("og_all", [128, 2 * T], BF16))
        if "scan" in ph:
            with ExitStack() as sctx:
                SE = sctx.enter_context
                attc = [SE(nc.sbuf_tensor(f"attc{p}", [128, 1152], BF16)) for p in range(2)]
                S_sb = [SE(nc.sbuf_tensor(f"S_sb{p}", [128, 128], BF16)) for p in range(2)]
                Z_sb = [SE(nc.sbuf_tensor(f"Z_sb{p}", [64, 256], BF16)) for p in range(2)]
                R_sb = [SE(nc.sbuf_tensor(f"R_sb{p}", [1, 128], F32)) for p in range(2)]
                kfm_sb = [SE(nc.sbuf_tensor(f"kfm_sb{h}", [128, 128], BF16)) for h in range(2)]
                qfm_sb = [SE(nc.sbuf_tensor(f"qfm_sb{h}", [128, 128], BF16)) for h in range(2)]
                amask_sb = [SE(nc.sbuf_tensor(f"amask_sb{h}", [128, 128], BF16)) for h in range(2)]
                enz_sb = [SE(nc.sbuf_tensor(f"enz_sb{h}", [128, 64], F32)) for h in range(2)]
                ok_sb = [SE(nc.sbuf_tensor(f"ok_sb{h}", [128, 64], F32)) for h in range(2)]
                esm_sb = [SE(nc.sbuf_tensor(f"esm_sb{h}", [128, 64], F32)) for h in range(2)]
                u_sb = [SE(nc.sbuf_tensor(f"u_sb{h}", [128, 64], BF16)) for h in range(2)]
                ufm_sb = [SE(nc.sbuf_tensor(f"ufm_sb{h}", [64, 128], BF16)) for h in range(2)]
                esfm_sb = [SE(nc.sbuf_tensor(f"esfm_sb{h}", [64, 128], BF16)) for h in range(2)]
                w2m_sb = [SE(nc.sbuf_tensor(f"w2m_sb{h}", [128, 128], BF16)) for h in range(2)]
                o_sb = [SE(nc.sbuf_tensor(f"o_sb{h}", [128, 128], F32)) for h in range(2)]
                gg_sb = [SE(nc.sbuf_tensor(f"gg_sb{h}", [128, 128], F32)) for h in range(2)]
                eg_sb = [SE(nc.sbuf_tensor(f"eg_sb{h}", [128, 128], F32)) for h in range(2)]
                og_sb = [SE(nc.sbuf_tensor(f"og_sb{h}", [128, 128], BF16)) for h in range(2)]
                sm1 = [SE(nc.sbuf_tensor(f"sm1_{h}", [128, 8], F32)) for h in range(2)]
                bnst2 = [SE(nc.sbuf_tensor(f"bnst2_{h}", [128, 6], F32)) for h in range(2)]
                bnag2 = [SE(nc.sbuf_tensor(f"bnag2_{h}", [128, 2], F32)) for h in range(2)]
                pS = SE(nc.psum_tensor("pS", [128, 128], F32))
                pZ = SE(nc.psum_tensor("pZ", [64, 256], F32))
                pR = SE(nc.psum_tensor("pR", [1, 128], F32))
                p_co = SE(nc.psum_tensor("p_co", [128, 256], F32))
                p_wo = SE(nc.psum_tensor("p_wo", [128, 256], F32))
                p_oo = SE(nc.psum_tensor("p_oo", [128, 256], F32))
                p_trS = SE(nc.psum_tensor("p_trS", [128, 1024], BF16))
                p_a = p_c

                upd_done = 0
                snap_done = 0
                attc_mm = [0, 0]
                attc_dv = [0, 0]
                trS_free = 0
                trU_free = 0
                co_free = 0
                wo_free = 0
                oo_free = 0
                am_free = 0
                for c in range(NCH):
                    p = c % 2
                    if attc_mm[p]:
                        nc.sync.wait_ge(mmS.h, attc_mm[p])
                        nc.sync.wait_ge(dvS.h, attc_dv[p])
                    vld = qs[4 + p].inc(nc.sync.dma_start(out=attc[p][:], in_=d_att[c * 128:(c + 1) * 128, :]), 16)
                    ap = attc[p]
                    qsl = [ap[:, h * 128:(h + 1) * 128] for h in range(2)]
                    ksl = [ap[:, 256 + h * 128:256 + (h + 1) * 128] for h in range(2)]
                    vsl = [ap[:, 512 + h * 128:512 + (h + 1) * 128] for h in range(2)]
                    gsl = [ap[:, 768 + h * 128:768 + (h + 1) * 128] for h in range(2)]
                    essl = [ap[:, 1024 + h * 64:1024 + (h + 1) * 64] for h in range(2)]
                    if c > 0:
                        dv.wait(mmS, upd_done)
                        dv.do(nc.vector.tensor_copy(S_sb[p][:], pS[:, :]))
                        dv.do(nc.vector.tensor_copy(Z_sb[p][:], pZ[:, :]))
                        snap_done = dv.do(nc.vector.tensor_copy(R_sb[p][:], pR[:, :]))
                    te.wait(qs[4 + p], vld)
                    if c > 0:
                        te.wait(dvS, snap_done)
                    for h in range(2):
                        te.do(nc.tensor.matmul(pS[:, h * 64:(h + 1) * 64], ksl[h], essl[h],
                                               start=(c == 0 and h == 0), stop=(c == NCH - 1), skip_group_check=True))
                        te.do(nc.tensor.matmul(pZ[0:64, h * 128:(h + 1) * 128], essl[h], vsl[h],
                                               start=(c == 0 and h == 0), stop=(c == NCH - 1), skip_group_check=True))
                    upd_done = te.do(nc.tensor.matmul(pR[:, :], s_ones_c128bf[:, :], ap[:, 1024:1152],
                                                      start=(c == 0), stop=(c == NCH - 1), skip_group_check=True))
                    te.wait(dvS, trS_free)
                    tks = []
                    for h in range(2):
                        te.do(nc.tensor.transpose(p_trS[:, h * 256:h * 256 + 128], ksl[h], s_idbf[:]))
                        tks.append(te.do(nc.tensor.transpose(p_trS[:, h * 256 + 128:h * 256 + 256], qsl[h], s_idbf[:])))
                    vq2 = [0, 0]
                    for h in range(2):
                        dv.wait(mmS, tks[1])
                        dv.do(nc.vector.tensor_copy(kfm_sb[h][:], p_trS[:, h * 256:h * 256 + 128]))
                        vq2[h] = dv.do(nc.vector.tensor_copy(qfm_sb[h][:], p_trS[:, h * 256 + 128:h * 256 + 256]))
                    va2 = [0, 0]
                    te.wait(dvS, am_free)
                    for h in range(2):
                        te.wait(dvS, vq2[h])
                        va2[h] = te.do(nc.tensor.matmul(p_a[:, h * 128:(h + 1) * 128], kfm_sb[h][:], qfm_sb[h][:], start=True, stop=True))
                    vam = [0, 0]
                    for h in range(2):
                        dv.wait(mmS, va2[1])
                        vam[h] = dv.do(nc.vector.tensor_tensor(amask_sb[h][:], p_a[:, h * 128:(h + 1) * 128], s_utri[:], ALU.mult))
                    am_free = max(vam)
                    vok = [0, 0]
                    te.wait(dvS, co_free)
                    for h in range(2):
                        te.wait(dvS, vam[h])
                        te.do(nc.tensor.matmul(p_co[:, h * 128:h * 128 + 64], s_utri[:], essl[h], start=(h == 0), stop=False, skip_group_check=True))
                        if c > 0:
                            te.do(nc.tensor.matmul(p_co[:, h * 128:h * 128 + 64], s_ones_r128[:, :], R_sb[p][:, h * 64:(h + 1) * 64], start=False, stop=True, skip_group_check=True))
                        te.do(nc.tensor.matmul(p_co[:, h * 128 + 64:h * 128 + 128], amask_sb[h][:], essl[h], start=False, stop=(c == 0), skip_group_check=True))
                        if c > 0:
                            vok[h] = te.do(nc.tensor.matmul(p_co[:, h * 128 + 64:h * 128 + 128], qfm_sb[h][:], S_sb[p][:, h * 64:(h + 1) * 64], start=False, stop=True, skip_group_check=True))
                        else:
                            vok[h] = mmS.n
                    vu = [0, 0]
                    vnm2 = [0, 0]
                    for h in range(2):
                        dv.wait(mmS, max(vok))
                        vez = dv.do(nc.vector.reciprocal(enz_sb[h][:], p_co[:, h * 128:h * 128 + 64]))
                        dv.wait(dvS, vez)
                        vko = dv.do(nc.vector.tensor_tensor(ok_sb[h][:], p_co[:, h * 128 + 64:h * 128 + 128], enz_sb[h][:], ALU.mult))
                        co_free = vko
                        dv.wait(dvS, vko)
                        vnm2[h] = dv.do(nc.vector.reduce_max(sm1[h][:, 0:1], ok_sb[h][:], axis=AX.X, negate=True))
                    vex2 = [0, 0]
                    for h in range(2):
                        sc.wait(dvS, vnm2[h])
                        vex2[h] = sc.do(nc.scalar.activation(esm_sb[h][:], ok_sb[h][:], AF.Exp, bias=sm1[h][:, 0:1], accum_out=sm1[h][:, 1:2]))
                    for h in range(2):
                        dv.wait(scS, vex2[h])
                        vri = dv.do(nc.vector.reciprocal(sm1[h][:, 2:3], sm1[h][:, 1:2]))
                        dv.wait(dvS, vri)
                        vue = dv.do(nc.vector.tensor_tensor(esm_sb[h][:], esm_sb[h][:], enz_sb[h][:], ALU.mult))
                        dv.wait(dvS, vue)
                        vu[h] = dv.do(nc.vector.tensor_scalar(u_sb[h][:], esm_sb[h][:], sm1[h][:, 2:3], None, ALU.mult))
                    te.wait(dvS, trU_free)
                    tus = []
                    for h in range(2):
                        te.wait(dvS, vu[h])
                        te.do(nc.tensor.transpose(p_trS[0:64, 512 + h * 256:512 + h * 256 + 128], u_sb[h][:], s_idbf[:]))
                        tus.append(te.do(nc.tensor.transpose(p_trS[0:64, 512 + h * 256 + 128:512 + h * 256 + 256], essl[h], s_idbf[:])))
                    vesf = [0, 0]
                    for h in range(2):
                        dv.wait(mmS, tus[1])
                        dv.do(nc.vector.tensor_copy(ufm_sb[h][:], p_trS[0:64, 512 + h * 256:512 + h * 256 + 128]))
                        vesf[h] = dv.do(nc.vector.tensor_copy(esfm_sb[h][:], p_trS[0:64, 512 + h * 256 + 128:512 + h * 256 + 256]))
                    trU_free = max(vesf)
                    vw2 = [0, 0]
                    te.wait(dvS, wo_free)
                    for h in range(2):
                        te.wait(dvS, vesf[h])
                        vw2[h] = te.do(nc.tensor.matmul(p_wo[:, h * 128:(h + 1) * 128], esfm_sb[h][:], ufm_sb[h][:], start=True, stop=True))
                    vwm = [0, 0]
                    for h in range(2):
                        dv.wait(mmS, vw2[1])
                        vwm[h] = dv.do(nc.vector.tensor_tensor(w2m_sb[h][:], p_wo[:, h * 128:(h + 1) * 128], s_utri[:], ALU.mult))
                    wo_free = max(vwm)
                    vo2 = [0, 0]
                    te.wait(scS, oo_free)
                    for h in range(2):
                        te.wait(dvS, vwm[h])
                        te.do(nc.tensor.matmul(p_oo[:, h * 128:(h + 1) * 128], w2m_sb[h][:], vsl[h], start=True, stop=(c == 0), skip_group_check=True))
                        if c > 0:
                            vo2[h] = te.do(nc.tensor.matmul(p_oo[:, h * 128:(h + 1) * 128], ufm_sb[h][:], Z_sb[p][0:64, h * 128:(h + 1) * 128], start=False, stop=True, skip_group_check=True))
                        else:
                            vo2[h] = mmS.n
                    vosb = [0, 0]
                    for h in range(2):
                        sc.wait(mmS, max(vo2))
                        vosb[h] = sc.do(nc.scalar.copy(o_sb[h][:], p_oo[:, h * 128:(h + 1) * 128]))
                    oo_free = max(vosb)
                    veg = [0, 0]
                    for h in range(2):
                        veg[h] = sc.do(nc.scalar.activation(eg_sb[h][:], gsl[h], AF.Exp, scale=-1.0))
                    vrstd2 = [0, 0]
                    for h in range(2):
                        dv.wait(scS, vosb[h])
                        vbn2 = dv.do(nc.vector.bn_stats(bnst2[h][:], o_sb[h][:]))
                        dv.wait(dvS, vbn2)
                        vag2 = dv.do(nc.vector.bn_aggr(bnag2[h][:], bnst2[h][:]))
                        dv.wait(dvS, vag2)
                        vq3 = dv.do(nc.vector.tensor_tensor(sm1[h][:, 3:4], bnag2[h][:, 0:1], bnag2[h][:, 0:1], ALU.mult))
                        dv.wait(dvS, vq3)
                        vm2 = dv.do(nc.vector.tensor_add(sm1[h][:, 4:5], bnag2[h][:, 1:2], sm1[h][:, 3:4]))
                        sc.wait(dvS, vm2)
                        vl2 = sc.do(nc.scalar.activation(sm1[h][:, 5:6], sm1[h][:, 4:5], AF.Ln, bias=s_eps[:, 0:1]))
                        sc.wait(scS, vl2)
                        vrstd2[h] = sc.do(nc.scalar.activation(sm1[h][:, 5:6], sm1[h][:, 5:6], AF.Exp, scale=-0.5))
                    vog = [0, 0]
                    for h in range(2):
                        dv.wait(scS, veg[h])
                        vd1 = dv.do(nc.vector.tensor_scalar(eg_sb[h][:], eg_sb[h][:], 1.0, None, ALU.add))
                        dv.wait(dvS, vd1)
                        vr1 = dv.do(nc.vector.reciprocal(eg_sb[h][:], eg_sb[h][:]))
                        dv.wait(dvS, vr1)
                        vgg = dv.do(nc.vector.tensor_tensor(gg_sb[h][:], gsl[h], eg_sb[h][:], ALU.mult))
                        dv.wait(scS, vrstd2[h])
                        von = dv.do(nc.vector.tensor_scalar(o_sb[h][:], o_sb[h][:], sm1[h][:, 5:6], None, ALU.mult))
                        dv.wait(dvS, max(von, vgg))
                        von2 = dv.do(nc.vector.tensor_tensor(o_sb[h][:], o_sb[h][:], s_gnwb[:], ALU.mult))
                        dv.wait(dvS, von2)
                        vog[h] = dv.do(nc.vector.tensor_tensor(og_sb[h][:], o_sb[h][:], gg_sb[h][:], ALU.mult))
                    te.wait(dvS, max(vq2))
                    tgs = []
                    for h in range(2):
                        te.wait(dvS, vog[h])
                        tgs.append(te.do(nc.tensor.transpose(p_trS[:, h * 256:h * 256 + 128], og_sb[h][:], s_idbf[:])))
                    vga = [0, 0]
                    for h in range(2):
                        dv.wait(mmS, tgs[1])
                        vga[h] = dv.do(nc.vector.tensor_copy(og_all[:, h * T + c * 128:h * T + (c + 1) * 128], p_trS[:, h * 256:h * 256 + 128]))
                    trS_free = max(vga)
                    attc_mm[p] = mmS.n
                    attc_dv[p] = dvS.n
                if t_og is not None:
                    nc.sync.wait_ge(dvS.h, dvS.n)
                    for r in range(2):
                        sy.inc(nc.sync.dma_start(out=bass.AP(t_og.tensor, r * 128 * T, [[T, 128], [1, T]]), in_=og_all[:, r * T:(r + 1) * T]), 16)
                    nc.sync.wait_ge(sy.h, sy.n)
            bar()

        if "attpart" in ph:
            with ExitStack() as fctx:
                FE = fctx.enter_context
                awo_sb = FE(nc.sbuf_tensor("awo_sb", [128, 2 * D], BF16))
                pstg2 = [FE(nc.sbuf_tensor(f"pstg2_{p}", [128, D], F32)) for p in range(2)]
                p_pp2 = [FE(nc.psum_tensor(f"p_pp2_{p}", [128, 512], F32)) for p in range(2)]
                for r in range(2):
                    gp.inc(nc.gpsimd.dma_start(out=awo_sb[:, r * D:(r + 1) * D], in_=aWo[r * 128:(r + 1) * 128, :]), 16)
                te.wait(gp, gp.n)
                part_st = [0, 0]
                evac_done = [0, 0]
                for tc in range(NCH):
                    pp = tc % 2
                    if part_st[pp]:
                        nc.vector.wait_ge(qs[pp].h, part_st[pp])
                    vev = 0
                    for dsl in range(4):
                        pb2 = dsl % 2
                        te.wait(dvS, evac_done[pb2])
                        vmm = 0
                        for r in range(2):
                            vmm = te.do(nc.tensor.matmul(
                                p_pp2[pb2][:, :],
                                og_all[:, r * T + tc * 128:r * T + (tc + 1) * 128],
                                awo_sb[:, r * D + dsl * 512:r * D + (dsl + 1) * 512],
                                start=(r == 0), stop=(r == 1)))
                        dv.wait(mmS, vmm)
                        vev = dv.do(nc.vector.tensor_copy(pstg2[pp][:, dsl * 512:(dsl + 1) * 512], p_pp2[pb2][:, :]))
                        evac_done[pb2] = vev
                    nc.sync.wait_ge(dvS.h, vev)
                    part_st[pp] = qs[pp].inc(nc.sync.dma_start(out=cc_a_in[tc * 128:(tc + 1) * 128, :], in_=pstg2[pp][:]), 16)
                if t_attpart is not None:
                    nc.sync.wait_ge(qs[0].h, qs[0].n)
                    nc.sync.wait_ge(qs[1].h, qs[1].n)
                    for tc in range(NCH):
                        sy.inc(nc.sync.dma_start(out=t_attpart[tc * 128:(tc + 1) * 128, :], in_=cc_a_in[tc * 128:(tc + 1) * 128, :]), 16)
                    nc.sync.wait_ge(sy.h, sy.n)
                nc.gpsimd.wait_ge(qs[0].h, qs[0].n)
                nc.gpsimd.wait_ge(qs[1].h, qs[1].n)
                if "ffnpart" in ph:
                    nc.gpsimd.wait_ge(ccS.h, ccu)
                if fake_cc:
                    ccS.inc(nc.gpsimd.dma_start(out=cc_a_out[:, :], in_=cc_a_in[0:TOWN, :]), 16)
                else:
                    ccS.inc(nc.gpsimd.collective_compute(
                        "ReduceScatter", ALU.add,
                        replica_groups=[list(range(NCORES))],
                        ins=[cc_a_in.tensor.ap().opt()],
                        outs=[cc_a_out.tensor.ap().opt()]), 1)
            bar()
        att_ctx.close()

        if "x2h2" in ph:
            x2a = E(nc.sbuf_tensor("x2a", [128, NJ * TOWN], F32))
            h2tm = E(nc.sbuf_tensor("h2tm", [128, NOC * D], BF16))
            wfm = E(nc.sbuf_tensor("wfm", [16, TOWN], F32))
            wtm = E(nc.sbuf_tensor("wtm", [128, NOC * 16], F32))
            slottm = E(nc.sbuf_tensor("slottm", [128, NOC * 16], F32))
            s_flnwb = gate_ctx.enter_context(nc.sbuf_tensor("s_flnwb", [128, D], F32))
            row16 = gate_ctx.enter_context(nc.sbuf_tensor("row16", [16, 128], F32))
            h2sb = gate_ctx.enter_context(nc.sbuf_tensor("h2sb", [128, NJ * TOWN], BF16))
            vtr = te.do(nc.tensor.transpose(p_c[0:16, 0:128], s_flnw[:], s_idf32[:]))
            dv.wait(mmS, vtr)
            vr16 = dv.do(nc.vector.tensor_copy(row16[:], p_c[0:16, 0:128]))
            for j in range(NJ):
                te.wait(dvS, dvS.n)
                vb = te.do(nc.tensor.matmul(p_c[:, 128:256], s_sel[:, j * 128:(j + 1) * 128], row16[:, :], start=True, stop=True))
                dv.wait(mmS, vb)
                dv.do(nc.vector.tensor_copy(s_flnwb[:, j * 128:(j + 1) * 128], p_c[:, 128:256]))
            with ExitStack() as xctx:
                XE = xctx.enter_context
                rsf = [XE(nc.sbuf_tensor(f"rsf{p}", [128, D], F32)) for p in range(2)]
                xno = [XE(nc.sbuf_tensor(f"xno{p}", [128, D], F32)) for p in range(2)]
                x2tm = XE(nc.sbuf_tensor("x2tm", [128, D], F32))
                bnst3 = XE(nc.sbuf_tensor("bnst3", [128, 24], F32))
                bnag3 = XE(nc.sbuf_tensor("bnag3", [128, 2], F32))
                sm3 = XE(nc.sbuf_tensor("sm3", [128, 4], F32))
                rr2row = XE(nc.sbuf_tensor("rr2row", [1, TOWN], F32))
                rr2sb = XE(nc.sbuf_tensor("rr2sb", [1, 128], F32))
                p_tx = XE(nc.psum_tensor("p_tx", [128, 512], F32))
                p_tx2 = XE(nc.psum_tensor("p_tx2", [128, 512], F32))
                p_rr2 = XE(nc.psum_tensor("p_rr2", [1, 128], F32))
                p_rb2 = XE(nc.psum_tensor("p_rb2", [128, 512], F32))

                if "ffnpart" in ph:
                    nc.sync.wait_ge(ccS.h, ccu)
                cid = nc.gpsimd.partition_id()
                x2_done = 0
                rr2_ev = 0
                xno_done = [0, 0]
                for oc in range(NOC):
                    p = oc % 2
                    if xno_done[p]:
                        nc.sync.wait_ge(dvS.h, xno_done[p])
                        nc.gpsimd.wait_ge(dvS.h, xno_done[p])
                    vld1 = qs[p].inc(nc.sync.dma_start(out=rsf[p][:], in_=cc_f_out[oc * 128:(oc + 1) * 128, :]), 16)
                    xsrc = bass.AP(d_xn32.tensor, (cid * TOWN + oc * 128) * D, [[D, 128], [1, D]])
                    gqp = gp if p == 0 else gq
                    vld2 = gqp.inc(nc.gpsimd.dma_start(out=xno[p][:], in_=xsrc), 16)
                    dv.wait(qs[p], vld1)
                    dv.wait(gqp, vld2)
                    dv.wait(mmS, x2_done)
                    vx2 = dv.do(nc.vector.tensor_add(x2tm[:], xno[p][:], rsf[p][:]))
                    xno_done[p] = vx2
                    dv.wait(dvS, vx2)
                    for k in range(4):
                        vbn = dv.do(nc.vector.bn_stats(bnst3[:, k * 6:(k + 1) * 6], x2tm[:, k * 512:(k + 1) * 512]))
                    dv.wait(dvS, vbn)
                    vag = dv.do(nc.vector.bn_aggr(bnag3[:], bnst3[:]))
                    dv.wait(dvS, vag)
                    vq4 = dv.do(nc.vector.tensor_tensor(sm3[:, 0:1], bnag3[:, 0:1], bnag3[:, 0:1], ALU.mult))
                    dv.wait(dvS, vq4)
                    vms = dv.do(nc.vector.tensor_add(sm3[:, 1:2], bnag3[:, 1:2], sm3[:, 0:1]))
                    sc.wait(dvS, vms)
                    vl3 = sc.do(nc.scalar.activation(sm3[:, 2:3], sm3[:, 1:2], AF.Ln, bias=s_eps[:, 0:1]))
                    sc.wait(scS, vl3)
                    vrr2 = sc.do(nc.scalar.activation(sm3[:, 2:3], sm3[:, 2:3], AF.Exp, scale=-0.5))
                    dv.wait(scS, vrr2)
                    vhtm = dv.do(nc.vector.tensor_scalar(h2tm[:, oc * D:(oc + 1) * D], x2tm[:], sm3[:, 2:3], None, ALU.mult))
                    dv.wait(dvS, vhtm)
                    dv.do(nc.vector.tensor_tensor(h2tm[:, oc * D:(oc + 1) * D], h2tm[:, oc * D:(oc + 1) * D], s_flnwb[:], ALU.mult))
                    te.wait(scS, vrr2)
                    te.wait(dvS, rr2_ev)
                    vtr2 = te.do(nc.tensor.transpose(p_rr2[:, :], sm3[:, 2:3], s_idf32[:]))
                    dv.wait(mmS, vtr2)
                    rr2_ev = dv.do(nc.vector.tensor_copy(rr2row[:, oc * 128:(oc + 1) * 128], p_rr2[:, :]))
                    vtx = 0
                    for j in range(NJ):
                        ptx = p_tx if (j // 4) % 2 == 0 else p_tx2
                        tp = nc.tensor.transpose(ptx[:, (j % 4) * 128:(j % 4 + 1) * 128], x2tm[:, j * 128:(j + 1) * 128], s_idf32[:])
                        if j % 4 == 3:
                            vt = te.do(tp)
                            dv.wait(mmS, vt)
                            for jj in range(j - 3, j + 1):
                                vtx = dv.do(nc.vector.tensor_copy(x2a[:, jj * TOWN + oc * 128:jj * TOWN + (oc + 1) * 128], ptx[:, (jj % 4) * 128:(jj % 4 + 1) * 128]))
                            te.wait(dvS, vtx)
                    x2_done = mmS.n
                vh2 = 0
                for tt2 in range(TOWN // 512):
                    te.wait(dvS, vh2)
                    vrb2 = te.do(nc.tensor.matmul(p_rb2[:, 0:512], s_ones_r128[:, :], rr2row[:, tt2 * 512:(tt2 + 1) * 512], start=True, stop=True))
                    dv.wait(mmS, vrb2)
                    for j in range(NJ):
                        vt1 = dv.do(nc.vector.tensor_tensor(h2sb[:, j * TOWN + tt2 * 512:j * TOWN + tt2 * 512 + 512],
                                                            x2a[:, j * TOWN + tt2 * 512:j * TOWN + tt2 * 512 + 512], p_rb2[:, 0:512], ALU.mult))
                        dv.wait(dvS, vt1)
                        vh2 = dv.do(nc.vector.tensor_scalar(h2sb[:, j * TOWN + tt2 * 512:j * TOWN + tt2 * 512 + 512],
                                                            h2sb[:, j * TOWN + tt2 * 512:j * TOWN + tt2 * 512 + 512], s_flnw[:, j:j + 1], None, ALU.mult))
                if t_x2 is not None:
                    nc.sync.wait_ge(dvS.h, dvS.n)
                    sy.inc(nc.sync.dma_start(out=t_x2[:, :], in_=x2a[:, :]), 16)
                if t_h2 is not None:
                    nc.sync.wait_ge(dvS.h, dvS.n)
                    sy.inc(nc.sync.dma_start(out=t_h2[:, :], in_=h2sb[:, :]), 16)
                nc.sync.wait_ge(sy.h, sy.n)
            bar()

        if "gate" in ph:
            with ExitStack() as gctx:
                GE = gctx.enter_context
                wg_sb = GE(nc.sbuf_tensor("wg_sb", [128, NJ * 16], BF16))
                lg = GE(nc.sbuf_tensor("lg", [128, 16], F32))
                lgt = GE(nc.sbuf_tensor("lgt", [128, 16], F32))
                msk = GE(nc.sbuf_tensor("msk", [128, 16], F32))
                gsm = GE(nc.sbuf_tensor("gsm", [128, 8], F32))
                p_lg = GE(nc.psum_tensor("p_lg", [128, 16], F32))
                p_wt = GE(nc.psum_tensor("p_wt", [16, 128], F32))
                vgw = gp.inc(nc.gpsimd.dma_start(out=wg_sb[:].rearrange("q (a b) -> q a b", a=NJ),
                                                 in_=bass.AP(fWg.tensor, 0, [[16, 128], [128 * 16, NJ], [1, 16]])), 16)
                te.wait(gp, gp.n)
                prev_lg = 0
                prev_wt_ev = 0
                for oc in range(NOC):
                    te.wait(dvS, prev_lg)
                    vml = 0
                    for j in range(NJ):
                        vml = te.do(nc.tensor.matmul(p_lg[:, :], h2sb[:, j * TOWN + oc * 128:j * TOWN + (oc + 1) * 128],
                                                     wg_sb[:, j * 16:(j + 1) * 16],
                                                     start=(j == 0), stop=(j == NJ - 1)))
                    dv.wait(mmS, vml)
                    prev_lg = dv.do(nc.vector.tensor_copy(lg[:], p_lg[:, :]))
                    dv.wait(dvS, prev_lg)
                    vcur = dv.do(nc.vector.tensor_copy(lgt[:], lg[:]))
                    for it in range(4):
                        dv.wait(dvS, vcur)
                        vmx = dv.do(nc.vector.reduce_max(gsm[:, it:it + 1], lgt[:], axis=AX.X))
                        if it < 3:
                            dv.wait(dvS, vmx)
                            vmk = dv.do(nc.vector.tensor_scalar(msk[:], lgt[:], gsm[:, it:it + 1], None, ALU.is_ge))
                            dv.wait(dvS, vmk)
                            vcur = dv.do(nc.vector.scalar_tensor_tensor(lgt[:], msk[:], -1e30, lgt[:], ALU.mult, ALU.add))
                    dv.wait(dvS, dvS.n)
                    vkp = dv.do(nc.vector.tensor_scalar(msk[:], lg[:], gsm[:, 3:4], None, ALU.is_ge))
                    vng = dv.do(nc.vector.tensor_scalar(gsm[:, 4:5], gsm[:, 0:1], -1.0, None, ALU.mult))
                    sc.wait(dvS, vng)
                    vem = sc.do(nc.scalar.activation(lgt[:], lg[:], AF.Exp, bias=gsm[:, 4:5]))
                    dv.wait(scS, vem)
                    vwu = dv.do(nc.vector.tensor_tensor(lgt[:], lgt[:], msk[:], ALU.mult))
                    dv.wait(dvS, vwu)
                    vde = dv.do(nc.vector.reduce_sum(gsm[:, 5:6], lgt[:], axis=AX.X))
                    dv.wait(dvS, vde)
                    vrd = dv.do(nc.vector.reciprocal(gsm[:, 6:7], gsm[:, 5:6]))
                    dv.wait(dvS, vrd)
                    vwt = dv.do(nc.vector.tensor_scalar(lgt[:], lgt[:], gsm[:, 6:7], None, ALU.mult))
                    dv.wait(dvS, vwt)
                    vwtm = dv.do(nc.vector.tensor_copy(wtm[:, oc * 16:(oc + 1) * 16], lgt[:]))
                    te.wait(dvS, vwt)
                    te.wait(dvS, prev_wt_ev)
                    vtw = te.do(nc.tensor.transpose(p_wt[:, :], lgt[:], s_idf32[:]))
                    dv.wait(mmS, vtw)
                    prev_wt_ev = dv.do(nc.vector.tensor_copy(wfm[:, oc * 128:(oc + 1) * 128], p_wt[:, :]))
                if t_gate is not None:
                    nc.sync.wait_ge(dvS.h, dvS.n)
                    sy.inc(nc.sync.dma_start(out=t_gate[:, :], in_=wfm[:, :]), 16)
                    nc.sync.wait_ge(sy.h, sy.n)
            bar()
        gate_ctx.close()

        if "moe" in ph:
            with ExitStack() as mctx:
                ME = mctx.enter_context
                w13 = [ME(nc.sbuf_tensor(f"w13_{p}", [128, 2 * EH], BF16)) for p in range(2)]
                w2b = [ME(nc.sbuf_tensor(f"w2b_{p}", [128, D], BF16)) for p in range(2)]
                h2g = ME(nc.sbuf_tensor("h2g", [128, NJ * CAP], BF16))
                a_t = ME(nc.sbuf_tensor("a_t", [128, NH * CAP], BF16))
                gbf = ME(nc.sbuf_tensor("gbf", [128, NOC * CAP], BF16))
                gpp = ME(nc.sbuf_tensor("gpp", [128, NSC * TOWN], BF16))
                yts = ME(nc.sbuf_tensor("yts", [128, NSC * D], BF16))
                ysb = [ME(nc.sbuf_tensor(f"ysb{p}", [128, CAP], BF16)) for p in range(2)]
                g32 = [ME(nc.sbuf_tensor(f"g32_{p}", [128, CAP], F32)) for p in range(2)]
                silb = ME(nc.sbuf_tensor("silb", [128, 2 * CAP], F32))
                wgb = ME(nc.sbuf_tensor("wgb", [128, CAP], F32))
                wrow = ME(nc.sbuf_tensor("wrow", [1, CAP], F32))
                mskf = ME(nc.sbuf_tensor("mskf", [128, NOC * 16], F32))
                mskb = ME(nc.sbuf_tensor("mskb", [128, NOC * 16], BF16))
                s_onesq = ME(nc.sbuf_tensor("s_onesq", [128, 128], BF16))
                s_iotacap = ME(nc.sbuf_tensor("s_iotacap", [128, CAP], F32))
                ps = [ME(nc.psum_tensor(f"psm{i}", [128, 512], F32)) for i in range(6)]
                ptb = ME(nc.psum_tensor("ptb", [128, 1024], BF16))

                sy.inc(nc.sync.dma_start(out=s_onesq[:], in_=ones_sq[:, :]), 16)
                sy.inc(nc.sync.dma_start(out=s_iotacap[:], in_=iota_cap[:, :]), 16)

                w13_done = [0, 0]
                w2_done = [0, 0]
                w13_ready = {}
                w2_ready = {}

                def issue_w13(e, hc):
                    p = hc % 2
                    if w13_done[p]:
                        nc.gpsimd.wait_ge(mmS.h, w13_done[p])
                    w13s[p].inc(nc.gpsimd.dma_start(out=w13[p][:, 0:EH], in_=w1[e * NH + hc, :, :]), 16)
                    w13_ready[hc] = w13s[p].inc(nc.gpsimd.dma_start(out=w13[p][:, EH:2 * EH], in_=w3[e * NH + hc, :, :]), 16)

                def issue_w2(e, dc):
                    p = dc % 2
                    if w2_done[p]:
                        nc.gpsimd.wait_ge(mmS.h, w2_done[p])
                    w2_ready[dc] = w2s[p].inc(nc.gpsimd.dma_start(out=w2b[p][:, :], in_=w2[e * NJ + dc, :, :]), 16)

                issue_w13(0, 0)
                issue_w13(0, 1)

                vmskl = 0
                for tc in range(NOC):
                    vm = dv.do(nc.vector.tensor_scalar(mskf[:, tc * 16:(tc + 1) * 16], wtm[:, tc * 16:(tc + 1) * 16], 0.0, None, ALU.is_gt))
                    dv.wait(dvS, vm)
                    vmskl = dv.do(nc.vector.tensor_copy(mskb[:, tc * 16:(tc + 1) * 16], mskf[:, tc * 16:(tc + 1) * 16]))
                te.wait(sy, sy.n)
                te.wait(dvS, vmskl)
                vsl = 0
                for tc in range(NOC):
                    for tp in range(tc + 1):
                        vsl = te.do(nc.tensor.matmul(p_c[:, tc * 16:(tc + 1) * 16],
                                                     s_utri[:] if tp == tc else s_onesq[:],
                                                     mskb[:, tp * 16:(tp + 1) * 16],
                                                     start=(tp == 0), stop=(tp == tc)))
                dv.wait(mmS, vsl)
                vst = dv.do(nc.vector.tensor_tensor(slottm[:], p_c[:, 0:128], mskf[:], ALU.mult))
                dv.wait(dvS, vst)
                vst = dv.do(nc.vector.tensor_scalar(slottm[:], slottm[:], 1.0, None, ALU.subtract))
                slottm_ready = vst
                pc_free = vst

                a_evac = [(0, 0), (0, 0)]
                y_evac = [0, 0]
                hg_evac = [0, 0]
                sc_evac = [0, 0]
                gather_last = 0
                ptb_done = 0
                ysb_rd = [0, 0]
                AS = {"vts": {}, "vwg": {}, "vtbl": 0, "pc_free": vst, "gb_free": 0}

                def a_ts(e, tc):
                    if tc == 0:
                        dv.wait(dvS, slottm_ready)
                        if AS["gb_free"]:
                            dv.wait(mmS, AS["gb_free"])
                        if gather_last:
                            dv.wait(mmS, gather_last)
                    if tc >= 2 and AS["vwg"].get(tc - 2):
                        dv.wait(mmS, AS["vwg"][tc - 2])
                    col = slottm[:, tc * 16 + e:tc * 16 + e + 1]
                    AS["vts"][tc] = dv.do(nc.vector.tensor_scalar(g32[tc % 2][:], s_iotacap[:], col, None, ALU.is_equal))
                    AS["vtbl"] = dv.do(nc.vector.tensor_scalar(gbf[:, tc * CAP:(tc + 1) * CAP], s_iotacap[:], col, None, ALU.is_equal))

                def a_wg(e, tc):
                    if tc == 0:
                        te.wait(dvS, AS["pc_free"])
                    te.wait(dvS, AS["vts"][tc])
                    AS["vwg"][tc] = te.do(nc.tensor.matmul(p_c[0:1, 0:CAP], wtm[:, tc * 16 + e:tc * 16 + e + 1], g32[tc % 2][:],
                                                           start=(tc == 0), stop=(tc == NOC - 1)))

                def a_fin(e):
                    dv.wait(mmS, AS["vwg"][NOC - 1])
                    vwr = dv.do(nc.vector.tensor_copy(wrow[:], p_c[0:1, 0:CAP]))
                    te.wait(dvS, vwr)
                    vbc = te.do(nc.tensor.matmul(p_c[:, 0:CAP], s_ones_r128[:, :], wrow[:, :], start=True, stop=True))
                    dv.wait(mmS, vbc)
                    vwgb = dv.do(nc.vector.tensor_copy(wgb[:], p_c[:, 0:CAP]))
                    AS["pc_free"] = vwgb
                    AS["vwg"] = {}
                    return vwgb

                for tc in range(NOC):
                    a_ts(0, tc)
                    if tc >= 1:
                        a_wg(0, tc - 1)
                a_wg(0, NOC - 1)
                vwgb_cur = a_fin(0)

                for e in range(NE):
                    vtbl_cur = AS["vtbl"]

                    te.wait(dvS, vtbl_cur)
                    for dc in range(NJ):
                        g = dc % 2
                        if sc_evac[g]:
                            te.wait(dvS, sc_evac[g])
                        if hg_evac[g]:
                            te.wait(dvS, hg_evac[g])
                        vmm = 0
                        for tc in range(NOC):
                            vmm = te.do(nc.tensor.matmul(ps[4 + g][:, 0:CAP], h2tm[:, tc * D + dc * 128:tc * D + (dc + 1) * 128],
                                                         gbf[:, tc * CAP:(tc + 1) * CAP], start=(tc == 0), stop=(tc == NOC - 1)))
                        dv.wait(mmS, vmm)
                        hg_evac[g] = dv.do(nc.vector.tensor_copy(h2g[:, dc * CAP:(dc + 1) * CAP], ps[4 + g][:, 0:CAP]))
                    gather_last = vmm

                    te.wait(dvS, hg_evac[0])
                    te.wait(dvS, hg_evac[1])
                    dv.wait(dvS, vwgb_cur)
                    gptr = 0
                    for hc in range(NH):
                        p = hc % 2
                        te.wait(w13s[p], w13_ready[hc])
                        dvn, scn = a_evac[p]
                        if dvn:
                            te.wait(dvS, dvn)
                            te.wait(scS, scn)
                        vmm = 0
                        for dc in range(NJ):
                            te.do(nc.tensor.matmul(ps[0 + p][:, 0:CAP], w13[p][:, dc * 128:(dc + 1) * 128],
                                                   h2g[:, dc * CAP:(dc + 1) * CAP], start=(dc == 0), stop=(dc == NJ - 1)))
                            vmm = te.do(nc.tensor.matmul(ps[2 + p][:, 0:CAP], w13[p][:, EH + dc * 128:EH + (dc + 1) * 128],
                                                         h2g[:, dc * CAP:(dc + 1) * CAP], start=(dc == 0), stop=(dc == NJ - 1)))
                        w13_done[p] = vmm
                        sc.wait(mmS, vmm)
                        vsa = sc.do(nc.scalar.activation(silb[:, p * CAP:(p + 1) * CAP], ps[0 + p][:, 0:CAP], AF.Silu))
                        dv.wait(scS, vsa)
                        vt1 = dv.do(nc.vector.tensor_tensor(silb[:, p * CAP:(p + 1) * CAP], silb[:, p * CAP:(p + 1) * CAP], ps[2 + p][:, 0:CAP], ALU.mult))
                        dv.wait(dvS, vt1)
                        vt2 = dv.do(nc.vector.tensor_tensor(a_t[:, hc * CAP:(hc + 1) * CAP], silb[:, p * CAP:(p + 1) * CAP], wgb[:], ALU.mult))
                        a_evac[p] = (vt2, vsa)
                        if hc + 2 < NH:
                            issue_w13(e, hc + 2)
                        elif hc == NH - 2:
                            issue_w2(e, 0)
                        elif hc == NH - 1:
                            issue_w2(e, 1)
                        if hc in (1, 2, 3):
                            te.wait(dvS, ptb_done)
                            vtt = 0
                            for k in range(8):
                                i = gptr * 8 + k
                                tci, sc3 = i // NSC, i % NSC
                                vtt = te.do(nc.tensor.transpose(ptb[:, k * 128:(k + 1) * 128],
                                                                gbf[:, tci * CAP + sc3 * 128:tci * CAP + sc3 * 128 + 128], s_idbf[:]))
                            dv.wait(mmS, vtt)
                            vpe = 0
                            for k in range(8):
                                i = gptr * 8 + k
                                tci, sc3 = i // NSC, i % NSC
                                vpe = dv.do(nc.vector.tensor_copy(gpp[:, sc3 * TOWN + tci * 128:sc3 * TOWN + (tci + 1) * 128], ptb[:, k * 128:(k + 1) * 128]))
                            ptb_done = vpe
                            gptr += 1
                            if gptr == 3:
                                AS["gb_free"] = vtt

                    te.wait(dvS, a_evac[0][0])
                    te.wait(dvS, a_evac[1][0])

                    def emit_ytr(pdc, pg, pye):
                        nonlocal ptb_done
                        te.wait(dvS, pye)
                        te.wait(dvS, ptb_done)
                        vtt = 0
                        for sc3 in range(NSC):
                            vtt = te.do(nc.tensor.transpose(ptb[:, sc3 * 128:(sc3 + 1) * 128], ysb[pg][:, sc3 * 128:(sc3 + 1) * 128], s_idbf[:]))
                        ysb_rd[pg] = vtt
                        dv.wait(mmS, vtt)
                        vpe = 0
                        for sc3 in range(NSC):
                            vpe = dv.do(nc.vector.tensor_copy(yts[:, sc3 * D + pdc * 128:sc3 * D + (pdc + 1) * 128], ptb[:, sc3 * 128:(sc3 + 1) * 128]))
                        ptb_done = vpe

                    pend = None
                    nxt = e + 1 < NE
                    for dc in range(NJ):
                        p = dc % 2
                        g = dc % 2
                        te.wait(w2s[p], w2_ready[dc])
                        if y_evac[g]:
                            te.wait(dvS, y_evac[g])
                        vmm = 0
                        for hc in range(NH):
                            vmm = te.do(nc.tensor.matmul(ps[4 + g][:, 0:CAP], w2b[p][:, hc * 128:(hc + 1) * 128],
                                                         a_t[:, hc * CAP:(hc + 1) * CAP], start=(hc == 0), stop=(hc == NH - 1)))
                        w2_done[p] = vmm
                        dv.wait(mmS, max(vmm, ysb_rd[g]))
                        vye = dv.do(nc.vector.tensor_copy(ysb[g][:], ps[4 + g][:, 0:CAP]))
                        y_evac[g] = vye
                        if dc + 2 < NJ:
                            issue_w2(e, dc + 2)
                        if nxt:
                            if dc == 0:
                                issue_w13(e + 1, 0)
                                issue_w13(e + 1, 1)
                            if 3 <= dc <= 10:
                                a_wg(e + 1, dc - 3)
                            if 1 <= dc <= 8:
                                a_ts(e + 1, dc - 1)
                            if dc == 11:
                                vwgb_nxt = a_fin(e + 1)
                        if pend is not None:
                            emit_ytr(*pend)
                        pend = (dc, g, vye)
                    emit_ytr(*pend)
                    if nxt:
                        vwgb_cur = vwgb_nxt

                    te.wait(dvS, ptb_done)
                    for dc in range(NJ):
                        for tt in range(2):
                            g = (dc * 2 + tt) % 2
                            if sc_evac[g]:
                                te.wait(dvS, sc_evac[g])
                            vmm = 0
                            for sc3 in range(NSC):
                                vmm = te.do(nc.tensor.matmul(ps[4 + g][:, 0:512], yts[:, sc3 * D + dc * 128:sc3 * D + (dc + 1) * 128],
                                                             gpp[:, sc3 * TOWN + tt * 512:sc3 * TOWN + (tt + 1) * 512], start=(sc3 == 0), stop=(sc3 == NSC - 1)))
                            dv.wait(mmS, vmm)
                            sc_evac[g] = dv.do(nc.vector.tensor_add(x2a[:, dc * TOWN + tt * 512:dc * TOWN + tt * 512 + 512],
                                                                    x2a[:, dc * TOWN + tt * 512:dc * TOWN + tt * 512 + 512], ps[4 + g][:, 0:512]))
                if t_moe is not None:
                    nc.sync.wait_ge(dvS.h, dvS.n)
                    sy.inc(nc.sync.dma_start(out=t_moe[:, :], in_=x2a[:, :]), 16)
                    nc.sync.wait_ge(sy.h, sy.n)
            bar()

        if "final" in ph:
            with ExitStack() as fctx:
                FE = fctx.enter_context
                sso = [FE(nc.sbuf_tensor(f"sso{p}", [128, D], F32)) for p in range(2)]
                xoo = [FE(nc.sbuf_tensor(f"xoo{p}", [128, D], F32)) for p in range(2)]
                ftm = FE(nc.sbuf_tensor("ftm", [128, D], F32))
                otm = [FE(nc.sbuf_tensor(f"otm{p}", [128, D], F32)) for p in range(2)]
                p_tf = FE(nc.psum_tensor("p_tf", [128, 512], F32))
                p_tf2 = FE(nc.psum_tensor("p_tf2", [128, 512], F32))
                if "attpart" in ph:
                    nc.sync.wait_ge(ccS.h, ccS.n)
                ftm_done = 0
                out_st = [0, 0]
                for oc in range(NOC):
                    p = oc % 2
                    vld1 = qs[p].inc(nc.sync.dma_start(out=sso[p][:], in_=cc_a_out[oc * 128:(oc + 1) * 128, :]), 16)
                    vld2 = qs[2 + p].inc(nc.sync.dma_start(out=xoo[p][:], in_=x_own[oc * 128:(oc + 1) * 128, :]), 16)
                    vtx = 0
                    for j in range(NJ):
                        ptx = p_tf if (j // 4) % 2 == 0 else p_tf2
                        tp = nc.tensor.transpose(ptx[:, (j % 4) * 128:(j % 4 + 1) * 128], x2a[:, j * TOWN + oc * 128:j * TOWN + (oc + 1) * 128], s_idf32[:])
                        if j % 4 == 3:
                            vt = te.do(tp)
                            dv.wait(mmS, vt)
                            dv.wait(scS, ftm_done)
                            for jj in range(j - 3, j + 1):
                                vtx = dv.do(nc.vector.tensor_copy(ftm[:, jj * 128:(jj + 1) * 128], ptx[:, (jj % 4) * 128:(jj % 4 + 1) * 128]))
                            te.wait(dvS, vtx)
                    sc.wait(dvS, vtx)
                    vsil = sc.do(nc.scalar.activation(ftm[:], ftm[:], AF.Silu))
                    ftm_done = vsil
                    dv.wait(scS, vsil)
                    dv.wait(qs[p], vld1)
                    dv.wait(qs[2 + p], vld2)
                    if out_st[p]:
                        dv.wait(st, st.n)
                    vp1 = dv.do(nc.vector.tensor_tensor(otm[p][:], ftm[:], sso[p][:], ALU.mult))
                    dv.wait(dvS, vp1)
                    vout = dv.do(nc.vector.tensor_add(otm[p][:], otm[p][:], xoo[p][:]))
                    nc.sync.wait_ge(dvS.h, vout)
                    out_st[p] = st.inc(nc.sync.dma_start(out=o_own[oc * 128:(oc + 1) * 128, :], in_=otm[p][:]), 16)
                nc.sync.wait_ge(st.h, st.n)
        else:
            zz = E(nc.sbuf_tensor("zz", [128, D], F32))
            vz = dv.do(nc.vector.memset(zz[:], 0.0))
            nc.sync.wait_ge(dvS.h, vz)
            for oc in range(NOC):
                st.inc(nc.sync.dma_start(out=o_own[oc * 128:(oc + 1) * 128, :], in_=zz[:]), 16)
            nc.sync.wait_ge(st.h, st.n)

    return nc, dbg



def _prep_inputs(inputs, ci):
    g = lambda k: np.asarray(inputs[k])
    x = np.ascontiguousarray(g("L0_attn_hidden_states").reshape(T, D))
    hs = slice(ci * AH * ADK, (ci + 1) * AH * ADK)
    fqs = slice(ci * FQH * FHD, (ci + 1) * FQH * FHD)
    fks = slice(ci * FHD, (ci + 1) * FHD)
    ams = slice(ci * AH * AM, (ci + 1) * AH * AM)
    aWs2T = np.ascontiguousarray(g("L0_attn_Ws2").T)
    ws2aug = np.concatenate([aWs2T[:, ams], g("L0_attn_bs2")[None, ams]], axis=0)
    return {
        "x_full": x,
        "x_own": np.ascontiguousarray(x[ci * TOWN:(ci + 1) * TOWN]),
        "aWq": np.ascontiguousarray(g("L0_attn_Wq").T[:, hs]),
        "aWk": np.ascontiguousarray(g("L0_attn_Wk").T[:, hs]),
        "aWv": np.ascontiguousarray(g("L0_attn_Wv").T[:, hs]),
        "aWg": np.ascontiguousarray(g("L0_attn_Wg").T[:, hs]),
        "aWo": np.ascontiguousarray(g("L0_attn_Wo").T[hs, :]),
        "aWs1": np.ascontiguousarray(g("L0_attn_Ws1").T),
        "aWs2": np.ascontiguousarray(ws2aug),
        "gnw": np.ascontiguousarray(g("L0_attn_g_norm_weight")[None, :]),
        "lnw": np.ascontiguousarray(g("L0_ffn_ln_w")[None, :]),
        "flnw": np.ascontiguousarray(g("L0_ffn_ff_ln_w")[None, :]),
        "fWq": np.ascontiguousarray(g("L0_ffn_wq")[:, fqs]),
        "fWk": np.ascontiguousarray(g("L0_ffn_wk")[:, fks]),
        "fWv": np.ascontiguousarray(g("L0_ffn_wv")[:, fks]),
        "fWo": np.ascontiguousarray(g("L0_ffn_wo")[fqs, :]),
        "fWg": np.ascontiguousarray(g("L0_ffn_wg")),
        "w1": np.ascontiguousarray(g("L0_ffn_w1").reshape(NE, NJ, 128, NH, 128).transpose(0, 3, 2, 1, 4).reshape(NE * NH, 128, NJ * 128)),
        "w3": np.ascontiguousarray(g("L0_ffn_w3").reshape(NE, NJ, 128, NH, 128).transpose(0, 3, 2, 1, 4).reshape(NE * NH, 128, NJ * 128)),
        "w2": np.ascontiguousarray(g("L0_ffn_w2").reshape(NE, NH, 128, NJ, 128).transpose(0, 3, 2, 1, 4).reshape(NE * NJ, 128, NH * 128)),
    }


ALL_PHASES = ("ln", "ffnproj", "flash", "ffnpart", "attproj", "scan", "attpart",
              "x2h2", "gate", "moe", "final")


def kernel(**inputs):
    nc, _ = build_program(phases=ALL_PHASES)
    in_maps = [_prep_inputs(inputs, ci) for ci in range(NCORES)]
    res = run_bass_kernel_spmd(nc, in_maps, core_ids=list(range(NCORES)))
    out = np.concatenate([np.asarray(res.results[ci]["o_own"]) for ci in range(NCORES)], axis=0)
    return out.reshape(1, T, D).astype(np.float32)


# version 10
# speedup vs baseline: 1.3199x; 1.1328x over previous
import os
import sys

for _p in ("/opt/trn_rl_repo",):
    if os.path.isdir(_p) and _p not in sys.path:
        sys.path.insert(0, _p)

import numpy as np
import ml_dtypes
from contextlib import ExitStack

import concourse.bass as bass
import concourse.mybir as mybir
from concourse.bass_utils import run_bass_kernel_spmd

F32 = mybir.dt.float32
BF16 = mybir.dt.bfloat16
AF = mybir.ActivationFunctionType
ALU = mybir.AluOpType
AX = mybir.AxisListType

NCORES = 8
T = 8192
D = 2048
NJ = D // 128
NCH = T // 128
TOWN = T // NCORES
NOC = TOWN // 128

AH = 2
ADK = 128
AM = 64
FQH = 4
FHD = 64
NE = 16
EH = 2048
NH = EH // 128

CAP = 384
NSC = CAP // 128
LN_EPS = 1e-5
CLAMP = 32.0
ATTN_ROPE_BASE = 10000.0
FFN_ROPE_BASE = 150000.0
ASCALE = float(ADK) ** -0.5
FSCALE = float(FHD) ** -0.5


class S:

    def __init__(self, ctx, nc, name):
        self.h = ctx.enter_context(nc.semaphore(name))
        self.n = 0

    def inc(self, instr, k=1):
        instr.then_inc(self.h, k)
        self.n += k
        return self.n


class Eng:

    def __init__(self, eng, sem):
        self.e = eng
        self.s = sem

    def do(self, instr):
        return self.s.inc(instr)

    def wait(self, sem, val):
        if val > 0:
            self.e.wait_ge(sem.h, val)


def _rope_tables_attn(tlen, dk, base):
    inv = (1.0 / base ** (np.arange(0, dk, 2, dtype=np.float32) / dk)).astype(np.float32)
    ang = (np.arange(tlen, dtype=np.float32)[:, None] * inv[None, :]).astype(np.float32)
    return np.cos(ang).astype(np.float32), np.sin(ang).astype(np.float32)


def _rope_tables_ffn(tlen, hd, base):
    half = hd // 2
    inv = (1.0 / base ** (np.arange(half, dtype=np.float32) / half)).astype(np.float32)
    ang = (np.arange(tlen, dtype=np.float32)[:, None] * inv[None, :]).astype(np.float32)
    return np.cos(ang).astype(np.float32), np.sin(ang).astype(np.float32)


def build_program(debug_taps=(), phases=("ln",), fake_cc=False):
    nc = bass.Bass("TRN2", target_bir_lowering=False)
    taps = set(debug_taps)
    ph = set(phases)
    ccu = 16 if fake_cc else 1

    x_full = nc.dram_tensor("x_full", [T, D], F32, kind="ExternalInput").ap()
    x_own = nc.dram_tensor("x_own", [TOWN, D], F32, kind="ExternalInput").ap()
    aWq = nc.dram_tensor("aWq", [D, AH * ADK], F32, kind="ExternalInput").ap()
    aWk = nc.dram_tensor("aWk", [D, AH * ADK], F32, kind="ExternalInput").ap()
    aWv = nc.dram_tensor("aWv", [D, AH * ADK], F32, kind="ExternalInput").ap()
    aWg = nc.dram_tensor("aWg", [D, AH * ADK], F32, kind="ExternalInput").ap()
    aWo = nc.dram_tensor("aWo", [AH * ADK, D], F32, kind="ExternalInput").ap()
    aWs1 = nc.dram_tensor("aWs1", [D, 16], F32, kind="ExternalInput").ap()
    aWs2 = nc.dram_tensor("aWs2", [17, AH * AM], F32, kind="ExternalInput").ap()
    gnw = nc.dram_tensor("gnw", [1, ADK], F32, kind="ExternalInput").ap()
    lnw = nc.dram_tensor("lnw", [1, D], F32, kind="ExternalInput").ap()
    flnw = nc.dram_tensor("flnw", [1, D], F32, kind="ExternalInput").ap()
    fWq = nc.dram_tensor("fWq", [D, FQH * FHD], F32, kind="ExternalInput").ap()
    fWk = nc.dram_tensor("fWk", [D, FHD], F32, kind="ExternalInput").ap()
    fWv = nc.dram_tensor("fWv", [D, FHD], F32, kind="ExternalInput").ap()
    fWo = nc.dram_tensor("fWo", [FQH * FHD, D], F32, kind="ExternalInput").ap()
    fWg = nc.dram_tensor("fWg", [D, NE], F32, kind="ExternalInput").ap()
    w1 = nc.dram_tensor("w1", [NE * NH, 128, NJ * 128], F32, kind="ExternalInput").ap()
    w3 = nc.dram_tensor("w3", [NE * NH, 128, NJ * 128], F32, kind="ExternalInput").ap()
    w2 = nc.dram_tensor("w2", [NE * NJ, 128, NH * 128], F32, kind="ExternalInput").ap()

    o_own = nc.dram_tensor("o_own", [TOWN, D], F32, kind="ExternalOutput").ap()

    ident_bf = nc.inline_tensor(np.eye(128, dtype=ml_dtypes.bfloat16), "ident_bf").ap()
    ident_f32 = nc.inline_tensor(np.eye(128, dtype=np.float32), "ident_f32").ap()
    utri_np = np.triu(np.ones((128, 128), dtype=np.float32)).astype(ml_dtypes.bfloat16)
    utri = nc.inline_tensor(utri_np, "utri").ap()
    ca, sa_ = _rope_tables_attn(T, ADK, ATTN_ROPE_BASE)
    cf, sf_ = _rope_tables_ffn(T, FHD, FFN_ROPE_BASE)
    ca4 = np.tile(ca, (1, 4))
    sa4 = np.tile(sa_, (1, 4))
    cf5 = np.tile(cf, (1, 5))
    sf5 = np.tile(sf_, (1, 5))
    cos_a4 = nc.inline_tensor(ca4.astype(ml_dtypes.bfloat16), "cos_a4").ap()
    sin_a4 = nc.inline_tensor(sa4.astype(ml_dtypes.bfloat16), "sin_a4").ap()
    cos_f5 = nc.inline_tensor(cf5.astype(ml_dtypes.bfloat16), "cos_f5").ap()
    sin_f5 = nc.inline_tensor(sf5.astype(ml_dtypes.bfloat16), "sin_f5").ap()
    dm = np.zeros((128, 4 * 512), dtype=np.float32)
    for di, delta in enumerate((0, 128, 256, 384)):
        dm[:, di * 512:(di + 1) * 512] = (np.arange(128)[:, None] + delta <= np.arange(512)[None, :])
    dmask = nc.inline_tensor(dm.astype(ml_dtypes.bfloat16), "dmask").ap()
    selnp = np.zeros((16, 16 * 128), dtype=np.float32)
    for e in range(16):
        selnp[e, e * 128:(e + 1) * 128] = 1.0
    sel = nc.inline_tensor(selnp, "sel").ap()
    ones_r128_np = np.ones((1, 128), dtype=np.float32)
    ones_r128 = nc.inline_tensor(ones_r128_np, "ones_r128").ap()
    iota_cap = nc.inline_tensor(np.tile(np.arange(CAP, dtype=np.float32)[None, :], (128, 1)), "iota_cap").ap()
    ones_sq = nc.inline_tensor(np.ones((128, 128), dtype=ml_dtypes.bfloat16), "ones_sq").ap()
    ones_r64 = nc.inline_tensor(np.ones((1, 64), dtype=np.float32), "ones_r64").ap()
    ones_c128bf = nc.inline_tensor(np.ones((128, 1), dtype=ml_dtypes.bfloat16), "ones_c128bf").ap()

    d_xnfm = nc.dram_tensor("d_xnfm", [T // 512, 128, NJ * 512], BF16).ap()
    d_rr = nc.dram_tensor("d_rr", [1, T], F32).ap()
    d_xn32 = nc.dram_tensor("d_xn32", [T, D], F32).ap()
    d_att = nc.dram_tensor("d_att", [T, 1152], BF16).ap()
    cc_f_in = nc.dram_tensor("cc_f_in", [T, D], F32).ap()
    cc_f_out = nc.dram_tensor("cc_f_out", [TOWN, D], F32).ap()
    cc_a_in = nc.dram_tensor("cc_a_in", [T, D], F32).ap()
    cc_a_out = nc.dram_tensor("cc_a_out", [TOWN, D], F32).ap()

    dbg = {}

    def tap(name, shape, dtype=F32):
        if name in taps:
            dbg[name] = nc.dram_tensor("dbg_" + name, shape, dtype, kind="ExternalOutput").ap()
            return dbg[name]
        return None

    t_xn = tap("xn", [NJ, 128, T], BF16)
    t_rr = tap("rr", [1, T])
    t_ffnq = tap("ffnq", [4, 64, T], BF16)
    t_ffnk = tap("ffnk", [64, T], BF16)
    t_ffnv = tap("ffnv", [128, NCH * 65], BF16)
    t_mhat = tap("mhat", [1, 8])
    t_flash = tap("flash", [2, 128, T], BF16)
    t_ffnpart = tap("ffnpart", [T, D])
    t_att = tap("att", [T, 1152], BF16)
    t_og = tap("og", [2, 128, T], BF16)
    t_attpart = tap("attpart", [T, D])
    t_x2 = tap("x2", [128, NJ * TOWN])
    t_h2 = tap("h2", [128, NJ * TOWN], BF16)
    t_gate = tap("gate", [16, TOWN])
    t_moe = tap("moe", [128, NJ * TOWN])

    with ExitStack() as ctx:
        E = ctx.enter_context

        s_idbf = E(nc.sbuf_tensor("s_idbf", [128, 128], BF16))
        s_idf32 = E(nc.sbuf_tensor("s_idf32", [128, 128], F32))
        s_utri = E(nc.sbuf_tensor("s_utri", [128, 128], BF16))
        s_ones_r128 = E(nc.sbuf_tensor("s_ones_r128", [1, 128], F32))
        s_ones_r64 = E(nc.sbuf_tensor("s_ones_r64", [1, 64], F32))
        s_ones_c128bf = E(nc.sbuf_tensor("s_ones_c128bf", [128, 1], BF16))
        s_lnw = E(nc.sbuf_tensor("s_lnw", [128, NJ], F32))
        s_flnw = E(nc.sbuf_tensor("s_flnw", [128, NJ], F32))
        s_gnwb = E(nc.sbuf_tensor("s_gnwb", [128, 128], F32))
        s_eps = E(nc.sbuf_tensor("s_eps", [128, 1], F32))
        s_dmask = E(nc.sbuf_tensor("s_dmask", [128, 4 * 512], BF16))
        s_sel = E(nc.sbuf_tensor("s_sel", [16, 16 * 128], F32))

        sy = S(ctx, nc, "sy")
        st = S(ctx, nc, "st")
        gp = S(ctx, nc, "gp")
        gq = S(ctx, nc, "gq")
        mmS = S(ctx, nc, "mm")
        dvS = S(ctx, nc, "dv")
        scS = S(ctx, nc, "sc")
        plS = S(ctx, nc, "pl")
        ccS = S(ctx, nc, "cc")
        w13s = [S(ctx, nc, f"w13s{p}") for p in range(3)]
        w2s = [S(ctx, nc, f"w2s{p}") for p in range(3)]
        qs = [S(ctx, nc, f"qs{k}") for k in range(12)]

        te = Eng(nc.tensor, mmS)
        dv = Eng(nc.vector, dvS)
        sc = Eng(nc.scalar, scS)
        pl = Eng(nc.gpsimd, plS)

        def bar():
            nc.all_engine_barrier()

        dv.do(nc.vector.memset(s_eps[:], LN_EPS))
        sy.inc(nc.sync.dma_start(out=s_idbf[:], in_=ident_bf[:, :]), 16)
        sy.inc(nc.sync.dma_start(out=s_idf32[:], in_=ident_f32[:, :]), 16)
        sy.inc(nc.sync.dma_start(out=s_utri[:], in_=utri[:, :]), 16)
        sy.inc(nc.sync.dma_start(out=s_ones_r128[:], in_=ones_r128[:, :]), 16)
        sy.inc(nc.sync.dma_start(out=s_ones_r64[:], in_=ones_r64[:, :]), 16)
        sy.inc(nc.sync.dma_start(out=s_ones_c128bf[:], in_=ones_c128bf[:, :]), 16)
        sy.inc(nc.sync.dma_start(out=s_dmask[:], in_=dmask[:, :]), 16)
        sy.inc(nc.sync.dma_start(out=s_sel[:], in_=sel[:, :]), 16)
        lnw_src = bass.AP(lnw.tensor, 0, [[1, 128], [128, NJ]])
        flnw_src = bass.AP(flnw.tensor, 0, [[1, 128], [128, NJ]])
        sy.inc(nc.sync.dma_start(out=s_lnw[:], in_=lnw_src, allow_slow_non_contiguous=True), 16)
        sy.inc(nc.sync.dma_start(out=s_flnw[:], in_=flnw_src, allow_slow_non_contiguous=True), 16)
        gnw_row = E(nc.sbuf_tensor("gnw_row", [1, ADK], F32))
        sy.inc(nc.sync.dma_start(out=gnw_row[:], in_=gnw[:, :]), 16)
        gate_ctx = ExitStack()
        te.wait(sy, sy.n)
        p_c = E(nc.psum_tensor("p_c", [128, 512], F32))
        vb = te.do(nc.tensor.matmul(p_c[:, 0:128], s_ones_r128[:, :], gnw_row[:, :], start=True, stop=True))
        dv.wait(mmS, vb)
        dv.do(nc.vector.tensor_copy(s_gnwb[:], p_c[:, 0:128]))
        bar()

        if "ln" in ph:
          with ExitStack() as lctx:
            LE = lctx.enter_context
            sx = [LE(nc.sbuf_tensor(f"sx{p}", [128, D], F32)) for p in range(2)]
            xn32 = [LE(nc.sbuf_tensor(f"xn32_{p}", [128, D], F32)) for p in range(2)]
            xnbf = [LE(nc.sbuf_tensor(f"xnbf_{p}", [128, D], BF16)) for p in range(2)]
            xnfm_st = [LE(nc.sbuf_tensor(f"xnfm_st{p}", [128, NJ * 512], BF16)) for p in range(2)]
            bnst = [LE(nc.sbuf_tensor(f"bnst{p}", [128, 24], F32)) for p in range(2)]
            bnag = [LE(nc.sbuf_tensor(f"bnag{p}", [128, 2], F32)) for p in range(2)]
            ln_sm = [LE(nc.sbuf_tensor(f"ln_sm{p}", [128, 4], F32)) for p in range(2)]
            p_trb = LE(nc.psum_tensor("p_trb", [128, 1024], BF16))
            p_trb2 = LE(nc.psum_tensor("p_trb2", [128, 1024], BF16))
            p_rr = LE(nc.psum_tensor("p_rr", [1, 128], F32))
            rr_sb = [LE(nc.sbuf_tensor(f"rr_sb{p}", [1, 128], F32)) for p in range(2)]

            store_done = [0, 0]
            rr_evac_prev = 0
            rr_store_prev = [0, 0]
            ldq = [qs[0], qs[1]]
            stq = [qs[2], qs[3]]
            rrq = [qs[7], qs[10]]
            xn32_done = [0, 0]
            vs_rstd_c = [0, 0]
            vx_done = [0, 0]

            def ln_A(c):
                p = c % 2
                if vx_done[p]:
                    nc.sync.wait_ge(dvS.h, vx_done[p])
                vld = ldq[p].inc(nc.sync.dma_start(out=sx[p][:], in_=x_full[c * 128:(c + 1) * 128, :]), 16)
                dv.wait(ldq[p], vld)
                vbn = 0
                for k in range(4):
                    vbn = dv.do(nc.vector.bn_stats(bnst[p][:, k * 6:(k + 1) * 6], sx[p][:, k * 512:(k + 1) * 512]))
                dv.wait(dvS, vbn)
                vagg = dv.do(nc.vector.bn_aggr(bnag[p][:], bnst[p][:]))
                sc.wait(dvS, vagg)
                vs = sc.do(nc.scalar.activation(ln_sm[p][:, 1:2], bnag[p][:, 1:2], AF.Ln, bias=s_eps[:, 0:1]))
                sc.wait(scS, vs)
                vs_rstd_c[p] = sc.do(nc.scalar.activation(ln_sm[p][:, 0:1], ln_sm[p][:, 1:2], AF.Exp, scale=-0.5))

            def ln_B(c):
                nonlocal rr_evac_prev
                p = c % 2
                if xn32_done[p]:
                    dv.wait(qs[8 + p], xn32_done[p])
                dv.wait(scS, vs_rstd_c[p])
                vx = dv.do(nc.vector.tensor_scalar(xn32[p][:], sx[p][:], bnag[p][:, 0:1], ln_sm[p][:, 0:1], ALU.subtract, ALU.mult))
                vx_done[p] = vx
                dv.wait(dvS, vx)
                vxb = dv.do(nc.vector.tensor_copy(xnbf[p][:], xn32[p][:]))
                v1 = dv.do(nc.vector.tensor_tensor(ln_sm[p][:, 3:4], ln_sm[p][:, 0:1], ln_sm[p][:, 0:1], ALU.mult))
                dv.wait(dvS, v1)
                v2 = dv.do(nc.vector.tensor_tensor(ln_sm[p][:, 1:2], bnag[p][:, 1:2], ln_sm[p][:, 3:4], ALU.mult))
                sc.wait(dvS, v2)
                vs = sc.do(nc.scalar.activation(ln_sm[p][:, 2:3], ln_sm[p][:, 1:2], AF.Ln, bias=s_eps[:, 0:1]))
                sc.wait(scS, vs)
                vs_rr = sc.do(nc.scalar.activation(ln_sm[p][:, 2:3], ln_sm[p][:, 2:3], AF.Exp, scale=-0.5))
                te.wait(scS, vs_rr)
                te.wait(dvS, rr_evac_prev)
                if rr_store_prev[p]:
                    dv.wait(rrq[p], rr_store_prev[p])
                vtm = te.do(nc.tensor.transpose(p_rr[:, :], ln_sm[p][:, 2:3], s_idf32[:]))
                dv.wait(mmS, vtm)
                rr_evac_prev = dv.do(nc.vector.tensor_copy(rr_sb[p][:, :], p_rr[:, :]))
                sc.wait(dvS, rr_evac_prev)
                rr_store_prev[p] = rrq[p].inc(nc.scalar.dma_start(out=d_rr[:, c * 128:(c + 1) * 128], in_=rr_sb[p][:, :]), 16)
                te.wait(dvS, vxb)
                tp2 = (c // 4) % 2
                csub = c % 4
                vcp = 0
                for j in range(NJ):
                    pt = p_trb if j < 8 else p_trb2
                    tp = nc.tensor.transpose(pt[:, (j % 8) * 128:(j % 8 + 1) * 128], xnbf[p][:, j * 128:(j + 1) * 128], s_idbf[:])
                    if j % 8 == 7:
                        vt = te.do(tp)
                        dv.wait(mmS, vt)
                        if csub == 0 and j == 7:
                            nc.vector.wait_ge(stq[tp2].h, store_done[tp2])
                        for jj in range(j - 7, j + 1):
                            vcp = dv.do(nc.vector.tensor_copy(xnfm_st[tp2][:, jj * 512 + csub * 128:jj * 512 + (csub + 1) * 128], pt[:, (jj % 8) * 128:(jj % 8 + 1) * 128]))
                te.wait(dvS, vcp)
                sc.wait(dvS, vcp)
                if csub == 3:
                    stq[tp2].inc(nc.scalar.dma_start(out=d_xnfm[c // 4, :, :], in_=xnfm_st[tp2][:]), 16)
                vstore = qs[8 + p].inc(nc.scalar.dma_start(out=d_xn32[c * 128:(c + 1) * 128, :], in_=xn32[p][:]), 16)
                if t_xn is not None and csub == 3:
                    stq[tp2].inc(nc.scalar.dma_start(out=bass.AP(t_xn.tensor, (c // 4) * 512, [[T, 128], [128 * T, NJ], [1, 512]]),
                                                     in_=xnfm_st[tp2][:].rearrange("q (a b) -> q a b", a=NJ)), 16)
                if csub == 3:
                    store_done[tp2] = stq[tp2].n
                xn32_done[p] = vstore

            for c in range(NCH):
                ln_A(c)
                if c >= 1:
                    ln_B(c - 1)
            ln_B(NCH - 1)

            nc.sync.wait_ge(sy.h, sy.n)
            nc.sync.wait_ge(stq[0].h, stq[0].n)
            nc.sync.wait_ge(stq[1].h, stq[1].n)
            nc.sync.wait_ge(qs[7].h, qs[7].n)
            nc.sync.wait_ge(qs[10].h, qs[10].n)
            nc.sync.wait_ge(qs[8].h, qs[8].n)
            nc.sync.wait_ge(qs[9].h, qs[9].n)
            bar()

        ffn_ctx = ExitStack()
        if "ffnproj" in ph or "flash" in ph:
            qres = ffn_ctx.enter_context(nc.sbuf_tensor("qres", [128, 4 * T], BF16))
            kres = ffn_ctx.enter_context(nc.sbuf_tensor("kres", [128, T], BF16))
            vres = ffn_ctx.enter_context(nc.sbuf_tensor("vres", [128, NCH * 65], BF16))
            mbias = ffn_ctx.enter_context(nc.sbuf_tensor("mbias", [128, 4], F32))
            ofm = ffn_ctx.enter_context(nc.sbuf_tensor("ofm", [128, 2 * T], BF16))
        if "ffnproj" in ph:
            with ExitStack() as fctx:
                FE = fctx.enter_context
                wqkv = FE(nc.sbuf_tensor("wqkv", [128, NJ * 384], BF16))
                tmpw = FE(nc.sbuf_tensor("tmpw", [128, 384], F32))
                xtl = [FE(nc.sbuf_tensor(f"xtl{p}", [128, NJ * 512], BF16)) for p in range(2)]
                stage = FE(nc.sbuf_tensor("stage", [128, 4 * 384], BF16))
                tmp1 = FE(nc.sbuf_tensor("tmp1", [128, 160], F32))
                tmp2 = FE(nc.sbuf_tensor("tmp2", [128, 160], F32))
                rrl = FE(nc.sbuf_tensor("rrl", [1, 512], F32))
                sqb = FE(nc.sbuf_tensor("sqb", [128, 320], F32))
                redb = FE(nc.sbuf_tensor("redb", [128, 5], F32))
                nmax = FE(nc.sbuf_tensor("nmax", [128, 5], F32))
                cosl = FE(nc.sbuf_tensor("cosl", [128, 4 * 160], BF16))
                sinl = FE(nc.sbuf_tensor("sinl", [128, 4 * 160], BF16))
                nrow = FE(nc.sbuf_tensor("nrow", [1, 5], F32))
                mrow = FE(nc.sbuf_tensor("mrow", [1, 4], F32))
                p_qkv = [FE(nc.psum_tensor(f"p_qkv{p}", [128, 384], F32)) for p in range(2)]
                p_rb = FE(nc.psum_tensor("p_rb", [128, 512], F32))
                p_tr = FE(nc.psum_tensor("p_tr", [128, 1024], BF16))

                v_ones = dv.do(nc.vector.memset(bass.AP(vres, 64, [[65 * NCH, 128], [65, NCH], [1, 1]]), 1.0))
                vnm = dv.do(nc.vector.memset(nmax[:], 0.0))
                dv.do(nc.vector.memset(qres[64:128, :], 0.0))
                dv.do(nc.vector.memset(kres[64:128, :], 0.0))

                vcast = 0
                for j in range(NJ):
                    if vcast:
                        nc.sync.wait_ge(dvS.h, vcast)
                    sy.inc(nc.sync.dma_start(out=tmpw[:, 0:256], in_=fWq[j * 128:(j + 1) * 128, :]), 16)
                    sy.inc(nc.sync.dma_start(out=tmpw[:, 256:320], in_=fWk[j * 128:(j + 1) * 128, :]), 16)
                    vwl = sy.inc(nc.sync.dma_start(out=tmpw[:, 320:384], in_=fWv[j * 128:(j + 1) * 128, :]), 16)
                    dv.wait(sy, vwl)
                    vq = dv.do(nc.vector.tensor_scalar(tmpw[:, 0:256], tmpw[:, 0:256], s_lnw[:, j:j + 1], float(FSCALE), ALU.mult, ALU.mult))
                    dv.wait(dvS, vq)
                    vkv = dv.do(nc.vector.tensor_scalar(tmpw[:, 256:384], tmpw[:, 256:384], s_lnw[:, j:j + 1], None, ALU.mult))
                    dv.wait(dvS, vkv)
                    vcast = dv.do(nc.vector.tensor_copy(wqkv[:, j * 384:(j + 1) * 384], tmpw[:]))
                    dv.wait(dvS, vcast)
                wq_ready = dvS.n

                ld_done = [0, 0]
                mm_qkv_done = [0, 0]
                rope_done = 0
                tr_evac = 0
                prev_tile_mm = 0
                prev_rb_mm = 0
                slot_mm = [0, 0]
                for tt in range(T // 512):
                    p2 = tt % 2
                    if slot_mm[p2]:
                        nc.sync.wait_ge(mmS.h, slot_mm[p2])
                    vld = qs[4 + p2].inc(nc.sync.dma_start(out=xtl[p2][:], in_=d_xnfm[tt, :, :]), 16)
                    nc.sync.wait_ge(dvS.h, rope_done)
                    nc.sync.wait_ge(mmS.h, prev_rb_mm)
                    qs[6].inc(nc.sync.dma_start(out=rrl[:, :], in_=d_rr[:, tt * 512:(tt + 1) * 512]), 16)
                    csrc = bass.AP(cos_f5.tensor, tt * 512 * 160, [[160, 128], [128 * 160, 4], [1, 160]])
                    ssrc = bass.AP(sin_f5.tensor, tt * 512 * 160, [[160, 128], [128 * 160, 4], [1, 160]])
                    qs[6].inc(nc.sync.dma_start(out=cosl[:].rearrange("q (a b) -> q a b", a=4), in_=csrc), 16)
                    vcs = qs[6].inc(nc.sync.dma_start(out=sinl[:].rearrange("q (a b) -> q a b", a=4), in_=ssrc), 16)
                    te.wait(dvS, rope_done)
                    te.wait(qs[6], qs[6].n)
                    vrb = te.do(nc.tensor.matmul(p_rb[:, 0:512], s_ones_r128[:, :], rrl[:, :], start=True, stop=True))
                    prev_rb_mm = vrb
                    dv.wait(qs[4 + p2], vld)
                    dv.wait(mmS, vrb)
                    dv.wait(dvS, wq_ready)
                    vxr = 0
                    for j in range(NJ):
                        vxr = dv.do(nc.vector.tensor_tensor(xtl[p2][:, j * 512:(j + 1) * 512], xtl[p2][:, j * 512:(j + 1) * 512], p_rb[:, 0:512], ALU.mult))
                    ld_done[p2] = vxr
                    slot_mm[p2] = None
                    rope_done_tile = 0
                    for sub in range(4):
                        pq = sub % 2
                        te.wait(dvS, vxr)
                        if mm_qkv_done[pq]:
                            te.wait(dvS, mm_qkv_done[pq])
                        vmm = 0
                        for j in range(NJ):
                            vmm = te.do(nc.tensor.matmul(p_qkv[pq][:, 0:384],
                                                         xtl[p2][:, j * 512 + sub * 128:j * 512 + (sub + 1) * 128],
                                                         wqkv[:, j * 384:(j + 1) * 384],
                                                         start=(j == 0), stop=(j == NJ - 1)))
                        dv.wait(mmS, vmm)
                        dv.wait(qs[6], vcs)
                        dv.wait(dvS, dvS.n)
                        h1 = bass.AP(p_qkv[pq], 0, [[384, 128], [64, 5], [1, 32]])
                        h2ap = bass.AP(p_qkv[pq], 32, [[384, 128], [64, 5], [1, 32]])
                        co = cosl[:, sub * 160:(sub + 1) * 160].rearrange("q (a b) -> q a b", a=5)
                        si = sinl[:, sub * 160:(sub + 1) * 160].rearrange("q (a b) -> q a b", a=5)
                        tm1 = tmp1[:].rearrange("q (a b) -> q a b", a=5)
                        tm2 = tmp2[:].rearrange("q (a b) -> q a b", a=5)
                        va = dv.do(nc.vector.tensor_tensor(tm1, h1, co, ALU.mult))
                        vb2 = dv.do(nc.vector.tensor_tensor(tm2, h2ap, si, ALU.mult))
                        dv.wait(dvS, vb2)
                        sg_h1 = bass.AP(stage, sub * 384, [[4 * 384, 128], [64, 5], [1, 32]])
                        sg_h2 = bass.AP(stage, sub * 384 + 32, [[4 * 384, 128], [64, 5], [1, 32]])
                        v_1 = dv.do(nc.vector.tensor_tensor(sg_h1, tm1, tm2, ALU.subtract))
                        dv.wait(dvS, v_1)
                        vc2 = dv.do(nc.vector.tensor_tensor(tm1, h2ap, co, ALU.mult))
                        vd2 = dv.do(nc.vector.tensor_tensor(tm2, h1, si, ALU.mult))
                        dv.wait(dvS, vd2)
                        v_2 = dv.do(nc.vector.tensor_tensor(sg_h2, tm1, tm2, ALU.add))
                        vv = dv.do(nc.vector.tensor_copy(vres[:, (tt * 4 + sub) * 65:(tt * 4 + sub) * 65 + 64], p_qkv[pq][:, 320:384]))
                        dv.wait(dvS, v_2)
                        vsq = dv.do(nc.vector.tensor_tensor(sqb[:], stage[:, sub * 384:sub * 384 + 320], stage[:, sub * 384:sub * 384 + 320], ALU.mult))
                        dv.wait(dvS, vsq)
                        vrd = dv.do(nc.vector.reduce_sum(redb[:], sqb[:].rearrange("q (a b) -> q a b", a=5), axis=AX.X))
                        dv.wait(dvS, vrd)
                        vmx = dv.do(nc.vector.tensor_max(nmax[:], nmax[:], redb[:]))
                        mm_qkv_done[pq] = vmx
                        rope_done_tile = max(rope_done_tile, vv, v_2)
                        te.wait(dvS, v_2)
                        te.wait(dvS, tr_evac)
                        t0_ = te.do(nc.tensor.transpose(p_tr[:, 0:128], stage[:, sub * 384:sub * 384 + 128], s_idbf[:]))
                        t1_ = te.do(nc.tensor.transpose(p_tr[:, 128:256], stage[:, sub * 384 + 128:sub * 384 + 256], s_idbf[:]))
                        t2_ = te.do(nc.tensor.transpose(p_tr[0:64, 256:384], stage[:, sub * 384 + 256:sub * 384 + 320], s_idbf[:]))
                        dv.wait(mmS, t2_)
                        tcol = tt * 512 + sub * 128
                        dv.do(nc.vector.tensor_copy(qres[0:64, 0 * T + tcol:0 * T + tcol + 128], p_tr[0:64, 0:128]))
                        dv.do(nc.vector.tensor_copy(qres[0:64, 1 * T + tcol:1 * T + tcol + 128], p_tr[64:128, 0:128]))
                        dv.do(nc.vector.tensor_copy(qres[0:64, 2 * T + tcol:2 * T + tcol + 128], p_tr[0:64, 128:256]))
                        dv.do(nc.vector.tensor_copy(qres[0:64, 3 * T + tcol:3 * T + tcol + 128], p_tr[64:128, 128:256]))
                        tr_evac = dv.do(nc.vector.tensor_copy(kres[0:64, tcol:tcol + 128], p_tr[0:64, 256:384]))
                    rope_done = max(rope_done_tile, tr_evac)
                    prev_tile_mm = vmm
                    slot_mm[p2] = vmm
                pl.wait(dvS, dvS.n)
                vgr = pl.do(nc.gpsimd.tensor_reduce(nrow[:], nmax[:], axis=AX.C, op=ALU.max))
                sc.wait(plS, vgr)
                vsr = sc.do(nc.scalar.activation(nrow[:], nrow[:], AF.Ln))
                sc.wait(scS, vsr)
                vsr = sc.do(nc.scalar.activation(nrow[:], nrow[:], AF.Exp, scale=0.5))
                dv.wait(scS, vsr)
                vmr = dv.do(nc.vector.tensor_scalar(mrow[:], nrow[:, 0:4], nrow[:, 4:5], -1.0, ALU.mult, ALU.mult))
                te.wait(dvS, vmr)
                for h in range(FQH):
                    vmb = te.do(nc.tensor.matmul(p_c[:, 0:1], s_ones_r128[:, :], mrow[:, h:h + 1], start=True, stop=True))
                    dv.wait(mmS, vmb)
                    vme = dv.do(nc.vector.tensor_copy(mbias[:, h:h + 1], p_c[:, 0:1]))
                    te.wait(dvS, vme)
                if t_ffnq is not None:
                    nc.sync.wait_ge(dvS.h, dvS.n)
                    for r in range(4):
                        sy.inc(nc.sync.dma_start(out=bass.AP(t_ffnq.tensor, r * 64 * T, [[T, 64], [1, T]]), in_=qres[0:64, r * T:(r + 1) * T]), 16)
                if t_ffnk is not None:
                    nc.sync.wait_ge(dvS.h, dvS.n)
                    sy.inc(nc.sync.dma_start(out=t_ffnk[:, :], in_=kres[0:64, :]), 16)
                if t_ffnv is not None:
                    nc.sync.wait_ge(dvS.h, dvS.n)
                    sy.inc(nc.sync.dma_start(out=t_ffnv[:, :], in_=vres[:, :]), 16)
                if t_mhat is not None:
                    nc.sync.wait_ge(dvS.h, dvS.n)
                    sy.inc(nc.sync.dma_start(out=t_mhat[:, 0:4], in_=mrow[:, :]), 16)
                    sy.inc(nc.sync.dma_start(out=t_mhat[:, 4:5], in_=nrow[:, 4:5]), 16)
                nc.sync.wait_ge(sy.h, sy.n)
            bar()

        if "flash" in ph:
            with ExitStack() as fctx:
                FE = fctx.enter_context
                psf = [FE(nc.psum_tensor(f"psf{p}", [128, 1024], F32)) for p in range(2)]
                p_o = [FE(nc.psum_tensor(f"p_o{p}", [65, 512], F32)) for p in range(2)]
                p_bc = FE(nc.psum_tensor("p_bc", [64, 512], F32))
                esb = FE(nc.sbuf_tensor("esb", [128, 2048], BF16))
                rl = [FE(nc.sbuf_tensor(f"rl{p}", [1, 512], F32)) for p in range(2)]
                rb = [FE(nc.sbuf_tensor(f"rb{p}", [64, 512], F32)) for p in range(2)]
                exp_done = [0, 0]
                av_done = [0, 0]
                fin_done = [0, 0]
                blocks = [(h, qb) for h in range(FQH) for qb in range(16)]

                def emit_stile(h, qb, b2):
                    par = b2 % 2
                    te.wait(scS, exp_done[par])
                    vmm = 0
                    for j2 in range(2):
                        kt = b2 * 2 + j2
                        vmm = te.do(nc.tensor.matmul(
                            psf[par][:, j2 * 512:(j2 + 1) * 512],
                            kres[:, kt * 128:(kt + 1) * 128],
                            qres[:, h * T + qb * 512:h * T + (qb + 1) * 512],
                            start=True, stop=True))
                    return vmm

                p_bc2 = [p_bc, None]
                obuf = [FE(nc.sbuf_tensor(f"obuf{p}", [64, 512], BF16)) for p in range(2)]
                pending_fin = [None]

                def emit_finalize():
                    if pending_fin[0] is None:
                        return
                    (fh, fqb, fopar, fav) = pending_fin[0]
                    pending_fin[0] = None
                    dv.wait(mmS, fav)
                    vrl = dv.do(nc.vector.reciprocal(rl[fopar][:, :], p_o[fopar][64:65, :]))
                    te.wait(dvS, vrl)
                    te.wait(dvS, fin_done[fopar])
                    bca = p_bc[:, :] if fopar == 0 else p_c[0:64, 0:512]
                    vbc = te.do(nc.tensor.matmul(bca, s_ones_r64[:, :], rl[fopar][:, :], start=True, stop=True))
                    dv.wait(mmS, vbc)
                    vrb = dv.do(nc.vector.tensor_copy(rb[fopar][:, :], bca))
                    dv.wait(dvS, vrb)
                    vtt = dv.do(nc.vector.tensor_tensor(obuf[fopar][:, :], p_o[fopar][0:64, :], rb[fopar][:, :], ALU.mult))
                    dv.wait(dvS, vtt)
                    fin_done[fopar] = dv.do(nc.vector.tensor_copy(
                        ofm[(fh % 2) * 64:(fh % 2) * 64 + 64, (fh // 2) * T + fqb * 512:(fh // 2) * T + (fqb + 1) * 512],
                        obuf[fopar][:, :]))

                for bi, (h, qb) in enumerate(blocks):
                    opar = bi % 2
                    nkt = 4 * (qb + 1)
                    nb = nkt // 2
                    st_mm = emit_stile(h, qb, 0)
                    stile_mm = {0: st_mm}
                    te.wait(dvS, fin_done[opar])
                    for b2 in range(nb):
                        par = b2 % 2
                        if b2 + 1 < nb:
                            stile_mm[b2 + 1] = emit_stile(h, qb, b2 + 1)
                        sc.wait(mmS, stile_mm[b2])
                        sc.wait(mmS, av_done[par])
                        vex = sc.do(nc.scalar.activation(esb[:, par * 1024:(par + 1) * 1024], psf[par][:, 0:1024],
                                                         AF.Exp, bias=mbias[:, h:h + 1]))
                        exp_done[par] = vex
                        vmask = 0
                        for j2 in range(2):
                            kt = b2 * 2 + j2
                            di = kt - 4 * qb
                            if di >= 0:
                                dv.wait(scS, vex)
                                eslice = esb[:, par * 1024 + j2 * 512:par * 1024 + (j2 + 1) * 512]
                                vmask = dv.do(nc.vector.tensor_tensor(eslice, eslice, s_dmask[:, di * 512:(di + 1) * 512], ALU.mult))
                        te.wait(scS, vex)
                        if vmask:
                            te.wait(dvS, vmask)
                        vav = 0
                        for j2 in range(2):
                            kt = b2 * 2 + j2
                            vav = te.do(nc.tensor.matmul(
                                p_o[opar][:, :],
                                vres[:, kt * 65:(kt + 1) * 65],
                                esb[:, par * 1024 + j2 * 512:par * 1024 + (j2 + 1) * 512],
                                start=(kt == 0), stop=(kt == nkt - 1)))
                        av_done[par] = vav
                        if b2 == 0:
                            emit_finalize()
                    pending_fin[0] = (h, qb, opar, av_done[par])
                emit_finalize()
                if t_flash is not None:
                    nc.sync.wait_ge(dvS.h, dvS.n)
                    for r in range(2):
                        sy.inc(nc.sync.dma_start(out=bass.AP(t_flash.tensor, r * 128 * T, [[T, 128], [1, T]]), in_=ofm[:, r * T:(r + 1) * T]), 16)
                    nc.sync.wait_ge(sy.h, sy.n)
            bar()

        if "ffnpart" in ph:
            with ExitStack() as fctx:
                FE = fctx.enter_context
                wo_sb = FE(nc.sbuf_tensor("wo_sb", [128, 2 * D], BF16))
                pstg = [FE(nc.sbuf_tensor(f"pstg{p}", [128, D], F32)) for p in range(2)]
                p_pp = [FE(nc.psum_tensor(f"p_pp{p}", [128, 512], F32)) for p in range(2)]
                for r in range(2):
                    gp.inc(nc.gpsimd.dma_start(out=wo_sb[:, r * D:(r + 1) * D], in_=fWo[r * 128:(r + 1) * 128, :]), 16)
                te.wait(gp, gp.n)
                part_st = [0, 0]
                evac_done = [0, 0]
                for tc in range(NCH):
                    pp = tc % 2
                    if part_st[pp]:
                        nc.vector.wait_ge(qs[pp].h, part_st[pp])
                    vev = 0
                    for dsl in range(4):
                        pb2 = dsl % 2
                        te.wait(dvS, evac_done[pb2])
                        vmm = 0
                        for r in range(2):
                            vmm = te.do(nc.tensor.matmul(
                                p_pp[pb2][:, :],
                                ofm[:, r * T + tc * 128:r * T + (tc + 1) * 128],
                                wo_sb[:, r * D + dsl * 512:r * D + (dsl + 1) * 512],
                                start=(r == 0), stop=(r == 1)))
                        dv.wait(mmS, vmm)
                        vev = dv.do(nc.vector.tensor_copy(pstg[pp][:, dsl * 512:(dsl + 1) * 512], p_pp[pb2][:, :]))
                        evac_done[pb2] = vev
                    nc.sync.wait_ge(dvS.h, vev)
                    part_st[pp] = qs[pp].inc(nc.sync.dma_start(out=cc_f_in[tc * 128:(tc + 1) * 128, :], in_=pstg[pp][:]), 16)
                if t_ffnpart is not None:
                    nc.sync.wait_ge(qs[0].h, qs[0].n)
                    nc.sync.wait_ge(qs[1].h, qs[1].n)
                    for tc in range(NCH):
                        sy.inc(nc.sync.dma_start(out=t_ffnpart[tc * 128:(tc + 1) * 128, :], in_=cc_f_in[tc * 128:(tc + 1) * 128, :]), 16)
                    nc.sync.wait_ge(sy.h, sy.n)
                nc.gpsimd.wait_ge(qs[0].h, qs[0].n)
                nc.gpsimd.wait_ge(qs[1].h, qs[1].n)
                if fake_cc:
                    ccS.inc(nc.gpsimd.dma_start(out=cc_f_out[:, :], in_=cc_f_in[0:TOWN, :]), 16)
                else:
                    ccS.inc(nc.gpsimd.collective_compute(
                        "ReduceScatter", ALU.add,
                        replica_groups=[list(range(NCORES))],
                        ins=[cc_f_in.tensor.ap().opt()],
                        outs=[cc_f_out.tensor.ap().opt()]), 1)
            bar()
        ffn_ctx.close()

        if "attproj" in ph:
            with ExitStack() as actx:
                AE = actx.enter_context
                awb = AE(nc.sbuf_tensor("awb", [128, NJ * 1024], BF16))
                ws1b = AE(nc.sbuf_tensor("ws1b", [128, NJ * 16], BF16))
                ws2b = AE(nc.sbuf_tensor("ws2b", [17, 128], BF16))
                tmpw2 = AE(nc.sbuf_tensor("tmpw2", [128, 1024], F32))
                xtl2 = [AE(nc.sbuf_tensor(f"xtl2_{p}", [128, NJ * 512], BF16)) for p in range(2)]
                astg = [AE(nc.sbuf_tensor(f"astg{p}", [128, 4 * 1152], BF16)) for p in range(2)]
                s16a = AE(nc.sbuf_tensor("s16a", [17, 512], BF16))
                atm1 = AE(nc.sbuf_tensor("atm1", [128, 256], F32))
                atm2 = AE(nc.sbuf_tensor("atm2", [128, 256], F32))
                cosl2 = [AE(nc.sbuf_tensor(f"cosl2_{p}", [128, 4 * 256], BF16)) for p in range(2)]
                sinl2 = [AE(nc.sbuf_tensor(f"sinl2_{p}", [128, 4 * 256], BF16)) for p in range(2)]
                p_qg = [AE(nc.psum_tensor(f"p_qg{p}", [128, 512], F32)) for p in range(2)]
                p_vg = [AE(nc.psum_tensor(f"p_vg{p}", [128, 512], F32)) for p in range(2)]
                p_s16 = AE(nc.psum_tensor("p_s16", [16, 512], F32))
                p_s = AE(nc.psum_tensor("p_s", [128, 128], F32))

                vw_ones = dv.do(nc.vector.memset(s16a[:, :], 1.0))
                vcast = 0
                for j in range(NJ):
                    if vcast:
                        nc.sync.wait_ge(dvS.h, vcast)
                    sy.inc(nc.sync.dma_start(out=tmpw2[:, 0:256], in_=aWq[j * 128:(j + 1) * 128, :]), 16)
                    sy.inc(nc.sync.dma_start(out=tmpw2[:, 256:512], in_=aWk[j * 128:(j + 1) * 128, :]), 16)
                    sy.inc(nc.sync.dma_start(out=tmpw2[:, 512:768], in_=aWv[j * 128:(j + 1) * 128, :]), 16)
                    vwl = sy.inc(nc.sync.dma_start(out=tmpw2[:, 768:1024], in_=aWg[j * 128:(j + 1) * 128, :]), 16)
                    dv.wait(sy, vwl)
                    vq = dv.do(nc.vector.tensor_scalar(tmpw2[:, 0:256], tmpw2[:, 0:256], float(ASCALE), None, ALU.mult))
                    dv.wait(dvS, vq)
                    vcast = dv.do(nc.vector.tensor_copy(awb[:, j * 1024:(j + 1) * 1024], tmpw2[:]))
                vg1 = gp.inc(nc.gpsimd.dma_start(out=ws1b[:].rearrange("q (a b) -> q a b", a=NJ),
                                                 in_=bass.AP(aWs1.tensor, 0, [[16, 128], [128 * 16, NJ], [1, 16]])), 16)
                vg2 = gp.inc(nc.gpsimd.dma_start(out=ws2b[:], in_=aWs2[:, :]), 16)

                ld2_done = [0, 0]
                ld2_sc = [0, 0]
                es_clip_done = 0
                vs16e = 0
                prev_mm2 = 0
                rope2_done_p = [0, 0]
                st2_done = [0, 0]
                es_mm_done = 0
                slot_mm2 = [0, 0]
                for tt in range(T // 512):
                    p2 = tt % 2
                    if slot_mm2[p2]:
                        nc.sync.wait_ge(mmS.h, slot_mm2[p2])
                    vld = qs[4 + p2].inc(nc.sync.dma_start(out=xtl2[p2][:], in_=d_xnfm[tt, :, :]), 16)
                    if st2_done[p2]:
                        dv.wait(qs[p2], st2_done[p2])
                        sc.wait(qs[p2], st2_done[p2])
                    nc.sync.wait_ge(dvS.h, rope2_done_p[p2])
                    csrc = bass.AP(cos_a4.tensor, tt * 512 * 256, [[256, 128], [128 * 256, 4], [1, 256]])
                    ssrc = bass.AP(sin_a4.tensor, tt * 512 * 256, [[256, 128], [128 * 256, 4], [1, 256]])
                    csq = qs[6] if p2 == 0 else qs[11]
                    csq.inc(nc.sync.dma_start(out=cosl2[p2][:].rearrange("q (a b) -> q a b", a=4), in_=csrc), 16)
                    vcs = csq.inc(nc.sync.dma_start(out=sinl2[p2][:].rearrange("q (a b) -> q a b", a=4), in_=ssrc), 16)
                    te.wait(qs[4 + p2], vld)
                    te.wait(gp, gp.n)
                    te.wait(dvS, vcast)
                    te.wait(dvS, vw_ones)
                    te.wait(dvS, vs16e)
                    vs16 = 0
                    for j in range(NJ):
                        vs16 = te.do(nc.tensor.matmul(p_s16[:, :], ws1b[:, j * 16:(j + 1) * 16],
                                                      xtl2[p2][:, j * 512:(j + 1) * 512],
                                                      start=(j == 0), stop=(j == NJ - 1)))
                    dv.wait(mmS, vs16)
                    dv.wait(mmS, es_mm_done)
                    vs16e = dv.do(nc.vector.tensor_copy(s16a[0:16, :], p_s16[:, :]))
                    vmm = 0
                    rope2_tile = 0
                    for sub in range(4):
                        pq = sub % 2
                        te.wait(dvS, ld2_done[pq])
                        te.wait(scS, ld2_sc[pq])
                        for j in range(NJ):
                            te.do(nc.tensor.matmul(p_qg[pq][:, :],
                                                   xtl2[p2][:, j * 512 + sub * 128:j * 512 + (sub + 1) * 128],
                                                   awb[:, j * 1024:j * 1024 + 512],
                                                   start=(j == 0), stop=(j == NJ - 1)))
                            vmm = te.do(nc.tensor.matmul(p_vg[pq][:, :],
                                                         xtl2[p2][:, j * 512 + sub * 128:j * 512 + (sub + 1) * 128],
                                                         awb[:, j * 1024 + 512:(j + 1) * 1024],
                                                         start=(j == 0), stop=(j == NJ - 1)))
                        dv.wait(mmS, vmm)
                        dv.wait(csq, vcs)
                        dv.wait(dvS, dvS.n)
                        dv.wait(scS, ld2_sc[1 - pq])
                        h1 = bass.AP(p_qg[pq], 0, [[512, 128], [128, 4], [1, 64]])
                        h2ap = bass.AP(p_qg[pq], 64, [[512, 128], [128, 4], [1, 64]])
                        co = cosl2[p2][:, sub * 256:(sub + 1) * 256].rearrange("q (a b) -> q a b", a=4)
                        si = sinl2[p2][:, sub * 256:(sub + 1) * 256].rearrange("q (a b) -> q a b", a=4)
                        tm1 = atm1[:].rearrange("q (a b) -> q a b", a=4)
                        tm2 = atm2[:].rearrange("q (a b) -> q a b", a=4)
                        sg_h1 = bass.AP(astg[p2], sub * 1152, [[4 * 1152, 128], [128, 4], [1, 64]])
                        sg_h2 = bass.AP(astg[p2], sub * 1152 + 64, [[4 * 1152, 128], [128, 4], [1, 64]])
                        dv.do(nc.vector.tensor_tensor(tm1, h1, co, ALU.mult))
                        vb2 = dv.do(nc.vector.tensor_tensor(tm2, h2ap, si, ALU.mult))
                        dv.wait(dvS, vb2)
                        v_1 = dv.do(nc.vector.tensor_tensor(sg_h1, tm1, tm2, ALU.subtract))
                        dv.wait(dvS, v_1)
                        dv.do(nc.vector.tensor_tensor(tm1, h2ap, co, ALU.mult))
                        vd2 = dv.do(nc.vector.tensor_tensor(tm2, h1, si, ALU.mult))
                        dv.wait(dvS, vd2)
                        v_2 = dv.do(nc.vector.tensor_tensor(sg_h2, tm1, tm2, ALU.add))
                        vvg = dv.do(nc.vector.tensor_copy(astg[p2][:, sub * 1152 + 512:sub * 1152 + 768], p_vg[pq][:, 0:256]))
                        sc.wait(mmS, vmm)
                        sc.wait(dvS, v_2)
                        vge = sc.do(nc.scalar.activation(atm1[:], p_vg[pq][:, 256:512], AF.Exp, scale=-1.0))
                        dv.wait(scS, vge)
                        vd1g = dv.do(nc.vector.tensor_scalar(atm1[:], atm1[:], 1.0, None, ALU.add))
                        dv.wait(dvS, vd1g)
                        vr1g = dv.do(nc.vector.reciprocal(atm2[:], atm1[:]))
                        dv.wait(dvS, vr1g)
                        vgg = dv.do(nc.vector.tensor_tensor(astg[p2][:, sub * 1152 + 768:sub * 1152 + 1024], p_vg[pq][:, 256:512], atm2[:], ALU.mult))
                        ld2_done[pq] = vgg
                        ld2_sc[pq] = vge
                        rope2_tile = max(rope2_tile, v_2, vvg, vgg)
                        te.wait(dvS, vs16e)
                        te.wait(dvS, es_clip_done)
                        ves = te.do(nc.tensor.matmul(p_s[:, :], s16a[:, sub * 128:(sub + 1) * 128], ws2b[:, :], start=True, stop=True))
                        es_mm_done = ves
                        dv.wait(mmS, ves)
                        vclip = dv.do(nc.vector.tensor_scalar(astg[p2][:, sub * 1152 + 1024:sub * 1152 + 1152], p_s[:, :], -CLAMP, CLAMP, ALU.max, ALU.min))
                        es_clip_done = vclip
                        sc.wait(dvS, vclip)
                        vexp = sc.do(nc.scalar.activation(astg[p2][:, sub * 1152 + 1024:sub * 1152 + 1152],
                                                          astg[p2][:, sub * 1152 + 1024:sub * 1152 + 1152], AF.Exp))
                        rope2_tile = max(rope2_tile, vclip)
                    rope2_done_p[p2] = rope2_tile
                    sc.wait(dvS, rope2_tile)
                    sc.wait(scS, vexp)
                    adst = bass.AP(d_att.tensor, tt * 512 * 1152, [[1152, 128], [128 * 1152, 4], [1, 1152]])
                    vstt = qs[p2].inc(nc.scalar.dma_start(out=adst, in_=astg[p2][:].rearrange("q (a b) -> q a b", a=4)), 16)
                    st2_done[p2] = vstt
                    prev_mm2 = vmm
                    slot_mm2[p2] = vmm
                if t_att is not None:
                    nc.sync.wait_ge(qs[0].h, qs[0].n)
                    nc.sync.wait_ge(qs[1].h, qs[1].n)
                    for tc in range(0, NCH, 8):
                        sy.inc(nc.sync.dma_start(out=t_att[tc * 128:(tc + 8) * 128, :], in_=d_att[tc * 128:(tc + 8) * 128, :]), 16)
                    nc.sync.wait_ge(sy.h, sy.n)
            nc.sync.wait_ge(qs[0].h, qs[0].n)
            nc.sync.wait_ge(qs[1].h, qs[1].n)
            bar()

        att_ctx = ExitStack()
        if "scan" in ph or "attpart" in ph:
            og_all = att_ctx.enter_context(nc.sbuf_tensor("og_all", [128, 2 * T], BF16))
        if "scan" in ph:
            with ExitStack() as sctx:
                SE = sctx.enter_context
                attc = [SE(nc.sbuf_tensor(f"attc{p}", [128, 1152], BF16)) for p in range(2)]
                S_sb = [SE(nc.sbuf_tensor(f"S_sb{p}", [128, 128], BF16)) for p in range(2)]
                Z_sb = [SE(nc.sbuf_tensor(f"Z_sb{p}", [64, 256], BF16)) for p in range(2)]
                R_sb = [SE(nc.sbuf_tensor(f"R_sb{p}", [1, 128], F32)) for p in range(2)]
                kfm_sb = [SE(nc.sbuf_tensor(f"kfm_sb{h}", [128, 128], BF16)) for h in range(2)]
                qfm_sb = [SE(nc.sbuf_tensor(f"qfm_sb{h}", [128, 128], BF16)) for h in range(2)]
                amask_sb = [SE(nc.sbuf_tensor(f"amask_sb{h}", [128, 128], BF16)) for h in range(2)]
                enz_sb = [SE(nc.sbuf_tensor(f"enz_sb{h}", [128, 64], F32)) for h in range(2)]
                ok_sb = [SE(nc.sbuf_tensor(f"ok_sb{h}", [128, 64], F32)) for h in range(2)]
                esm_sb = [SE(nc.sbuf_tensor(f"esm_sb{h}", [128, 64], F32)) for h in range(2)]
                u_sb = [SE(nc.sbuf_tensor(f"u_sb{h}", [128, 64], BF16)) for h in range(2)]
                ufm_sb = [SE(nc.sbuf_tensor(f"ufm_sb{h}", [64, 128], BF16)) for h in range(2)]
                esfm_sb = [SE(nc.sbuf_tensor(f"esfm_sb{h}", [64, 128], BF16)) for h in range(2)]
                w2m_sb = [SE(nc.sbuf_tensor(f"w2m_sb{h}", [128, 128], BF16)) for h in range(2)]
                o_sb = [SE(nc.sbuf_tensor(f"o_sb{h}", [128, 128], F32)) for h in range(2)]
                og_sb = [SE(nc.sbuf_tensor(f"og_sb{h}", [128, 128], BF16)) for h in range(2)]
                sm1 = [SE(nc.sbuf_tensor(f"sm1_{h}", [128, 8], F32)) for h in range(2)]
                bnst2 = [SE(nc.sbuf_tensor(f"bnst2_{h}", [128, 6], F32)) for h in range(2)]
                bnag2 = [SE(nc.sbuf_tensor(f"bnag2_{h}", [128, 2], F32)) for h in range(2)]
                pS = SE(nc.psum_tensor("pS", [128, 128], F32))
                pZ = SE(nc.psum_tensor("pZ", [64, 256], F32))
                pR = SE(nc.psum_tensor("pR", [1, 128], F32))
                p_co = SE(nc.psum_tensor("p_co", [128, 256], F32))
                p_wo = SE(nc.psum_tensor("p_wo", [128, 256], F32))
                p_oo = SE(nc.psum_tensor("p_oo", [128, 256], F32))
                p_trS = SE(nc.psum_tensor("p_trS", [128, 1024], BF16))
                p_a = p_c

                upd_done = 0
                snap_done = 0
                attc_mm = [0, 0]
                attc_dv = [0, 0]
                trS_free = 0
                trU_free = 0
                co_free = 0
                wo_free = 0
                oo_free = 0
                am_free = 0
                for c in range(NCH):
                    p = c % 2
                    if attc_mm[p]:
                        nc.sync.wait_ge(mmS.h, attc_mm[p])
                        nc.sync.wait_ge(dvS.h, attc_dv[p])
                    vld = qs[4 + p].inc(nc.sync.dma_start(out=attc[p][:], in_=d_att[c * 128:(c + 1) * 128, :]), 16)
                    ap = attc[p]
                    qsl = [ap[:, h * 128:(h + 1) * 128] for h in range(2)]
                    ksl = [ap[:, 256 + h * 128:256 + (h + 1) * 128] for h in range(2)]
                    vsl = [ap[:, 512 + h * 128:512 + (h + 1) * 128] for h in range(2)]
                    gsl = [ap[:, 768 + h * 128:768 + (h + 1) * 128] for h in range(2)]
                    essl = [ap[:, 1024 + h * 64:1024 + (h + 1) * 64] for h in range(2)]
                    if c > 0:
                        dv.wait(mmS, upd_done)
                        dv.do(nc.vector.tensor_copy(S_sb[p][:], pS[:, :]))
                        dv.do(nc.vector.tensor_copy(Z_sb[p][:], pZ[:, :]))
                        snap_done = dv.do(nc.vector.tensor_copy(R_sb[p][:], pR[:, :]))
                    te.wait(qs[4 + p], vld)
                    if c > 0:
                        te.wait(dvS, snap_done)
                    for h in range(2):
                        te.do(nc.tensor.matmul(pS[:, h * 64:(h + 1) * 64], ksl[h], essl[h],
                                               start=(c == 0 and h == 0), stop=(c == NCH - 1), skip_group_check=True))
                        te.do(nc.tensor.matmul(pZ[0:64, h * 128:(h + 1) * 128], essl[h], vsl[h],
                                               start=(c == 0 and h == 0), stop=(c == NCH - 1), skip_group_check=True))
                    upd_done = te.do(nc.tensor.matmul(pR[:, :], s_ones_c128bf[:, :], ap[:, 1024:1152],
                                                      start=(c == 0), stop=(c == NCH - 1), skip_group_check=True))
                    te.wait(dvS, trS_free)
                    tks = []
                    for h in range(2):
                        te.do(nc.tensor.transpose(p_trS[:, h * 256:h * 256 + 128], ksl[h], s_idbf[:]))
                        tks.append(te.do(nc.tensor.transpose(p_trS[:, h * 256 + 128:h * 256 + 256], qsl[h], s_idbf[:])))
                    vq2 = [0, 0]
                    for h in range(2):
                        dv.wait(mmS, tks[1])
                        dv.do(nc.vector.tensor_copy(kfm_sb[h][:], p_trS[:, h * 256:h * 256 + 128]))
                        vq2[h] = dv.do(nc.vector.tensor_copy(qfm_sb[h][:], p_trS[:, h * 256 + 128:h * 256 + 256]))
                    va2 = [0, 0]
                    te.wait(dvS, am_free)
                    for h in range(2):
                        te.wait(dvS, vq2[h])
                        va2[h] = te.do(nc.tensor.matmul(p_a[:, h * 128:(h + 1) * 128], kfm_sb[h][:], qfm_sb[h][:], start=True, stop=True))
                    vam = [0, 0]
                    for h in range(2):
                        dv.wait(mmS, va2[1])
                        vam[h] = dv.do(nc.vector.tensor_tensor(amask_sb[h][:], p_a[:, h * 128:(h + 1) * 128], s_utri[:], ALU.mult))
                    am_free = max(vam)
                    vok = [0, 0]
                    te.wait(dvS, co_free)
                    for h in range(2):
                        te.wait(dvS, vam[h])
                        te.do(nc.tensor.matmul(p_co[:, h * 128:h * 128 + 64], s_utri[:], essl[h], start=(h == 0), stop=False, skip_group_check=True))
                        if c > 0:
                            te.do(nc.tensor.matmul(p_co[:, h * 128:h * 128 + 64], s_ones_r128[:, :], R_sb[p][:, h * 64:(h + 1) * 64], start=False, stop=True, skip_group_check=True))
                        te.do(nc.tensor.matmul(p_co[:, h * 128 + 64:h * 128 + 128], amask_sb[h][:], essl[h], start=False, stop=(c == 0), skip_group_check=True))
                        if c > 0:
                            vok[h] = te.do(nc.tensor.matmul(p_co[:, h * 128 + 64:h * 128 + 128], qfm_sb[h][:], S_sb[p][:, h * 64:(h + 1) * 64], start=False, stop=True, skip_group_check=True))
                        else:
                            vok[h] = mmS.n
                    vu = [0, 0]
                    vnm2 = [0, 0]
                    for h in range(2):
                        dv.wait(mmS, max(vok))
                        vez = dv.do(nc.vector.reciprocal(enz_sb[h][:], p_co[:, h * 128:h * 128 + 64]))
                        dv.wait(dvS, vez)
                        vko = dv.do(nc.vector.tensor_tensor(ok_sb[h][:], p_co[:, h * 128 + 64:h * 128 + 128], enz_sb[h][:], ALU.mult))
                        co_free = vko
                        dv.wait(dvS, vko)
                        vnm2[h] = dv.do(nc.vector.reduce_max(sm1[h][:, 0:1], ok_sb[h][:], axis=AX.X, negate=True))
                    vex2 = [0, 0]
                    for h in range(2):
                        sc.wait(dvS, vnm2[h])
                        vex2[h] = sc.do(nc.scalar.activation(esm_sb[h][:], ok_sb[h][:], AF.Exp, bias=sm1[h][:, 0:1], accum_out=sm1[h][:, 1:2]))
                    for h in range(2):
                        dv.wait(scS, vex2[h])
                        vri = dv.do(nc.vector.reciprocal(sm1[h][:, 2:3], sm1[h][:, 1:2]))
                        dv.wait(dvS, vri)
                        vue = dv.do(nc.vector.tensor_tensor(esm_sb[h][:], esm_sb[h][:], enz_sb[h][:], ALU.mult))
                        dv.wait(dvS, vue)
                        vu[h] = dv.do(nc.vector.tensor_scalar(u_sb[h][:], esm_sb[h][:], sm1[h][:, 2:3], None, ALU.mult))
                    te.wait(dvS, trU_free)
                    tus = []
                    for h in range(2):
                        te.wait(dvS, vu[h])
                        te.do(nc.tensor.transpose(p_trS[0:64, 512 + h * 256:512 + h * 256 + 128], u_sb[h][:], s_idbf[:]))
                        tus.append(te.do(nc.tensor.transpose(p_trS[0:64, 512 + h * 256 + 128:512 + h * 256 + 256], essl[h], s_idbf[:])))
                    vesf = [0, 0]
                    for h in range(2):
                        dv.wait(mmS, tus[1])
                        dv.do(nc.vector.tensor_copy(ufm_sb[h][:], p_trS[0:64, 512 + h * 256:512 + h * 256 + 128]))
                        vesf[h] = dv.do(nc.vector.tensor_copy(esfm_sb[h][:], p_trS[0:64, 512 + h * 256 + 128:512 + h * 256 + 256]))
                    trU_free = max(vesf)
                    vw2 = [0, 0]
                    te.wait(dvS, wo_free)
                    for h in range(2):
                        te.wait(dvS, vesf[h])
                        vw2[h] = te.do(nc.tensor.matmul(p_wo[:, h * 128:(h + 1) * 128], esfm_sb[h][:], ufm_sb[h][:], start=True, stop=True))
                    vwm = [0, 0]
                    for h in range(2):
                        dv.wait(mmS, vw2[1])
                        vwm[h] = dv.do(nc.vector.tensor_tensor(w2m_sb[h][:], p_wo[:, h * 128:(h + 1) * 128], s_utri[:], ALU.mult))
                    wo_free = max(vwm)
                    vo2 = [0, 0]
                    te.wait(scS, oo_free)
                    for h in range(2):
                        te.wait(dvS, vwm[h])
                        te.do(nc.tensor.matmul(p_oo[:, h * 128:(h + 1) * 128], w2m_sb[h][:], vsl[h], start=True, stop=(c == 0), skip_group_check=True))
                        if c > 0:
                            vo2[h] = te.do(nc.tensor.matmul(p_oo[:, h * 128:(h + 1) * 128], ufm_sb[h][:], Z_sb[p][0:64, h * 128:(h + 1) * 128], start=False, stop=True, skip_group_check=True))
                        else:
                            vo2[h] = mmS.n
                    vosb = [0, 0]
                    for h in range(2):
                        sc.wait(mmS, max(vo2))
                        vosb[h] = sc.do(nc.scalar.copy(o_sb[h][:], p_oo[:, h * 128:(h + 1) * 128]))
                    oo_free = max(vosb)
                    vrstd2 = [0, 0]
                    for h in range(2):
                        dv.wait(scS, vosb[h])
                        vbn2 = dv.do(nc.vector.bn_stats(bnst2[h][:], o_sb[h][:]))
                        dv.wait(dvS, vbn2)
                        vag2 = dv.do(nc.vector.bn_aggr(bnag2[h][:], bnst2[h][:]))
                        dv.wait(dvS, vag2)
                        vq3 = dv.do(nc.vector.tensor_tensor(sm1[h][:, 3:4], bnag2[h][:, 0:1], bnag2[h][:, 0:1], ALU.mult))
                        dv.wait(dvS, vq3)
                        vm2 = dv.do(nc.vector.tensor_add(sm1[h][:, 4:5], bnag2[h][:, 1:2], sm1[h][:, 3:4]))
                        sc.wait(dvS, vm2)
                        vl2 = sc.do(nc.scalar.activation(sm1[h][:, 5:6], sm1[h][:, 4:5], AF.Ln, bias=s_eps[:, 0:1]))
                        sc.wait(scS, vl2)
                        vrstd2[h] = sc.do(nc.scalar.activation(sm1[h][:, 5:6], sm1[h][:, 5:6], AF.Exp, scale=-0.5))
                    vog = [0, 0]
                    for h in range(2):
                        dv.wait(scS, vrstd2[h])
                        von = dv.do(nc.vector.tensor_scalar(o_sb[h][:], o_sb[h][:], sm1[h][:, 5:6], None, ALU.mult))
                        dv.wait(dvS, von)
                        von2 = dv.do(nc.vector.tensor_tensor(o_sb[h][:], o_sb[h][:], s_gnwb[:], ALU.mult))
                        dv.wait(dvS, von2)
                        vog[h] = dv.do(nc.vector.tensor_tensor(og_sb[h][:], o_sb[h][:], gsl[h], ALU.mult))
                    te.wait(dvS, max(vq2))
                    tgs = []
                    for h in range(2):
                        te.wait(dvS, vog[h])
                        tgs.append(te.do(nc.tensor.transpose(p_trS[:, h * 256:h * 256 + 128], og_sb[h][:], s_idbf[:])))
                    vga = [0, 0]
                    for h in range(2):
                        dv.wait(mmS, tgs[1])
                        vga[h] = dv.do(nc.vector.tensor_copy(og_all[:, h * T + c * 128:h * T + (c + 1) * 128], p_trS[:, h * 256:h * 256 + 128]))
                    trS_free = max(vga)
                    attc_mm[p] = mmS.n
                    attc_dv[p] = dvS.n
                if t_og is not None:
                    nc.sync.wait_ge(dvS.h, dvS.n)
                    for r in range(2):
                        sy.inc(nc.sync.dma_start(out=bass.AP(t_og.tensor, r * 128 * T, [[T, 128], [1, T]]), in_=og_all[:, r * T:(r + 1) * T]), 16)
                    nc.sync.wait_ge(sy.h, sy.n)
            bar()

        if "attpart" in ph:
            with ExitStack() as fctx:
                FE = fctx.enter_context
                awo_sb = FE(nc.sbuf_tensor("awo_sb", [128, 2 * D], BF16))
                pstg2 = [FE(nc.sbuf_tensor(f"pstg2_{p}", [128, D], F32)) for p in range(2)]
                p_pp2 = [FE(nc.psum_tensor(f"p_pp2_{p}", [128, 512], F32)) for p in range(2)]
                for r in range(2):
                    gp.inc(nc.gpsimd.dma_start(out=awo_sb[:, r * D:(r + 1) * D], in_=aWo[r * 128:(r + 1) * 128, :]), 16)
                te.wait(gp, gp.n)
                part_st = [0, 0]
                evac_done = [0, 0]
                for tc in range(NCH):
                    pp = tc % 2
                    if part_st[pp]:
                        nc.vector.wait_ge(qs[pp].h, part_st[pp])
                    vev = 0
                    for dsl in range(4):
                        pb2 = dsl % 2
                        te.wait(dvS, evac_done[pb2])
                        vmm = 0
                        for r in range(2):
                            vmm = te.do(nc.tensor.matmul(
                                p_pp2[pb2][:, :],
                                og_all[:, r * T + tc * 128:r * T + (tc + 1) * 128],
                                awo_sb[:, r * D + dsl * 512:r * D + (dsl + 1) * 512],
                                start=(r == 0), stop=(r == 1)))
                        dv.wait(mmS, vmm)
                        vev = dv.do(nc.vector.tensor_copy(pstg2[pp][:, dsl * 512:(dsl + 1) * 512], p_pp2[pb2][:, :]))
                        evac_done[pb2] = vev
                    nc.sync.wait_ge(dvS.h, vev)
                    part_st[pp] = qs[pp].inc(nc.sync.dma_start(out=cc_a_in[tc * 128:(tc + 1) * 128, :], in_=pstg2[pp][:]), 16)
                if t_attpart is not None:
                    nc.sync.wait_ge(qs[0].h, qs[0].n)
                    nc.sync.wait_ge(qs[1].h, qs[1].n)
                    for tc in range(NCH):
                        sy.inc(nc.sync.dma_start(out=t_attpart[tc * 128:(tc + 1) * 128, :], in_=cc_a_in[tc * 128:(tc + 1) * 128, :]), 16)
                    nc.sync.wait_ge(sy.h, sy.n)
                nc.gpsimd.wait_ge(qs[0].h, qs[0].n)
                nc.gpsimd.wait_ge(qs[1].h, qs[1].n)
                if "ffnpart" in ph:
                    nc.gpsimd.wait_ge(ccS.h, ccu)
                if fake_cc:
                    ccS.inc(nc.gpsimd.dma_start(out=cc_a_out[:, :], in_=cc_a_in[0:TOWN, :]), 16)
                else:
                    ccS.inc(nc.gpsimd.collective_compute(
                        "ReduceScatter", ALU.add,
                        replica_groups=[list(range(NCORES))],
                        ins=[cc_a_in.tensor.ap().opt()],
                        outs=[cc_a_out.tensor.ap().opt()]), 1)
            bar()
        att_ctx.close()

        if "x2h2" in ph:
            x2a = E(nc.sbuf_tensor("x2a", [128, NJ * TOWN], F32))
            h2tm = E(nc.sbuf_tensor("h2tm", [128, NOC * D], BF16))
            wfm = E(nc.sbuf_tensor("wfm", [16, TOWN], F32)) if t_gate is not None else None
            wtm = E(nc.sbuf_tensor("wtm", [128, NOC * 16], F32))
            slottm = E(nc.sbuf_tensor("slottm", [128, NOC * 16], F32))
            s_flnwb = gate_ctx.enter_context(nc.sbuf_tensor("s_flnwb", [128, D], F32))
            row16 = gate_ctx.enter_context(nc.sbuf_tensor("row16", [16, 128], F32))
            h2sb = gate_ctx.enter_context(nc.sbuf_tensor("h2sb", [128, NJ * TOWN], BF16))
            vtr = te.do(nc.tensor.transpose(p_c[0:16, 0:128], s_flnw[:], s_idf32[:]))
            dv.wait(mmS, vtr)
            vr16 = dv.do(nc.vector.tensor_copy(row16[:], p_c[0:16, 0:128]))
            for j in range(NJ):
                te.wait(dvS, dvS.n)
                vb = te.do(nc.tensor.matmul(p_c[:, 128:256], s_sel[:, j * 128:(j + 1) * 128], row16[:, :], start=True, stop=True))
                dv.wait(mmS, vb)
                dv.do(nc.vector.tensor_copy(s_flnwb[:, j * 128:(j + 1) * 128], p_c[:, 128:256]))
            with ExitStack() as xctx:
                XE = xctx.enter_context
                rsf = [XE(nc.sbuf_tensor(f"rsf{p}", [128, D], F32)) for p in range(2)]
                xno = [XE(nc.sbuf_tensor(f"xno{p}", [128, D], F32)) for p in range(2)]
                x2tm = XE(nc.sbuf_tensor("x2tm", [128, D], F32))
                bnst3 = XE(nc.sbuf_tensor("bnst3", [128, 24], F32))
                bnag3 = XE(nc.sbuf_tensor("bnag3", [128, 2], F32))
                sm3 = XE(nc.sbuf_tensor("sm3", [128, 4], F32))
                rr2row = XE(nc.sbuf_tensor("rr2row", [1, TOWN], F32))
                rr2sb = XE(nc.sbuf_tensor("rr2sb", [1, 128], F32))
                p_tx = XE(nc.psum_tensor("p_tx", [128, 512], F32))
                p_tx2 = XE(nc.psum_tensor("p_tx2", [128, 512], F32))
                p_rr2 = XE(nc.psum_tensor("p_rr2", [1, 128], F32))
                p_rb2 = XE(nc.psum_tensor("p_rb2", [128, 512], F32))

                if "ffnpart" in ph:
                    nc.sync.wait_ge(ccS.h, ccu)
                cid = nc.gpsimd.partition_id()
                x2_done = 0
                rr2_ev = 0
                xno_done = [0, 0]
                for oc in range(NOC):
                    p = oc % 2
                    if xno_done[p]:
                        nc.sync.wait_ge(dvS.h, xno_done[p])
                        nc.gpsimd.wait_ge(dvS.h, xno_done[p])
                    vld1 = qs[p].inc(nc.sync.dma_start(out=rsf[p][:], in_=cc_f_out[oc * 128:(oc + 1) * 128, :]), 16)
                    xsrc = bass.AP(d_xn32.tensor, (cid * TOWN + oc * 128) * D, [[D, 128], [1, D]])
                    gqp = gp if p == 0 else gq
                    vld2 = gqp.inc(nc.gpsimd.dma_start(out=xno[p][:], in_=xsrc), 16)
                    dv.wait(qs[p], vld1)
                    dv.wait(gqp, vld2)
                    dv.wait(mmS, x2_done)
                    vx2 = dv.do(nc.vector.tensor_add(x2tm[:], xno[p][:], rsf[p][:]))
                    xno_done[p] = vx2
                    dv.wait(dvS, vx2)
                    for k in range(4):
                        vbn = dv.do(nc.vector.bn_stats(bnst3[:, k * 6:(k + 1) * 6], x2tm[:, k * 512:(k + 1) * 512]))
                    dv.wait(dvS, vbn)
                    vag = dv.do(nc.vector.bn_aggr(bnag3[:], bnst3[:]))
                    dv.wait(dvS, vag)
                    vq4 = dv.do(nc.vector.tensor_tensor(sm3[:, 0:1], bnag3[:, 0:1], bnag3[:, 0:1], ALU.mult))
                    dv.wait(dvS, vq4)
                    vms = dv.do(nc.vector.tensor_add(sm3[:, 1:2], bnag3[:, 1:2], sm3[:, 0:1]))
                    sc.wait(dvS, vms)
                    vl3 = sc.do(nc.scalar.activation(sm3[:, 2:3], sm3[:, 1:2], AF.Ln, bias=s_eps[:, 0:1]))
                    sc.wait(scS, vl3)
                    vrr2 = sc.do(nc.scalar.activation(sm3[:, 2:3], sm3[:, 2:3], AF.Exp, scale=-0.5))
                    dv.wait(scS, vrr2)
                    vhtm = dv.do(nc.vector.tensor_scalar(h2tm[:, oc * D:(oc + 1) * D], x2tm[:], sm3[:, 2:3], None, ALU.mult))
                    dv.wait(dvS, vhtm)
                    dv.do(nc.vector.tensor_tensor(h2tm[:, oc * D:(oc + 1) * D], h2tm[:, oc * D:(oc + 1) * D], s_flnwb[:], ALU.mult))
                    te.wait(scS, vrr2)
                    te.wait(dvS, rr2_ev)
                    vtr2 = te.do(nc.tensor.transpose(p_rr2[:, :], sm3[:, 2:3], s_idf32[:]))
                    dv.wait(mmS, vtr2)
                    rr2_ev = dv.do(nc.vector.tensor_copy(rr2row[:, oc * 128:(oc + 1) * 128], p_rr2[:, :]))
                    vtx = 0
                    for j in range(NJ):
                        ptx = p_tx if (j // 4) % 2 == 0 else p_tx2
                        tp = nc.tensor.transpose(ptx[:, (j % 4) * 128:(j % 4 + 1) * 128], x2tm[:, j * 128:(j + 1) * 128], s_idf32[:])
                        if j % 4 == 3:
                            vt = te.do(tp)
                            dv.wait(mmS, vt)
                            for jj in range(j - 3, j + 1):
                                vtx = dv.do(nc.vector.tensor_copy(x2a[:, jj * TOWN + oc * 128:jj * TOWN + (oc + 1) * 128], ptx[:, (jj % 4) * 128:(jj % 4 + 1) * 128]))
                            te.wait(dvS, vtx)
                    x2_done = mmS.n
                vh2 = 0
                for tt2 in range(TOWN // 512):
                    te.wait(dvS, vh2)
                    vrb2 = te.do(nc.tensor.matmul(p_rb2[:, 0:512], s_ones_r128[:, :], rr2row[:, tt2 * 512:(tt2 + 1) * 512], start=True, stop=True))
                    dv.wait(mmS, vrb2)
                    for j in range(NJ):
                        vt1 = dv.do(nc.vector.tensor_tensor(h2sb[:, j * TOWN + tt2 * 512:j * TOWN + tt2 * 512 + 512],
                                                            x2a[:, j * TOWN + tt2 * 512:j * TOWN + tt2 * 512 + 512], p_rb2[:, 0:512], ALU.mult))
                        dv.wait(dvS, vt1)
                        vh2 = dv.do(nc.vector.tensor_scalar(h2sb[:, j * TOWN + tt2 * 512:j * TOWN + tt2 * 512 + 512],
                                                            h2sb[:, j * TOWN + tt2 * 512:j * TOWN + tt2 * 512 + 512], s_flnw[:, j:j + 1], None, ALU.mult))
                if t_x2 is not None:
                    nc.sync.wait_ge(dvS.h, dvS.n)
                    sy.inc(nc.sync.dma_start(out=t_x2[:, :], in_=x2a[:, :]), 16)
                if t_h2 is not None:
                    nc.sync.wait_ge(dvS.h, dvS.n)
                    sy.inc(nc.sync.dma_start(out=t_h2[:, :], in_=h2sb[:, :]), 16)
                nc.sync.wait_ge(sy.h, sy.n)
            bar()

        if "gate" in ph:
            with ExitStack() as gctx:
                GE = gctx.enter_context
                wg_sb = GE(nc.sbuf_tensor("wg_sb", [128, NJ * 16], BF16))
                lgA = GE(nc.sbuf_tensor("lgA", [128, 128], F32))
                lgW = GE(nc.sbuf_tensor("lgW", [128, 128], F32))
                msk8 = GE(nc.sbuf_tensor("msk8", [128, 128], F32))
                gsm8 = GE(nc.sbuf_tensor("gsm8", [128, 64], F32))
                p_lg8 = GE(nc.psum_tensor("p_lg8", [128, 128], F32))
                p_wt = GE(nc.psum_tensor("p_wt", [16, 128], F32))
                vgw = gp.inc(nc.gpsimd.dma_start(out=wg_sb[:].rearrange("q (a b) -> q a b", a=NJ),
                                                 in_=bass.AP(fWg.tensor, 0, [[16, 128], [128 * 16, NJ], [1, 16]])), 16)
                te.wait(gp, gp.n)
                vml = 0
                for oc in range(NOC):
                    for j in range(NJ):
                        vml = te.do(nc.tensor.matmul(p_lg8[:, oc * 16:(oc + 1) * 16], h2sb[:, j * TOWN + oc * 128:j * TOWN + (oc + 1) * 128],
                                                     wg_sb[:, j * 16:(j + 1) * 16], start=(j == 0), stop=(j == NJ - 1)))
                dv.wait(mmS, vml)
                v0 = dv.do(nc.vector.tensor_copy(lgA[:], p_lg8[:, :]))
                dv.wait(dvS, v0)
                v1 = dv.do(nc.vector.tensor_copy(lgW[:], lgA[:]))
                dv.wait(dvS, v1)
                for it in range(4):
                    vmx = 0
                    for oc in range(NOC):
                        vmx = dv.do(nc.vector.reduce_max(gsm8[:, oc * 8 + it:oc * 8 + it + 1], lgW[:, oc * 16:(oc + 1) * 16], axis=AX.X))
                    if it < 3:
                        dv.wait(dvS, vmx)
                        vmk = 0
                        for oc in range(NOC):
                            vmk = dv.do(nc.vector.tensor_scalar(msk8[:, oc * 16:(oc + 1) * 16], lgW[:, oc * 16:(oc + 1) * 16],
                                                                gsm8[:, oc * 8 + it:oc * 8 + it + 1], None, ALU.is_ge))
                        dv.wait(dvS, vmk)
                        vstt = 0
                        for oc in range(NOC):
                            vstt = dv.do(nc.vector.scalar_tensor_tensor(lgW[:, oc * 16:(oc + 1) * 16], msk8[:, oc * 16:(oc + 1) * 16],
                                                                        -1e30, lgW[:, oc * 16:(oc + 1) * 16], ALU.mult, ALU.add))
                        dv.wait(dvS, vstt)
                dv.wait(dvS, dvS.n)
                vng = 0
                for oc in range(NOC):
                    dv.do(nc.vector.tensor_scalar(msk8[:, oc * 16:(oc + 1) * 16], lgA[:, oc * 16:(oc + 1) * 16],
                                                  gsm8[:, oc * 8 + 3:oc * 8 + 4], None, ALU.is_ge))
                    vng = dv.do(nc.vector.tensor_scalar(gsm8[:, oc * 8 + 4:oc * 8 + 5], gsm8[:, oc * 8:oc * 8 + 1], -1.0, None, ALU.mult))
                sc.wait(dvS, vng)
                vem = 0
                for oc in range(NOC):
                    vem = sc.do(nc.scalar.activation(lgW[:, oc * 16:(oc + 1) * 16], lgA[:, oc * 16:(oc + 1) * 16], AF.Exp,
                                                     bias=gsm8[:, oc * 8 + 4:oc * 8 + 5]))
                dv.wait(scS, vem)
                vwu = dv.do(nc.vector.tensor_tensor(lgW[:], lgW[:], msk8[:], ALU.mult))
                dv.wait(dvS, vwu)
                vde = 0
                for oc in range(NOC):
                    vde = dv.do(nc.vector.reduce_sum(gsm8[:, oc * 8 + 5:oc * 8 + 6], lgW[:, oc * 16:(oc + 1) * 16], axis=AX.X))
                dv.wait(dvS, vde)
                vrd = 0
                for oc in range(NOC):
                    vrd = dv.do(nc.vector.reciprocal(gsm8[:, oc * 8 + 6:oc * 8 + 7], gsm8[:, oc * 8 + 5:oc * 8 + 6]))
                dv.wait(dvS, vrd)
                vwt = 0
                for oc in range(NOC):
                    vwt = dv.do(nc.vector.tensor_scalar(wtm[:, oc * 16:(oc + 1) * 16], lgW[:, oc * 16:(oc + 1) * 16],
                                                        gsm8[:, oc * 8 + 6:oc * 8 + 7], None, ALU.mult))
                if wfm is not None:
                    prev_wt_ev = 0
                    for oc in range(NOC):
                        te.wait(dvS, vwt)
                        te.wait(dvS, prev_wt_ev)
                        vtw = te.do(nc.tensor.transpose(p_wt[:, :], wtm[:, oc * 16:(oc + 1) * 16], s_idf32[:]))
                        dv.wait(mmS, vtw)
                        prev_wt_ev = dv.do(nc.vector.tensor_copy(wfm[:, oc * 128:(oc + 1) * 128], p_wt[:, :]))
                if t_gate is not None:
                    nc.sync.wait_ge(dvS.h, dvS.n)
                    sy.inc(nc.sync.dma_start(out=t_gate[:, :], in_=wfm[:, :]), 16)
                    nc.sync.wait_ge(sy.h, sy.n)
            bar()
        gate_ctx.close()

        if "moe" in ph:
            with ExitStack() as mctx:
                ME = mctx.enter_context
                NW13 = 3
                NW2 = 2
                w13 = [ME(nc.sbuf_tensor(f"w13_{p}", [128, 2 * EH], BF16)) for p in range(NW13)]
                w2b = [ME(nc.sbuf_tensor(f"w2b_{p}", [128, D], BF16)) for p in range(NW2)]
                h2g = ME(nc.sbuf_tensor("h2g", [128, NJ * CAP], BF16))
                a_t = ME(nc.sbuf_tensor("a_t", [128, NH * CAP], BF16))
                gbf = ME(nc.sbuf_tensor("gbf", [128, NOC * CAP], BF16))
                gpp = ME(nc.sbuf_tensor("gpp", [128, NSC * TOWN], BF16))
                yts = ME(nc.sbuf_tensor("yts", [128, NSC * D], BF16))
                ysb = [ME(nc.sbuf_tensor(f"ysb{p}", [128, CAP], BF16)) for p in range(2)]
                g32 = [ME(nc.sbuf_tensor(f"g32_{p}", [128, CAP], F32)) for p in range(2)]
                silb = ME(nc.sbuf_tensor("silb", [128, 2 * CAP], F32))
                wgb = ME(nc.sbuf_tensor("wgb", [128, CAP], F32))
                wrow = ME(nc.sbuf_tensor("wrow", [1, CAP], F32))
                mskf = ME(nc.sbuf_tensor("mskf", [128, NOC * 16], F32))
                mskb = ME(nc.sbuf_tensor("mskb", [128, NOC * 16], BF16))
                s_onesq = ME(nc.sbuf_tensor("s_onesq", [128, 128], BF16))
                s_iotacap = ME(nc.sbuf_tensor("s_iotacap", [128, CAP], F32))
                ps = [ME(nc.psum_tensor(f"psm{i}", [128, 512], F32)) for i in range(6)]
                ptb = ME(nc.psum_tensor("ptb", [128, 1024], BF16))

                sy.inc(nc.sync.dma_start(out=s_onesq[:], in_=ones_sq[:, :]), 16)
                sy.inc(nc.sync.dma_start(out=s_iotacap[:], in_=iota_cap[:, :]), 16)

                w13_done = [0] * NW13
                w2_done = [0] * NW2
                w13_ready = {}
                w2_ready = {}
                w13_seq = [0]
                w2_seq = [0]

                def issue_w13(e, hc):
                    p = w13_seq[0] % NW13
                    w13_seq[0] += 1
                    if w13_done[p]:
                        nc.gpsimd.wait_ge(mmS.h, w13_done[p])
                    w13s[p].inc(nc.gpsimd.dma_start(out=w13[p][:, 0:EH], in_=w1[e * NH + hc, :, :]), 16)
                    w13_ready[(e, hc)] = (p, w13s[p].inc(nc.gpsimd.dma_start(out=w13[p][:, EH:2 * EH], in_=w3[e * NH + hc, :, :]), 16))

                def issue_w2(e, dc):
                    p = w2_seq[0] % NW2
                    w2_seq[0] += 1
                    if w2_done[p]:
                        nc.gpsimd.wait_ge(mmS.h, w2_done[p])
                    w2_ready[(e, dc)] = (p, w2s[p].inc(nc.gpsimd.dma_start(out=w2b[p][:, :], in_=w2[e * NJ + dc, :, :]), 16))

                issue_w13(0, 0)
                issue_w13(0, 1)
                issue_w13(0, 2)

                vmskl = 0
                for tc in range(NOC):
                    vm = dv.do(nc.vector.tensor_scalar(mskf[:, tc * 16:(tc + 1) * 16], wtm[:, tc * 16:(tc + 1) * 16], 0.0, None, ALU.is_gt))
                    dv.wait(dvS, vm)
                    vmskl = dv.do(nc.vector.tensor_copy(mskb[:, tc * 16:(tc + 1) * 16], mskf[:, tc * 16:(tc + 1) * 16]))
                te.wait(sy, sy.n)
                te.wait(dvS, vmskl)
                vsl = 0
                for tc in range(NOC):
                    for tp in range(tc + 1):
                        vsl = te.do(nc.tensor.matmul(p_c[:, tc * 16:(tc + 1) * 16],
                                                     s_utri[:] if tp == tc else s_onesq[:],
                                                     mskb[:, tp * 16:(tp + 1) * 16],
                                                     start=(tp == 0), stop=(tp == tc)))
                dv.wait(mmS, vsl)
                vst = dv.do(nc.vector.tensor_tensor(slottm[:], p_c[:, 0:128], mskf[:], ALU.mult))
                dv.wait(dvS, vst)
                vst = dv.do(nc.vector.tensor_scalar(slottm[:], slottm[:], 1.0, None, ALU.subtract))
                slottm_ready = vst
                pc_free = vst

                a_evac = [(0, 0), (0, 0)]
                y_evac = [0, 0]
                hg_evac = [0, 0]
                sc_evac = [0, 0]
                gather_last = 0
                ptb_done = 0
                ysb_rd = [0, 0]
                AS = {"vts": {}, "vwg": {}, "vtbl": 0, "pc_free": vst, "gb_free": 0}

                def a_ts(e, tc):
                    if tc == 0:
                        dv.wait(dvS, slottm_ready)
                        if AS["gb_free"]:
                            dv.wait(mmS, AS["gb_free"])
                        if gather_last:
                            dv.wait(mmS, gather_last)
                    if tc >= 2 and AS["vwg"].get(tc - 2):
                        dv.wait(mmS, AS["vwg"][tc - 2])
                    col = slottm[:, tc * 16 + e:tc * 16 + e + 1]
                    AS["vts"][tc] = dv.do(nc.vector.tensor_scalar(g32[tc % 2][:], s_iotacap[:], col, None, ALU.is_equal))
                    AS["vtbl"] = dv.do(nc.vector.tensor_scalar(gbf[:, tc * CAP:(tc + 1) * CAP], s_iotacap[:], col, None, ALU.is_equal))

                def a_wg(e, tc):
                    if tc == 0:
                        te.wait(dvS, AS["pc_free"])
                    te.wait(dvS, AS["vts"][tc])
                    AS["vwg"][tc] = te.do(nc.tensor.matmul(p_c[0:1, 0:CAP], wtm[:, tc * 16 + e:tc * 16 + e + 1], g32[tc % 2][:],
                                                           start=(tc == 0), stop=(tc == NOC - 1)))

                def a_fin(e):
                    dv.wait(mmS, AS["vwg"][NOC - 1])
                    vwr = dv.do(nc.vector.tensor_copy(wrow[:], p_c[0:1, 0:CAP]))
                    te.wait(dvS, vwr)
                    vbc = te.do(nc.tensor.matmul(p_c[:, 0:CAP], s_ones_r128[:, :], wrow[:, :], start=True, stop=True))
                    dv.wait(mmS, vbc)
                    vwgb = dv.do(nc.vector.tensor_copy(wgb[:], p_c[:, 0:CAP]))
                    AS["pc_free"] = vwgb
                    AS["vwg"] = {}
                    return vwgb

                for tc in range(NOC):
                    a_ts(0, tc)
                    if tc >= 1:
                        a_wg(0, tc - 1)
                a_wg(0, NOC - 1)
                vwgb_cur = a_fin(0)

                for e in range(NE):
                    vtbl_cur = AS["vtbl"]

                    te.wait(dvS, vtbl_cur)
                    for dc in range(NJ):
                        g = dc % 2
                        if sc_evac[g]:
                            te.wait(dvS, sc_evac[g])
                        if hg_evac[g]:
                            te.wait(dvS, hg_evac[g])
                        vmm = 0
                        for tc in range(NOC):
                            vmm = te.do(nc.tensor.matmul(ps[4 + g][:, 0:CAP], h2tm[:, tc * D + dc * 128:tc * D + (dc + 1) * 128],
                                                         gbf[:, tc * CAP:(tc + 1) * CAP], start=(tc == 0), stop=(tc == NOC - 1)))
                        dv.wait(mmS, vmm)
                        hg_evac[g] = dv.do(nc.vector.tensor_copy(h2g[:, dc * CAP:(dc + 1) * CAP], ps[4 + g][:, 0:CAP]))
                    gather_last = vmm

                    te.wait(dvS, hg_evac[0])
                    te.wait(dvS, hg_evac[1])
                    dv.wait(dvS, vwgb_cur)
                    gptr = 0
                    for hc in range(NH):
                        p = hc % 2
                        wp, wrdy = w13_ready[(e, hc)]
                        te.wait(w13s[wp], wrdy)
                        dvn, scn = a_evac[p]
                        if dvn:
                            te.wait(dvS, dvn)
                            te.wait(scS, scn)
                        vmm = 0
                        for dc in range(NJ):
                            te.do(nc.tensor.matmul(ps[0 + p][:, 0:CAP], w13[wp][:, dc * 128:(dc + 1) * 128],
                                                   h2g[:, dc * CAP:(dc + 1) * CAP], start=(dc == 0), stop=(dc == NJ - 1)))
                            vmm = te.do(nc.tensor.matmul(ps[2 + p][:, 0:CAP], w13[wp][:, EH + dc * 128:EH + (dc + 1) * 128],
                                                         h2g[:, dc * CAP:(dc + 1) * CAP], start=(dc == 0), stop=(dc == NJ - 1)))
                        w13_done[wp] = vmm
                        sc.wait(mmS, vmm)
                        vsa = sc.do(nc.scalar.activation(silb[:, p * CAP:(p + 1) * CAP], ps[0 + p][:, 0:CAP], AF.Silu))
                        dv.wait(scS, vsa)
                        vt1 = dv.do(nc.vector.tensor_tensor(silb[:, p * CAP:(p + 1) * CAP], silb[:, p * CAP:(p + 1) * CAP], ps[2 + p][:, 0:CAP], ALU.mult))
                        dv.wait(dvS, vt1)
                        vt2 = dv.do(nc.vector.tensor_tensor(a_t[:, hc * CAP:(hc + 1) * CAP], silb[:, p * CAP:(p + 1) * CAP], wgb[:], ALU.mult))
                        a_evac[p] = (vt2, vsa)
                        if hc + 3 < NH:
                            issue_w13(e, hc + 3)
                        elif hc == NH - 2:
                            issue_w2(e, 0)
                        elif hc == NH - 1:
                            issue_w2(e, 1)
                        if hc in (1, 2, 3):
                            te.wait(dvS, ptb_done)
                            vtt = 0
                            for k in range(8):
                                i = gptr * 8 + k
                                tci, sc3 = i // NSC, i % NSC
                                vtt = te.do(nc.tensor.transpose(ptb[:, k * 128:(k + 1) * 128],
                                                                gbf[:, tci * CAP + sc3 * 128:tci * CAP + sc3 * 128 + 128], s_idbf[:]))
                            dv.wait(mmS, vtt)
                            vpe = 0
                            for k in range(8):
                                i = gptr * 8 + k
                                tci, sc3 = i // NSC, i % NSC
                                vpe = dv.do(nc.vector.tensor_copy(gpp[:, sc3 * TOWN + tci * 128:sc3 * TOWN + (tci + 1) * 128], ptb[:, k * 128:(k + 1) * 128]))
                            ptb_done = vpe
                            gptr += 1
                            if gptr == 3:
                                AS["gb_free"] = vtt

                    te.wait(dvS, a_evac[0][0])
                    te.wait(dvS, a_evac[1][0])

                    def emit_ytr(pdc, pg, pye):
                        nonlocal ptb_done
                        te.wait(dvS, pye)
                        te.wait(dvS, ptb_done)
                        vtt = 0
                        for sc3 in range(NSC):
                            vtt = te.do(nc.tensor.transpose(ptb[:, sc3 * 128:(sc3 + 1) * 128], ysb[pg][:, sc3 * 128:(sc3 + 1) * 128], s_idbf[:]))
                        ysb_rd[pg] = vtt
                        dv.wait(mmS, vtt)
                        vpe = 0
                        for sc3 in range(NSC):
                            vpe = dv.do(nc.vector.tensor_copy(yts[:, sc3 * D + pdc * 128:sc3 * D + (pdc + 1) * 128], ptb[:, sc3 * 128:(sc3 + 1) * 128]))
                        ptb_done = vpe

                    pend = None
                    nxt = e + 1 < NE
                    for dc in range(NJ):
                        p = dc % 2
                        g = dc % 2
                        wp2, wrdy2 = w2_ready[(e, dc)]
                        te.wait(w2s[wp2], wrdy2)
                        if y_evac[g]:
                            te.wait(dvS, y_evac[g])
                        vmm = 0
                        for hc in range(NH):
                            vmm = te.do(nc.tensor.matmul(ps[4 + g][:, 0:CAP], w2b[wp2][:, hc * 128:(hc + 1) * 128],
                                                         a_t[:, hc * CAP:(hc + 1) * CAP], start=(hc == 0), stop=(hc == NH - 1)))
                        w2_done[wp2] = vmm
                        dv.wait(mmS, max(vmm, ysb_rd[g]))
                        vye = dv.do(nc.vector.tensor_copy(ysb[g][:], ps[4 + g][:, 0:CAP]))
                        y_evac[g] = vye
                        if dc + 2 < NJ:
                            issue_w2(e, dc + 2)
                        if nxt:
                            if dc == 0:
                                issue_w13(e + 1, 0)
                                issue_w13(e + 1, 1)
                            if dc == 1:
                                issue_w13(e + 1, 2)
                            if 3 <= dc <= 10:
                                a_wg(e + 1, dc - 3)
                            if 1 <= dc <= 8:
                                a_ts(e + 1, dc - 1)
                            if dc == 11:
                                vwgb_nxt = a_fin(e + 1)
                        if pend is not None:
                            emit_ytr(*pend)
                        pend = (dc, g, vye)
                    emit_ytr(*pend)
                    if nxt:
                        vwgb_cur = vwgb_nxt

                    te.wait(dvS, ptb_done)
                    for dc in range(NJ):
                        for tt in range(2):
                            g = (dc * 2 + tt) % 2
                            if sc_evac[g]:
                                te.wait(dvS, sc_evac[g])
                            vmm = 0
                            for sc3 in range(NSC):
                                vmm = te.do(nc.tensor.matmul(ps[4 + g][:, 0:512], yts[:, sc3 * D + dc * 128:sc3 * D + (dc + 1) * 128],
                                                             gpp[:, sc3 * TOWN + tt * 512:sc3 * TOWN + (tt + 1) * 512], start=(sc3 == 0), stop=(sc3 == NSC - 1)))
                            dv.wait(mmS, vmm)
                            sc_evac[g] = dv.do(nc.vector.tensor_add(x2a[:, dc * TOWN + tt * 512:dc * TOWN + tt * 512 + 512],
                                                                    x2a[:, dc * TOWN + tt * 512:dc * TOWN + tt * 512 + 512], ps[4 + g][:, 0:512]))
                if t_moe is not None:
                    nc.sync.wait_ge(dvS.h, dvS.n)
                    sy.inc(nc.sync.dma_start(out=t_moe[:, :], in_=x2a[:, :]), 16)
                    nc.sync.wait_ge(sy.h, sy.n)
            bar()

        if "final" in ph:
            with ExitStack() as fctx:
                FE = fctx.enter_context
                sso = [FE(nc.sbuf_tensor(f"sso{p}", [128, D], F32)) for p in range(2)]
                xoo = [FE(nc.sbuf_tensor(f"xoo{p}", [128, D], F32)) for p in range(2)]
                ftm = FE(nc.sbuf_tensor("ftm", [128, D], F32))
                otm = [FE(nc.sbuf_tensor(f"otm{p}", [128, D], F32)) for p in range(2)]
                p_tf = FE(nc.psum_tensor("p_tf", [128, 512], F32))
                p_tf2 = FE(nc.psum_tensor("p_tf2", [128, 512], F32))
                if "attpart" in ph:
                    nc.sync.wait_ge(ccS.h, ccS.n)
                ftm_done = 0
                out_st = [0, 0]
                for oc in range(NOC):
                    p = oc % 2
                    vld1 = qs[p].inc(nc.sync.dma_start(out=sso[p][:], in_=cc_a_out[oc * 128:(oc + 1) * 128, :]), 16)
                    vld2 = qs[2 + p].inc(nc.sync.dma_start(out=xoo[p][:], in_=x_own[oc * 128:(oc + 1) * 128, :]), 16)
                    vtx = 0
                    for j in range(NJ):
                        ptx = p_tf if (j // 4) % 2 == 0 else p_tf2
                        tp = nc.tensor.transpose(ptx[:, (j % 4) * 128:(j % 4 + 1) * 128], x2a[:, j * TOWN + oc * 128:j * TOWN + (oc + 1) * 128], s_idf32[:])
                        if j % 4 == 3:
                            vt = te.do(tp)
                            dv.wait(mmS, vt)
                            dv.wait(scS, ftm_done)
                            for jj in range(j - 3, j + 1):
                                vtx = dv.do(nc.vector.tensor_copy(ftm[:, jj * 128:(jj + 1) * 128], ptx[:, (jj % 4) * 128:(jj % 4 + 1) * 128]))
                            te.wait(dvS, vtx)
                    sc.wait(dvS, vtx)
                    vsil = sc.do(nc.scalar.activation(ftm[:], ftm[:], AF.Silu))
                    ftm_done = vsil
                    dv.wait(scS, vsil)
                    dv.wait(qs[p], vld1)
                    dv.wait(qs[2 + p], vld2)
                    if out_st[p]:
                        dv.wait(st, st.n)
                    vp1 = dv.do(nc.vector.tensor_tensor(otm[p][:], ftm[:], sso[p][:], ALU.mult))
                    dv.wait(dvS, vp1)
                    vout = dv.do(nc.vector.tensor_add(otm[p][:], otm[p][:], xoo[p][:]))
                    nc.sync.wait_ge(dvS.h, vout)
                    out_st[p] = st.inc(nc.sync.dma_start(out=o_own[oc * 128:(oc + 1) * 128, :], in_=otm[p][:]), 16)
                nc.sync.wait_ge(st.h, st.n)
        else:
            zz = E(nc.sbuf_tensor("zz", [128, D], F32))
            vz = dv.do(nc.vector.memset(zz[:], 0.0))
            nc.sync.wait_ge(dvS.h, vz)
            for oc in range(NOC):
                st.inc(nc.sync.dma_start(out=o_own[oc * 128:(oc + 1) * 128, :], in_=zz[:]), 16)
            nc.sync.wait_ge(st.h, st.n)

    return nc, dbg



def _prep_inputs(inputs, ci):
    g = lambda k: np.asarray(inputs[k])
    x = np.ascontiguousarray(g("L0_attn_hidden_states").reshape(T, D))
    hs = slice(ci * AH * ADK, (ci + 1) * AH * ADK)
    fqs = slice(ci * FQH * FHD, (ci + 1) * FQH * FHD)
    fks = slice(ci * FHD, (ci + 1) * FHD)
    ams = slice(ci * AH * AM, (ci + 1) * AH * AM)
    aWs2T = np.ascontiguousarray(g("L0_attn_Ws2").T)
    ws2aug = np.concatenate([aWs2T[:, ams], g("L0_attn_bs2")[None, ams]], axis=0)
    return {
        "x_full": x,
        "x_own": np.ascontiguousarray(x[ci * TOWN:(ci + 1) * TOWN]),
        "aWq": np.ascontiguousarray(g("L0_attn_Wq").T[:, hs]),
        "aWk": np.ascontiguousarray(g("L0_attn_Wk").T[:, hs]),
        "aWv": np.ascontiguousarray(g("L0_attn_Wv").T[:, hs]),
        "aWg": np.ascontiguousarray(g("L0_attn_Wg").T[:, hs]),
        "aWo": np.ascontiguousarray(g("L0_attn_Wo").T[hs, :]),
        "aWs1": np.ascontiguousarray(g("L0_attn_Ws1").T),
        "aWs2": np.ascontiguousarray(ws2aug),
        "gnw": np.ascontiguousarray(g("L0_attn_g_norm_weight")[None, :]),
        "lnw": np.ascontiguousarray(g("L0_ffn_ln_w")[None, :]),
        "flnw": np.ascontiguousarray(g("L0_ffn_ff_ln_w")[None, :]),
        "fWq": np.ascontiguousarray(g("L0_ffn_wq")[:, fqs]),
        "fWk": np.ascontiguousarray(g("L0_ffn_wk")[:, fks]),
        "fWv": np.ascontiguousarray(g("L0_ffn_wv")[:, fks]),
        "fWo": np.ascontiguousarray(g("L0_ffn_wo")[fqs, :]),
        "fWg": np.ascontiguousarray(g("L0_ffn_wg")),
        "w1": np.ascontiguousarray(g("L0_ffn_w1").reshape(NE, NJ, 128, NH, 128).transpose(0, 3, 2, 1, 4).reshape(NE * NH, 128, NJ * 128)),
        "w3": np.ascontiguousarray(g("L0_ffn_w3").reshape(NE, NJ, 128, NH, 128).transpose(0, 3, 2, 1, 4).reshape(NE * NH, 128, NJ * 128)),
        "w2": np.ascontiguousarray(g("L0_ffn_w2").reshape(NE, NH, 128, NJ, 128).transpose(0, 3, 2, 1, 4).reshape(NE * NJ, 128, NH * 128)),
    }


ALL_PHASES = ("ln", "ffnproj", "flash", "ffnpart", "attproj", "scan", "attpart",
              "x2h2", "gate", "moe", "final")


def kernel(**inputs):
    nc, _ = build_program(phases=ALL_PHASES)
    in_maps = [_prep_inputs(inputs, ci) for ci in range(NCORES)]
    res = run_bass_kernel_spmd(nc, in_maps, core_ids=list(range(NCORES)))
    out = np.concatenate([np.asarray(res.results[ci]["o_own"]) for ci in range(NCORES)], axis=0)
    return out.reshape(1, T, D).astype(np.float32)
